# version 02
# baseline (speedup 1.0000x reference)
import math
import sys

sys.path.insert(0, "/opt/trn_rl_repo")

import numpy as np
import concourse.bass as bass
import concourse.mybir as mybir
from concourse.bass import IndirectOffsetOnAxis
from concourse.bass_utils import run_bass_kernel_spmd
from contextlib import ExitStack

F32 = mybir.dt.float32
BF16 = mybir.dt.bfloat16
I32 = mybir.dt.int32
AF = mybir.ActivationFunctionType
ALU = mybir.AluOpType
AX = mybir.AxisListType

NCORES = 8
B_PER_CORE = 2
L = 2048
D = 1024
T = B_PER_CORE * L
NT128 = T // 128
DG = D // 128

DEBUG_TAPS = {}



def _tri(n=128):
    r = np.arange(n)
    return r[:, None], r[None, :]


def make_consts():
    c = {}
    i, j = _tri()
    c["ident"] = np.eye(128, dtype=np.float32)
    c["tril01"] = (i <= j).astype(np.float32)
    c["stril01"] = (i < j).astype(np.float32)
    c["striu01"] = (i > j).astype(np.float32)
    c["diagbig"] = (-1e30 * np.eye(128)).astype(np.float32)
    c["ones"] = np.ones((128, 512), dtype=np.float32)
    c["iota8"] = np.arange(8, dtype=np.float32).reshape(1, 8)
    kk = np.arange(128)[:, None]
    qq = np.arange(512)[None, :]
    cm = np.zeros((128, 4, 512), np.float32)
    for j in range(4):
        cm[:, j, :] = np.where(kk <= qq - j * 128, 0.0, -1e30)
    c["cmask"] = cm.reshape(128, 4 * 512)
    seln = np.zeros((32, 32, 128), np.float32)
    for n in range(32):
        seln[n, n, :] = 1.0
    c["seln"] = seln.reshape(32, 32 * 128)
    d = np.arange(D)
    ang = 2.0 * np.pi * (d[:, None] * d[None, :] % D) / D
    c["fr"] = np.cos(ang).astype(np.float32)
    c["fs"] = np.sin(ang).astype(np.float32)
    c["fsneg"] = (-np.sin(ang)).astype(np.float32)
    c["gr"] = (np.cos(ang) / D).astype(np.float32)
    c["gsneg"] = (-np.sin(ang) / D).astype(np.float32)
    return c



class K:

    SEMS = ["sdma", "gdma", "pe", "act", "dve", "pool", "st"]
    DMA_SEMS = {"sdma", "gdma", "st"}

    def __init__(self, name):
        self.nc = bass.Bass("TRN2", target_bir_lowering=False)
        self.es = ExitStack()
        self.sems = {}
        self.cnt = {}
        for s in self.SEMS:
            self.sems[s] = self.es.enter_context(self.nc.semaphore("sem_" + s))
            self.cnt[s] = 0
        self._seen = {}
        self.inputs = {}
        self.outputs = {}
        self.sb = {}
        self._constcols = {}
        self.block = None
        self.taps = {}

    def din(self, name, shape, dtype=F32):
        ap = self.nc.dram_tensor(name, list(shape), dtype, kind="ExternalInput").ap()
        self.inputs[name] = ap
        return ap

    def dout(self, name, shape, dtype=F32):
        ap = self.nc.dram_tensor(name, list(shape), dtype, kind="ExternalOutput").ap()
        self.outputs[name] = ap
        return ap

    def dram(self, name, shape, dtype=F32):
        return self.nc.dram_tensor(name, list(shape), dtype, kind="Internal").ap()

    def sbuf(self, name, shape, dtype=F32):
        t = self.es.enter_context(self.nc.sbuf_tensor(name, list(shape), dtype))
        self.sb[name] = t
        return t

    def psum(self, name, shape, dtype=F32):
        return self.es.enter_context(self.nc.psum_tensor(name, list(shape), dtype))

    def inc(self, instr, sem):
        n = 16 if sem in self.DMA_SEMS else 1
        instr.then_inc(self.sems[sem], n)
        self.cnt[sem] += n
        return self.cnt[sem]

    def wait(self, eng, sem, val=None):
        eng.wait_ge(self.sems[sem], self.cnt[sem] if val is None else val)

    def snap(self):
        return dict(self.cnt)

    def wait_snap(self, eng, snap, sems=None, engname=None):
        seen = self._seen.setdefault(engname, {}) if engname else None
        for s in (sems or self.SEMS):
            v = snap.get(s, 0)
            if v > 0 and (seen is None or seen.get(s, 0) < v):
                eng.wait_ge(self.sems[s], v)
                if seen is not None:
                    seen[s] = v

    def barrier(self):
        snap = self.snap()
        for name in ("sync", "gpsimd", "scalar", "vector", "tensor"):
            mk = getattr(self.block, name)

            def _f(eng, snap=snap, name=name):
                self.wait_snap(eng, snap, engname=name)
            mk(_f)

    def constcol(self, value):
        key = float(value)
        if key not in self._constcols:
            raise KeyError(f"constcol {value} not pre-registered")
        return self._constcols[key]

    def build_preamble(self, const_values):
        pool = self.sbuf("constpool", [128, max(1, len(const_values))], F32)
        @self.block.vector
        def _(v):
            for i, val in enumerate(const_values):
                ins = v.memset(pool[:, i:i + 1], float(val))
                self._constcols[float(val)] = pool[:, i:i + 1]
            self.inc(ins, "dve")

    def tap_dram(self, name, shape, dtype=F32):
        if DEBUG_TAPS.get(name):
            ap = self.dout("tap_" + name, shape, dtype)
            self.taps[name] = ap
            return ap
        return None



CONST_TABLES = {
    "ident":   ((128, 128), (F32, BF16)),
    "tril01":  ((128, 128), (F32, BF16)),
    "stril01": ((128, 128), (F32,)),
    "striu01": ((128, 128), (F32, BF16)),
    "diagbig": ((128, 128), (BF16,)),
    "ones":    ((128, 512), (F32, BF16)),
    "iota8":   ((1, 8), (F32,)),
    "seln":    ((32, 32 * 128), (F32,)),
    "cmask":   ((128, 4 * 512), (BF16,)),
}


def ph_preamble(k: K, const_values, tables):
    pool = k.sbuf("constpool", [128, max(1, len(const_values))], F32)
    for name in tables:
        shape, dts = CONST_TABLES[name]
        ap = k.din("c_" + name, shape, F32)
        for dt in dts:
            sfx = "" if dt is F32 else "_bf"
            k.sbuf("cs_" + name + sfx, list(shape), dt)

    @k.block.vector
    def _(v):
        ins = None
        for i, val in enumerate(const_values):
            ins = v.memset(pool[:, i:i + 1], float(val))
            k._constcols[float(val)] = pool[:, i:i + 1]
            k.nc.const_aps.register(F32, float(val), pool[:, i:i + 1])
        if ins is not None:
            k.inc(ins, "dve")

    @k.block.sync
    def _(s):
        for name in tables:
            shape, dts = CONST_TABLES[name]
            if F32 in dts:
                ins = s.dma_start(out=k.sb["cs_" + name][:shape[0], :],
                                  in_=k.inputs["c_" + name])
                k.inc(ins, "sdma")

    @k.block.gpsimd
    def _(g):
        for name in tables:
            shape, dts = CONST_TABLES[name]
            if BF16 in dts:
                ins = g.dma_start(out=k.sb["cs_" + name + "_bf"][:shape[0], :],
                                  in_=k.inputs["c_" + name])
                k.inc(ins, "gdma")

    k.barrier()


def ones_col(k):
    return k.constcol(1.0)


def ones_row(k, n):
    return k.sb["cs_ones"][0:1, 0:n]



def ph_residual_ln(k: K, srcs, x_out, xn_sb, xn_tok_hbm=None, tag="ln", router=None):
    nc = k.nc
    nsrc = len(srcs)
    with ExitStack() as ph:
        sb = lambda n, s, d=F32: ph.enter_context(nc.sbuf_tensor(f"{tag}_{n}", s, d))
        ps = lambda n, s, d=F32: ph.enter_context(nc.psum_tensor(f"{tag}_{n}", s, d))
        ld = [sb(f"ld{b}", [128, nsrc, DG * 128]) for b in range(2)]
        xw = [sb(f"x{b}", [128, DG * 128]) for b in range(2)]
        xsq = sb("sq", [128, DG * 128])
        rows = sb("rows", [1, 5 * 128])
        tmp = sb("tmp", [128, 128])
        xnf = sb("xnf", [128, 128])
        prt = ps("prt", [128, 16]) if router is not None else None
        ps_stat = ps("pst", [1, 256])
        ps_bc = ps("pbc", [128, 256])
        if xn_tok_hbm is not None:
            xnt_ps = ps("tps", [128, DG * 128], BF16)
            xnt_sb = sb("tsb", [128, DG * 128], BF16)

        ld_done = [0] * NT128; sum_done = [0] * NT128; sq_done = [0] * NT128
        stat_done = [0] * NT128; row_done = [0] * NT128; bc_done = [0] * NT128
        xn_done = [0] * NT128; xst_done = [0] * NT128; tr_done = [0] * NT128
        tcp_done = [0] * NT128; tst_done = [0] * NT128

        for i in range(NT128):
            b = i % 2

            @k.block.sync
            def _(s, i=i, b=b):
                if i >= 2:
                    s.wait_ge(k.sems["dve"], xn_done[i - 2])
                    s.wait_ge(k.sems["act"], sq_done[i - 2])
                    s.wait_ge(k.sems["pe"], stat_done[i - 2])
                    s.wait_ge(k.sems["st"], xst_done[i - 2])
                for j in range(nsrc):
                    ld_done[i] = k.inc(s.dma_start(out=ld[b][:, j, :], in_=srcs[j](i)), "sdma")

            @k.block.vector
            def _(v, i=i, b=b):
                v.wait_ge(k.sems["sdma"], ld_done[i])
                if nsrc == 1:
                    ins = v.tensor_copy(out=xw[b][:], in_=ld[b][:, 0, :])
                else:
                    ins = v.tensor_add(out=xw[b][:], in0=ld[b][:, 0, :], in1=ld[b][:, 1, :])
                    for j in range(2, nsrc):
                        ins = v.tensor_add(out=xw[b][:], in0=xw[b][:], in1=ld[b][:, j, :])
                sum_done[i] = k.inc(ins, "dve")

            if x_out is not None:
                @k.block.sync
                def _(s, i=i, b=b):
                    s.wait_ge(k.sems["dve"], sum_done[i])
                    ins = s.dma_start(out=x_out(i), in_=xw[b][:])
                    xst_done[i] = k.inc(ins, "st")
            else:
                xst_done[i] = k.cnt["st"]

            @k.block.scalar
            def _(sc, i=i, b=b):
                sc.wait_ge(k.sems["dve"], sum_done[i])
                if i >= 1:
                    sc.wait_ge(k.sems["pe"], stat_done[i - 1])
                ins = sc.activation(out=xsq[:], in_=xw[b][:], func=AF.Square)
                sq_done[i] = k.inc(ins, "act")

            @k.block.tensor
            def _(t, i=i, b=b):
                t.wait_ge(k.sems["act"], sq_done[i])
                if i >= 1:
                    t.wait_ge(k.sems["dve"], row_done[i - 1])
                oc = ones_col(k)
                for g in range(DG):
                    t.matmul(ps_stat[0:1, 0:128], oc, xw[b][:, g * 128:(g + 1) * 128],
                             start=(g == 0), stop=(g == DG - 1))
                for g in range(DG):
                    ins = t.matmul(ps_stat[0:1, 128:256], oc, xsq[:, g * 128:(g + 1) * 128],
                                   start=(g == 0), stop=(g == DG - 1))
                stat_done[i] = k.inc(ins, "pe")

            @k.block.vector
            def _(v, i=i, b=b):
                v.wait_ge(k.sems["pe"], stat_done[i])
                if i >= 1:
                    v.wait_ge(k.sems["pe"], bc_done[i - 1])
                m = rows[0:1, 0:128]; m2 = rows[0:1, 128:256]
                var = rows[0:1, 256:384]; mrstd = rows[0:1, 512:640]
                v.tensor_scalar(out=m, in0=ps_stat[0:1, 0:128], scalar1=1.0 / D,
                                scalar2=None, op0=ALU.mult)
                v.tensor_mul(out=m2, in0=m, in1=m)
                v.tensor_scalar(out=var, in0=ps_stat[0:1, 128:256], scalar1=1.0 / D,
                                scalar2=None, op0=ALU.mult)
                ins = v.tensor_sub(out=var, in0=var, in1=m2)
                row_done[i] = k.inc(ins, "dve")

            @k.block.scalar
            def _(sc, i=i):
                sc.wait_ge(k.sems["dve"], row_done[i])
                var = rows[0:1, 256:384]; rstd = rows[0:1, 384:512]
                sc.activation(out=rstd, in_=var, func=AF.Ln, bias=1e-5)
                ins = sc.activation(out=rstd, in_=rstd, func=AF.Exp, scale=-0.5)
                sq_done[i] = k.inc(ins, "act")

            @k.block.vector
            def _(v, i=i):
                v.wait_ge(k.sems["act"], sq_done[i])
                rstd = rows[0:1, 384:512]; m = rows[0:1, 0:128]
                mrstd = rows[0:1, 512:640]
                ins = v.tensor_mul(out=mrstd, in0=m, in1=rstd)
                row_done[i] = k.inc(ins, "dve")

            @k.block.tensor
            def _(t, i=i):
                t.wait_ge(k.sems["dve"], row_done[i])
                if i >= 1:
                    t.wait_ge(k.sems["dve"], xn_done[i - 1])
                orow = ones_row(k, 128)
                t.matmul(ps_bc[:, 0:128], orow, rows[0:1, 384:512], start=True, stop=True)
                ins = t.matmul(ps_bc[:, 128:256], orow, rows[0:1, 512:640], start=True, stop=True)
                bc_done[i] = k.inc(ins, "pe")

            if router is None:
                @k.block.vector
                def _(v, i=i, b=b):
                    v.wait_ge(k.sems["pe"], bc_done[i])
                    ins = None
                    for g in range(DG):
                        v.tensor_mul(out=tmp[:], in0=xw[b][:, g * 128:(g + 1) * 128],
                                     in1=ps_bc[:, 0:128])
                        ins = v.tensor_sub(out=xn_sb[:, g, i * 128:(i + 1) * 128],
                                           in0=tmp[:], in1=ps_bc[:, 128:256])
                    xn_done[i] = k.inc(ins, "dve")
            else:
                rw_sb, logits_sb = router
                E = rw_sb.shape[2]
                for g in range(DG):
                    @k.block.vector
                    def _(v, i=i, b=b, g=g):
                        if g == 0:
                            v.wait_ge(k.sems["pe"], bc_done[i])
                        else:
                            v.wait_ge(k.sems["pe"], k.cnt["pe"])
                        v.tensor_mul(out=tmp[:], in0=xw[b][:, g * 128:(g + 1) * 128],
                                     in1=ps_bc[:, 0:128])
                        v.tensor_sub(out=xnf[:], in0=tmp[:], in1=ps_bc[:, 128:256])
                        ins = v.tensor_copy(out=xn_sb[:, g, i * 128:(i + 1) * 128],
                                            in_=xnf[:])
                        xn_done[i] = k.inc(ins, "dve")

                    @k.block.tensor
                    def _(t, i=i, g=g, E=E):
                        t.wait_ge(k.sems["dve"], k.cnt["dve"])
                        ins = t.matmul(prt[:, 0:E], xnf[:], rw_sb[:, g, :],
                                       start=(g == 0), stop=(g == DG - 1))
                        k.inc(ins, "pe")

                @k.block.vector
                def _(v, i=i, E=E):
                    v.wait_ge(k.sems["pe"], k.cnt["pe"])
                    ins = v.tensor_copy(out=logits_sb[:, i, :], in_=prt[:, 0:E])
                    xn_done[i] = k.inc(ins, "dve")

            if xn_tok_hbm is not None:
                @k.block.tensor
                def _(t, i=i):
                    t.wait_ge(k.sems["dve"], xn_done[i])
                    if i >= 1:
                        t.wait_ge(k.sems["act"], tcp_done[i - 1])
                    idb = k.sb["cs_ident_bf"]
                    for g in range(DG):
                        ins = t.transpose(xnt_ps[:, g * 128:(g + 1) * 128],
                                          xn_sb[:, g, i * 128:(i + 1) * 128], idb[:])
                    tr_done[i] = k.inc(ins, "pe")

                @k.block.scalar
                def _(sc, i=i):
                    sc.wait_ge(k.sems["pe"], tr_done[i])
                    if i >= 1:
                        sc.wait_ge(k.sems["st"], tst_done[i - 1])
                    ins = sc.copy(out=xnt_sb[:], in_=xnt_ps[:])
                    tcp_done[i] = k.inc(ins, "act")

                @k.block.sync
                def _(s, i=i):
                    s.wait_ge(k.sems["act"], tcp_done[i])
                    ins = s.dma_start(out=xn_tok_hbm[i * 128:(i + 1) * 128, :],
                                      in_=xnt_sb[:])
                    tst_done[i] = k.inc(ins, "st")

        k.barrier()



def ph_dump_sbuf(k: K, name, sb_ap):
    shape = list(sb_ap.shape)
    out = k.dout("tap_" + name, shape, sb_ap.dtype)

    @k.block.sync
    def _(s):
        ins = s.dma_start(out=out, in_=sb_ap)
        k.inc(ins, "st")
    k.barrier()


def ph_finish(k: K):
    k.barrier()



def swz(dram_ap, i):
    return dram_ap[i]


def build_A(stop_after=None, taps=()):
    k = K("A")
    xin = k.din("xin", [NT128, 128, DG * 128], F32)
    k.din("l0f_wr", [D, D], F32)
    k.din("l0f_wi", [D, D], F32)
    k.din("l0f_bias", [128, DG], F32)
    for nm in ("fr", "fs", "fsneg", "gr", "gsneg"):
        k.din("c_" + nm, [D, D], F32)
    k.din("l1f_wqT", [D, D], F32)
    k.din("l1f_routT", [D, 255], F32)
    k.din("l1f_Da", [128, 8 * 256], F32)
    k.din("l1f_Db", [128, 8 * 256], F32)
    k.din("l1f_lv", [256, D], F32)
    k.din("l1f_woutT", [D, D], F32)
    k.din("l1f_wob", [128, DG], F32)
    k.din("l1f_c", [128, 16], F32)
    k.din("rt_w", [D, 8], F32)
    k.din("rt_b", [1, 8], F32)
    k.din("l0a_wbc", [D, 32], F32)
    k.din("l0a_wdt", [D, 64], F32)
    k.din("l0a_wdp", [64, D], F32)
    k.din("l0a_bdp", [128, DG], F32)
    k.din("l0a_alog", [D, 16], F32)
    k.din("l0a_d", [128, DG], F32)
    k.din("l1a_qpT", [D, 1024], F32)
    k.din("l1a_kpT", [D, 64], F32)
    k.din("l1a_vpT", [D, 64], F32)
    k.din("l1a_opT", [D, 1024], F32)
    counts_out = k.dout("counts", [8, 1], F32)
    x2_out = k.dout("x2", [NT128, 128, DG * 128], F32)

    k.block = k.es.enter_context(k.nc.Block())
    ph_preamble(k, CONSTVALS, ["ident", "tril01", "stril01", "striu01",
                               "ones", "iota8", "seln", "cmask"])

    xn_sb = k.sbuf("xn", [128, DG, T], BF16)

    ph_residual_ln(k, [lambda i: xin[i]], None, xn_sb, tag="ln0")
    if "xn0" in taps:
        ph_dump_sbuf(k, "xn0", xn_sb[:])
    if stop_after == "ln0":
        ph_finish(k)
        return k
    if stop_after in ("l1f0", "l1a0", "rt0", "l2a0", "l2f0", "l3a0", "l3f0"):
        tmp = k.dout("bdump", [NT128, 128, DG * 128], F32)
        if stop_after == "l1f0":
            ph_l1_ffn(k, xn_sb, tmp)
        elif stop_after == "l1a0":
            ph_l1_attn(k, xn_sb, tmp)
        elif stop_after == "rt0":
            pass
        ph_finish(k)
        return k
    mk0a = k.dout if stop_after == "l0a" else k.dram
    mk0f = k.dout if stop_after == "l0f" else k.dram
    a0 = mk0a("a0", [NT128, 128, DG * 128], F32)
    f0 = mk0f("f0", [NT128, 128, DG * 128], F32)
    if stop_after != "l0f":
        ph_l0_attn(k, xn_sb, a0)
    if stop_after == "l0a":
        ph_finish(k)
        return k
    ph_l0_ffn(k, xn_sb, f0)
    if stop_after == "l0f":
        ph_finish(k)
        return k

    x1 = k.dram("x1", [NT128, 128, DG * 128], F32)
    ph_residual_ln(k, [lambda i: xin[i], lambda i: a0[i], lambda i: f0[i]],
                   lambda i: x1[i], xn_sb, tag="ln1")
    if "xn1" in taps:
        ph_dump_sbuf(k, "xn1", xn_sb[:])
    if stop_after == "ln1":
        ph_finish(k)
        return k
    mk1a = k.dout if stop_after == "l1a" else k.dram
    mk1f = k.dout if stop_after == "l1f" else k.dram
    a1 = mk1a("a1", [NT128, 128, DG * 128], F32)
    f1 = mk1f("f1", [NT128, 128, DG * 128], F32)
    if stop_after != "l1f":
        ph_l1_attn(k, xn_sb, a1)
    if stop_after == "l1a":
        ph_finish(k)
        return k
    ph_l1_ffn(k, xn_sb, f1)
    if stop_after == "l1f":
        ph_finish(k)
        return k

    rtw_f = k.sbuf("rtw_f", [128, DG, 8], F32)
    lg2_sb = k.sbuf("lg2", [128, NT128, 8], F32)

    @S(k, "sync")
    def _(s):
        k.inc(s.dma_start(out=rtw_f[:], in_=k.inputs["rt_w"].rearrange("(c p) n -> p c n", p=128)), "sdma")
    k.barrier()
    ph_residual_ln(k, [lambda i: x1[i], lambda i: a1[i], lambda i: f1[i]],
                   lambda i: x2_out[i], xn_sb, tag="ln2", router=(rtw_f, lg2_sb))
    if "xn2" in taps:
        ph_dump_sbuf(k, "xn2", xn_sb[:])
    ph_router_counts(k, lg2_sb, counts_out, taps)
    ph_finish(k)
    return k


CONSTVALS = [1.0, 0.0, -1.0, 0.5, -0.5, 1e-5, float(np.pi / 2), float(-np.pi), 1e-9, 2.0]



def ph_l0_ffn(k: K, xn_sb, f0):
    nc = k.nc
    wr_d = k.inputs["l0f_wr"]; wi_d = k.inputs["l0f_wi"]
    fr_d = k.inputs["c_fr"]; fs_d = k.inputs["c_fs"]; fsneg_d = k.inputs["c_fsneg"]
    gr_d = k.inputs["c_gr"]; gsneg_d = k.inputs["c_gsneg"]
    bias_d = k.inputs["l0f_bias"]

    with ExitStack() as ph:
        sb = lambda n, s, d=BF16: ph.enter_context(nc.sbuf_tensor("l0fs_" + n, s, d))
        ps = lambda n, s, d=F32: ph.enter_context(nc.psum_tensor("l0fs_" + n, s, d))
        wr = sb("wr", [128, DG, 1024]); wi = sb("wi", [128, DG, 1024])
        slab = sb("slab", [128, 3, DG, 512])
        t1 = sb("t1", [128, DG, 1024]); t2 = sb("t2", [128, DG, 1024])
        msb = sb("m", [128, DG, 1024])
        bias = sb("bias", [128, DG], F32)
        pst = ps("pst", [128, 2, 1024])
        ld_done = {}

        @k.block.gpsimd
        def _(g):
            k.inc(g.dma_start(out=wr[:], in_=wr_d.rearrange("(c p) n -> p c n", p=128)), "gdma")
            ins = g.dma_start(out=wi[:], in_=wi_d.rearrange("(c p) n -> p c n", p=128))
            ld_done["w"] = k.inc(ins, "gdma")

        @k.block.sync
        def _(s):
            ins = s.dma_start(out=bias[:], in_=bias_d)
            ld_done["b"] = k.inc(ins, "sdma")

        for dc in range(2):
            @k.block.gpsimd
            def _(g, dc=dc):
                if dc >= 1:
                    g.wait_ge(k.sems["pe"], k.cnt["pe"])
                for j, src in enumerate((fr_d, fs_d, fsneg_d)):
                    k.inc(g.dma_start(
                        out=slab[:, j, :, :],
                        in_=src[:, dc * 512:(dc + 1) * 512].rearrange("(c p) n -> p c n", p=128)), "gdma")

            for kt in range(DG):
                pb = kt % 2

                @k.block.tensor
                def _(t, dc=dc, kt=kt, pb=pb):
                    if kt == 0:
                        t.wait_ge(k.sems["gdma"], k.cnt["gdma"])
                    if kt >= 2 or dc >= 1:
                        t.wait_ge(k.sems["act"], k.cnt["act"])
                    for m in range(DG):
                        t.matmul(pst[:, pb, 0:512], wr[:, m, kt * 128:(kt + 1) * 128],
                                 slab[:, 0, m, :], start=(m == 0), stop=False)
                    for m in range(DG):
                        t.matmul(pst[:, pb, 0:512], wi[:, m, kt * 128:(kt + 1) * 128],
                                 slab[:, 1, m, :], start=False, stop=(m == DG - 1))
                    for m in range(DG):
                        t.matmul(pst[:, pb, 512:1024], wi[:, m, kt * 128:(kt + 1) * 128],
                                 slab[:, 0, m, :], start=(m == 0), stop=False)
                    for m in range(DG):
                        ins = t.matmul(pst[:, pb, 512:1024], wr[:, m, kt * 128:(kt + 1) * 128],
                                       slab[:, 2, m, :], start=False, stop=(m == DG - 1))
                    k.inc(ins, "pe")

                @k.block.scalar
                def _(sc, dc=dc, kt=kt, pb=pb):
                    sc.wait_ge(k.sems["pe"], k.cnt["pe"])
                    sc.copy(out=t1[:, kt, dc * 512:(dc + 1) * 512], in_=pst[:, pb, 0:512])
                    ins = sc.copy(out=t2[:, kt, dc * 512:(dc + 1) * 512], in_=pst[:, pb, 512:1024])
                    k.inc(ins, "act")
        k.barrier()

        for oc in range(2):
            @k.block.gpsimd
            def _(g, oc=oc):
                if oc >= 1:
                    g.wait_ge(k.sems["pe"], k.cnt["pe"])
                for j, src in enumerate((gr_d, gsneg_d)):
                    k.inc(g.dma_start(
                        out=slab[:, j, :, :],
                        in_=src[:, oc * 512:(oc + 1) * 512].rearrange("(c p) n -> p c n", p=128)), "gdma")

            for dt_ in range(DG):
                pb = dt_ % 2

                @k.block.tensor
                def _(t, oc=oc, dt_=dt_, pb=pb):
                    if dt_ == 0:
                        t.wait_ge(k.sems["gdma"], k.cnt["gdma"])
                    if dt_ >= 2 or oc >= 1:
                        t.wait_ge(k.sems["act"], k.cnt["act"])
                    for m in range(DG):
                        t.matmul(pst[:, pb, 0:512], t1[:, m, dt_ * 128:(dt_ + 1) * 128],
                                 slab[:, 0, m, :], start=(m == 0), stop=False)
                    for m in range(DG):
                        ins = t.matmul(pst[:, pb, 0:512], t2[:, m, dt_ * 128:(dt_ + 1) * 128],
                                       slab[:, 1, m, :], start=False, stop=(m == DG - 1))
                    k.inc(ins, "pe")

                @k.block.scalar
                def _(sc, oc=oc, dt_=dt_, pb=pb):
                    sc.wait_ge(k.sems["pe"], k.cnt["pe"])
                    ins = sc.copy(out=msb[:, dt_, oc * 512:(oc + 1) * 512], in_=pst[:, pb, 0:512])
                    k.inc(ins, "act")
        k.barrier()

        ya = sb("ya", [128, 2, 512], F32)
        st_done = {}
        for tc in range(T // 512):
            for og in range(DG):
                pb = (tc * DG + og) % 2

                @k.block.tensor
                def _(t, tc=tc, og=og, pb=pb):
                    t.wait_ge(k.sems["sdma"], ld_done["b"])
                    if (tc, og) not in ((0, 0), (0, 1)):
                        t.wait_ge(k.sems["act"], k.cnt["act"])
                    for g in range(DG):
                        ins = t.matmul(pst[:, pb, 0:512], msb[:, g, og * 128:(og + 1) * 128],
                                       xn_sb[:, g, tc * 512:(tc + 1) * 512],
                                       start=(g == 0), stop=(g == DG - 1))
                    k.inc(ins, "pe")

                @k.block.scalar
                def _(sc, tc=tc, og=og, pb=pb):
                    sc.wait_ge(k.sems["pe"], k.cnt["pe"])
                    if (tc, og) not in ((0, 0), (0, 1)):
                        sc.wait_ge(k.sems["st"], st_done[pb])
                    ins = sc.activation(out=ya[:, pb, :], in_=pst[:, pb, 0:512],
                                        func=AF.Identity, bias=bias[:, og:og + 1], scale=1.0)
                    k.inc(ins, "act")

                @k.block.sync
                def _(s, tc=tc, og=og, pb=pb):
                    s.wait_ge(k.sems["act"], k.cnt["act"])
                    i0 = tc * 4
                    dst = f0[i0:i0 + 4, :, og * 128:(og + 1) * 128].rearrange("i p n -> p i n")
                    ins = s.dma_start(out=dst,
                                      in_=ya[:, pb, :].rearrange("p (i n) -> p i n", i=4))
                    st_done[pb] = k.inc(ins, "st")
        k.barrier()



def col128(v):
    v = np.asarray(v, np.float32).reshape(-1)
    return np.ascontiguousarray(v.reshape(-1, 128).T)


def prep_weights_A(mi):
    w = {}
    w["l0f_wr"] = np.ascontiguousarray(mi["L0_ffn_w_freq_real"], np.float32)
    w["l0f_wi"] = np.ascontiguousarray(mi["L0_ffn_w_freq_imag"], np.float32)
    w["l0f_bias"] = col128(mi["L0_ffn_bias"])
    w["l1f_wqT"] = np.ascontiguousarray(mi["L1_ffn_Wq"].T, np.float32)
    w["l1f_routT"] = np.ascontiguousarray(mi["L1_ffn_routers"].T, np.float32)
    w["l1f_lv"] = np.ascontiguousarray(mi["L1_ffn_leaf_values"], np.float32)
    w["l1f_woutT"] = np.ascontiguousarray(mi["L1_ffn_Wout_w"].T, np.float32)
    w["l1f_wob"] = col128(mi["L1_ffn_Wout_b"])
    da = np.zeros((128, 8, 256), np.float32)
    db = np.zeros((128, 8, 256), np.float32)
    cc = np.zeros((128, 16), np.float32)
    for lev in range(8):
        n = 2 ** lev
        half = 256 // (2 * n)
        for m in range(256):
            j = m // (2 * half)
            side = (m // half) % 2
            jj = n - 1 + j
            val = 1.0 if side == 1 else -1.0
            if jj < 128:
                da[jj, lev, m] = val
            else:
                db[jj - 128, lev, m] = val
            cval = 0.0 if side == 1 else 1.0
            hf, p = divmod(m, 128)
            cc[p, hf * 8 + lev] = cval
    w["l1f_Da"] = da.reshape(128, 8 * 256)
    w["l1f_Db"] = db.reshape(128, 8 * 256)
    w["l1f_c"] = cc
    w["l0a_wbc"] = np.ascontiguousarray(mi["L0_attn_W_bc"], np.float32)
    w["l0a_wdt"] = np.ascontiguousarray(mi["L0_attn_W_dt"], np.float32)
    w["l0a_wdp"] = np.ascontiguousarray(mi["L0_attn_W_dtproj"], np.float32)
    w["l0a_bdp"] = col128(mi["L0_attn_b_dtproj"])
    w["l0a_alog"] = np.ascontiguousarray(mi["L0_attn_A_log"], np.float32)
    w["l0a_d"] = col128(mi["L0_attn_D_param"])
    w["l1a_qpT"] = np.ascontiguousarray(
        mi["L1_attn_query_proj"].transpose(2, 0, 1).reshape(D, 1024), np.float32)
    w["l1a_kpT"] = np.ascontiguousarray(mi["L1_attn_key_proj"], np.float32)
    w["l1a_vpT"] = np.ascontiguousarray(mi["L1_attn_value_proj"], np.float32)
    w["l1a_opT"] = np.ascontiguousarray(
        mi["L1_attn_output_proj"].transpose(1, 2, 0).reshape(1024, D), np.float32)
    w["rt_w"] = np.ascontiguousarray(mi["L2_ffn_router_w"], np.float32)
    w["rt_b"] = np.ascontiguousarray(mi["L2_ffn_router_b"].reshape(1, 8), np.float32)
    return w



def S(k: K, engname, sems=None):
    snap = k.snap()
    mk = getattr(k.block, engname)

    def deco(fn):
        def body(eng):
            k.wait_snap(eng, snap, sems, engname=engname)
            fn(eng)
        mk(body)
    return deco



def ph_l1_ffn(k: K, xn_sb, f1):
    nc = k.nc
    with ExitStack() as ph:
        sb = lambda n, s, d=BF16: ph.enter_context(nc.sbuf_tensor("l1fs_" + n, s, d))
        ps = lambda n, s, d=F32: ph.enter_context(nc.psum_tensor("l1fs_" + n, s, d))
        wq = sb("wq", [128, DG, 1024]); rt = sb("rt", [128, DG, 255])
        da = sb("da", [128, 8, 256]); db = sb("db", [128, 8, 256])
        lva = sb("lva", [128, 1024]); lvb = sb("lvb", [128, 1024])
        wo = sb("wo", [128, DG, 1024]); wob = sb("wob", [128, DG], F32)
        ccol = sb("ccol", [128, 16], F32)
        q = sb("q", [128, DG, 512]); gate = sb("gate", [128, 2, 512], F32)
        gbf = sb("gbf", [128, 2, 512]); lp = sb("lp", [128, 2, 512], F32)
        fsb = sb("fsb", [128, 512], F32); lpb = sb("lpb", [128, 2, 512])
        ret = sb("ret", [128, DG, 512]); ya = sb("ya", [128, 512], F32)
        pq = ps("pq", [128, 512]); pf = ps("pf", [128, 512])
        po = ps("po", [128, 512])

        @S(k, "gpsimd")
        def _(g):
            k.inc(g.dma_start(out=wq[:], in_=k.inputs["l1f_wqT"].rearrange("(c p) n -> p c n", p=128)), "gdma")
            k.inc(g.dma_start(out=rt[:], in_=k.inputs["l1f_routT"].rearrange("(c p) n -> p c n", p=128)), "gdma")
            k.inc(g.dma_start(out=da[:], in_=k.inputs["l1f_Da"].rearrange("p (l m) -> p l m", l=8)), "gdma")
            k.inc(g.dma_start(out=db[:], in_=k.inputs["l1f_Db"].rearrange("p (l m) -> p l m", l=8)), "gdma")
            k.inc(g.dma_start(out=lva[:], in_=k.inputs["l1f_lv"][0:128, :]), "gdma")
            k.inc(g.dma_start(out=lvb[:], in_=k.inputs["l1f_lv"][128:256, :]), "gdma")
            k.inc(g.dma_start(out=wo[:], in_=k.inputs["l1f_woutT"].rearrange("(c p) n -> p c n", p=128)), "gdma")

        @S(k, "sync")
        def _(s):
            k.inc(s.dma_start(out=wob[:], in_=k.inputs["l1f_wob"]), "sdma")
            k.inc(s.dma_start(out=ccol[:], in_=k.inputs["l1f_c"]), "sdma")
        k.barrier()

        for tc in range(T // 512):
            ts_ = slice(tc * 512, (tc + 1) * 512)

            for og in range(DG):
                @S(k, "tensor")
                def _(t, og=og, ts_=ts_):
                    for g in range(DG):
                        ins = t.matmul(pq[:], wq[:, g, og * 128:(og + 1) * 128],
                                       xn_sb[:, g, ts_], start=(g == 0), stop=(g == DG - 1))
                    k.inc(ins, "pe")

                @S(k, "scalar")
                def _(sc, og=og):
                    k.inc(sc.copy(out=q[:, og, :], in_=pq[:]), "act")

            for jt in range(2):
                nrow = 128 if jt == 0 else 127

                @S(k, "tensor")
                def _(t, jt=jt, nrow=nrow):
                    for g in range(DG):
                        ins = t.matmul(pq[0:nrow, :], rt[:, g, jt * 128:jt * 128 + nrow],
                                       q[:, g, :], start=(g == 0), stop=(g == DG - 1))
                    k.inc(ins, "pe")

                @S(k, "scalar")
                def _(sc, jt=jt, nrow=nrow):
                    k.inc(sc.activation(out=gate[0:nrow, jt, :], in_=pq[0:nrow, :],
                                        func=AF.Exp, scale=-1.0), "act")

            @S(k, "vector")
            def _(v):
                v.tensor_scalar(out=gate[:, :, :], in0=gate[:, :, :], scalar1=1.0,
                                scalar2=None, op0=ALU.add)
                v.reciprocal(out=gate[:, :, :], in_=gate[:, :, :])
                k.inc(v.tensor_copy(out=gbf[:], in_=gate[:]), "dve")

            for lev in range(8):
                for half in range(2):
                    @S(k, "tensor")
                    def _(t, lev=lev, half=half):
                        t.matmul(pf[:], da[:, lev, half * 128:(half + 1) * 128],
                                 gbf[:, 0, :], start=True, stop=False)
                        ins = t.matmul(pf[:], db[0:127, lev, half * 128:(half + 1) * 128],
                                       gbf[0:127, 1, :], start=False, stop=True)
                        k.inc(ins, "pe")

                    @S(k, "vector")
                    def _(v, lev=lev, half=half):
                        ins = v.tensor_scalar(out=fsb[:], in0=pf[:],
                                              scalar1=ccol[:, half * 8 + lev:half * 8 + lev + 1],
                                              scalar2=None, op0=ALU.add)
                        if lev == 0:
                            ins = v.tensor_copy(out=lp[:, half, :], in_=fsb[:])
                        else:
                            ins = v.tensor_mul(out=lp[:, half, :], in0=lp[:, half, :], in1=fsb[:])
                        k.inc(ins, "dve")

            @S(k, "vector")
            def _(v):
                k.inc(v.tensor_copy(out=lpb[:], in_=lp[:]), "dve")

            for og in range(DG):
                @S(k, "tensor")
                def _(t, og=og):
                    t.matmul(pq[:], lva[:, og * 128:(og + 1) * 128], lpb[:, 0, :],
                             start=True, stop=False)
                    ins = t.matmul(pq[:], lvb[:, og * 128:(og + 1) * 128], lpb[:, 1, :],
                                   start=False, stop=True)
                    k.inc(ins, "pe")

                @S(k, "scalar")
                def _(sc, og=og):
                    k.inc(sc.copy(out=ret[:, og, :], in_=pq[:]), "act")

            for og in range(DG):
                @S(k, "tensor")
                def _(t, og=og):
                    for g in range(DG):
                        ins = t.matmul(po[:], wo[:, g, og * 128:(og + 1) * 128],
                                       ret[:, g, :], start=(g == 0), stop=(g == DG - 1))
                    k.inc(ins, "pe")

                @S(k, "scalar")
                def _(sc, og=og):
                    k.inc(sc.activation(out=ya[:], in_=po[:], func=AF.Identity,
                                        bias=wob[:, og:og + 1], scale=1.0), "act")

                @S(k, "sync")
                def _(s, tc=tc, og=og):
                    i0 = tc * 4
                    dst = f1[i0:i0 + 4, :, og * 128:(og + 1) * 128].rearrange("i p n -> p i n")
                    k.inc(s.dma_start(out=dst, in_=ya[:].rearrange("p (i n) -> p i n", i=4)), "st")
        k.barrier()



def ph_router_counts(k: K, logits_sb, counts_out, taps=()):
    nc = k.nc
    with ExitStack() as ph:
        sb = lambda n, s, d=F32: ph.enter_context(nc.sbuf_tensor("rt_" + n, s, d))
        ps = lambda n, s, d=F32: ph.enter_context(nc.psum_tensor("rt_" + n, s, d))
        brow = sb("brow", [1, 8], F32)
        bbc = sb("bbc", [128, 8], F32)
        iob = sb("iob", [128, 8], F32)
        iobig = sb("iobig", [128, 8], F32)
        lg = sb("lg", [128, 8], F32)
        mx = sb("mx", [128, 1], F32)
        cand = sb("cand", [128, 8], F32)
        eid = sb("eid", [128, 1], F32)
        oh = sb("oh", [128, 8], F32)
        cnt_sb = sb("cnt", [8, 1], F32)
        plg = ps("plg", [128, 8])
        pbc = ps("pbc", [128, 16])
        pcnt = ps("pcnt", [8, 1])
        tap_eid = None
        if "eid" in taps:
            tap_eid = k.dout("tap_eid", [NT128, 128, 1], F32)

        @S(k, "sync")
        def _(s):
            k.inc(s.dma_start(out=brow[:], in_=k.inputs["rt_b"]), "sdma")
        k.barrier()

        @S(k, "tensor")
        def _(t):
            t.matmul(pbc[:, 0:8], ones_row(k, 128), brow[:], start=True, stop=True)
            k.inc(t.matmul(pbc[:, 8:16], ones_row(k, 128), k.sb["cs_iota8"][0:1, :],
                           start=True, stop=True), "pe")

        @S(k, "vector")
        def _(v):
            v.tensor_copy(out=bbc[:], in_=pbc[:, 0:8])
            v.tensor_copy(out=iob[:], in_=pbc[:, 8:16])
            k.inc(v.tensor_scalar(out=iobig[:], in0=pbc[:, 8:16], scalar1=1e9,
                                  scalar2=None, op0=ALU.add), "dve")

        for i in range(NT128):
            @S(k, "vector")
            def _(v, i=i):
                k.inc(v.tensor_add(out=lg[:], in0=logits_sb[:, i, :], in1=bbc[:]), "dve")
                v.wait_ge(k.sems["dve"], k.cnt["dve"])
                k.inc(v.reduce_max(out=mx[:], in_=lg[:], axis=AX.X), "dve")
                v.wait_ge(k.sems["dve"], k.cnt["dve"])
                k.inc(v.tensor_scalar(out=oh[:], in0=lg[:], scalar1=mx[:, 0:1], scalar2=None,
                                      op0=ALU.is_ge), "dve")
                v.wait_ge(k.sems["dve"], k.cnt["dve"])
                k.inc(v.scalar_tensor_tensor(out=cand[:], in0=oh[:], scalar=65536.0,
                                             in1=iob[:], op0=ALU.mult, op1=ALU.subtract), "dve")
                v.wait_ge(k.sems["dve"], k.cnt["dve"])
                k.inc(v.reduce_max(out=eid[:], in_=cand[:], axis=AX.X), "dve")
                v.wait_ge(k.sems["dve"], k.cnt["dve"])
                k.inc(v.tensor_scalar(out=eid[:], in0=eid[:], scalar1=-1.0, scalar2=65536.0,
                                      op0=ALU.mult, op1=ALU.add), "dve")
                v.wait_ge(k.sems["dve"], k.cnt["dve"])
                k.inc(v.tensor_scalar(out=oh[:], in0=iob[:], scalar1=eid[:, 0:1],
                                      scalar2=None, op0=ALU.is_equal), "dve")

            @S(k, "tensor")
            def _(t, i=i):
                k.inc(t.matmul(pcnt[:], oh[:], k.constcol(1.0), start=(i == 0),
                               stop=(i == NT128 - 1)), "pe")

            if tap_eid is not None:
                @S(k, "sync")
                def _(s, i=i):
                    k.inc(s.dma_start(out=tap_eid[i], in_=eid[:]), "st")

        @S(k, "scalar")
        def _(sc):
            k.inc(sc.copy(out=cnt_sb[:], in_=pcnt[:]), "act")

        @S(k, "sync")
        def _(s):
            k.inc(s.dma_start(out=counts_out, in_=cnt_sb[:]), "st")
        k.barrier()



def ph_l0_attn(k: K, xn_sb, a0):
    nc = k.nc
    NB, NC512 = B_PER_CORE, L // 512
    xbc_hbm = k.dram("l0a_xbc", [32, T], F32)
    with ExitStack() as ph:
        sb = lambda n, s, d=F32: ph.enter_context(nc.sbuf_tensor("l0as_" + n, s, d))
        ps = lambda n, s, d=F32: ph.enter_context(nc.psum_tensor("l0as_" + n, s, d))
        wbc = sb("wbc", [128, DG, 32], BF16)
        wdt = sb("wdt", [128, DG, 64], BF16)
        wdp = sb("wdp", [64, DG, 128], BF16)
        bdp = sb("bdp", [128, DG])
        acoef = sb("acoef", [128, DG, 16])
        dcol = sb("dcol", [128, DG])
        dtl = sb("dtl", [64, T], BF16)
        xbcrow = sb("xbcrow", [32, 512])
        delta = sb("delta", [128, T])
        dx = sb("dx", [128, T])
        t1 = sb("t1", [128, 512]); t2 = sb("t2", [128, 512])
        abar = sb("abar", [128, 512]); bx = sb("bx", [128, 512])
        hn = sb("hn", [128, 512]); yt = sb("yt", [128, 512])
        yacc = sb("yacc", [128, 512]); hcar = sb("hcar", [128, 16])
        pj = ps("pj", [128, 512])
        pbc = ps("pbc", [128, 2, 512])

        @S(k, "gpsimd")
        def _(g):
            k.inc(g.dma_start(out=wbc[:], in_=k.inputs["l0a_wbc"].rearrange("(c p) n -> p c n", p=128)), "gdma")
            k.inc(g.dma_start(out=wdt[:], in_=k.inputs["l0a_wdt"].rearrange("(c p) n -> p c n", p=128)), "gdma")
            k.inc(g.dma_start(out=wdp[:], in_=k.inputs["l0a_wdp"].rearrange("r (c n) -> r c n", c=DG)), "gdma")

        @S(k, "sync")
        def _(s):
            k.inc(s.dma_start(out=bdp[:], in_=k.inputs["l0a_bdp"]), "sdma")
            k.inc(s.dma_start(out=acoef[:], in_=k.inputs["l0a_alog"].rearrange("(c p) n -> p c n", p=128)), "sdma")
            k.inc(s.dma_start(out=dcol[:], in_=k.inputs["l0a_d"]), "sdma")
        k.barrier()

        @S(k, "scalar")
        def _(sc):
            k.inc(sc.activation(out=acoef[:], in_=acoef[:], func=AF.Exp), "act")

        @S(k, "vector")
        def _(v):
            k.inc(v.tensor_scalar(out=acoef[:], in0=acoef[:], scalar1=-1.0,
                                  scalar2=None, op0=ALU.mult), "dve")

        for tc in range(T // 512):
            ts_ = slice(tc * 512, (tc + 1) * 512)

            @S(k, "tensor")
            def _(t, ts_=ts_):
                for g in range(DG):
                    t.matmul(pj[0:32, :], wbc[:, g, :], xn_sb[:, g, ts_],
                             start=(g == 0), stop=(g == DG - 1))
                for g in range(DG):
                    ins = t.matmul(pj[64:128, :], wdt[:, g, :], xn_sb[:, g, ts_],
                                   start=(g == 0), stop=(g == DG - 1))
                k.inc(ins, "pe")

            @S(k, "scalar")
            def _(sc, ts_=ts_):
                k.inc(sc.copy(out=xbcrow[:], in_=pj[0:32, :]), "act")
                k.inc(sc.copy(out=dtl[:, ts_], in_=pj[64:128, :]), "act")

            @S(k, "sync")
            def _(s, ts_=ts_):
                k.inc(s.dma_start(out=xbc_hbm[:, ts_], in_=xbcrow[:]), "st")

        k.barrier()

        for dg in range(DG):
            for tc in range(T // 512):
                ts_ = slice(tc * 512, (tc + 1) * 512)

                @S(k, "tensor")
                def _(t, dg=dg, ts_=ts_):
                    k.inc(t.matmul(pj[:], wdp[:, dg, :], dtl[:, ts_],
                                   start=True, stop=True), "pe")

                @S(k, "vector")
                def _(v, dg=dg, ts_=ts_):
                    k.inc(v.tensor_scalar(out=t1[:], in0=pj[:], scalar1=bdp[:, dg:dg + 1],
                                          scalar2=None, op0=ALU.add), "dve")

                @S(k, "scalar")
                def _(sc, ts_=ts_):
                    sc.activation(out=t2[:], in_=t1[:], func=AF.Abs)
                    sc.activation(out=t2[:], in_=t2[:], func=AF.Exp, scale=-1.0)
                    k.inc(sc.activation(out=t2[:], in_=t2[:], func=AF.Ln, bias=1.0), "act")

                @S(k, "vector")
                def _(v, dg=dg, ts_=ts_):
                    k.inc(v.tensor_scalar(out=t1[:], in0=t1[:], scalar1=0.0,
                                          scalar2=None, op0=ALU.max), "dve")
                    v.wait_ge(k.sems["dve"], k.cnt["dve"])
                    k.inc(v.tensor_add(out=delta[:, ts_], in0=t1[:], in1=t2[:]), "dve")
                    v.wait_ge(k.sems["dve"], k.cnt["dve"])
                    k.inc(v.tensor_mul(out=dx[:, ts_], in0=delta[:, ts_],
                                       in1=xn_sb[:, dg, ts_]), "dve")

            for b in range(NB):
                @S(k, "vector")
                def _(v):
                    k.inc(v.memset(hcar[:], 0.0), "dve")
                for c5 in range(NC512):
                    tglob = slice(b * L + c5 * 512, b * L + (c5 + 1) * 512)

                    @S(k, "sync")
                    def _(s, b=b, c5=c5):
                        k.inc(s.dma_start(out=xbcrow[:],
                                          in_=xbc_hbm[:, b * L + c5 * 512: b * L + (c5 + 1) * 512]), "sdma")

                    for n in range(16):
                        @S(k, "tensor")
                        def _(t, n=n):
                            sel = k.sb["cs_seln"]
                            t.matmul(pbc[:, 0, :], sel[:, n * 128:(n + 1) * 128],
                                     xbcrow[:], start=True, stop=True)
                            k.inc(t.matmul(pbc[:, 1, :], sel[:, (16 + n) * 128:(17 + n) * 128],
                                           xbcrow[:], start=True, stop=True), "pe")

                        @S(k, "scalar")
                        def _(sc, dg=dg, n=n, tglob=tglob):
                            k.inc(sc.activation(out=abar[:], in_=delta[:, tglob], func=AF.Exp,
                                                scale=acoef[:, dg, n:n + 1]), "act")

                        @S(k, "vector")
                        def _(v, dg=dg, n=n, tglob=tglob, first=(n == 0)):
                            k.inc(v.tensor_mul(out=bx[:], in0=dx[:, tglob], in1=pbc[:, 0, :]), "dve")
                            v.wait_ge(k.sems["dve"], k.cnt["dve"])
                            k.inc(v.tensor_tensor_scan(out=hn[:], data0=abar[:], data1=bx[:],
                                                       initial=hcar[:, n:n + 1],
                                                       op0=ALU.mult, op1=ALU.add), "dve")
                            v.wait_ge(k.sems["dve"], k.cnt["dve"])
                            k.inc(v.tensor_copy(out=hcar[:, n:n + 1], in_=hn[:, 511:512]), "dve")
                            k.inc(v.tensor_mul(out=yt[:], in0=hn[:], in1=pbc[:, 1, :]), "dve")
                            v.wait_ge(k.sems["dve"], k.cnt["dve"])
                            if first:
                                k.inc(v.tensor_copy(out=yacc[:], in_=yt[:]), "dve")
                            else:
                                k.inc(v.tensor_add(out=yacc[:], in0=yacc[:], in1=yt[:]), "dve")
                            v.wait_ge(k.sems["dve"], k.cnt["dve"])

                    @S(k, "vector")
                    def _(v, dg=dg, tglob=tglob):
                        k.inc(v.scalar_tensor_tensor(out=yt[:], in0=xn_sb[:, dg, tglob],
                                                     scalar=dcol[:, dg:dg + 1], in1=yacc[:],
                                                     op0=ALU.mult, op1=ALU.add), "dve")

                    @S(k, "sync")
                    def _(s, dg=dg, b=b, c5=c5):
                        i0 = (b * L + c5 * 512) // 128
                        dst = a0[i0:i0 + 4, :, dg * 128:(dg + 1) * 128].rearrange("i p n -> p i n")
                        k.inc(s.dma_start(out=dst, in_=yt[:].rearrange("p (i n) -> p i n", i=4)), "st")
        k.barrier()



FLASH_LIMIT = None
FLASH_STAGE = "all"


def ph_flash_attn(k: K, xn_sb, out_dram, pfx, shared_kv, causal, bias_col=None):
    nc = k.nc
    osb_hbm = k.dram(pfx + "osb", [D, L], BF16)
    H, HD = 16, 64
    NSEQ = B_PER_CORE
    if FLASH_LIMIT is not None:
        NSEQ, H = FLASH_LIMIT
    HK = HD if shared_kv else H * HD
    NQC = L // 512
    NKT = L // 128
    with ExitStack() as ph:
        sb = lambda n, s, d=BF16: ph.enter_context(nc.sbuf_tensor(pfx + "s" + n, s, d))
        ps = lambda n, s, d=F32: ph.enter_context(nc.psum_tensor(pfx + "s" + n, s, d))
        if shared_kv:
            qpT = sb("qpT", [128, DG, 1024])[:, :, :]
            kpT = sb("kpT", [128, DG, HK])[:, :, :]
            vpT = sb("vpT", [128, DG, HK])[:, :, :]
            opT = sb("opT", [128, DG, 1024])[:, :, :]
        else:
            wkv = sb("wkv", [128, DG, 2048])
            qpT = wkv[:, :, 0:1024]
            kpT = wkv[:, :, 1024:2048]
            vpT = wkv[:, :, 0:1024]
            opT = wkv[:, :, 0:1024]
        kfeat = sb("kfeat", [128, 1, L]) if shared_kv else None
        vtok = sb("vtok", [128, NKT, HK])
        kaug = sb("kaug", [66, L])
        qaug = sb("qaug", [66, L])
        sqbuf = sb("sqbuf", [64, 512], F32)
        norms = sb("norms", [1, L + 520], F32)
        ebf = sb("ebf", [128, 2, 512])
        otile = sb("otile", [64, 512], F32)
        obf = sb("obf", [64, 512])
        oin = sb("oin", [128, DG, 512])
        ya = sb("ya", [128, 512], F32)
        mtile = ya
        pk = ps("pk", [128, 512])
        pe_ = ps("pe", [128, 2, 512])
        pden = ps("pden", [1, 512])
        pqn = ps("pqn", [1, 512])
        pov = ps("pov", [64, 512])
        pbc = ps("pbc", [64, 512])
        po = ps("po", [128, 512])

        if shared_kv:
            @S(k, "gpsimd")
            def _(g):
                k.inc(g.dma_start(out=qpT, in_=k.inputs[pfx + "qpT"].rearrange("(c p) n -> p c n", p=128)), "gdma")
                k.inc(g.dma_start(out=kpT, in_=k.inputs[pfx + "kpT"].rearrange("(c p) n -> p c n", p=128)), "gdma")
                k.inc(g.dma_start(out=vpT, in_=k.inputs[pfx + "vpT"].rearrange("(c p) n -> p c n", p=128)), "gdma")
                k.inc(g.dma_start(out=opT, in_=k.inputs[pfx + "opT"].rearrange("(c p) n -> p c n", p=128)), "gdma")
            k.barrier()

        for s in range(NSEQ):
            toff = s * L
            if not shared_kv:
                @S(k, "gpsimd")
                def _(g):
                    k.inc(g.dma_start(out=vpT, in_=k.inputs[pfx + "vpT"].rearrange("(c p) n -> p c n", p=128)), "gdma")
                k.barrier()
            if shared_kv:
                for tc in range(L // 512):
                    @S(k, "tensor")
                    def _(t, tc=tc):
                        for g in range(DG):
                            ins = t.matmul(pk[0:64, :], kpT[:, g, 0:64],
                                           xn_sb[:, g, toff + tc * 512: toff + (tc + 1) * 512],
                                           start=(g == 0), stop=(g == DG - 1))
                        k.inc(ins, "pe")

                    @S(k, "scalar")
                    def _(sc, tc=tc):
                        k.inc(sc.copy(out=kfeat[0:64, 0, tc * 512:(tc + 1) * 512],
                                      in_=pk[0:64, :]), "act")
            for it in range(NKT):
                for hc in range(max(1, HK // 512)):
                    w_ = min(512, HK)

                    @S(k, "tensor")
                    def _(t, it=it, hc=hc, w_=w_):
                        for g in range(DG):
                            ins = t.matmul(pk[:, 0:w_],
                                           xn_sb[:, g, toff + it * 128: toff + (it + 1) * 128],
                                           vpT[:, g, hc * 512:hc * 512 + w_],
                                           start=(g == 0), stop=(g == DG - 1))
                        k.inc(ins, "pe")

                    @S(k, "scalar")
                    def _(sc, it=it, hc=hc, w_=w_):
                        k.inc(sc.copy(out=vtok[:, it, hc * 512:hc * 512 + w_],
                                      in_=pk[:, 0:w_]), "act")

            if FLASH_STAGE == "kv":
                continue
            if not shared_kv:
                @S(k, "gpsimd")
                def _(g):
                    k.inc(g.dma_start(out=qpT, in_=k.inputs[pfx + "qpT"].rearrange("(c p) n -> p c n", p=128)), "gdma")
                    k.inc(g.dma_start(out=kpT, in_=k.inputs[pfx + "kpT"].rearrange("(c p) n -> p c n", p=128)), "gdma")
                k.barrier()

            for h in range(H):
                def kslc(sl):
                    return kaug[0:64, sl]
                @S(k, "vector")
                def _(v):
                    k.inc(v.memset(kaug[64:65, :], 1.0), "dve")

                if shared_kv:
                    @S(k, "scalar")
                    def _(sc):
                        k.inc(sc.copy(out=kaug[0:64, :], in_=kfeat[0:64, 0, :]), "act")
                else:
                    for tc in range(L // 512):
                        @S(k, "tensor")
                        def _(t, h=h, tc=tc):
                            for g in range(DG):
                                ins = t.matmul(pk[0:64, :], kpT[:, g, h * 64:(h + 1) * 64],
                                               xn_sb[:, g, toff + tc * 512: toff + (tc + 1) * 512],
                                               start=(g == 0), stop=(g == DG - 1))
                            k.inc(ins, "pe")

                        @S(k, "scalar")
                        def _(sc, tc=tc):
                            k.inc(sc.copy(out=kaug[0:64, tc * 512:(tc + 1) * 512],
                                          in_=pk[0:64, :]), "act")

                if FLASH_STAGE == "kaug1":
                    continue
                for tc in range(L // 512):
                    @S(k, "scalar")
                    def _(sc, tc=tc):
                        k.inc(sc.activation(out=sqbuf[:], in_=kslc(slice(tc * 512, (tc + 1) * 512)),
                                            func=AF.Square), "act")

                    if FLASH_STAGE == "kaug2a":
                        continue

                    @S(k, "tensor")
                    def _(t, tc=tc):
                        k.inc(t.matmul(pden[0:1, :], k.constcol(1.0)[0:64, :], sqbuf[:],
                                       start=True, stop=True), "pe")

                    if FLASH_STAGE == "kaug2b":
                        continue

                    @S(k, "vector")
                    def _(v, tc=tc):
                        k.inc(v.reduce_max(out=norms[0:1, L + 513:L + 514],
                                           in_=pden[0:1, :], axis=AX.X), "dve")
                        v.wait_ge(k.sems["dve"], k.cnt["dve"])
                        if tc == 0:
                            k.inc(v.tensor_copy(out=norms[0:1, L + 512:L + 513],
                                                in_=norms[0:1, L + 513:L + 514]), "dve")
                        else:
                            k.inc(v.tensor_max(out=norms[0:1, L + 512:L + 513],
                                               in0=norms[0:1, L + 512:L + 513],
                                               in1=norms[0:1, L + 513:L + 514]), "dve")

                if FLASH_STAGE == "kaug2":
                    continue

                @S(k, "scalar")
                def _(sc):
                    sc.activation(out=norms[0:1, L + 514:L + 515],
                                  in_=norms[0:1, L + 512:L + 513], func=AF.Ln)
                    k.inc(sc.activation(out=norms[0:1, L + 514:L + 515],
                                        in_=norms[0:1, L + 514:L + 515],
                                        func=AF.Exp, scale=0.5), "act")

                if FLASH_STAGE == "kaug":
                    continue
                for tc in range(L // 512):
                    @S(k, "tensor")
                    def _(t, h=h, tc=tc):
                        for g in range(DG):
                            ins = t.matmul(pk[0:64, :], qpT[:, g, h * 64:(h + 1) * 64],
                                           xn_sb[:, g, toff + tc * 512: toff + (tc + 1) * 512],
                                           start=(g == 0), stop=(g == DG - 1))
                        k.inc(ins, "pe")

                    @S(k, "scalar")
                    def _(sc, tc=tc):
                        k.inc(sc.activation(out=qaug[0:64, tc * 512:(tc + 1) * 512],
                                            in_=pk[0:64, :], func=AF.Identity, scale=0.125), "act")
                        k.inc(sc.activation(out=sqbuf[:], in_=pk[0:64, :], func=AF.Square), "act")

                    @S(k, "tensor")
                    def _(t, tc=tc):
                        k.inc(t.matmul(pqn[0:1, :], k.constcol(1.0)[0:64, :], sqbuf[:],
                                       start=True, stop=True), "pe")

                    @S(k, "vector")
                    def _(v, tc=tc):
                        k.inc(v.tensor_copy(out=norms[0:1, tc * 512:(tc + 1) * 512],
                                            in_=pqn[0:1, :]), "dve")

                @S(k, "scalar")
                def _(sc):
                    sc.activation(out=norms[0:1, 0:L], in_=norms[0:1, 0:L], func=AF.Ln)
                    k.inc(sc.activation(out=norms[0:1, 0:L], in_=norms[0:1, 0:L],
                                        func=AF.Exp, scale=0.5), "act")

                @S(k, "vector")
                def _(v):
                    k.inc(v.tensor_scalar(out=norms[0:1, 0:L], in0=norms[0:1, 0:L],
                                          scalar1=norms[0:1, L + 514:L + 515],
                                          scalar2=-0.125, op0=ALU.mult, op1=ALU.mult), "dve")
                    v.wait_ge(k.sems["dve"], k.cnt["dve"])
                    k.inc(v.tensor_copy(out=qaug[64:65, :], in_=norms[0:1, 0:L]), "dve")

                if DEBUG_TAPS.get("flashdbg") and s == 0 and h == 0:
                    ph_dump_sbuf(k, "kaug", kaug[0:66, :])
                    ph_dump_sbuf(k, "qaug", qaug[0:66, :])
                    ph_dump_sbuf(k, "norms", norms[0:1, :])
                if FLASH_STAGE == "cq":
                    continue
                for qc in range(NQC):
                    ktmax = NKT if not causal else min(NKT, (qc + 1) * 4)
                    pe_ct = {}; act_ct = {}; mm2_ct = {}
                    for kt in range(ktmax):
                        pb = kt % 2

                        @S(k, "tensor")
                        def _(t, h=h, qc=qc, kt=kt, pb=pb):
                            k.inc(t.matmul(pe_[:, pb, :], kaug[0:65, kt * 128:(kt + 1) * 128],
                                           qaug[0:65, qc * 512:(qc + 1) * 512],
                                           start=True, stop=True), "pe")

                        off = kt * 128 - qc * 512
                        if causal and 0 <= off <= 384:
                            @S(k, "vector")
                            def _(v, off=off, pb=pb):
                                j = off // 128
                                k.inc(v.tensor_add(out=mtile[:], in0=pe_[:, pb, :],
                                                   in1=k.sb["cs_cmask_bf"][:, j * 512:(j + 1) * 512]), "dve")

                            @S(k, "scalar")
                            def _(sc, pb=pb):
                                k.inc(sc.activation(out=ebf[:, pb, :], in_=mtile[:],
                                                    func=AF.Exp), "act")
                        else:
                            @S(k, "scalar")
                            def _(sc, pb=pb):
                                k.inc(sc.activation(out=ebf[:, pb, :], in_=pe_[:, pb, :],
                                                    func=AF.Exp), "act")

                        @S(k, "tensor")
                        def _(t, s=s, kt=kt, pb=pb, first=(kt == 0), last=(kt == ktmax - 1)):
                            t.matmul(pden[0:1, :], k.sb["cs_ones_bf"][:, 0:1], ebf[:, pb, :],
                                     start=first, stop=last)
                            k.inc(t.matmul(pov[:], vtok[:, kt, (0 if shared_kv else h) * 64:
                                                        (1 if shared_kv else h + 1) * 64],
                                           ebf[:, pb, :], start=first, stop=last), "pe")

                    @S(k, "vector")
                    def _(v):
                        k.inc(v.reciprocal(out=norms[0:1, L:L + 512], in_=pden[0:1, :]), "dve")

                    @S(k, "tensor")
                    def _(t):
                        k.inc(t.matmul(pbc[:], ones_row(k, 64), norms[0:1, L:L + 512],
                                       start=True, stop=True), "pe")

                    @S(k, "scalar")
                    def _(sc):
                        k.inc(sc.copy(out=otile[:], in_=pov[:]), "act")

                    @S(k, "vector")
                    def _(v, h=h, qc=qc):
                        k.inc(v.tensor_mul(out=obf[:], in0=otile[:], in1=pbc[:]), "dve")

                    @S(k, "sync")
                    def _(s, h=h, qc=qc):
                        k.inc(s.dma_start(out=osb_hbm[h * 64:(h + 1) * 64,
                                                      qc * 512:(qc + 1) * 512],
                                          in_=obf[:]), "st")

            if not shared_kv:
                @S(k, "gpsimd")
                def _(g):
                    k.inc(g.dma_start(out=opT, in_=k.inputs[pfx + "opT"].rearrange("(c p) n -> p c n", p=128)), "gdma")
                k.barrier()
            for tc in range(L // 512):
                @S(k, "sync")
                def _(s, tc=tc):
                    k.inc(s.dma_start(out=oin[:],
                                      in_=osb_hbm[:, tc * 512:(tc + 1) * 512]
                                      .rearrange("(g p) t -> p g t", p=128)), "sdma")

                for og in range(DG):
                    @S(k, "tensor")
                    def _(t, og=og, tc=tc):
                        for g in range(DG):
                            ins = t.matmul(po[:], opT[:, g, og * 128:(og + 1) * 128],
                                           oin[:, g, :],
                                           start=(g == 0), stop=(g == DG - 1))
                        k.inc(ins, "pe")

                    @S(k, "scalar")
                    def _(sc, og=og):
                        if bias_col is not None:
                            ins = sc.activation(out=ya[:], in_=po[:], func=AF.Identity,
                                                bias=bias_col[:, og:og + 1], scale=1.0)
                        else:
                            ins = sc.copy(out=ya[:], in_=po[:])
                        k.inc(ins, "act")

                    @S(k, "sync")
                    def _(sy, s=s, og=og, tc=tc):
                        i0 = (s * L + tc * 512) // 128
                        dst = out_dram[i0:i0 + 4, :, og * 128:(og + 1) * 128].rearrange("i p n -> p i n")
                        k.inc(sy.dma_start(out=dst, in_=ya[:].rearrange("p (i n) -> p i n", i=4)), "st")
        k.barrier()


def ph_l1_attn(k: K, xn_sb, a1):
    ph_flash_attn(k, xn_sb, a1, "l1a_", shared_kv=True, causal=False)


def ph_l3_attn(k: K, xn_sb, a3, bo_sb):
    ph_flash_attn(k, xn_sb, a3, "l3a_", shared_kv=False, causal=True, bias_col=bo_sb)



def ph_residual_sum(k: K, srcs, x_out, tag="sum"):
    nc = k.nc
    nsrc = len(srcs)
    with ExitStack() as ph:
        ld = [ph.enter_context(nc.sbuf_tensor(f"{tag}_ld{b}", [128, nsrc, DG * 128], F32))
              for b in range(2)]
        xw = [ph.enter_context(nc.sbuf_tensor(f"{tag}_x{b}", [128, DG * 128], F32))
              for b in range(2)]
        for i in range(NT128):
            b = i % 2

            @S(k, "sync")
            def _(s, i=i, b=b):
                for j in range(nsrc):
                    k.inc(s.dma_start(out=ld[b][:, j, :], in_=srcs[j](i)), "sdma")

            @S(k, "vector")
            def _(v, i=i, b=b):
                v.tensor_add(out=xw[b][:], in0=ld[b][:, 0, :], in1=ld[b][:, 1, :])
                for j in range(2, nsrc):
                    v.tensor_add(out=xw[b][:], in0=xw[b][:], in1=ld[b][:, j, :])
                k.inc(v.nop() if False else v.tensor_copy(out=xw[b][:], in_=xw[b][:]), "dve")

            @S(k, "sync")
            def _(s, i=i, b=b):
                k.inc(s.dma_start(out=x_out(i), in_=xw[b][:]), "st")
    k.barrier()



def build_B(stop_after=None, taps=()):
    k = K("B")
    x2 = k.din("x2", [NT128, 128, DG * 128], F32)
    counts_all = k.din("counts_all", [8, 8], F32)
    premask = k.din("premask", [8, 1], F32)
    xout = k.dout("xout", [NT128, 128, DG * 128], F32)
    for nm in ("qpT", "kpT", "vpT", "opT"):
        k.din("l3a_" + nm, [D, 1024], F32)
    k.din("l3a_bo", [128, DG], F32)
    k.din("l3f_wgT", [D, 8], F32)
    k.din("l3f_w1T", [8, D, 4096], F32)
    k.din("l3f_b1", [8, 128, 32], F32)
    k.din("l3f_woT", [2048, 1024], F32)
    k.din("l3f_bo", [128, DG], F32)
    decl_l2(k)

    k.block = k.es.enter_context(k.nc.Block())
    ph_preamble(k, CONSTVALS, ["ident", "tril01", "stril01", "striu01",
                               "diagbig", "ones", "iota8", "cmask"])
    xn_sb = k.sbuf("xn", [128, DG, T], BF16)
    bo_sb = k.sbuf("l3a_bo_sb", [128, DG], F32)
    xn_tok = k.dram("xn_tok", [T, D], BF16)

    @S(k, "sync")
    def _(s):
        k.inc(s.dma_start(out=bo_sb[:], in_=k.inputs["l3a_bo"]), "sdma")

    rtw_f = k.sbuf("rtw_f", [128, DG, 8], F32)
    wg_f = k.sbuf("wg_f", [128, DG, 8], F32)
    lg2_sb = k.sbuf("lg2", [128, NT128, 8], F32)
    lg3_sb = k.sbuf("lg3", [128, NT128, 8], F32)

    @S(k, "sync")
    def _(s):
        k.inc(s.dma_start(out=rtw_f[:], in_=k.inputs["rt_w"].rearrange("(c p) n -> p c n", p=128)), "sdma")
        k.inc(s.dma_start(out=wg_f[:], in_=k.inputs["l3f_wgT"].rearrange("(c p) n -> p c n", p=128)), "sdma")
    k.barrier()
    ph_residual_ln(k, [lambda i: x2[i]], None, xn_sb, xn_tok_hbm=xn_tok, tag="ln2",
                   router=(rtw_f, lg2_sb))
    if "xn2" in taps:
        ph_dump_sbuf(k, "xn2", xn_sb[:])
    if stop_after == "ln2":
        ph_finish(k)
        return k
    if stop_after in ("l3a2", "l3f2", "l2a2", "l2f2"):
        tmp = k.dout("bdump", [NT128, 128, DG * 128], F32)
        if stop_after == "l3a2":
            ph_l3_attn(k, xn_sb, tmp, bo_sb)
        elif stop_after == "l3f2":
            ph_l3_ffn(k, xn_sb, tmp, lg3_sb, test_logits=(wg_f,))
        elif stop_after == "l2a2":
            ph_l2_attn(k, xn_sb, tmp)
        elif stop_after == "l2f2":
            ph_l2_ffn(k, xn_sb, xn_tok, counts_all, premask, tmp, lg2_sb)
        ph_finish(k)
        return k

    a2 = k.dram("a2", [NT128, 128, DG * 128], F32)
    f2 = k.dram("f2", [NT128, 128, DG * 128], F32)
    ph_l2_ffn(k, xn_sb, xn_tok, counts_all, premask, f2, lg2_sb)
    ph_l2_attn(k, xn_sb, a2)

    x3 = k.dram("x3", [NT128, 128, DG * 128], F32)
    ph_residual_ln(k, [lambda i: x2[i], lambda i: a2[i], lambda i: f2[i]],
                   lambda i: x3[i], xn_sb, tag="ln3", router=(wg_f, lg3_sb))
    a3 = k.dram("a3", [NT128, 128, DG * 128], F32)
    f3 = k.dram("f3", [NT128, 128, DG * 128], F32)
    ph_l3_attn(k, xn_sb, a3, bo_sb)
    ph_l3_ffn(k, xn_sb, f3, lg3_sb)
    ph_residual_sum(k, [lambda i: x3[i], lambda i: a3[i], lambda i: f3[i]],
                    lambda i: xout[i], tag="fin")
    ph_finish(k)
    return k


def decl_l2(k: K):
    k.din("rt_w", [D, 8], F32)
    k.din("rt_b", [1, 8], F32)
    k.din("l2a_wipT", [D, 5312], F32)
    k.din("l2a_dtb", [1, 32], F32)
    k.din("l2a_alog", [1, 32], F32)
    k.din("l2a_dd", [1, 32], F32)
    k.din("l2a_bnw", [1, 64], F32)
    k.din("l2a_cnw", [1, 64], F32)
    k.din("l2a_bbias", [1, 2048], F32)
    k.din("l2a_cbias", [1, 2048], F32)
    k.din("l2a_normw", [128, 16], F32)
    k.din("l2a_wsmall", [D, 1216], F32)
    k.din("l2a_wopT", [2048, 1024], F32)
    k.din("l2f_w1", [8, D, 4096], F32)
    k.din("l2f_b1", [8, 128, 32], F32)
    k.din("l2f_w2", [8, 4096, 1024], F32)
    k.din("l2f_b2", [8, 1, 1024], F32)
    k.din("l2f_ow1", [D, 4096], F32)
    k.din("l2f_ob1", [128, 32], F32)
    k.din("l2f_ow2", [4096, 1024], F32)
    k.din("l2f_ob2", [1, 1024], F32)


def prep_weights_B(mi):
    w = {}
    w["rt_w"] = np.ascontiguousarray(mi["L2_ffn_router_w"], np.float32)
    w["rt_b"] = np.ascontiguousarray(mi["L2_ffn_router_b"].reshape(1, 8), np.float32)
    w["l3a_qpT"] = np.ascontiguousarray(mi["L3_attn_Wq"].T, np.float32)
    w["l3a_kpT"] = np.ascontiguousarray(mi["L3_attn_Wk"].T, np.float32)
    w["l3a_vpT"] = np.ascontiguousarray(mi["L3_attn_Wv"].T, np.float32)
    w["l3a_opT"] = np.ascontiguousarray(mi["L3_attn_Wo"].T, np.float32)
    w["l3a_bo"] = col128(mi["L3_attn_bo"])
    w["l3f_wgT"] = np.ascontiguousarray(mi["L3_ffn_Wg"].T, np.float32)
    w["l3f_w1T"] = np.ascontiguousarray(mi["L3_ffn_W1"].transpose(0, 2, 1), np.float32)
    w["l3f_b1"] = np.ascontiguousarray(
        np.stack([col128(mi["L3_ffn_b1"][e]) for e in range(8)]), np.float32)
    w["l3f_woT"] = np.ascontiguousarray(mi["L3_ffn_Wo"].T, np.float32)
    w["l3f_bo"] = col128(mi["L3_ffn_bo"])
    w["l2a_wipT"] = np.ascontiguousarray(mi["L2_attn_in_proj_w"].T, np.float32)
    w["l2a_dtb"] = mi["L2_attn_dt_bias"].reshape(1, 32).astype(np.float32)
    w["l2a_alog"] = mi["L2_attn_A_log"].reshape(1, 32).astype(np.float32)
    w["l2a_dd"] = mi["L2_attn_D"].reshape(1, 32).astype(np.float32)
    w["l2a_bnw"] = mi["L2_attn_B_norm_w"].reshape(1, 64).astype(np.float32)
    w["l2a_cnw"] = mi["L2_attn_C_norm_w"].reshape(1, 64).astype(np.float32)
    w["l2a_bbias"] = mi["L2_attn_B_bias"].reshape(1, 2048).astype(np.float32)
    w["l2a_cbias"] = mi["L2_attn_C_bias"].reshape(1, 2048).astype(np.float32)
    w["l2a_normw"] = col128(mi["L2_attn_norm_w"])
    wipT = np.asarray(mi["L2_attn_in_proj_w"].T, np.float32)
    w["l2a_wsmall"] = np.ascontiguousarray(np.concatenate(
        [wipT[:, 4224:4256], wipT[:, 5280:5312], wipT[:, 4096:4160],
         wipT[:, 4160:4224], wipT[:, 4256:5280]], axis=1))
    w["l2a_wopT"] = np.ascontiguousarray(mi["L2_attn_out_proj_w"].T, np.float32)
    w["l2f_w1"] = np.ascontiguousarray(mi["L2_ffn_w1"], np.float32)
    w["l2f_b1"] = np.ascontiguousarray(
        np.stack([col128(mi["L2_ffn_b1"][e]) for e in range(8)]), np.float32)
    w["l2f_w2"] = np.ascontiguousarray(mi["L2_ffn_w2"], np.float32)
    w["l2f_b2"] = mi["L2_ffn_b2"].reshape(8, 1, 1024).astype(np.float32)
    w["l2f_ow1"] = np.ascontiguousarray(mi["L2_ffn_ow1"], np.float32)
    w["l2f_ob1"] = col128(mi["L2_ffn_ob1"])
    w["l2f_ow2"] = np.ascontiguousarray(mi["L2_ffn_ow2"], np.float32)
    w["l2f_ob2"] = mi["L2_ffn_ob2"].reshape(1, 1024).astype(np.float32)
    return w



def ph_l3_ffn(k: K, xn_sb, f3, logits_sb, test_logits=None):
    nc = k.nc
    g_hbm = k.dram("l3f_glu", [2048, T], BF16)
    combf_hbm = k.dram("l3f_combf", [8, T], F32)
    with ExitStack() as ph:
        sb = lambda n, s, d=BF16: ph.enter_context(nc.sbuf_tensor("l3fs_" + n, s, d))
        ps = lambda n, s, d=F32: ph.enter_context(nc.psum_tensor("l3fs_" + n, s, d))
        wbig_t = sb("wbig", [128, 8 * 4096])
        wbig = wbig_t[:].rearrange("p (c n) -> p c n", c=8)
        b1c = sb("b1c", [128, 32], F32)
        boc = sb("boc", [128, DG], F32)
        iob = sb("iob", [128, 8], F32)
        gs = sb("gs", [128, 8], F32)
        cols = sb("cols", [128, 8], F32)
        comb = sb("comb", [128, 8], F32)
        crow = sb("crow", [1, 512], F32)
        hsb = sb("hsb", [128, 32, 512])
        glu = sb("glu", [128, 16, 512])
        tfp = sb("tfp", [128, 512], F32)
        eo = sb("eo", [128, 512], F32)
        zero = sb("zero", [128, DG * 128], F32)
        pg = ps("pg", [128, 8])
        ph_ = ps("ph", [128, 512])
        pcb = ps("pcb", [128, 512])
        ptr = ps("ptr", [1, 128])

        @S(k, "sync")
        def _(s):
            k.inc(s.dma_start(out=boc[:], in_=k.inputs["l3f_bo"]), "sdma")
        k.barrier()

        @S(k, "tensor")
        def _(t):
            k.inc(t.matmul(pg[:], ones_row(k, 128), k.sb["cs_iota8"][0:1, :],
                           start=True, stop=True), "pe")

        @S(k, "vector")
        def _(v):
            k.inc(v.tensor_copy(out=iob[:], in_=pg[:]), "dve")
            k.inc(v.memset(zero[:], 0.0), "dve")
        for i in range(NT128):
            @S(k, "sync")
            def _(s, i=i):
                k.inc(s.dma_start(out=f3[i], in_=zero[:]), "st")
        k.barrier()

        tap_comb = k.dout("tap_comb", [NT128, 128, 8], F32) if DEBUG_TAPS.get("comb") else None
        if test_logits is not None:
            wgf = test_logits[0]
            xnb = sb("xnb32", [128, 128], F32)
            for i in range(NT128):
                for g in range(DG):
                    @S(k, "vector")
                    def _(v, i=i, g=g):
                        k.inc(v.tensor_copy(out=xnb[:], in_=xn_sb[:, g, i * 128:(i + 1) * 128]), "dve")

                    @S(k, "tensor")
                    def _(t, i=i, g=g):
                        k.inc(t.matmul(pg[:], xnb[:], wgf[:, g, :],
                                       start=(g == 0), stop=(g == DG - 1)), "pe")

                @S(k, "vector")
                def _(v, i=i):
                    k.inc(v.tensor_copy(out=logits_sb[:, i, :], in_=pg[:]), "dve")
            k.barrier()
        for i in range(NT128):
            @S(k, "vector")
            def _(v, i=i):
                k.inc(v.tensor_copy(out=gs[:], in_=logits_sb[:, i, :]), "dve")
                v.wait_ge(k.sems["dve"], k.cnt["dve"])
                k.inc(v.reduce_max(out=cols[:, 0:1], in_=gs[:], axis=AX.X), "dve")
                v.wait_ge(k.sems["dve"], k.cnt["dve"])
                k.inc(v.tensor_scalar(out=cols[:, 0:1], in0=cols[:, 0:1], scalar1=-1.0,
                                      scalar2=None, op0=ALU.mult), "dve")

            @S(k, "scalar")
            def _(sc, i=i):
                k.inc(sc.activation(out=gs[:], in_=gs[:], func=AF.Exp,
                                    bias=cols[:, 0:1], scale=1.0,
                                    accum_out=cols[:, 1:2]), "act")

            @S(k, "vector")
            def _(v, i=i):
                def step(ins):
                    c = k.inc(ins, "dve")
                    v.wait_ge(k.sems["dve"], c)
                step(v.reciprocal(out=cols[:, 1:2], in_=cols[:, 1:2]))
                step(v.tensor_scalar(out=gs[:], in0=gs[:], scalar1=cols[:, 1:2],
                                     scalar2=None, op0=ALU.mult))
                step(v.reduce_max(out=cols[:, 2:3], in_=gs[:], axis=AX.X))
                step(v.tensor_scalar(out=tfp[:, 0:8], in0=gs[:], scalar1=cols[:, 2:3],
                                     scalar2=None, op0=ALU.is_ge))
                step(v.scalar_tensor_tensor(out=tfp[:, 8:16], in0=tfp[:, 0:8],
                                            scalar=65536.0, in1=iob[:],
                                            op0=ALU.mult, op1=ALU.subtract))
                step(v.reduce_max(out=cols[:, 3:4], in_=tfp[:, 8:16], axis=AX.X))
                step(v.tensor_scalar(out=cols[:, 3:4], in0=cols[:, 3:4], scalar1=-1.0,
                                     scalar2=65536.0, op0=ALU.mult, op1=ALU.add))
                step(v.tensor_scalar(out=tfp[:, 0:8], in0=iob[:], scalar1=cols[:, 3:4],
                                     scalar2=None, op0=ALU.is_equal))
                step(v.scalar_tensor_tensor(out=tfp[:, 8:16], in0=tfp[:, 0:8],
                                            scalar=-1e9, in1=gs[:],
                                            op0=ALU.mult, op1=ALU.add))
                step(v.reduce_max(out=cols[:, 4:5], in_=tfp[:, 8:16], axis=AX.X))
                step(v.tensor_add(out=cols[:, 5:6], in0=cols[:, 2:3], in1=cols[:, 4:5]))
                step(v.tensor_scalar(out=cols[:, 5:6], in0=cols[:, 5:6], scalar1=1e-9,
                                     scalar2=None, op0=ALU.add))
                step(v.reciprocal(out=cols[:, 5:6], in_=cols[:, 5:6]))
                step(v.tensor_scalar(out=tfp[:, 0:8], in0=gs[:], scalar1=cols[:, 4:5],
                                     scalar2=None, op0=ALU.is_ge))
                step(v.tensor_mul(out=tfp[:, 8:16], in0=gs[:], in1=tfp[:, 0:8]))
                step(v.tensor_scalar(out=comb[:], in0=tfp[:, 8:16],
                                     scalar1=cols[:, 5:6], scalar2=None, op0=ALU.mult))

            if tap_comb is not None:
                @S(k, "sync")
                def _(s, i=i):
                    k.inc(s.dma_start(out=tap_comb[i], in_=comb[:]), "st")
            for e in range(8):
                @S(k, "tensor")
                def _(t, i=i, e=e):
                    k.inc(t.transpose(ptr[:], comb[:, e:e + 1], k.sb["cs_ident"][:]), "pe")

                @S(k, "vector")
                def _(v, i=i, e=e):
                    k.inc(v.tensor_copy(out=crow[0:1, 0:128], in_=ptr[0:1, :]), "dve")

                @S(k, "sync")
                def _(s, i=i, e=e):
                    k.inc(s.dma_start(out=combf_hbm[e:e + 1, i * 128:(i + 1) * 128],
                                      in_=crow[0:1, 0:128]), "st")
        k.barrier()

        for e in range(8):
            @S(k, "gpsimd")
            def _(g, e=e):
                k.inc(g.dma_start(out=wbig,
                                  in_=k.inputs["l3f_w1T"][e].rearrange("(c p) n -> p c n", p=128)), "gdma")

            @S(k, "sync")
            def _(s, e=e):
                k.inc(s.dma_start(out=b1c[:], in_=k.inputs["l3f_b1"][e]), "sdma")
            k.barrier()

            for tc in range(T // 512):
                ts_ = slice(tc * 512, (tc + 1) * 512)
                for jg in range(32):
                    @S(k, "tensor")
                    def _(t, jg=jg, ts_=ts_):
                        for g in range(DG):
                            ins = t.matmul(ph_[:], wbig[:, g, jg * 128:(jg + 1) * 128],
                                           xn_sb[:, g, ts_], start=(g == 0), stop=(g == DG - 1))
                        k.inc(ins, "pe")

                    @S(k, "scalar")
                    def _(sc, jg=jg):
                        k.inc(sc.activation(out=hsb[:, jg, :], in_=ph_[:], func=AF.Identity,
                                            bias=b1c[:, jg:jg + 1], scale=1.0), "act")

                for jh in range(16):
                    @S(k, "scalar")
                    def _(sc, jh=jh):
                        k.inc(sc.activation(out=tfp[:], in_=hsb[:, 16 + jh, :], func=AF.Exp,
                                            scale=-1.0), "act")

                    @S(k, "vector")
                    def _(v, jh=jh):
                        k.inc(v.tensor_scalar(out=tfp[:], in0=tfp[:], scalar1=1.0,
                                              scalar2=None, op0=ALU.add), "dve")
                        v.wait_ge(k.sems["dve"], k.cnt["dve"])
                        k.inc(v.reciprocal(out=tfp[:], in_=tfp[:]), "dve")
                        v.wait_ge(k.sems["dve"], k.cnt["dve"])
                        k.inc(v.tensor_mul(out=glu[:, jh, :], in0=hsb[:, jh, :], in1=tfp[:]), "dve")

                @S(k, "sync")
                def _(s, tc=tc, ts_=ts_):
                    k.inc(s.dma_start(out=g_hbm[:, ts_].rearrange("(c p) t -> p c t", p=128),
                                      in_=glu[:]), "st")

            @S(k, "gpsimd")
            def _(g, e=e):
                k.inc(g.dma_start(
                    out=wbig_t[:, 0:16384].rearrange("p (c n) -> p c n", c=16),
                    in_=k.inputs["l3f_woT"].rearrange("(c p) n -> p c n", p=128)), "gdma")
            k.barrier()
            wot = wbig_t[:, 0:16384].rearrange("p (c n) -> p c n", c=16)

            for tc in range(T // 512):
                ts_ = slice(tc * 512, (tc + 1) * 512)

                @S(k, "sync")
                def _(s, e=e, tc=tc, ts_=ts_):
                    k.inc(s.dma_start(out=glu[:],
                                      in_=g_hbm[:, ts_].rearrange("(c p) t -> p c t", p=128)), "sdma")
                    k.inc(s.dma_start(out=crow[0:1, 0:512],
                                      in_=combf_hbm[e:e + 1, ts_]), "sdma")

                @S(k, "tensor")
                def _(t, ts_=ts_):
                    k.inc(t.matmul(pcb[:], ones_row(k, 128), crow[0:1, 0:512],
                                   start=True, stop=True), "pe")

                for og in range(DG):
                    @S(k, "tensor")
                    def _(t, og=og):
                        for g2 in range(16):
                            ins = t.matmul(ph_[:], wot[:, g2, og * 128:(og + 1) * 128],
                                           glu[:, g2, :], start=(g2 == 0), stop=(g2 == 15))
                        k.inc(ins, "pe")

                    @S(k, "scalar")
                    def _(sc, og=og):
                        k.inc(sc.activation(out=eo[:], in_=ph_[:], func=AF.Identity,
                                            bias=boc[:, og:og + 1], scale=1.0), "act")

                    @S(k, "vector")
                    def _(v, og=og):
                        k.inc(v.tensor_mul(out=eo[:], in0=eo[:], in1=pcb[:]), "dve")

                    @S(k, "gpsimd")
                    def _(s, tc=tc, og=og):
                        i0 = tc * 4
                        dst = f3[i0:i0 + 4, :, og * 128:(og + 1) * 128].rearrange("i p n -> p i n")
                        k.inc(s.dma_start(out=dst, in_=eo[:].rearrange("p (i n) -> p i n", i=4),
                                          accum_op=ALU.add), "st")
        k.barrier()




def ph_l2_ffn(k: K, xn_sb, xn_tok, counts_all_d, premask_d, f2, logits_sb):
    nc = k.nc
    C = 1280
    bufloc = k.dram("l2f_bufloc", [8 * C, D], BF16)
    yall = k.dram("l2f_yall", [8 * C, D], BF16)
    h2_hbm = k.dram("l2f_h2", [4096, C], BF16)
    g2_hbm = k.dram("l2f_g2", [4096, T], BF16)
    with ExitStack() as ph:
        sb = lambda n, s, d=BF16: ph.enter_context(nc.sbuf_tensor("l2fs_" + n, s, d))
        ps = lambda n, s, d=F32: ph.enter_context(nc.psum_tensor("l2fs_" + n, s, d))
        wbig_t = sb("wbig", [128, 8 * 4096])
        w1v = wbig_t[:].rearrange("p (c n) -> p c n", c=8)
        w2v = wbig_t[:].rearrange("p (c n) -> p c n", c=32)
        cnts = sb("cnts", [8, 8], F32)
        pmask = sb("pmask", [8, 1], F32)
        brow = sb("brow", [1, 8], F32)
        smalls = sb("smalls", [128, 40], F32)
        cols = sb("cols", [128, 8], F32)
        carry = sb("carry", [1, 8], F32)
        slots = sb("slots", [128, NT128], I32)
        valid = sb("valid", [128, NT128], F32)
        vcomp = sb("vcomp", [128, NT128], F32)
        xtk = sb("xtk", [128, 1024])
        bufeT = sb("bufeT", [128, DG, C])
        hrow = sb("hrow", [128, C])
        hTl = sb("hTl", [128, 32, 128])
        ysb = sb("ysb", [128, 1024], F32)
        b2bc = sb("b2bc", [128, 1024], F32)
        ygat = sb("ygat", [128, 1024])
        zero = sb("zero", [128, 1024])
        otf = sb("otf", [128, 1024], F32)
        b1c = sb("b1c", [128, 32], F32)
        b2row = sb("b2row", [1, 1024], F32)
        pall = ps("pall", [128, 4, 512])
        ptr = ps("ptr", [128, 128])
        ptrb = ps("ptrb", [128, 128], BF16)
        prow = ps("prow", [1, 16])

        @S(k, "sync")
        def _(s):
            k.inc(s.dma_start(out=cnts[:], in_=counts_all_d), "sdma")
            k.inc(s.dma_start(out=pmask[:], in_=premask_d), "sdma")
            k.inc(s.dma_start(out=brow[:], in_=k.inputs["rt_b"]), "sdma")

        bc_reg = [None]

        @S(k, "gpsimd")
        def _(g):
            bc_reg[0] = g.alloc_register("l2f_bc")
            g.reg_mov(bc_reg[0], 8 * C - 1)
        k.barrier()

        @S(k, "tensor")
        def _(t):
            k.inc(t.matmul(prow[0:1, 0:8], pmask[:], cnts[:], start=True, stop=True), "pe")

        @S(k, "vector")
        def _(v):
            k.inc(v.tensor_copy(out=smalls[0:1, 32:40], in_=prow[0:1, 0:8]), "dve")

        @S(k, "tensor")
        def _(t):
            t.matmul(pall[:, 0, 0:8], ones_row(k, 128), brow[:], start=True, stop=True)
            t.matmul(pall[:, 0, 8:16], ones_row(k, 128), k.sb["cs_iota8"][0:1, :],
                     start=True, stop=True)
            k.inc(t.matmul(pall[:, 0, 16:24], ones_row(k, 128), smalls[0:1, 32:40],
                           start=True, stop=True), "pe")

        @S(k, "vector")
        def _(v):
            k.inc(v.tensor_copy(out=smalls[:, 0:24], in_=pall[:, 0, 0:24]), "dve")
            k.inc(v.memset(zero[:], 0.0), "dve")
            k.inc(v.memset(carry[:], 0.0), "dve")

        for i in range(8 * C // 128):
            @S(k, "sync")
            def _(s, i=i):
                k.inc(s.dma_start(out=bufloc[i * 128:(i + 1) * 128, :], in_=zero[:]), "st")
        k.barrier()

        ge8 = smalls[:, 24:32]
        for i in range(NT128):
            @S(k, "vector")
            def _(v, i=i):
                def step(ins):
                    c = k.inc(ins, "dve")
                    v.wait_ge(k.sems["dve"], c)
                lg = cols[:, 0:8]
                step(v.tensor_add(out=lg, in0=logits_sb[:, i, :], in1=smalls[:, 0:8]))
                step(v.reduce_max(out=smalls[:, 36:37], in_=lg, axis=AX.X))
                step(v.tensor_scalar(out=ge8, in0=lg, scalar1=smalls[:, 36:37],
                                     scalar2=None, op0=ALU.is_ge))
                step(v.scalar_tensor_tensor(out=ge8, in0=ge8, scalar=65536.0,
                                            in1=smalls[:, 8:16], op0=ALU.mult,
                                            op1=ALU.subtract))
                step(v.reduce_max(out=smalls[:, 36:37], in_=ge8, axis=AX.X))
                step(v.tensor_scalar(out=smalls[:, 36:37], in0=smalls[:, 36:37],
                                     scalar1=-1.0, scalar2=65536.0,
                                     op0=ALU.mult, op1=ALU.add))
                step(v.tensor_scalar(out=ge8, in0=smalls[:, 8:16],
                                     scalar1=smalls[:, 36:37], scalar2=None,
                                     op0=ALU.is_equal))

            @S(k, "tensor")
            def _(t, i=i):
                t.matmul(ptr[:, 0:8], k.sb["cs_stril01"][:], ge8, start=True, stop=False)
                k.inc(t.matmul(ptr[:, 0:8], ones_row(k, 128), carry[:],
                               start=False, stop=True), "pe")

            @S(k, "vector")
            def _(v, i=i):
                def step(ins):
                    c = k.inc(ins, "dve")
                    v.wait_ge(k.sems["dve"], c)
                step(v.tensor_mul(out=cols[:, 0:8], in0=ge8, in1=ptr[:, 0:8]))
                step(v.reduce_sum(out=smalls[:, 37:38], in_=cols[:, 0:8], axis=AX.X))
                step(v.tensor_mul(out=cols[:, 0:8], in0=ge8, in1=smalls[:, 16:24]))
                step(v.reduce_sum(out=smalls[:, 38:39], in_=cols[:, 0:8], axis=AX.X))
                step(v.tensor_add(out=smalls[:, 39:40], in0=smalls[:, 37:38],
                                  in1=smalls[:, 38:39]))
                step(v.tensor_scalar(out=valid[:, i:i + 1], in0=smalls[:, 39:40],
                                     scalar1=1280.0, scalar2=None, op0=ALU.is_lt))
                step(v.tensor_scalar(out=vcomp[:, i:i + 1], in0=valid[:, i:i + 1],
                                     scalar1=-1.0, scalar2=1.0,
                                     op0=ALU.mult, op1=ALU.add))
                step(v.tensor_scalar(out=smalls[:, 36:37], in0=smalls[:, 36:37],
                                     scalar1=1280.0, scalar2=None, op0=ALU.mult))
                step(v.tensor_add(out=smalls[:, 36:37], in0=smalls[:, 36:37],
                                  in1=smalls[:, 37:38]))
                step(v.tensor_scalar(out=smalls[:, 38:39], in0=smalls[:, 37:38],
                                     scalar1=1280.0, scalar2=1e7, op0=ALU.is_ge,
                                     op1=ALU.mult))
                step(v.tensor_add(out=smalls[:, 36:37], in0=smalls[:, 36:37],
                                  in1=smalls[:, 38:39]))
                step(v.tensor_copy(out=slots[:, i:i + 1], in_=smalls[:, 36:37]))

            @S(k, "tensor")
            def _(t, i=i):
                k.inc(t.matmul(prow[0:1, 0:8], k.constcol(1.0), ge8,
                               start=(i == 0), stop=(i == NT128 - 1)), "pe")

            @S(k, "scalar")
            def _(sc, i=i):
                k.inc(sc.copy(out=carry[:], in_=prow[0:1, 0:8]), "act")

            @S(k, "sync")
            def _(s, i=i):
                k.inc(s.dma_start(out=xtk[:], in_=xn_tok[i * 128:(i + 1) * 128, :]), "sdma")

            @S(k, "gpsimd")
            def _(g, i=i):
                k.inc(g.indirect_dma_start(
                    out=bufloc, out_offset=IndirectOffsetOnAxis(slots[:, i:i + 1], 0),
                    in_=xtk[:], in_offset=None,
                    bounds_check=bc_reg[0], oob_is_err=False), "gdma")
        k.barrier()

        for e in range(8):
            for ct in range(C // 128):
                @S(k, "sync")
                def _(s, e=e, ct=ct):
                    k.inc(s.dma_start(out=xtk[:],
                                      in_=bufloc[e * C + ct * 128: e * C + (ct + 1) * 128, :]), "sdma")

                for g in range(DG):
                    @S(k, "tensor")
                    def _(t, ct=ct, g=g):
                        k.inc(t.transpose(ptrb[:], xtk[:, g * 128:(g + 1) * 128],
                                          k.sb["cs_ident_bf"][:]), "pe")

                    @S(k, "scalar")
                    def _(sc, ct=ct, g=g):
                        k.inc(sc.copy(out=bufeT[:, g, ct * 128:(ct + 1) * 128],
                                      in_=ptrb[:]), "act")

            @S(k, "gpsimd")
            def _(g, e=e):
                k.inc(g.dma_start(out=wbig_t[:].rearrange("p (c n) -> p c n", c=8),
                                  in_=k.inputs["l2f_w1"][e].rearrange("(c p) n -> p c n", p=128)), "gdma")

            @S(k, "sync")
            def _(s, e=e):
                k.inc(s.dma_start(out=b1c[:], in_=k.inputs["l2f_b1"][e]), "sdma")
            k.barrier()

            for jg in range(32):
                @S(k, "tensor")
                def _(t, jg=jg):
                    for cc in range(3):
                        w_ = 512 if cc < 2 else 256
                        for g in range(DG):
                            ins = t.matmul(pall[:, cc, 0:w_], w1v[:, g, jg * 128:(jg + 1) * 128],
                                           bufeT[:, g, cc * 512:cc * 512 + w_],
                                           start=(g == 0), stop=(g == DG - 1))
                    k.inc(ins, "pe")

                @S(k, "scalar")
                def _(sc, jg=jg):
                    sc.activation(out=hrow[:, 0:512], in_=pall[:, 0, :], func=AF.Relu,
                                  bias=b1c[:, jg:jg + 1], scale=1.0)
                    sc.activation(out=hrow[:, 512:1024], in_=pall[:, 1, :], func=AF.Relu,
                                  bias=b1c[:, jg:jg + 1], scale=1.0)
                    k.inc(sc.activation(out=hrow[:, 1024:1280], in_=pall[:, 2, 0:256],
                                        func=AF.Relu, bias=b1c[:, jg:jg + 1], scale=1.0), "act")

                @S(k, "sync")
                def _(s, jg=jg):
                    k.inc(s.dma_start(out=h2_hbm[jg * 128:(jg + 1) * 128, :], in_=hrow[:]), "st")

            @S(k, "gpsimd")
            def _(g, e=e):
                k.inc(g.dma_start(out=wbig_t[:].rearrange("p (c n) -> p c n", c=32),
                                  in_=k.inputs["l2f_w2"][e].rearrange("(c p) n -> p c n", p=128)), "gdma")

            @S(k, "sync")
            def _(s, e=e):
                k.inc(s.dma_start(out=b2row[:], in_=k.inputs["l2f_b2"][e]), "sdma")
            k.barrier()

            @S(k, "tensor")
            def _(t):
                t.matmul(pall[:, 2, :], ones_row(k, 128), b2row[0:1, 0:512],
                         start=True, stop=True)
                k.inc(t.matmul(pall[:, 3, :], ones_row(k, 128), b2row[0:1, 512:1024],
                               start=True, stop=True), "pe")

            @S(k, "vector")
            def _(v):
                v.tensor_copy(out=b2bc[:, 0:512], in_=pall[:, 2, :])
                k.inc(v.tensor_copy(out=b2bc[:, 512:1024], in_=pall[:, 3, :]), "dve")

            for ct in range(C // 128):
                @S(k, "sync")
                def _(s, ct=ct):
                    k.inc(s.dma_start(out=hTl[:],
                                      in_=h2_hbm[:, ct * 128:(ct + 1) * 128]
                                      .rearrange("(g p) c -> p g c", p=128)), "sdma")

                @S(k, "tensor")
                def _(t, ct=ct):
                    for jg in range(32):
                        ins = t.matmul(pall[:, 0, :], hTl[:, jg, :], w2v[:, jg, 0:512],
                                       start=(jg == 0), stop=(jg == 31))
                        ins = t.matmul(pall[:, 1, :], hTl[:, jg, :], w2v[:, jg, 512:1024],
                                       start=(jg == 0), stop=(jg == 31))
                    k.inc(ins, "pe")

                @S(k, "vector")
                def _(v, ct=ct):
                    v.tensor_add(out=ysb[:, 0:512], in0=pall[:, 0, :], in1=b2bc[:, 0:512])
                    k.inc(v.tensor_add(out=ysb[:, 512:1024], in0=pall[:, 1, :],
                                       in1=b2bc[:, 512:1024]), "dve")
                    v.wait_ge(k.sems["dve"], k.cnt["dve"])
                    k.inc(v.tensor_copy(out=ygat[:], in_=ysb[:]), "dve")

                @S(k, "sync")
                def _(s, e=e, ct=ct):
                    k.inc(s.dma_start(out=yall[e * C + ct * 128:e * C + (ct + 1) * 128, :],
                                      in_=ygat[:]), "st")
        k.barrier()

        @S(k, "gpsimd")
        def _(g):
            k.inc(g.dma_start(out=wbig_t[:].rearrange("p (c n) -> p c n", c=8),
                              in_=k.inputs["l2f_ow1"].rearrange("(c p) n -> p c n", p=128)), "gdma")

        @S(k, "sync")
        def _(s):
            k.inc(s.dma_start(out=b1c[:], in_=k.inputs["l2f_ob1"]), "sdma")
        k.barrier()

        for tc in range(T // 512):
            ts_ = slice(tc * 512, (tc + 1) * 512)
            for jg in range(32):
                @S(k, "tensor")
                def _(t, jg=jg, ts_=ts_):
                    for g in range(DG):
                        ins = t.matmul(pall[:, 0, :], w1v[:, g, jg * 128:(jg + 1) * 128],
                                       xn_sb[:, g, ts_], start=(g == 0), stop=(g == DG - 1))
                    k.inc(ins, "pe")

                @S(k, "scalar")
                def _(sc, jg=jg):
                    k.inc(sc.activation(out=hrow[:, 0:512], in_=pall[:, 0, :], func=AF.Relu,
                                        bias=b1c[:, jg:jg + 1], scale=1.0), "act")

                @S(k, "sync")
                def _(s, tc=tc, jg=jg):
                    k.inc(s.dma_start(out=g2_hbm[jg * 128:(jg + 1) * 128, tc * 512:(tc + 1) * 512],
                                      in_=hrow[:, 0:512]), "st")
        k.barrier()

        @S(k, "gpsimd")
        def _(g):
            k.inc(g.dma_start(out=wbig_t[:].rearrange("p (c n) -> p c n", c=32),
                              in_=k.inputs["l2f_ow2"].rearrange("(c p) n -> p c n", p=128)), "gdma")

        @S(k, "sync")
        def _(s):
            k.inc(s.dma_start(out=b2row[:], in_=k.inputs["l2f_ob2"]), "sdma")
        k.barrier()

        @S(k, "tensor")
        def _(t):
            t.matmul(pall[:, 2, :], ones_row(k, 128), b2row[0:1, 0:512], start=True, stop=True)
            k.inc(t.matmul(pall[:, 3, :], ones_row(k, 128), b2row[0:1, 512:1024],
                           start=True, stop=True), "pe")

        @S(k, "vector")
        def _(v):
            v.tensor_copy(out=b2bc[:, 0:512], in_=pall[:, 2, :])
            k.inc(v.tensor_copy(out=b2bc[:, 512:1024], in_=pall[:, 3, :]), "dve")

        for i in range(NT128):
            @S(k, "sync")
            def _(s, i=i):
                k.inc(s.dma_start(out=hTl[:],
                                  in_=g2_hbm[:, i * 128:(i + 1) * 128]
                                  .rearrange("(g p) t -> p g t", p=128)), "sdma")

            @S(k, "vector")
            def _(v, i=i):
                k.inc(v.memset(ygat[:], 0.0), "dve")

            @S(k, "gpsimd")
            def _(g, i=i):
                k.inc(g.indirect_dma_start(
                    out=ygat[:], out_offset=None,
                    in_=yall, in_offset=IndirectOffsetOnAxis(slots[:, i:i + 1], 0),
                    bounds_check=bc_reg[0], oob_is_err=False), "gdma")

            @S(k, "tensor")
            def _(t, i=i):
                for jg in range(32):
                    ins = t.matmul(pall[:, 0, :], hTl[:, jg, :], w2v[:, jg, 0:512],
                                   start=(jg == 0), stop=(jg == 31))
                    ins = t.matmul(pall[:, 1, :], hTl[:, jg, :], w2v[:, jg, 512:1024],
                                   start=(jg == 0), stop=(jg == 31))
                k.inc(ins, "pe")

            @S(k, "vector")
            def _(v, i=i):
                def step(ins):
                    c = k.inc(ins, "dve")
                    v.wait_ge(k.sems["dve"], c)
                step(v.tensor_add(out=ysb[:, 0:512], in0=pall[:, 0, :], in1=b2bc[:, 0:512]))
                step(v.tensor_add(out=ysb[:, 512:1024], in0=pall[:, 1, :], in1=b2bc[:, 512:1024]))
                step(v.tensor_scalar(out=ysb[:], in0=ysb[:], scalar1=vcomp[:, i:i + 1],
                                     scalar2=None, op0=ALU.mult))
                step(v.tensor_scalar(out=otf[:], in0=ygat[:], scalar1=valid[:, i:i + 1],
                                     scalar2=None, op0=ALU.mult))
                step(v.tensor_add(out=otf[:], in0=otf[:], in1=ysb[:]))

            for g in range(DG):
                @S(k, "tensor")
                def _(t, i=i, g=g):
                    k.inc(t.transpose(ptr[:], otf[:, g * 128:(g + 1) * 128],
                                      k.sb["cs_ident"][:]), "pe")

                @S(k, "scalar")
                def _(sc, i=i, g=g):
                    k.inc(sc.copy(out=ysb[:, 0:128], in_=ptr[:]), "act")

                @S(k, "sync")
                def _(s, i=i, g=g):
                    k.inc(s.dma_start(out=f2[i][:, g * 128:(g + 1) * 128],
                                      in_=ysb[:, 0:128]), "st")
        k.barrier()



def ph_l2_attn(k: K, xn_sb, a2):
    nc = k.nc
    H, P, N, DI = 32, 64, 64, 2048
    NCH = L // 128
    zf_hbm = k.dram("l2a_zf", [DI, T], BF16)
    xf_hbm = k.dram("l2a_xf", [DI, T], BF16)
    y_hbm = k.dram("l2a_y", [DI, T], BF16)
    bm_hbm = k.dram("l2a_bm", [L, DI], BF16)
    cm_hbm = k.dram("l2a_cm", [L, DI], BF16)
    q_hbm = k.dram("l2a_q", [L + 128, 32], F32)

    with ExitStack() as ph:
        sb = lambda n, s, d=BF16: ph.enter_context(nc.sbuf_tensor("l2a1_" + n, s, d))
        ps = lambda n, s, d=F32: ph.enter_context(nc.psum_tensor("l2a1_" + n, s, d))
        slab = sb("slab", [128, DG, 512])
        ytile = sb("ytile", [128, 512])
        pj = ps("pj", [128, 2, 512])
        for jslab in range(8):
            @S(k, "gpsimd")
            def _(g, jslab=jslab):
                k.inc(g.dma_start(out=slab[:],
                                  in_=k.inputs["l2a_wipT"][:, jslab * 512:(jslab + 1) * 512]
                                  .rearrange("(c p) n -> p c n", p=128)), "gdma")
            k.barrier()
            for jt in range(4):
                jglob = jslab * 512 + jt * 128
                dst_h = zf_hbm if jglob < DI else xf_hbm
                row0 = jglob if jglob < DI else jglob - DI
                for tc in range(T // 512):
                    @S(k, "tensor")
                    def _(t, jslab=jslab, jt=jt, tc=tc):
                        for g in range(DG):
                            ins = t.matmul(pj[:, tc % 2, :], slab[:, g, jt * 128:(jt + 1) * 128],
                                           xn_sb[:, g, tc * 512:(tc + 1) * 512],
                                           start=(g == 0), stop=(g == DG - 1))
                        k.inc(ins, "pe")

                    @S(k, "scalar")
                    def _(sc, tc=tc):
                        k.inc(sc.copy(out=ytile[:], in_=pj[:, tc % 2, :]), "act")

                    @S(k, "sync")
                    def _(s, dst_h=dst_h, row0=row0, tc=tc):
                        k.inc(s.dma_start(out=dst_h[row0:row0 + 128, tc * 512:(tc + 1) * 512],
                                          in_=ytile[:]), "st")
        k.barrier()

    with ExitStack() as ph:
        sb = lambda n, s, d=BF16: ph.enter_context(nc.sbuf_tensor("l2a2_" + n, s, d))
        ps = lambda n, s, d=F32: ph.enter_context(nc.psum_tensor("l2a2_" + n, s, d))
        wsm = sb("wsm", [128, DG, 1216])
        rows32 = sb("rows32", [1, 32], F32)
        dtb_bc = sb("dtb_bc", [128, 32], F32)
        a_bc = sb("a_bc", [128, 32], F32)
        bnw_bc = sb("bnw_bc", [128, 64], F32)
        cnw_bc = sb("cnw_bc", [128, 64], F32)
        bbias_bc = sb("bbias_bc", [128, DI], F32)
        cbias_bc = sb("cbias_bc", [128, DI], F32)
        drep = sb("drep", [128, 32], F32)
        dt_tok = sb("dt_tok", [128, NCH, 32], F32)
        la_tok = sb("la_tok", [128, NCH, 32], F32)
        g_tok = sb("g_tok", [128, NCH, 32], F32)
        q_tok = sb("q_tok", [128, NCH, 32], F32)
        lne_tok = sb("lne_tok", [128, NCH, 32], F32)
        gme_tok = sb("gme_tok", [128, NCH, 32], F32)
        s32a = sb("s32a", [128, 64], F32)
        s32b = sb("s32b", [128, 64], F32)
        rstdc = sb("rstdc", [128, 4], F32)
        bn_t = sb("bn_t", [128, 64], F32)
        cn_t = sb("cn_t", [128, 64], F32)
        thsb = sb("thsb", [128, 1024], F32)
        tcs = sb("tcs", [128, 1024], F32)
        tc2 = sb("tc2", [128, 1024], F32)
        kint = sb("kint", [128, 1024], I32)
        kf = sb("kf", [128, 1024], F32)
        sin_t = sb("sin_t", [128, 1024])
        cos_t = sb("cos_t", [128, 1024])
        vf = sb("vf", [128, DI], F32)
        bm_ch = sb("bm_ch", [128, DI])
        cm_ch = sb("cm_ch", [128, DI])
        carry_row = sb("carry_row", [1, 1024], F32)
        xt_ch = sb("xt_ch", [128, 16, 128])
        xfch = sb("xfch", [128, 16, 128])
        ych = sb("ych", [128, 16, 128])
        hcar = sb("hcar", [64, H, 64], F32)
        gm = sb("gm", [128, 128], F32)
        wg_t = sb("wg_t", [128, 128])
        wf_t = sb("wf_t", [128, 128])
        mt_t = sb("mt_t", [128, 128])
        bfh = sb("bfh", [64, 128])
        cfh = sb("cfh", [64, 128])
        crs = sb("crs", [64, 128])
        bw_t = sb("bw_t", [128, 64])
        er = sb("er", [1, 128], F32)
        wcf = sb("wcf", [128, 1], F32)
        hbf = sb("hbf", [64, 64])
        s64 = sb("s64", [64, 1], F32)
        lastriu = sb("lastriu", [128, 128])
        pS = ps("pS", [128, 3, 512])
        pG = ps("pG", [128, 128])
        pY = ps("pY", [64, 128])
        pH = ps("pH", [64, 64])
        pE = ps("pE", [64, 128])
        pGb = ps("pGb", [128, 128], BF16)
        ptb = pGb[0:64, :]

        @S(k, "sync")
        def _(s):
            k.inc(s.dma_start(out=rows32[:], in_=k.inputs["l2a_dtb"]), "sdma")

        @S(k, "gpsimd")
        def _(g):
            k.inc(g.dma_start(out=wsm[:],
                              in_=k.inputs["l2a_wsmall"].rearrange("(c p) n -> p c n", p=128)), "gdma")
        k.barrier()

        def bcast_row(srcrow, dst, width, extra=None):
            @S(k, "tensor")
            def _(t):
                k.inc(t.matmul(pS[:, 0, 0:width], ones_row(k, 128), srcrow,
                               start=True, stop=True), "pe")

            @S(k, "vector")
            def _(v):
                k.inc(v.tensor_copy(out=dst, in_=pS[:, 0, 0:width]), "dve")

        bcast_row(rows32[0:1, :], dtb_bc[:], 32)

        @S(k, "sync")
        def _(s):
            k.inc(s.dma_start(out=rows32[:], in_=k.inputs["l2a_alog"]), "sdma")

        @S(k, "scalar")
        def _(sc):
            k.inc(sc.activation(out=rows32[:], in_=rows32[:], func=AF.Exp), "act")

        @S(k, "vector")
        def _(v):
            k.inc(v.tensor_scalar(out=rows32[:], in0=rows32[:], scalar1=-1.0,
                                  scalar2=None, op0=ALU.mult), "dve")
        bcast_row(rows32[0:1, :], a_bc[:], 32)

        @S(k, "sync")
        def _(s):
            k.inc(s.dma_start(out=s32a[0:1, 0:64], in_=k.inputs["l2a_bnw"]), "sdma")
            k.inc(s.dma_start(out=s32b[0:1, 0:64], in_=k.inputs["l2a_cnw"]), "sdma")
        bcast_row(s32a[0:1, 0:64], bnw_bc[:], 64)
        bcast_row(s32b[0:1, 0:64], cnw_bc[:], 64)

        @S(k, "sync")
        def _(s):
            k.inc(s.dma_start(out=vf[0:1, 0:DI], in_=k.inputs["l2a_bbias"]), "sdma")
        bcast_row(vf[0:1, 0:512], bbias_bc[:, 0:512], 512)
        bcast_row(vf[0:1, 512:1024], bbias_bc[:, 512:1024], 512)
        bcast_row(vf[0:1, 1024:1536], bbias_bc[:, 1024:1536], 512)
        bcast_row(vf[0:1, 1536:2048], bbias_bc[:, 1536:2048], 512)

        @S(k, "sync")
        def _(s):
            k.inc(s.dma_start(out=vf[0:1, 0:DI], in_=k.inputs["l2a_cbias"]), "sdma")
        bcast_row(vf[0:1, 0:512], cbias_bc[:, 0:512], 512)
        bcast_row(vf[0:1, 512:1024], cbias_bc[:, 512:1024], 512)
        bcast_row(vf[0:1, 1024:1536], cbias_bc[:, 1024:1536], 512)
        bcast_row(vf[0:1, 1536:2048], cbias_bc[:, 1536:2048], 512)

        @S(k, "sync")
        def _(s):
            k.inc(s.dma_start(out=rows32[:], in_=k.inputs["l2a_dd"]), "sdma")

        @S(k, "tensor")
        def _(t):
            k.inc(t.matmul(pG[:, 0:32], ones_row(k, 128), rows32[:], start=True, stop=True), "pe")

        @S(k, "vector")
        def _(v):
            k.inc(v.tensor_copy(out=drep[:], in_=pG[:, 0:32]), "dve")
        k.barrier()

        TWO_PI = float(2.0 * np.pi)
        for s_ in range(B_PER_CORE):
            toff = s_ * L

            @S(k, "vector")
            def _(v):
                k.inc(v.memset(carry_row[:], 0.0), "dve")
            for ch in range(NCH):
                tg = slice(toff + ch * 128, toff + ch * 128 + 128)

                @S(k, "tensor")
                def _(t, ch=ch, tg=tg):
                    for cc in range(3):
                        w_ = 512 if cc < 2 else 192
                        for g in range(DG):
                            ins = t.matmul(pS[:, cc, 0:w_], xn_sb[:, g, tg],
                                           wsm[:, g, cc * 512:cc * 512 + w_],
                                           start=(g == 0), stop=(g == DG - 1))
                    k.inc(ins, "pe")

                @S(k, "vector")
                def _(v, ch=ch):
                    k.inc(v.tensor_add(out=s32a[:, 0:32], in0=pS[:, 0, 0:32], in1=dtb_bc[:, 0:32]), "dve")

                @S(k, "scalar")
                def _(sc, ch=ch):
                    sc.activation(out=s32b[:, 0:32], in_=s32a[:, 0:32], func=AF.Abs)
                    sc.activation(out=s32b[:, 0:32], in_=s32b[:, 0:32], func=AF.Exp, scale=-1.0)
                    k.inc(sc.activation(out=s32b[:, 0:32], in_=s32b[:, 0:32], func=AF.Ln, bias=1.0), "act")

                @S(k, "vector")
                def _(v, ch=ch):
                    def step(ins):
                        c = k.inc(ins, "dve")
                        v.wait_ge(k.sems["dve"], c)
                    step(v.tensor_scalar(out=s32a[:, 0:32], in0=s32a[:, 0:32], scalar1=0.0,
                                         scalar2=None, op0=ALU.max))
                    step(v.tensor_add(out=dt_tok[:, ch, :], in0=s32a[:, 0:32], in1=s32b[:, 0:32]))
                    step(v.tensor_mul(out=la_tok[:, ch, :], in0=dt_tok[:, ch, :], in1=a_bc[:, 0:32]))

                @S(k, "scalar")
                def _(sc, ch=ch):
                    k.inc(sc.activation(out=s32a[:, 0:32], in_=pS[:, 0, 32:64], func=AF.Exp,
                                        scale=-1.0), "act")

                @S(k, "vector")
                def _(v, ch=ch):
                    def step(ins):
                        c = k.inc(ins, "dve")
                        v.wait_ge(k.sems["dve"], c)
                    step(v.tensor_scalar(out=s32a[:, 0:32], in0=s32a[:, 0:32], scalar1=1.0,
                                         scalar2=None, op0=ALU.add))
                    step(v.reciprocal(out=s32a[:, 0:32], in_=s32a[:, 0:32]))
                    step(v.tensor_mul(out=g_tok[:, ch, :], in0=s32a[:, 0:32],
                                      in1=dt_tok[:, ch, :]))
                    step(v.tensor_sub(out=q_tok[:, ch, :], in0=dt_tok[:, ch, :],
                                      in1=g_tok[:, ch, :]))

                for which, nw_bc, out_t in (("b", bnw_bc, bn_t), ("c", cnw_bc, cn_t)):
                    off0 = 64 if which == "b" else 128

                    @S(k, "scalar")
                    def _(sc, off0=off0):
                        k.inc(sc.activation(out=s32a[:, 0:64], in_=pS[:, 0, off0:off0 + 64],
                                            func=AF.Square), "act")

                    @S(k, "vector")
                    def _(v, off0=off0):
                        k.inc(v.reduce_sum(out=rstdc[:, 0:1], in_=s32a[:, 0:64], axis=AX.X), "dve")

                    @S(k, "scalar")
                    def _(sc):
                        sc.activation(out=rstdc[:, 1:2], in_=rstdc[:, 0:1], func=AF.Ln,
                                      bias=1e-5, scale=1.0 / 64.0)
                        k.inc(sc.activation(out=rstdc[:, 1:2], in_=rstdc[:, 1:2],
                                            func=AF.Exp, scale=-0.5), "act")

                    @S(k, "vector")
                    def _(v, off0=off0, nw_bc=nw_bc, out_t=out_t):
                        def step(ins):
                            c = k.inc(ins, "dve")
                            v.wait_ge(k.sems["dve"], c)
                        step(v.tensor_scalar(out=out_t[:], in0=pS[:, 0, off0:off0 + 64],
                                             scalar1=rstdc[:, 1:2], scalar2=None, op0=ALU.mult))
                        step(v.tensor_mul(out=out_t[:], in0=out_t[:], in1=nw_bc[:]))

                @S(k, "vector")
                def _(v, ch=ch):
                    k.inc(v.tensor_copy(out=thsb[:], in_=pS[:].rearrange("p c f -> p (c f)")[:, 192:1216]), "dve")

                @S(k, "tensor")
                def _(t, ch=ch):
                    for half in range(2):
                        hs = slice(half * 512, (half + 1) * 512)
                        t.matmul(pS[:, half, :], k.sb["cs_tril01"][:], thsb[:, hs],
                                 start=True, stop=False)
                        ins = t.matmul(pS[:, half, :], ones_row(k, 128)[0:1, 0:128]
                                       if False else ones_row(k, 128),
                                       carry_row[0:1, hs], start=False, stop=True)
                    k.inc(ins, "pe")

                @S(k, "vector")
                def _(v, ch=ch):
                    k.inc(v.tensor_copy(out=tcs[:, 0:512], in_=pS[:, 0, :]), "dve")
                    k.inc(v.tensor_copy(out=tcs[:, 512:1024], in_=pS[:, 1, :]), "dve")

                for half in range(2):
                    hs = slice(half * 512, (half + 1) * 512)

                    @S(k, "tensor")
                    def _(t, hs=hs):
                        k.inc(t.matmul(pS[0:1, 2, 0:512], k.constcol(1.0), thsb[:, hs],
                                       start=True, stop=True), "pe")

                    @S(k, "vector")
                    def _(v, hs=hs):
                        k.inc(v.tensor_add(out=carry_row[0:1, hs], in0=carry_row[0:1, hs],
                                           in1=pS[0:1, 2, 0:512]), "dve")

                @S(k, "vector")
                def _(v, ch=ch):
                    def step(ins):
                        c = k.inc(ins, "dve")
                        v.wait_ge(k.sems["dve"], c)
                    step(v.tensor_scalar(out=kf[:], in0=tcs[:], scalar1=1.0 / TWO_PI,
                                         scalar2=None, op0=ALU.mult))
                    step(v.tensor_copy(out=kint[:], in_=kf[:]))
                    step(v.tensor_copy(out=kf[:], in_=kint[:]))
                    step(v.scalar_tensor_tensor(out=tc2[:], in0=kf[:], scalar=-TWO_PI,
                                                in1=tcs[:], op0=ALU.mult, op1=ALU.add))
                    step(v.tensor_scalar(out=tcs[:], in0=tcs[:], scalar1=float(np.pi / 2),
                                         scalar2=1.0 / TWO_PI, op0=ALU.add, op1=ALU.mult))
                    step(v.tensor_copy(out=kint[:], in_=tcs[:]))
                    step(v.tensor_copy(out=kf[:], in_=kint[:]))

                @S(k, "scalar")
                def _(sc, ch=ch):
                    k.inc(sc.activation(out=sin_t[:], in_=tc2[:], func=AF.Sin), "act")

                @S(k, "vector")
                def _(v, ch=ch):
                    k.inc(v.tensor_sub(out=tc2[:], in0=tcs[:], in1=kf[:]), "dve")
                    v.wait_ge(k.sems["dve"], k.cnt["dve"])
                    k.inc(v.tensor_scalar(out=tc2[:], in0=tc2[:], scalar1=TWO_PI,
                                          scalar2=None, op0=ALU.mult), "dve")

                @S(k, "scalar")
                def _(sc, ch=ch):
                    k.inc(sc.activation(out=cos_t[:], in_=tc2[:], func=AF.Sin), "act")

                for which, bias_bc, nsb, dsth in (("b", bbias_bc, bn_t, bm_hbm),
                                                  ("c", cbias_bc, cn_t, cm_hbm)):
                    osb_t = bm_ch if which == "b" else cm_ch

                    @S(k, "vector")
                    def _(v, bias_bc=bias_bc, nsb=nsb, osb_t=osb_t):
                        ins = None
                        for h in range(H):
                            ins = v.tensor_add(out=vf[:, h * 64:(h + 1) * 64],
                                               in0=bias_bc[:, h * 64:(h + 1) * 64], in1=nsb[:])
                        k.inc(ins, "dve")
                        v.wait_ge(k.sems["dve"], k.cnt["dve"])
                        v1 = vf[:].rearrange("p (h n) -> p h n", h=H)[:, :, 0:32]
                        v2 = vf[:].rearrange("p (h n) -> p h n", h=H)[:, :, 32:64]
                        o1 = osb_t[:].rearrange("p (h n) -> p h n", h=H)[:, :, 0:32]
                        o2 = osb_t[:].rearrange("p (h n) -> p h n", h=H)[:, :, 32:64]
                        cw = cos_t[:].rearrange("p (h n) -> p h n", h=H, n=32)
                        sw = sin_t[:].rearrange("p (h n) -> p h n", h=H, n=32)
                        scr = tcs[:].rearrange("p (h n) -> p h n", h=H, n=32)
                        def step(ins):
                            c = k.inc(ins, "dve")
                            v.wait_ge(k.sems["dve"], c)
                        step(v.tensor_mul(out=o1, in0=v1, in1=cw))
                        step(v.tensor_mul(out=o2, in0=v2, in1=sw))
                        step(v.tensor_sub(out=o1, in0=o1, in1=o2))
                        step(v.tensor_mul(out=o2, in0=v2, in1=cw))
                        step(v.tensor_mul(out=scr, in0=v1, in1=sw))
                        step(v.tensor_add(out=o2, in0=o2, in1=scr))

                    @S(k, "sync")
                    def _(s, ch=ch, dsth=dsth, osb_t=osb_t):
                        k.inc(s.dma_start(out=dsth[ch * 128:(ch + 1) * 128, :],
                                          in_=osb_t[:]), "st")
            @S(k, "vector")
            def _(v):
                k.inc(v.memset(s32a[:], 0.0), "dve")

            @S(k, "sync")
            def _(s):
                k.inc(s.dma_start(out=q_hbm[0:L, :].rearrange("(c p) n -> p c n", p=128),
                                  in_=q_tok[:]), "st")
                k.inc(s.dma_start(out=q_hbm[L:L + 128, :], in_=s32a[:, 0:32]), "st")
                s.wait_ge(k.sems["st"], k.cnt["st"])
                k.inc(s.dma_start(out=q_tok[:],
                                  in_=q_hbm[1:L + 1, :].rearrange("(c p) n -> p c n", p=128)), "sdma")

            @S(k, "vector")
            def _(v):
                k.inc(v.tensor_add(out=gme_tok[:], in0=g_tok[:], in1=q_tok[:]), "dve")

            @S(k, "scalar")
            def _(sc):
                k.inc(sc.activation(out=lne_tok[:], in_=gme_tok[:], func=AF.Ln), "act")

            @S(k, "vector")
            def _(v):
                k.inc(v.tensor_sub(out=gme_tok[:], in0=g_tok[:], in1=gme_tok[:]), "dve")

            @S(k, "vector")
            def _(v):
                k.inc(v.memset(hcar[:], 0.0), "dve")

            for ch in range(NCH):
                tg = slice(toff + ch * 128, toff + ch * 128 + 128)

                @S(k, "sync")
                def _(s, s_=s_, ch=ch, tg=tg):
                    k.inc(s.dma_start(out=bm_ch[:], in_=bm_hbm[ch * 128:(ch + 1) * 128, :]), "sdma")
                    k.inc(s.dma_start(out=cm_ch[:], in_=cm_hbm[ch * 128:(ch + 1) * 128, :]), "sdma")
                    k.inc(s.dma_start(out=xfch[:],
                                      in_=xf_hbm[:, tg].rearrange("(g p) t -> p g t", p=128)), "sdma")
                for g in range(16):
                    @S(k, "tensor")
                    def _(t, g=g):
                        k.inc(t.transpose(pGb[:], xfch[:, g, :], k.sb["cs_ident_bf"][:]), "pe")

                    @S(k, "scalar")
                    def _(sc, g=g):
                        k.inc(sc.copy(out=xt_ch[:, g, :], in_=pGb[:]), "act")

                for h in range(H):
                    la_col = la_tok[:, ch, h:h + 1]
                    lne_col = lne_tok[:, ch, h:h + 1]
                    gme_col = gme_tok[:, ch, h:h + 1]
                    xsl = xt_ch[:, h // 2, (h % 2) * 64:(h % 2) * 64 + 64]
                    bsl = bm_ch[:, h * 64:(h + 1) * 64]
                    csl = cm_ch[:, h * 64:(h + 1) * 64]

                    @S(k, "vector")
                    def _(v, la_col=la_col):
                        k.inc(v.tensor_scalar(out=lastriu[:], in0=k.sb["cs_striu01_bf"][:],
                                              scalar1=la_col, scalar2=None, op0=ALU.mult), "dve")

                    @S(k, "tensor")
                    def _(t, h=h):
                        t.matmul(pG[:], lastriu[:], k.sb["cs_tril01_bf"][:], start=True, stop=False)
                        t.matmul(pG[:], k.sb["cs_diagbig_bf"][:], k.sb["cs_striu01_bf"][:],
                                 start=False, stop=True)
                        k.inc(t.matmul(pE[0:1, :], la_col, k.sb["cs_tril01"][:],
                                       start=True, stop=True), "pe")
                        k.inc(t.transpose(ptb, bsl, k.sb["cs_ident_bf"][:]), "pe")

                    @S(k, "scalar")
                    def _(sc, h=h):
                        k.inc(sc.copy(out=bfh[:], in_=ptb), "act")
                        k.inc(sc.activation(out=er[:], in_=pE[0:1, :], func=AF.Exp), "act")

                    @S(k, "tensor")
                    def _(t, h=h):
                        k.inc(t.transpose(ptb, csl, k.sb["cs_ident_bf"][:]), "pe")

                    @S(k, "scalar")
                    def _(sc, h=h):
                        k.inc(sc.copy(out=cfh[:], in_=ptb), "act")

                    @S(k, "vector")
                    def _(v, lne_col=lne_col):
                        k.inc(v.tensor_scalar(out=gm[:], in0=pG[:], scalar1=lne_col,
                                              scalar2=None, op0=ALU.add), "dve")

                    @S(k, "scalar")
                    def _(sc):
                        k.inc(sc.activation(out=wg_t[:], in_=gm[:], func=AF.Exp), "act")

                    @S(k, "vector")
                    def _(v, gme_col=gme_col):
                        k.inc(v.scalar_tensor_tensor(out=wf_t[:], in0=k.sb["cs_ident_bf"][:],
                                                     scalar=gme_col, in1=wg_t[:],
                                                     op0=ALU.mult, op1=ALU.add), "dve")

                    @S(k, "tensor")
                    def _(t):
                        k.inc(t.matmul(pG[:], bfh[:], cfh[:], start=True, stop=True), "pe")

                    @S(k, "vector")
                    def _(v, bsl=bsl):
                        k.inc(v.tensor_mul(out=mt_t[:], in0=wf_t[:], in1=pG[:]), "dve")
                        k.inc(v.tensor_copy(out=wcf[:], in_=wg_t[:, 127:128]), "dve")
                        v.wait_ge(k.sems["dve"], k.cnt["dve"])
                        k.inc(v.tensor_scalar(out=bw_t[:], in0=bsl,
                                              scalar1=wcf[:, 0:1], scalar2=None,
                                              op0=ALU.mult), "dve")

                    @S(k, "tensor")
                    def _(t):
                        k.inc(t.matmul(pE[:, 0:128] if False else pE[:],
                                       ones_row(k, 64), er[:], start=True, stop=True), "pe")

                    @S(k, "vector")
                    def _(v, h=h):
                        k.inc(v.tensor_mul(out=crs[:], in0=cfh[:], in1=pE[:]), "dve")
                        k.inc(v.tensor_copy(out=hbf[:], in_=hcar[:, h, :]), "dve")

                    @S(k, "tensor")
                    def _(t, h=h):
                        t.matmul(pY[:], xsl, mt_t[:], start=True, stop=False)
                        k.inc(t.matmul(pY[:], hbf[:], crs[:], start=False, stop=True), "pe")
                        k.inc(t.matmul(pH[:], bw_t[:], xsl, start=True, stop=True), "pe")

                    @S(k, "scalar")
                    def _(sc):
                        k.inc(sc.copy(out=s64[:], in_=pE[:, 127:128]), "act")

                    @S(k, "vector")
                    def _(v, h=h):
                        def step(ins):
                            c = k.inc(ins, "dve")
                            v.wait_ge(k.sems["dve"], c)
                        step(v.scalar_tensor_tensor(out=hcar[:, h, :], in0=hcar[:, h, :],
                                                    scalar=s64[:, 0:1], in1=pH[:],
                                                    op0=ALU.mult, op1=ALU.add))
                        xslf = xfch[64 * (h % 2):64 * (h % 2) + 64, h // 2, :]
                        step(v.scalar_tensor_tensor(out=ych[64 * (h % 2):64 * (h % 2) + 64,
                                                             h // 2, :],
                                                    in0=xslf,
                                                    scalar=drep[64 * (h % 2):64 * (h % 2) + 64,
                                                                h:h + 1],
                                                    in1=pY[:], op0=ALU.mult, op1=ALU.add))

                @S(k, "sync")
                def _(s, s_=s_, ch=ch, tg=tg):
                    k.inc(s.dma_start(out=y_hbm[:, tg].rearrange("(g p) t -> p g t", p=128),
                                      in_=ych[:]), "st")
        k.barrier()

    with ExitStack() as ph:
        sb = lambda n, s, d=BF16: ph.enter_context(nc.sbuf_tensor("l2a3_" + n, s, d))
        ps = lambda n, s, d=F32: ph.enter_context(nc.psum_tensor("l2a3_" + n, s, d))
        wop = sb("wop", [128, 16, 1024])
        nrmc = sb("nrmc", [128, 16], F32)
        y2 = sb("y2", [128, 16, 512])
        zt = sb("zt", [128, 16, 512])
        sq = sb("sq", [128, 512], F32)
        r_row = sb("r_row", [1, 512], F32)
        ytile = sb("ytile", [128, 512], F32)
        pj = ps("pj", [128, 2, 512])
        pst = ps("pst", [1, 512])
        pbc = ps("pbc", [128, 512])

        @S(k, "gpsimd")
        def _(g):
            k.inc(g.dma_start(out=wop[:],
                              in_=k.inputs["l2a_wopT"].rearrange("(c p) n -> p c n", p=128)), "gdma")

        @S(k, "sync")
        def _(s):
            k.inc(s.dma_start(out=nrmc[:], in_=k.inputs["l2a_normw"]), "sdma")
        k.barrier()

        for tc in range(T // 512):
            ts_ = slice(tc * 512, (tc + 1) * 512)

            @S(k, "sync")
            def _(s, ts_=ts_):
                k.inc(s.dma_start(out=y2[:], in_=y_hbm[:, ts_]
                                  .rearrange("(g p) t -> p g t", p=128)), "sdma")
                k.inc(s.dma_start(out=zt[:], in_=zf_hbm[:, ts_]
                                  .rearrange("(g p) t -> p g t", p=128)), "sdma")

            for g in range(16):
                @S(k, "scalar")
                def _(sc, g=g):
                    k.inc(sc.activation(out=sq[:], in_=zt[:, g, :], func=AF.Silu), "act")

                @S(k, "vector")
                def _(v, g=g):
                    k.inc(v.tensor_mul(out=y2[:, g, :], in0=y2[:, g, :], in1=sq[:]), "dve")

            for g in range(16):
                @S(k, "scalar")
                def _(sc, g=g):
                    k.inc(sc.activation(out=sq[:], in_=y2[:, g, :], func=AF.Square), "act")

                @S(k, "tensor")
                def _(t, g=g):
                    k.inc(t.matmul(pst[:], k.constcol(1.0), sq[:],
                                   start=(g == 0), stop=(g == 15)), "pe")

            @S(k, "scalar")
            def _(sc):
                sc.activation(out=r_row[:], in_=pst[0:1, :], func=AF.Ln,
                              bias=1e-5, scale=1.0 / DI)
                k.inc(sc.activation(out=r_row[:], in_=r_row[:], func=AF.Exp, scale=-0.5), "act")

            @S(k, "tensor")
            def _(t):
                k.inc(t.matmul(pbc[:], ones_row(k, 128), r_row[:], start=True, stop=True), "pe")

            for g in range(16):
                @S(k, "vector")
                def _(v, g=g):
                    k.inc(v.tensor_mul(out=y2[:, g, :], in0=y2[:, g, :], in1=pbc[:]), "dve")
                    v.wait_ge(k.sems["dve"], k.cnt["dve"])
                    k.inc(v.tensor_scalar(out=y2[:, g, :], in0=y2[:, g, :],
                                          scalar1=nrmc[:, g:g + 1], scalar2=None,
                                          op0=ALU.mult), "dve")

            for og in range(DG):
                @S(k, "tensor")
                def _(t, og=og):
                    for g in range(16):
                        ins = t.matmul(pj[:, og % 2, :], wop[:, g, og * 128:(og + 1) * 128],
                                       y2[:, g, :], start=(g == 0), stop=(g == 15))
                    k.inc(ins, "pe")

                @S(k, "scalar")
                def _(sc, og=og):
                    k.inc(sc.copy(out=ytile[:], in_=pj[:, og % 2, :]), "act")

                @S(k, "sync")
                def _(s, tc=tc, og=og):
                    i0 = tc * 4
                    dst = a2[i0:i0 + 4, :, og * 128:(og + 1) * 128].rearrange("i p n -> p i n")
                    k.inc(s.dma_start(out=dst, in_=ytile[:].rearrange("p (i n) -> p i n", i=4)), "st")
        k.barrier()



def _swz(xc):
    return np.ascontiguousarray(
        xc.reshape(NT128, 128, DG, 128).transpose(0, 3, 2, 1).reshape(NT128, 128, DG * 128))


def _unswz(a):
    return np.asarray(a, np.float32).reshape(NT128, 128, DG, 128) \
        .transpose(0, 3, 2, 1).reshape(T, D)


_CACHE = {}


def kernel(**inputs):
    x = np.asarray(inputs["L0_attn_x"], np.float32)
    consts = make_consts()
    wA = prep_weights_A(inputs)
    wB = prep_weights_B(inputs)

    if "A" not in _CACHE:
        _CACHE["A"] = build_A()
        _CACHE["B"] = build_B()
    kA, kB = _CACHE["A"], _CACHE["B"]

    in_mapsA = []
    for c in range(NCORES):
        m = {"xin": _swz(x[2 * c:2 * c + 2].reshape(T, D))}
        for name in kA.inputs:
            if name.startswith("c_"):
                m[name] = consts[name[2:]]
            elif name in wA:
                m[name] = wA[name]
        assert len(m) == len(kA.inputs), (sorted(kA.inputs), sorted(m))
        in_mapsA.append(m)
    resA = run_bass_kernel_spmd(kA.nc, in_mapsA, core_ids=list(range(NCORES)))

    counts_all = np.ascontiguousarray(
        np.stack([resA.results[c]["counts"][:, 0] for c in range(NCORES)])).astype(np.float32)

    in_mapsB = []
    for c in range(NCORES):
        m = {"x2": resA.results[c]["x2"],
             "counts_all": counts_all,
             "premask": (np.arange(8) < c).astype(np.float32).reshape(8, 1)}
        for name in kB.inputs:
            if name.startswith("c_"):
                m[name] = consts[name[2:]]
            elif name in wB:
                m[name] = wB[name]
        assert len(m) == len(kB.inputs), (sorted(kB.inputs), sorted(m))
        in_mapsB.append(m)
    resB = run_bass_kernel_spmd(kB.nc, in_mapsB, core_ids=list(range(NCORES)))

    out = np.stack([_unswz(resB.results[c]["xout"]) for c in range(NCORES)])
    return out.reshape(16, 2048, D).astype(np.float32)


# version 03
# speedup vs baseline: 1.2286x; 1.2286x over previous
import math
import sys

sys.path.insert(0, "/opt/trn_rl_repo")

import numpy as np
import concourse.bass as bass
import concourse.mybir as mybir
from concourse.bass import IndirectOffsetOnAxis
from concourse.bass_utils import run_bass_kernel_spmd
from contextlib import ExitStack

F32 = mybir.dt.float32
BF16 = mybir.dt.bfloat16
I32 = mybir.dt.int32
AF = mybir.ActivationFunctionType
ALU = mybir.AluOpType
AX = mybir.AxisListType

NCORES = 8
B_PER_CORE = 2
L = 2048
D = 1024
T = B_PER_CORE * L
NT128 = T // 128
DG = D // 128

DEBUG_TAPS = {}



def _tri(n=128):
    r = np.arange(n)
    return r[:, None], r[None, :]


def make_consts():
    c = {}
    i, j = _tri()
    c["ident"] = np.eye(128, dtype=np.float32)
    c["tril01"] = (i <= j).astype(np.float32)
    c["stril01"] = (i < j).astype(np.float32)
    c["striu01"] = (i > j).astype(np.float32)
    c["diagbig"] = (-1e30 * np.eye(128)).astype(np.float32)
    c["ones"] = np.ones((128, 512), dtype=np.float32)
    c["iota8"] = np.arange(8, dtype=np.float32).reshape(1, 8)
    kk = np.arange(128)[:, None]
    qq = np.arange(512)[None, :]
    cm = np.zeros((128, 4, 512), np.float32)
    for j in range(4):
        cm[:, j, :] = np.where(kk <= qq - j * 128, 0.0, -1e30)
    c["cmask"] = cm.reshape(128, 4 * 512)
    seln = np.zeros((32, 32, 128), np.float32)
    for n in range(32):
        seln[n, n, :] = 1.0
    c["seln"] = seln.reshape(32, 32 * 128)
    d = np.arange(D)
    ang = 2.0 * np.pi * (d[:, None] * d[None, :] % D) / D
    c["fr"] = np.cos(ang).astype(np.float32)
    c["fs"] = np.sin(ang).astype(np.float32)
    c["fsneg"] = (-np.sin(ang)).astype(np.float32)
    c["gr"] = (np.cos(ang) / D).astype(np.float32)
    c["gsneg"] = (-np.sin(ang) / D).astype(np.float32)
    return c



class K:

    SEMS = ["sdma", "gdma", "pe", "act", "dve", "pool", "st"]
    DMA_SEMS = {"sdma", "gdma", "st"}

    def __init__(self, name):
        self.nc = bass.Bass("TRN2", target_bir_lowering=False)
        self.es = ExitStack()
        self.sems = {}
        self.cnt = {}
        for s in self.SEMS:
            self.sems[s] = self.es.enter_context(self.nc.semaphore("sem_" + s))
            self.cnt[s] = 0
        self._seen = {}
        self.inputs = {}
        self.outputs = {}
        self.sb = {}
        self._constcols = {}
        self.block = None
        self.taps = {}

    def din(self, name, shape, dtype=F32):
        ap = self.nc.dram_tensor(name, list(shape), dtype, kind="ExternalInput").ap()
        self.inputs[name] = ap
        return ap

    def dout(self, name, shape, dtype=F32):
        ap = self.nc.dram_tensor(name, list(shape), dtype, kind="ExternalOutput").ap()
        self.outputs[name] = ap
        return ap

    def dram(self, name, shape, dtype=F32):
        return self.nc.dram_tensor(name, list(shape), dtype, kind="Internal").ap()

    def sbuf(self, name, shape, dtype=F32):
        t = self.es.enter_context(self.nc.sbuf_tensor(name, list(shape), dtype))
        self.sb[name] = t
        return t

    def psum(self, name, shape, dtype=F32):
        return self.es.enter_context(self.nc.psum_tensor(name, list(shape), dtype))

    def inc(self, instr, sem):
        n = 16 if sem in self.DMA_SEMS else 1
        instr.then_inc(self.sems[sem], n)
        self.cnt[sem] += n
        return self.cnt[sem]

    def wait(self, eng, sem, val=None):
        eng.wait_ge(self.sems[sem], self.cnt[sem] if val is None else val)

    def snap(self):
        return dict(self.cnt)

    def wait_snap(self, eng, snap, sems=None, engname=None):
        seen = self._seen.setdefault(engname, {}) if engname else None
        for s in (sems or self.SEMS):
            v = snap.get(s, 0)
            if v > 0 and (seen is None or seen.get(s, 0) < v):
                eng.wait_ge(self.sems[s], v)
                if seen is not None:
                    seen[s] = v

    def barrier(self):
        snap = self.snap()
        for name in ("sync", "gpsimd", "scalar", "vector", "tensor"):
            mk = getattr(self.block, name)

            def _f(eng, snap=snap, name=name):
                self.wait_snap(eng, snap, engname=name)
            mk(_f)

    def constcol(self, value):
        key = float(value)
        if key not in self._constcols:
            raise KeyError(f"constcol {value} not pre-registered")
        return self._constcols[key]

    def build_preamble(self, const_values):
        pool = self.sbuf("constpool", [128, max(1, len(const_values))], F32)
        @self.block.vector
        def _(v):
            for i, val in enumerate(const_values):
                ins = v.memset(pool[:, i:i + 1], float(val))
                self._constcols[float(val)] = pool[:, i:i + 1]
            self.inc(ins, "dve")

    def tap_dram(self, name, shape, dtype=F32):
        if DEBUG_TAPS.get(name):
            ap = self.dout("tap_" + name, shape, dtype)
            self.taps[name] = ap
            return ap
        return None



CONST_TABLES = {
    "ident":   ((128, 128), (F32, BF16)),
    "tril01":  ((128, 128), (F32, BF16)),
    "stril01": ((128, 128), (F32,)),
    "striu01": ((128, 128), (F32, BF16)),
    "diagbig": ((128, 128), (BF16,)),
    "ones":    ((128, 512), (F32, BF16)),
    "iota8":   ((1, 8), (F32,)),
    "seln":    ((32, 32 * 128), (F32,)),
    "cmask":   ((128, 4 * 512), (BF16,)),
}


def ph_preamble(k: K, const_values, tables):
    pool = k.sbuf("constpool", [128, max(1, len(const_values))], F32)
    for name in tables:
        shape, dts = CONST_TABLES[name]
        ap = k.din("c_" + name, shape, F32)
        for dt in dts:
            sfx = "" if dt is F32 else "_bf"
            k.sbuf("cs_" + name + sfx, list(shape), dt)

    @k.block.vector
    def _(v):
        ins = None
        for i, val in enumerate(const_values):
            ins = v.memset(pool[:, i:i + 1], float(val))
            k._constcols[float(val)] = pool[:, i:i + 1]
            k.nc.const_aps.register(F32, float(val), pool[:, i:i + 1])
        if ins is not None:
            k.inc(ins, "dve")

    @k.block.sync
    def _(s):
        for name in tables:
            shape, dts = CONST_TABLES[name]
            if F32 in dts:
                ins = s.dma_start(out=k.sb["cs_" + name][:shape[0], :],
                                  in_=k.inputs["c_" + name])
                k.inc(ins, "sdma")

    @k.block.gpsimd
    def _(g):
        for name in tables:
            shape, dts = CONST_TABLES[name]
            if BF16 in dts:
                ins = g.dma_start(out=k.sb["cs_" + name + "_bf"][:shape[0], :],
                                  in_=k.inputs["c_" + name])
                k.inc(ins, "gdma")

    k.barrier()


def ones_col(k):
    return k.constcol(1.0)


def ones_row(k, n):
    return k.sb["cs_ones"][0:1, 0:n]



def ph_residual_ln(k: K, srcs, x_out, xn_sb, xn_tok_hbm=None, tag="ln", router=None):
    nc = k.nc
    nsrc = len(srcs)
    with ExitStack() as ph:
        sb = lambda n, s, d=F32: ph.enter_context(nc.sbuf_tensor(f"{tag}_{n}", s, d))
        ps = lambda n, s, d=F32: ph.enter_context(nc.psum_tensor(f"{tag}_{n}", s, d))
        ld = [sb(f"ld{b}", [128, nsrc, DG * 128]) for b in range(2)]
        xw = [sb(f"x{b}", [128, DG * 128]) for b in range(2)]
        xsq = sb("sq", [128, DG * 128])
        rows = sb("rows", [1, 5 * 128])
        tmp = sb("tmp", [128, 128])
        xnf = sb("xnf", [128, 128])
        prt = ps("prt", [128, 16]) if router is not None else None
        ps_stat = ps("pst", [1, 256])
        ps_bc = ps("pbc", [128, 256])
        if xn_tok_hbm is not None:
            xnt_ps = ps("tps", [128, DG * 128], BF16)
            xnt_sb = sb("tsb", [128, DG * 128], BF16)

        ld_done = [0] * NT128; sum_done = [0] * NT128; sq_done = [0] * NT128
        stat_done = [0] * NT128; row_done = [0] * NT128; bc_done = [0] * NT128
        xn_done = [0] * NT128; xst_done = [0] * NT128; tr_done = [0] * NT128
        tcp_done = [0] * NT128; tst_done = [0] * NT128

        for i in range(NT128):
            b = i % 2

            @k.block.sync
            def _(s, i=i, b=b):
                if i >= 2:
                    s.wait_ge(k.sems["dve"], xn_done[i - 2])
                    s.wait_ge(k.sems["act"], sq_done[i - 2])
                    s.wait_ge(k.sems["pe"], stat_done[i - 2])
                    s.wait_ge(k.sems["st"], xst_done[i - 2])
                for j in range(nsrc):
                    ld_done[i] = k.inc(s.dma_start(out=ld[b][:, j, :], in_=srcs[j](i)), "sdma")

            @k.block.vector
            def _(v, i=i, b=b):
                v.wait_ge(k.sems["sdma"], ld_done[i])
                if nsrc == 1:
                    ins = v.tensor_copy(out=xw[b][:], in_=ld[b][:, 0, :])
                else:
                    ins = v.tensor_add(out=xw[b][:], in0=ld[b][:, 0, :], in1=ld[b][:, 1, :])
                    for j in range(2, nsrc):
                        ins = v.tensor_add(out=xw[b][:], in0=xw[b][:], in1=ld[b][:, j, :])
                sum_done[i] = k.inc(ins, "dve")

            if x_out is not None:
                @k.block.sync
                def _(s, i=i, b=b):
                    s.wait_ge(k.sems["dve"], sum_done[i])
                    ins = s.dma_start(out=x_out(i), in_=xw[b][:])
                    xst_done[i] = k.inc(ins, "st")
            else:
                xst_done[i] = k.cnt["st"]

            @k.block.scalar
            def _(sc, i=i, b=b):
                sc.wait_ge(k.sems["dve"], sum_done[i])
                if i >= 1:
                    sc.wait_ge(k.sems["pe"], stat_done[i - 1])
                ins = sc.activation(out=xsq[:], in_=xw[b][:], func=AF.Square)
                sq_done[i] = k.inc(ins, "act")

            @k.block.tensor
            def _(t, i=i, b=b):
                t.wait_ge(k.sems["act"], sq_done[i])
                if i >= 1:
                    t.wait_ge(k.sems["dve"], row_done[i - 1])
                oc = ones_col(k)
                for g in range(DG):
                    t.matmul(ps_stat[0:1, 0:128], oc, xw[b][:, g * 128:(g + 1) * 128],
                             start=(g == 0), stop=(g == DG - 1))
                for g in range(DG):
                    ins = t.matmul(ps_stat[0:1, 128:256], oc, xsq[:, g * 128:(g + 1) * 128],
                                   start=(g == 0), stop=(g == DG - 1))
                stat_done[i] = k.inc(ins, "pe")

            @k.block.vector
            def _(v, i=i, b=b):
                v.wait_ge(k.sems["pe"], stat_done[i])
                if i >= 1:
                    v.wait_ge(k.sems["pe"], bc_done[i - 1])
                m = rows[0:1, 0:128]; m2 = rows[0:1, 128:256]
                var = rows[0:1, 256:384]; mrstd = rows[0:1, 512:640]
                v.tensor_scalar(out=m, in0=ps_stat[0:1, 0:128], scalar1=1.0 / D,
                                scalar2=None, op0=ALU.mult)
                v.tensor_mul(out=m2, in0=m, in1=m)
                v.tensor_scalar(out=var, in0=ps_stat[0:1, 128:256], scalar1=1.0 / D,
                                scalar2=None, op0=ALU.mult)
                ins = v.tensor_sub(out=var, in0=var, in1=m2)
                row_done[i] = k.inc(ins, "dve")

            @k.block.scalar
            def _(sc, i=i):
                sc.wait_ge(k.sems["dve"], row_done[i])
                var = rows[0:1, 256:384]; rstd = rows[0:1, 384:512]
                sc.activation(out=rstd, in_=var, func=AF.Ln, bias=1e-5)
                ins = sc.activation(out=rstd, in_=rstd, func=AF.Exp, scale=-0.5)
                sq_done[i] = k.inc(ins, "act")

            @k.block.vector
            def _(v, i=i):
                v.wait_ge(k.sems["act"], sq_done[i])
                rstd = rows[0:1, 384:512]; m = rows[0:1, 0:128]
                mrstd = rows[0:1, 512:640]
                ins = v.tensor_mul(out=mrstd, in0=m, in1=rstd)
                row_done[i] = k.inc(ins, "dve")

            @k.block.tensor
            def _(t, i=i):
                t.wait_ge(k.sems["dve"], row_done[i])
                if i >= 1:
                    t.wait_ge(k.sems["dve"], xn_done[i - 1])
                orow = ones_row(k, 128)
                t.matmul(ps_bc[:, 0:128], orow, rows[0:1, 384:512], start=True, stop=True)
                ins = t.matmul(ps_bc[:, 128:256], orow, rows[0:1, 512:640], start=True, stop=True)
                bc_done[i] = k.inc(ins, "pe")

            if router is None:
                @k.block.vector
                def _(v, i=i, b=b):
                    v.wait_ge(k.sems["pe"], bc_done[i])
                    ins = None
                    for g in range(DG):
                        v.tensor_mul(out=tmp[:], in0=xw[b][:, g * 128:(g + 1) * 128],
                                     in1=ps_bc[:, 0:128])
                        ins = v.tensor_sub(out=xn_sb[:, g, i * 128:(i + 1) * 128],
                                           in0=tmp[:], in1=ps_bc[:, 128:256])
                    xn_done[i] = k.inc(ins, "dve")
            else:
                rw_sb, logits_sb = router
                E = rw_sb.shape[2]
                for g in range(DG):
                    @k.block.vector
                    def _(v, i=i, b=b, g=g):
                        if g == 0:
                            v.wait_ge(k.sems["pe"], bc_done[i])
                        else:
                            v.wait_ge(k.sems["pe"], k.cnt["pe"])
                        v.tensor_mul(out=tmp[:], in0=xw[b][:, g * 128:(g + 1) * 128],
                                     in1=ps_bc[:, 0:128])
                        v.tensor_sub(out=xnf[:], in0=tmp[:], in1=ps_bc[:, 128:256])
                        ins = v.tensor_copy(out=xn_sb[:, g, i * 128:(i + 1) * 128],
                                            in_=xnf[:])
                        xn_done[i] = k.inc(ins, "dve")

                    @k.block.tensor
                    def _(t, i=i, g=g, E=E):
                        t.wait_ge(k.sems["dve"], k.cnt["dve"])
                        ins = t.matmul(prt[:, 0:E], xnf[:], rw_sb[:, g, :],
                                       start=(g == 0), stop=(g == DG - 1))
                        k.inc(ins, "pe")

                @k.block.vector
                def _(v, i=i, E=E):
                    v.wait_ge(k.sems["pe"], k.cnt["pe"])
                    ins = v.tensor_copy(out=logits_sb[:, i, :], in_=prt[:, 0:E])
                    xn_done[i] = k.inc(ins, "dve")

            if xn_tok_hbm is not None:
                @k.block.tensor
                def _(t, i=i):
                    t.wait_ge(k.sems["dve"], xn_done[i])
                    if i >= 1:
                        t.wait_ge(k.sems["act"], tcp_done[i - 1])
                    idb = k.sb["cs_ident_bf"]
                    for g in range(DG):
                        ins = t.transpose(xnt_ps[:, g * 128:(g + 1) * 128],
                                          xn_sb[:, g, i * 128:(i + 1) * 128], idb[:])
                    tr_done[i] = k.inc(ins, "pe")

                @k.block.scalar
                def _(sc, i=i):
                    sc.wait_ge(k.sems["pe"], tr_done[i])
                    if i >= 1:
                        sc.wait_ge(k.sems["st"], tst_done[i - 1])
                    ins = sc.copy(out=xnt_sb[:], in_=xnt_ps[:])
                    tcp_done[i] = k.inc(ins, "act")

                @k.block.sync
                def _(s, i=i):
                    s.wait_ge(k.sems["act"], tcp_done[i])
                    ins = s.dma_start(out=xn_tok_hbm[i * 128:(i + 1) * 128, :],
                                      in_=xnt_sb[:])
                    tst_done[i] = k.inc(ins, "st")

        k.barrier()



def ph_dump_sbuf(k: K, name, sb_ap):
    shape = list(sb_ap.shape)
    out = k.dout("tap_" + name, shape, sb_ap.dtype)

    @k.block.sync
    def _(s):
        ins = s.dma_start(out=out, in_=sb_ap)
        k.inc(ins, "st")
    k.barrier()


def ph_finish(k: K):
    k.barrier()



def swz(dram_ap, i):
    return dram_ap[i]


def build_A(stop_after=None, taps=()):
    k = K("A")
    xin = k.din("xin", [NT128, 128, DG * 128], F32)
    k.din("l0f_wr", [D, D], F32)
    k.din("l0f_wi", [D, D], F32)
    k.din("l0f_bias", [128, DG], F32)
    for nm in ("fr", "fs", "fsneg", "gr", "gsneg"):
        k.din("c_" + nm, [D, D], F32)
    k.din("l1f_wqT", [D, D], F32)
    k.din("l1f_routT", [D, 255], F32)
    k.din("l1f_Da", [128, 8 * 256], F32)
    k.din("l1f_Db", [128, 8 * 256], F32)
    k.din("l1f_lv", [256, D], F32)
    k.din("l1f_woutT", [D, D], F32)
    k.din("l1f_wob", [128, DG], F32)
    k.din("l1f_c", [128, 16], F32)
    k.din("rt_w", [D, 8], F32)
    k.din("rt_b", [1, 8], F32)
    k.din("l0a_wbc", [D, 32], F32)
    k.din("l0a_wdt", [D, 64], F32)
    k.din("l0a_wdp", [64, D], F32)
    k.din("l0a_bdp", [128, DG], F32)
    k.din("l0a_alog", [D, 16], F32)
    k.din("l0a_d", [128, DG], F32)
    k.din("l1a_qpT", [D, 1024], F32)
    k.din("l1a_kpT", [D, 64], F32)
    k.din("l1a_vpT", [D, 64], F32)
    k.din("l1a_opT", [D, 1024], F32)
    counts_out = k.dout("counts", [8, 1], F32)
    x2_out = k.dout("x2", [NT128, 128, DG * 128], F32)

    k.block = k.es.enter_context(k.nc.Block())
    ph_preamble(k, CONSTVALS, ["ident", "tril01", "stril01", "striu01",
                               "ones", "iota8", "seln", "cmask"])

    xn_sb = k.sbuf("xn", [128, DG, T], BF16)

    ph_residual_ln(k, [lambda i: xin[i]], None, xn_sb, tag="ln0")
    if "xn0" in taps:
        ph_dump_sbuf(k, "xn0", xn_sb[:])
    if stop_after == "ln0":
        ph_finish(k)
        return k
    if stop_after in ("l1f0", "l1a0", "rt0", "l2a0", "l2f0", "l3a0", "l3f0"):
        tmp = k.dout("bdump", [NT128, 128, DG * 128], F32)
        if stop_after == "l1f0":
            ph_l1_ffn(k, xn_sb, tmp)
        elif stop_after == "l1a0":
            ph_l1_attn(k, xn_sb, tmp)
        elif stop_after == "rt0":
            pass
        ph_finish(k)
        return k
    mk0a = k.dout if stop_after == "l0a" else k.dram
    mk0f = k.dout if stop_after == "l0f" else k.dram
    a0 = mk0a("a0", [NT128, 128, DG * 128], F32)
    f0 = mk0f("f0", [NT128, 128, DG * 128], F32)
    if stop_after != "l0f":
        ph_l0_attn(k, xn_sb, a0)
    if stop_after == "l0a":
        ph_finish(k)
        return k
    ph_l0_ffn(k, xn_sb, f0)
    if stop_after == "l0f":
        ph_finish(k)
        return k

    x1 = k.dram("x1", [NT128, 128, DG * 128], F32)
    ph_residual_ln(k, [lambda i: xin[i], lambda i: a0[i], lambda i: f0[i]],
                   lambda i: x1[i], xn_sb, tag="ln1")
    if "xn1" in taps:
        ph_dump_sbuf(k, "xn1", xn_sb[:])
    if stop_after == "ln1":
        ph_finish(k)
        return k
    mk1a = k.dout if stop_after == "l1a" else k.dram
    mk1f = k.dout if stop_after == "l1f" else k.dram
    a1 = mk1a("a1", [NT128, 128, DG * 128], F32)
    f1 = mk1f("f1", [NT128, 128, DG * 128], F32)
    if stop_after != "l1f":
        ph_l1_attn(k, xn_sb, a1)
    if stop_after == "l1a":
        ph_finish(k)
        return k
    ph_l1_ffn(k, xn_sb, f1)
    if stop_after == "l1f":
        ph_finish(k)
        return k

    rtw_f = k.sbuf("rtw_f", [128, DG, 8], F32)
    lg2_sb = k.sbuf("lg2", [128, NT128, 8], F32)

    @S(k, "sync")
    def _(s):
        k.inc(s.dma_start(out=rtw_f[:], in_=k.inputs["rt_w"].rearrange("(c p) n -> p c n", p=128)), "sdma")
    k.barrier()
    ph_residual_ln(k, [lambda i: x1[i], lambda i: a1[i], lambda i: f1[i]],
                   lambda i: x2_out[i], xn_sb, tag="ln2", router=(rtw_f, lg2_sb))
    if "xn2" in taps:
        ph_dump_sbuf(k, "xn2", xn_sb[:])
    ph_router_counts(k, lg2_sb, counts_out, taps)
    ph_finish(k)
    return k


CONSTVALS = [1.0, 0.0, -1.0, 0.5, -0.5, 1e-5, float(np.pi / 2), float(-np.pi), 1e-9, 2.0]



def ph_l0_ffn(k: K, xn_sb, f0):
    nc = k.nc
    wr_d = k.inputs["l0f_wr"]; wi_d = k.inputs["l0f_wi"]
    fr_d = k.inputs["c_fr"]; fs_d = k.inputs["c_fs"]; fsneg_d = k.inputs["c_fsneg"]
    gr_d = k.inputs["c_gr"]; gsneg_d = k.inputs["c_gsneg"]
    bias_d = k.inputs["l0f_bias"]

    with ExitStack() as ph:
        sb = lambda n, s, d=BF16: ph.enter_context(nc.sbuf_tensor("l0fs_" + n, s, d))
        ps = lambda n, s, d=F32: ph.enter_context(nc.psum_tensor("l0fs_" + n, s, d))
        wr = sb("wr", [128, DG, 1024]); wi = sb("wi", [128, DG, 1024])
        slab = sb("slab", [128, 3, DG, 512])
        t1 = sb("t1", [128, DG, 1024]); t2 = sb("t2", [128, DG, 1024])
        msb = sb("m", [128, DG, 1024])
        bias = sb("bias", [128, DG], F32)
        pst = ps("pst", [128, 2, 1024])
        ld_done = {}

        @k.block.gpsimd
        def _(g):
            k.inc(g.dma_start(out=wr[:], in_=wr_d.rearrange("(c p) n -> p c n", p=128)), "gdma")
            ins = g.dma_start(out=wi[:], in_=wi_d.rearrange("(c p) n -> p c n", p=128))
            ld_done["w"] = k.inc(ins, "gdma")

        @k.block.sync
        def _(s):
            ins = s.dma_start(out=bias[:], in_=bias_d)
            ld_done["b"] = k.inc(ins, "sdma")

        for dc in range(2):
            @k.block.gpsimd
            def _(g, dc=dc):
                if dc >= 1:
                    g.wait_ge(k.sems["pe"], k.cnt["pe"])
                for j, src in enumerate((fr_d, fs_d, fsneg_d)):
                    k.inc(g.dma_start(
                        out=slab[:, j, :, :],
                        in_=src[:, dc * 512:(dc + 1) * 512].rearrange("(c p) n -> p c n", p=128)), "gdma")

            for kt in range(DG):
                pb = kt % 2

                @k.block.tensor
                def _(t, dc=dc, kt=kt, pb=pb):
                    if kt == 0:
                        t.wait_ge(k.sems["gdma"], k.cnt["gdma"])
                    if kt >= 2 or dc >= 1:
                        t.wait_ge(k.sems["act"], k.cnt["act"])
                    for m in range(DG):
                        t.matmul(pst[:, pb, 0:512], wr[:, m, kt * 128:(kt + 1) * 128],
                                 slab[:, 0, m, :], start=(m == 0), stop=False)
                    for m in range(DG):
                        t.matmul(pst[:, pb, 0:512], wi[:, m, kt * 128:(kt + 1) * 128],
                                 slab[:, 1, m, :], start=False, stop=(m == DG - 1))
                    for m in range(DG):
                        t.matmul(pst[:, pb, 512:1024], wi[:, m, kt * 128:(kt + 1) * 128],
                                 slab[:, 0, m, :], start=(m == 0), stop=False)
                    for m in range(DG):
                        ins = t.matmul(pst[:, pb, 512:1024], wr[:, m, kt * 128:(kt + 1) * 128],
                                       slab[:, 2, m, :], start=False, stop=(m == DG - 1))
                    k.inc(ins, "pe")

                @k.block.scalar
                def _(sc, dc=dc, kt=kt, pb=pb):
                    sc.wait_ge(k.sems["pe"], k.cnt["pe"])
                    sc.copy(out=t1[:, kt, dc * 512:(dc + 1) * 512], in_=pst[:, pb, 0:512])
                    ins = sc.copy(out=t2[:, kt, dc * 512:(dc + 1) * 512], in_=pst[:, pb, 512:1024])
                    k.inc(ins, "act")
        k.barrier()

        for oc in range(2):
            @k.block.gpsimd
            def _(g, oc=oc):
                if oc >= 1:
                    g.wait_ge(k.sems["pe"], k.cnt["pe"])
                for j, src in enumerate((gr_d, gsneg_d)):
                    k.inc(g.dma_start(
                        out=slab[:, j, :, :],
                        in_=src[:, oc * 512:(oc + 1) * 512].rearrange("(c p) n -> p c n", p=128)), "gdma")

            for dt_ in range(DG):
                pb = dt_ % 2

                @k.block.tensor
                def _(t, oc=oc, dt_=dt_, pb=pb):
                    if dt_ == 0:
                        t.wait_ge(k.sems["gdma"], k.cnt["gdma"])
                    if dt_ >= 2 or oc >= 1:
                        t.wait_ge(k.sems["act"], k.cnt["act"])
                    for m in range(DG):
                        t.matmul(pst[:, pb, 0:512], t1[:, m, dt_ * 128:(dt_ + 1) * 128],
                                 slab[:, 0, m, :], start=(m == 0), stop=False)
                    for m in range(DG):
                        ins = t.matmul(pst[:, pb, 0:512], t2[:, m, dt_ * 128:(dt_ + 1) * 128],
                                       slab[:, 1, m, :], start=False, stop=(m == DG - 1))
                    k.inc(ins, "pe")

                @k.block.scalar
                def _(sc, oc=oc, dt_=dt_, pb=pb):
                    sc.wait_ge(k.sems["pe"], k.cnt["pe"])
                    ins = sc.copy(out=msb[:, dt_, oc * 512:(oc + 1) * 512], in_=pst[:, pb, 0:512])
                    k.inc(ins, "act")
        k.barrier()

        ya = sb("ya", [128, 2, 512], F32)
        st_done = {}
        for tc in range(T // 512):
            for og in range(DG):
                pb = (tc * DG + og) % 2

                @k.block.tensor
                def _(t, tc=tc, og=og, pb=pb):
                    t.wait_ge(k.sems["sdma"], ld_done["b"])
                    if (tc, og) not in ((0, 0), (0, 1)):
                        t.wait_ge(k.sems["act"], k.cnt["act"])
                    for g in range(DG):
                        ins = t.matmul(pst[:, pb, 0:512], msb[:, g, og * 128:(og + 1) * 128],
                                       xn_sb[:, g, tc * 512:(tc + 1) * 512],
                                       start=(g == 0), stop=(g == DG - 1))
                    k.inc(ins, "pe")

                @k.block.scalar
                def _(sc, tc=tc, og=og, pb=pb):
                    sc.wait_ge(k.sems["pe"], k.cnt["pe"])
                    if (tc, og) not in ((0, 0), (0, 1)):
                        sc.wait_ge(k.sems["st"], st_done[pb])
                    ins = sc.activation(out=ya[:, pb, :], in_=pst[:, pb, 0:512],
                                        func=AF.Identity, bias=bias[:, og:og + 1], scale=1.0)
                    k.inc(ins, "act")

                @k.block.sync
                def _(s, tc=tc, og=og, pb=pb):
                    s.wait_ge(k.sems["act"], k.cnt["act"])
                    i0 = tc * 4
                    dst = f0[i0:i0 + 4, :, og * 128:(og + 1) * 128].rearrange("i p n -> p i n")
                    ins = s.dma_start(out=dst,
                                      in_=ya[:, pb, :].rearrange("p (i n) -> p i n", i=4))
                    st_done[pb] = k.inc(ins, "st")
        k.barrier()



def col128(v):
    v = np.asarray(v, np.float32).reshape(-1)
    return np.ascontiguousarray(v.reshape(-1, 128).T)


def prep_weights_A(mi):
    w = {}
    w["l0f_wr"] = np.ascontiguousarray(mi["L0_ffn_w_freq_real"], np.float32)
    w["l0f_wi"] = np.ascontiguousarray(mi["L0_ffn_w_freq_imag"], np.float32)
    w["l0f_bias"] = col128(mi["L0_ffn_bias"])
    w["l1f_wqT"] = np.ascontiguousarray(mi["L1_ffn_Wq"].T, np.float32)
    w["l1f_routT"] = np.ascontiguousarray(mi["L1_ffn_routers"].T, np.float32)
    w["l1f_lv"] = np.ascontiguousarray(mi["L1_ffn_leaf_values"], np.float32)
    w["l1f_woutT"] = np.ascontiguousarray(mi["L1_ffn_Wout_w"].T, np.float32)
    w["l1f_wob"] = col128(mi["L1_ffn_Wout_b"])
    da = np.zeros((128, 8, 256), np.float32)
    db = np.zeros((128, 8, 256), np.float32)
    cc = np.zeros((128, 16), np.float32)
    for lev in range(8):
        n = 2 ** lev
        half = 256 // (2 * n)
        for m in range(256):
            j = m // (2 * half)
            side = (m // half) % 2
            jj = n - 1 + j
            val = 1.0 if side == 1 else -1.0
            if jj < 128:
                da[jj, lev, m] = val
            else:
                db[jj - 128, lev, m] = val
            cval = 0.0 if side == 1 else 1.0
            hf, p = divmod(m, 128)
            cc[p, hf * 8 + lev] = cval
    w["l1f_Da"] = da.reshape(128, 8 * 256)
    w["l1f_Db"] = db.reshape(128, 8 * 256)
    w["l1f_c"] = cc
    w["l0a_wbc"] = np.ascontiguousarray(mi["L0_attn_W_bc"], np.float32)
    w["l0a_wdt"] = np.ascontiguousarray(mi["L0_attn_W_dt"], np.float32)
    w["l0a_wdp"] = np.ascontiguousarray(mi["L0_attn_W_dtproj"], np.float32)
    w["l0a_bdp"] = col128(mi["L0_attn_b_dtproj"])
    w["l0a_alog"] = np.ascontiguousarray(mi["L0_attn_A_log"], np.float32)
    w["l0a_d"] = col128(mi["L0_attn_D_param"])
    w["l1a_qpT"] = np.ascontiguousarray(
        mi["L1_attn_query_proj"].transpose(2, 0, 1).reshape(D, 1024), np.float32)
    w["l1a_kpT"] = np.ascontiguousarray(mi["L1_attn_key_proj"], np.float32)
    w["l1a_vpT"] = np.ascontiguousarray(mi["L1_attn_value_proj"], np.float32)
    w["l1a_opT"] = np.ascontiguousarray(
        mi["L1_attn_output_proj"].transpose(1, 2, 0).reshape(1024, D), np.float32)
    w["rt_w"] = np.ascontiguousarray(mi["L2_ffn_router_w"], np.float32)
    w["rt_b"] = np.ascontiguousarray(mi["L2_ffn_router_b"].reshape(1, 8), np.float32)
    return w



def S(k: K, engname, sems=None):
    snap = k.snap()
    mk = getattr(k.block, engname)

    def deco(fn):
        def body(eng):
            k.wait_snap(eng, snap, sems, engname=engname)
            fn(eng)
        mk(body)
    return deco



def ph_l1_ffn(k: K, xn_sb, f1):
    nc = k.nc
    with ExitStack() as ph:
        sb = lambda n, s, d=BF16: ph.enter_context(nc.sbuf_tensor("l1fs_" + n, s, d))
        ps = lambda n, s, d=F32: ph.enter_context(nc.psum_tensor("l1fs_" + n, s, d))
        wq = sb("wq", [128, DG, 1024]); rt = sb("rt", [128, DG, 255])
        da = sb("da", [128, 8, 256]); db = sb("db", [128, 8, 256])
        lva = sb("lva", [128, 1024]); lvb = sb("lvb", [128, 1024])
        wo = sb("wo", [128, DG, 1024]); wob = sb("wob", [128, DG], F32)
        ccol = sb("ccol", [128, 16], F32)
        q = sb("q", [128, DG, 512]); gate = sb("gate", [128, 2, 512], F32)
        gbf = sb("gbf", [128, 2, 512]); lp = sb("lp", [128, 2, 512], F32)
        fsb = sb("fsb", [128, 512], F32); lpb = sb("lpb", [128, 2, 512])
        ret = sb("ret", [128, DG, 512]); ya = sb("ya", [128, 512], F32)
        pq = ps("pq", [128, 512]); pf = ps("pf", [128, 512])
        po = ps("po", [128, 512])

        @S(k, "gpsimd")
        def _(g):
            k.inc(g.dma_start(out=wq[:], in_=k.inputs["l1f_wqT"].rearrange("(c p) n -> p c n", p=128)), "gdma")
            k.inc(g.dma_start(out=rt[:], in_=k.inputs["l1f_routT"].rearrange("(c p) n -> p c n", p=128)), "gdma")
            k.inc(g.dma_start(out=da[:], in_=k.inputs["l1f_Da"].rearrange("p (l m) -> p l m", l=8)), "gdma")
            k.inc(g.dma_start(out=db[:], in_=k.inputs["l1f_Db"].rearrange("p (l m) -> p l m", l=8)), "gdma")
            k.inc(g.dma_start(out=lva[:], in_=k.inputs["l1f_lv"][0:128, :]), "gdma")
            k.inc(g.dma_start(out=lvb[:], in_=k.inputs["l1f_lv"][128:256, :]), "gdma")
            k.inc(g.dma_start(out=wo[:], in_=k.inputs["l1f_woutT"].rearrange("(c p) n -> p c n", p=128)), "gdma")

        @S(k, "sync")
        def _(s):
            k.inc(s.dma_start(out=wob[:], in_=k.inputs["l1f_wob"]), "sdma")
            k.inc(s.dma_start(out=ccol[:], in_=k.inputs["l1f_c"]), "sdma")
        k.barrier()

        for tc in range(T // 512):
            ts_ = slice(tc * 512, (tc + 1) * 512)

            for og in range(DG):
                @S(k, "tensor")
                def _(t, og=og, ts_=ts_):
                    for g in range(DG):
                        ins = t.matmul(pq[:], wq[:, g, og * 128:(og + 1) * 128],
                                       xn_sb[:, g, ts_], start=(g == 0), stop=(g == DG - 1))
                    k.inc(ins, "pe")

                @S(k, "scalar")
                def _(sc, og=og):
                    k.inc(sc.copy(out=q[:, og, :], in_=pq[:]), "act")

            for jt in range(2):
                nrow = 128 if jt == 0 else 127

                @S(k, "tensor")
                def _(t, jt=jt, nrow=nrow):
                    for g in range(DG):
                        ins = t.matmul(pq[0:nrow, :], rt[:, g, jt * 128:jt * 128 + nrow],
                                       q[:, g, :], start=(g == 0), stop=(g == DG - 1))
                    k.inc(ins, "pe")

                @S(k, "scalar")
                def _(sc, jt=jt, nrow=nrow):
                    k.inc(sc.activation(out=gate[0:nrow, jt, :], in_=pq[0:nrow, :],
                                        func=AF.Exp, scale=-1.0), "act")

            @S(k, "vector")
            def _(v):
                v.tensor_scalar(out=gate[:, :, :], in0=gate[:, :, :], scalar1=1.0,
                                scalar2=None, op0=ALU.add)
                v.reciprocal(out=gate[:, :, :], in_=gate[:, :, :])
                k.inc(v.tensor_copy(out=gbf[:], in_=gate[:]), "dve")

            for lev in range(8):
                for half in range(2):
                    @S(k, "tensor")
                    def _(t, lev=lev, half=half):
                        t.matmul(pf[:], da[:, lev, half * 128:(half + 1) * 128],
                                 gbf[:, 0, :], start=True, stop=False)
                        ins = t.matmul(pf[:], db[0:127, lev, half * 128:(half + 1) * 128],
                                       gbf[0:127, 1, :], start=False, stop=True)
                        k.inc(ins, "pe")

                    @S(k, "vector")
                    def _(v, lev=lev, half=half):
                        ins = v.tensor_scalar(out=fsb[:], in0=pf[:],
                                              scalar1=ccol[:, half * 8 + lev:half * 8 + lev + 1],
                                              scalar2=None, op0=ALU.add)
                        if lev == 0:
                            ins = v.tensor_copy(out=lp[:, half, :], in_=fsb[:])
                        else:
                            ins = v.tensor_mul(out=lp[:, half, :], in0=lp[:, half, :], in1=fsb[:])
                        k.inc(ins, "dve")

            @S(k, "vector")
            def _(v):
                k.inc(v.tensor_copy(out=lpb[:], in_=lp[:]), "dve")

            for og in range(DG):
                @S(k, "tensor")
                def _(t, og=og):
                    t.matmul(pq[:], lva[:, og * 128:(og + 1) * 128], lpb[:, 0, :],
                             start=True, stop=False)
                    ins = t.matmul(pq[:], lvb[:, og * 128:(og + 1) * 128], lpb[:, 1, :],
                                   start=False, stop=True)
                    k.inc(ins, "pe")

                @S(k, "scalar")
                def _(sc, og=og):
                    k.inc(sc.copy(out=ret[:, og, :], in_=pq[:]), "act")

            for og in range(DG):
                @S(k, "tensor")
                def _(t, og=og):
                    for g in range(DG):
                        ins = t.matmul(po[:], wo[:, g, og * 128:(og + 1) * 128],
                                       ret[:, g, :], start=(g == 0), stop=(g == DG - 1))
                    k.inc(ins, "pe")

                @S(k, "scalar")
                def _(sc, og=og):
                    k.inc(sc.activation(out=ya[:], in_=po[:], func=AF.Identity,
                                        bias=wob[:, og:og + 1], scale=1.0), "act")

                @S(k, "sync")
                def _(s, tc=tc, og=og):
                    i0 = tc * 4
                    dst = f1[i0:i0 + 4, :, og * 128:(og + 1) * 128].rearrange("i p n -> p i n")
                    k.inc(s.dma_start(out=dst, in_=ya[:].rearrange("p (i n) -> p i n", i=4)), "st")
        k.barrier()



def ph_router_counts(k: K, logits_sb, counts_out, taps=()):
    nc = k.nc
    with ExitStack() as ph:
        sb = lambda n, s, d=F32: ph.enter_context(nc.sbuf_tensor("rt_" + n, s, d))
        ps = lambda n, s, d=F32: ph.enter_context(nc.psum_tensor("rt_" + n, s, d))
        brow = sb("brow", [1, 8], F32)
        bbc = sb("bbc", [128, 8], F32)
        iob = sb("iob", [128, 8], F32)
        iobig = sb("iobig", [128, 8], F32)
        lg = sb("lg", [128, 8], F32)
        mx = sb("mx", [128, 1], F32)
        cand = sb("cand", [128, 8], F32)
        eid = sb("eid", [128, 1], F32)
        oh = sb("oh", [128, 8], F32)
        cnt_sb = sb("cnt", [8, 1], F32)
        plg = ps("plg", [128, 8])
        pbc = ps("pbc", [128, 16])
        pcnt = ps("pcnt", [8, 1])
        tap_eid = None
        if "eid" in taps:
            tap_eid = k.dout("tap_eid", [NT128, 128, 1], F32)

        @S(k, "sync")
        def _(s):
            k.inc(s.dma_start(out=brow[:], in_=k.inputs["rt_b"]), "sdma")
        k.barrier()

        @S(k, "tensor")
        def _(t):
            t.matmul(pbc[:, 0:8], ones_row(k, 128), brow[:], start=True, stop=True)
            k.inc(t.matmul(pbc[:, 8:16], ones_row(k, 128), k.sb["cs_iota8"][0:1, :],
                           start=True, stop=True), "pe")

        @S(k, "vector")
        def _(v):
            v.tensor_copy(out=bbc[:], in_=pbc[:, 0:8])
            v.tensor_copy(out=iob[:], in_=pbc[:, 8:16])
            k.inc(v.tensor_scalar(out=iobig[:], in0=pbc[:, 8:16], scalar1=1e9,
                                  scalar2=None, op0=ALU.add), "dve")

        for i in range(NT128):
            @S(k, "vector")
            def _(v, i=i):
                k.inc(v.tensor_add(out=lg[:], in0=logits_sb[:, i, :], in1=bbc[:]), "dve")
                v.wait_ge(k.sems["dve"], k.cnt["dve"])
                k.inc(v.reduce_max(out=mx[:], in_=lg[:], axis=AX.X), "dve")
                v.wait_ge(k.sems["dve"], k.cnt["dve"])
                k.inc(v.tensor_scalar(out=oh[:], in0=lg[:], scalar1=mx[:, 0:1], scalar2=None,
                                      op0=ALU.is_ge), "dve")
                v.wait_ge(k.sems["dve"], k.cnt["dve"])
                k.inc(v.scalar_tensor_tensor(out=cand[:], in0=oh[:], scalar=65536.0,
                                             in1=iob[:], op0=ALU.mult, op1=ALU.subtract), "dve")
                v.wait_ge(k.sems["dve"], k.cnt["dve"])
                k.inc(v.reduce_max(out=eid[:], in_=cand[:], axis=AX.X), "dve")
                v.wait_ge(k.sems["dve"], k.cnt["dve"])
                k.inc(v.tensor_scalar(out=eid[:], in0=eid[:], scalar1=-1.0, scalar2=65536.0,
                                      op0=ALU.mult, op1=ALU.add), "dve")
                v.wait_ge(k.sems["dve"], k.cnt["dve"])
                k.inc(v.tensor_scalar(out=oh[:], in0=iob[:], scalar1=eid[:, 0:1],
                                      scalar2=None, op0=ALU.is_equal), "dve")

            @S(k, "tensor")
            def _(t, i=i):
                k.inc(t.matmul(pcnt[:], oh[:], k.constcol(1.0), start=(i == 0),
                               stop=(i == NT128 - 1)), "pe")

            if tap_eid is not None:
                @S(k, "sync")
                def _(s, i=i):
                    k.inc(s.dma_start(out=tap_eid[i], in_=eid[:]), "st")

        @S(k, "scalar")
        def _(sc):
            k.inc(sc.copy(out=cnt_sb[:], in_=pcnt[:]), "act")

        @S(k, "sync")
        def _(s):
            k.inc(s.dma_start(out=counts_out, in_=cnt_sb[:]), "st")
        k.barrier()



def ph_l0_attn(k: K, xn_sb, a0):
    nc = k.nc
    NB, NC512 = B_PER_CORE, L // 512
    xbc_hbm = k.dram("l0a_xbc", [32, T], F32)
    with ExitStack() as ph:
        sb = lambda n, s, d=F32: ph.enter_context(nc.sbuf_tensor("l0as_" + n, s, d))
        ps = lambda n, s, d=F32: ph.enter_context(nc.psum_tensor("l0as_" + n, s, d))
        wbc = sb("wbc", [128, DG, 32], BF16)
        wdt = sb("wdt", [128, DG, 64], BF16)
        wdp = sb("wdp", [64, DG, 128], BF16)
        bdp = sb("bdp", [128, DG])
        acoef = sb("acoef", [128, DG, 16])
        dcol = sb("dcol", [128, DG])
        dtl = sb("dtl", [64, T], BF16)
        xbcrow = sb("xbcrow", [32, 512])
        delta = sb("delta", [128, T])
        dx = sb("dx", [128, T])
        t1 = sb("t1", [128, 512]); t2 = sb("t2", [128, 512])
        abar = sb("abar", [128, 512]); bx = sb("bx", [128, 512])
        hn = sb("hn", [128, 512]); yt = sb("yt", [128, 512])
        yacc = sb("yacc", [128, 512]); hcar = sb("hcar", [128, 16])
        pj = ps("pj", [128, 512])
        pbc = ps("pbc", [128, 2, 512])

        @S(k, "gpsimd")
        def _(g):
            k.inc(g.dma_start(out=wbc[:], in_=k.inputs["l0a_wbc"].rearrange("(c p) n -> p c n", p=128)), "gdma")
            k.inc(g.dma_start(out=wdt[:], in_=k.inputs["l0a_wdt"].rearrange("(c p) n -> p c n", p=128)), "gdma")
            k.inc(g.dma_start(out=wdp[:], in_=k.inputs["l0a_wdp"].rearrange("r (c n) -> r c n", c=DG)), "gdma")

        @S(k, "sync")
        def _(s):
            k.inc(s.dma_start(out=bdp[:], in_=k.inputs["l0a_bdp"]), "sdma")
            k.inc(s.dma_start(out=acoef[:], in_=k.inputs["l0a_alog"].rearrange("(c p) n -> p c n", p=128)), "sdma")
            k.inc(s.dma_start(out=dcol[:], in_=k.inputs["l0a_d"]), "sdma")
        k.barrier()

        @S(k, "scalar")
        def _(sc):
            k.inc(sc.activation(out=acoef[:], in_=acoef[:], func=AF.Exp), "act")

        @S(k, "vector")
        def _(v):
            k.inc(v.tensor_scalar(out=acoef[:], in0=acoef[:], scalar1=-1.0,
                                  scalar2=None, op0=ALU.mult), "dve")

        for tc in range(T // 512):
            ts_ = slice(tc * 512, (tc + 1) * 512)

            @S(k, "tensor")
            def _(t, ts_=ts_):
                for g in range(DG):
                    t.matmul(pj[0:32, :], wbc[:, g, :], xn_sb[:, g, ts_],
                             start=(g == 0), stop=(g == DG - 1))
                for g in range(DG):
                    ins = t.matmul(pj[64:128, :], wdt[:, g, :], xn_sb[:, g, ts_],
                                   start=(g == 0), stop=(g == DG - 1))
                k.inc(ins, "pe")

            @S(k, "scalar")
            def _(sc, ts_=ts_):
                k.inc(sc.copy(out=xbcrow[:], in_=pj[0:32, :]), "act")
                k.inc(sc.copy(out=dtl[:, ts_], in_=pj[64:128, :]), "act")

            @S(k, "sync")
            def _(s, ts_=ts_):
                k.inc(s.dma_start(out=xbc_hbm[:, ts_], in_=xbcrow[:]), "st")

        k.barrier()

        for dg in range(DG):
            for tc in range(T // 512):
                ts_ = slice(tc * 512, (tc + 1) * 512)

                @S(k, "tensor")
                def _(t, dg=dg, ts_=ts_):
                    k.inc(t.matmul(pj[:], wdp[:, dg, :], dtl[:, ts_],
                                   start=True, stop=True), "pe")

                @S(k, "vector")
                def _(v, dg=dg, ts_=ts_):
                    k.inc(v.tensor_scalar(out=t1[:], in0=pj[:], scalar1=bdp[:, dg:dg + 1],
                                          scalar2=None, op0=ALU.add), "dve")

                @S(k, "scalar")
                def _(sc, ts_=ts_):
                    sc.activation(out=t2[:], in_=t1[:], func=AF.Abs)
                    sc.activation(out=t2[:], in_=t2[:], func=AF.Exp, scale=-1.0)
                    k.inc(sc.activation(out=t2[:], in_=t2[:], func=AF.Ln, bias=1.0), "act")

                @S(k, "vector")
                def _(v, dg=dg, ts_=ts_):
                    k.inc(v.tensor_scalar(out=t1[:], in0=t1[:], scalar1=0.0,
                                          scalar2=None, op0=ALU.max), "dve")
                    v.wait_ge(k.sems["dve"], k.cnt["dve"])
                    k.inc(v.tensor_add(out=delta[:, ts_], in0=t1[:], in1=t2[:]), "dve")
                    v.wait_ge(k.sems["dve"], k.cnt["dve"])
                    k.inc(v.tensor_mul(out=dx[:, ts_], in0=delta[:, ts_],
                                       in1=xn_sb[:, dg, ts_]), "dve")

            for b in range(NB):
                @S(k, "vector")
                def _(v):
                    k.inc(v.memset(hcar[:], 0.0), "dve")
                for c5 in range(NC512):
                    tglob = slice(b * L + c5 * 512, b * L + (c5 + 1) * 512)

                    @S(k, "sync")
                    def _(s, b=b, c5=c5):
                        k.inc(s.dma_start(out=xbcrow[:],
                                          in_=xbc_hbm[:, b * L + c5 * 512: b * L + (c5 + 1) * 512]), "sdma")

                    for n in range(16):
                        @S(k, "tensor")
                        def _(t, n=n):
                            sel = k.sb["cs_seln"]
                            t.matmul(pbc[:, 0, :], sel[:, n * 128:(n + 1) * 128],
                                     xbcrow[:], start=True, stop=True)
                            k.inc(t.matmul(pbc[:, 1, :], sel[:, (16 + n) * 128:(17 + n) * 128],
                                           xbcrow[:], start=True, stop=True), "pe")

                        @S(k, "scalar")
                        def _(sc, dg=dg, n=n, tglob=tglob):
                            k.inc(sc.activation(out=abar[:], in_=delta[:, tglob], func=AF.Exp,
                                                scale=acoef[:, dg, n:n + 1]), "act")

                        @S(k, "vector")
                        def _(v, dg=dg, n=n, tglob=tglob, first=(n == 0)):
                            k.inc(v.tensor_mul(out=bx[:], in0=dx[:, tglob], in1=pbc[:, 0, :]), "dve")
                            v.wait_ge(k.sems["dve"], k.cnt["dve"])
                            k.inc(v.tensor_tensor_scan(out=hn[:], data0=abar[:], data1=bx[:],
                                                       initial=hcar[:, n:n + 1],
                                                       op0=ALU.mult, op1=ALU.add), "dve")
                            v.wait_ge(k.sems["dve"], k.cnt["dve"])
                            k.inc(v.tensor_copy(out=hcar[:, n:n + 1], in_=hn[:, 511:512]), "dve")
                            k.inc(v.tensor_mul(out=yt[:], in0=hn[:], in1=pbc[:, 1, :]), "dve")
                            v.wait_ge(k.sems["dve"], k.cnt["dve"])
                            if first:
                                k.inc(v.tensor_copy(out=yacc[:], in_=yt[:]), "dve")
                            else:
                                k.inc(v.tensor_add(out=yacc[:], in0=yacc[:], in1=yt[:]), "dve")
                            v.wait_ge(k.sems["dve"], k.cnt["dve"])

                    @S(k, "vector")
                    def _(v, dg=dg, tglob=tglob):
                        k.inc(v.scalar_tensor_tensor(out=yt[:], in0=xn_sb[:, dg, tglob],
                                                     scalar=dcol[:, dg:dg + 1], in1=yacc[:],
                                                     op0=ALU.mult, op1=ALU.add), "dve")

                    @S(k, "sync")
                    def _(s, dg=dg, b=b, c5=c5):
                        i0 = (b * L + c5 * 512) // 128
                        dst = a0[i0:i0 + 4, :, dg * 128:(dg + 1) * 128].rearrange("i p n -> p i n")
                        k.inc(s.dma_start(out=dst, in_=yt[:].rearrange("p (i n) -> p i n", i=4)), "st")
        k.barrier()



FLASH_LIMIT = None
FLASH_STAGE = "all"


def ph_flash_attn(k: K, xn_sb, out_dram, pfx, shared_kv, causal, bias_col=None):
    nc = k.nc
    osb_hbm = k.dram(pfx + "osb", [D, L], BF16)
    H, HD = 16, 64
    NSEQ = B_PER_CORE
    if FLASH_LIMIT is not None:
        NSEQ, H = FLASH_LIMIT
    HK = HD if shared_kv else H * HD
    NQC = L // 512
    NKT = L // 128
    with ExitStack() as ph:
        sb = lambda n, s, d=BF16: ph.enter_context(nc.sbuf_tensor(pfx + "s" + n, s, d))
        ps = lambda n, s, d=F32: ph.enter_context(nc.psum_tensor(pfx + "s" + n, s, d))
        if shared_kv:
            qpT = sb("qpT", [128, DG, 1024])[:, :, :]
            kpT = sb("kpT", [128, DG, HK])[:, :, :]
            vpT = sb("vpT", [128, DG, HK])[:, :, :]
            opT = sb("opT", [128, DG, 1024])[:, :, :]
        else:
            wkv = sb("wkv", [128, DG, 2048])
            qpT = wkv[:, :, 0:1024]
            kpT = wkv[:, :, 1024:2048]
            vpT = wkv[:, :, 0:1024]
            opT = wkv[:, :, 0:1024]
        kfeat = sb("kfeat", [128, 1, L]) if shared_kv else None
        vtok = sb("vtok", [128, NKT, HK])
        kaug = sb("kaug", [66, L])
        qaug = sb("qaug", [66, L])
        sqbuf = sb("sqbuf", [64, 512], F32)
        norms = sb("norms", [1, L + 520], F32)
        ebf = sb("ebf", [128, 2, 512])
        otile = sb("otile", [64, 512], F32)
        obf = sb("obf", [64, 512])
        oin = sb("oin", [128, DG, 512])
        ya = sb("ya", [128, 512], F32)
        mtile = ya
        pk = ps("pk", [128, 512])
        pe_ = ps("pe", [128, 2, 512])
        pden = ps("pden", [1, 512])
        pqn = ps("pqn", [1, 512])
        pov = ps("pov", [64, 512])
        pbc = ps("pbc", [64, 512])
        po = ps("po", [128, 512])

        if shared_kv:
            @S(k, "gpsimd")
            def _(g):
                k.inc(g.dma_start(out=qpT, in_=k.inputs[pfx + "qpT"].rearrange("(c p) n -> p c n", p=128)), "gdma")
                k.inc(g.dma_start(out=kpT, in_=k.inputs[pfx + "kpT"].rearrange("(c p) n -> p c n", p=128)), "gdma")
                k.inc(g.dma_start(out=vpT, in_=k.inputs[pfx + "vpT"].rearrange("(c p) n -> p c n", p=128)), "gdma")
                k.inc(g.dma_start(out=opT, in_=k.inputs[pfx + "opT"].rearrange("(c p) n -> p c n", p=128)), "gdma")
            k.barrier()

        for s in range(NSEQ):
            toff = s * L
            if not shared_kv:
                @S(k, "gpsimd")
                def _(g):
                    k.inc(g.dma_start(out=vpT, in_=k.inputs[pfx + "vpT"].rearrange("(c p) n -> p c n", p=128)), "gdma")
                k.barrier()
            if shared_kv:
                for tc in range(L // 512):
                    @S(k, "tensor")
                    def _(t, tc=tc):
                        for g in range(DG):
                            ins = t.matmul(pk[0:64, :], kpT[:, g, 0:64],
                                           xn_sb[:, g, toff + tc * 512: toff + (tc + 1) * 512],
                                           start=(g == 0), stop=(g == DG - 1))
                        k.inc(ins, "pe")

                    @S(k, "scalar")
                    def _(sc, tc=tc):
                        k.inc(sc.copy(out=kfeat[0:64, 0, tc * 512:(tc + 1) * 512],
                                      in_=pk[0:64, :]), "act")
            for it in range(NKT):
                for hc in range(max(1, HK // 512)):
                    w_ = min(512, HK)

                    @S(k, "tensor")
                    def _(t, it=it, hc=hc, w_=w_):
                        for g in range(DG):
                            ins = t.matmul(pk[:, 0:w_],
                                           xn_sb[:, g, toff + it * 128: toff + (it + 1) * 128],
                                           vpT[:, g, hc * 512:hc * 512 + w_],
                                           start=(g == 0), stop=(g == DG - 1))
                        k.inc(ins, "pe")

                    @S(k, "scalar")
                    def _(sc, it=it, hc=hc, w_=w_):
                        k.inc(sc.copy(out=vtok[:, it, hc * 512:hc * 512 + w_],
                                      in_=pk[:, 0:w_]), "act")

            if FLASH_STAGE == "kv":
                continue
            if not shared_kv:
                @S(k, "gpsimd")
                def _(g):
                    k.inc(g.dma_start(out=qpT, in_=k.inputs[pfx + "qpT"].rearrange("(c p) n -> p c n", p=128)), "gdma")
                    k.inc(g.dma_start(out=kpT, in_=k.inputs[pfx + "kpT"].rearrange("(c p) n -> p c n", p=128)), "gdma")
                k.barrier()

            for h in range(H):
                def kslc(sl):
                    return kaug[0:64, sl]
                @S(k, "vector")
                def _(v):
                    k.inc(v.memset(kaug[64:65, :], 1.0), "dve")

                if shared_kv:
                    @S(k, "scalar")
                    def _(sc):
                        k.inc(sc.copy(out=kaug[0:64, :], in_=kfeat[0:64, 0, :]), "act")
                else:
                    for tc in range(L // 512):
                        @S(k, "tensor")
                        def _(t, h=h, tc=tc):
                            for g in range(DG):
                                ins = t.matmul(pk[0:64, :], kpT[:, g, h * 64:(h + 1) * 64],
                                               xn_sb[:, g, toff + tc * 512: toff + (tc + 1) * 512],
                                               start=(g == 0), stop=(g == DG - 1))
                            k.inc(ins, "pe")

                        @S(k, "scalar")
                        def _(sc, tc=tc):
                            k.inc(sc.copy(out=kaug[0:64, tc * 512:(tc + 1) * 512],
                                          in_=pk[0:64, :]), "act")

                if FLASH_STAGE == "kaug1":
                    continue
                for tc in range(L // 512):
                    @S(k, "scalar")
                    def _(sc, tc=tc):
                        k.inc(sc.activation(out=sqbuf[:], in_=kslc(slice(tc * 512, (tc + 1) * 512)),
                                            func=AF.Square), "act")

                    if FLASH_STAGE == "kaug2a":
                        continue

                    @S(k, "tensor")
                    def _(t, tc=tc):
                        k.inc(t.matmul(pden[0:1, :], k.constcol(1.0)[0:64, :], sqbuf[:],
                                       start=True, stop=True), "pe")

                    if FLASH_STAGE == "kaug2b":
                        continue

                    @S(k, "vector")
                    def _(v, tc=tc):
                        k.inc(v.reduce_max(out=norms[0:1, L + 513:L + 514],
                                           in_=pden[0:1, :], axis=AX.X), "dve")
                        v.wait_ge(k.sems["dve"], k.cnt["dve"])
                        if tc == 0:
                            k.inc(v.tensor_copy(out=norms[0:1, L + 512:L + 513],
                                                in_=norms[0:1, L + 513:L + 514]), "dve")
                        else:
                            k.inc(v.tensor_max(out=norms[0:1, L + 512:L + 513],
                                               in0=norms[0:1, L + 512:L + 513],
                                               in1=norms[0:1, L + 513:L + 514]), "dve")

                if FLASH_STAGE == "kaug2":
                    continue

                @S(k, "scalar")
                def _(sc):
                    sc.activation(out=norms[0:1, L + 514:L + 515],
                                  in_=norms[0:1, L + 512:L + 513], func=AF.Ln)
                    k.inc(sc.activation(out=norms[0:1, L + 514:L + 515],
                                        in_=norms[0:1, L + 514:L + 515],
                                        func=AF.Exp, scale=0.5), "act")

                if FLASH_STAGE == "kaug":
                    continue
                for tc in range(L // 512):
                    @S(k, "tensor")
                    def _(t, h=h, tc=tc):
                        for g in range(DG):
                            ins = t.matmul(pk[0:64, :], qpT[:, g, h * 64:(h + 1) * 64],
                                           xn_sb[:, g, toff + tc * 512: toff + (tc + 1) * 512],
                                           start=(g == 0), stop=(g == DG - 1))
                        k.inc(ins, "pe")

                    @S(k, "scalar")
                    def _(sc, tc=tc):
                        k.inc(sc.activation(out=qaug[0:64, tc * 512:(tc + 1) * 512],
                                            in_=pk[0:64, :], func=AF.Identity, scale=0.125), "act")
                        k.inc(sc.activation(out=sqbuf[:], in_=pk[0:64, :], func=AF.Square), "act")

                    @S(k, "tensor")
                    def _(t, tc=tc):
                        k.inc(t.matmul(pqn[0:1, :], k.constcol(1.0)[0:64, :], sqbuf[:],
                                       start=True, stop=True), "pe")

                    @S(k, "vector")
                    def _(v, tc=tc):
                        k.inc(v.tensor_copy(out=norms[0:1, tc * 512:(tc + 1) * 512],
                                            in_=pqn[0:1, :]), "dve")

                @S(k, "scalar")
                def _(sc):
                    sc.activation(out=norms[0:1, 0:L], in_=norms[0:1, 0:L], func=AF.Ln)
                    k.inc(sc.activation(out=norms[0:1, 0:L], in_=norms[0:1, 0:L],
                                        func=AF.Exp, scale=0.5), "act")

                @S(k, "vector")
                def _(v):
                    k.inc(v.tensor_scalar(out=norms[0:1, 0:L], in0=norms[0:1, 0:L],
                                          scalar1=norms[0:1, L + 514:L + 515],
                                          scalar2=-0.125, op0=ALU.mult, op1=ALU.mult), "dve")
                    v.wait_ge(k.sems["dve"], k.cnt["dve"])
                    k.inc(v.tensor_copy(out=qaug[64:65, :], in_=norms[0:1, 0:L]), "dve")

                if DEBUG_TAPS.get("flashdbg") and s == 0 and h == 0:
                    ph_dump_sbuf(k, "kaug", kaug[0:66, :])
                    ph_dump_sbuf(k, "qaug", qaug[0:66, :])
                    ph_dump_sbuf(k, "norms", norms[0:1, :])
                if FLASH_STAGE == "cq":
                    continue
                for qc in range(NQC):
                    ktmax = NKT if not causal else min(NKT, (qc + 1) * 4)
                    pe_ct = {}; act_ct = {}; mm2_ct = {}

                    def emit_stile(kt, qc=qc, h=h):
                        pb = kt % 2

                        def body(t, kt=kt, pb=pb):
                            if kt >= 2:
                                t.wait_ge(k.sems["act"], act_ct[kt - 2])
                            pe_ct[kt] = k.inc(
                                t.matmul(pe_[:, pb, :], kaug[0:65, kt * 128:(kt + 1) * 128],
                                         qaug[0:65, qc * 512:(qc + 1) * 512],
                                         start=True, stop=True), "pe")
                        return body

                    @S(k, "tensor")
                    def _(t, emit=emit_stile(0)):
                        emit(t)

                    for kt in range(ktmax):
                        pb = kt % 2
                        off = kt * 128 - qc * 512
                        if causal and 0 <= off <= 384:
                            @k.block.vector
                            def _(v, kt=kt, pb=pb, off=off):
                                v.wait_ge(k.sems["pe"], pe_ct[kt])
                                v.wait_ge(k.sems["act"], k.cnt["act"])
                                if kt >= 2:
                                    v.wait_ge(k.sems["pe"], mm2_ct[kt - 2])
                                j = off // 128
                                k.inc(v.tensor_add(out=mtile[:], in0=pe_[:, pb, :],
                                                   in1=k.sb["cs_cmask_bf"][:, j * 512:(j + 1) * 512]), "dve")

                            @k.block.scalar
                            def _(sc, kt=kt, pb=pb):
                                sc.wait_ge(k.sems["dve"], k.cnt["dve"])
                                act_ct[kt] = k.inc(sc.activation(out=ebf[:, pb, :], in_=mtile[:],
                                                                 func=AF.Exp), "act")
                        else:
                            @k.block.scalar
                            def _(sc, kt=kt, pb=pb):
                                sc.wait_ge(k.sems["pe"], pe_ct[kt])
                                if kt >= 2:
                                    sc.wait_ge(k.sems["pe"], mm2_ct[kt - 2])
                                act_ct[kt] = k.inc(sc.activation(out=ebf[:, pb, :],
                                                                 in_=pe_[:, pb, :],
                                                                 func=AF.Exp), "act")

                        @k.block.tensor
                        def _(t, kt=kt, pb=pb, first=(kt == 0), last=(kt == ktmax - 1)):
                            if kt + 1 < ktmax:
                                emit_stile(kt + 1)(t)
                            t.wait_ge(k.sems["act"], act_ct[kt])
                            t.matmul(pden[0:1, :], k.sb["cs_ones_bf"][:, 0:1], ebf[:, pb, :],
                                     start=first, stop=last)
                            mm2_ct[kt] = k.inc(
                                t.matmul(pov[:], vtok[:, kt, (0 if shared_kv else h) * 64:
                                                     (1 if shared_kv else h + 1) * 64],
                                         ebf[:, pb, :], start=first, stop=last), "pe")

                    @S(k, "vector")
                    def _(v):
                        k.inc(v.reciprocal(out=norms[0:1, L:L + 512], in_=pden[0:1, :]), "dve")

                    @S(k, "tensor")
                    def _(t):
                        k.inc(t.matmul(pbc[:], ones_row(k, 64), norms[0:1, L:L + 512],
                                       start=True, stop=True), "pe")

                    @S(k, "scalar")
                    def _(sc):
                        k.inc(sc.copy(out=otile[:], in_=pov[:]), "act")

                    @S(k, "vector")
                    def _(v, h=h, qc=qc):
                        k.inc(v.tensor_mul(out=obf[:], in0=otile[:], in1=pbc[:]), "dve")

                    @S(k, "sync")
                    def _(s, h=h, qc=qc):
                        k.inc(s.dma_start(out=osb_hbm[h * 64:(h + 1) * 64,
                                                      qc * 512:(qc + 1) * 512],
                                          in_=obf[:]), "st")

            if not shared_kv:
                @S(k, "gpsimd")
                def _(g):
                    k.inc(g.dma_start(out=opT, in_=k.inputs[pfx + "opT"].rearrange("(c p) n -> p c n", p=128)), "gdma")
                k.barrier()
            for tc in range(L // 512):
                @S(k, "sync")
                def _(s, tc=tc):
                    k.inc(s.dma_start(out=oin[:],
                                      in_=osb_hbm[:, tc * 512:(tc + 1) * 512]
                                      .rearrange("(g p) t -> p g t", p=128)), "sdma")

                for og in range(DG):
                    @S(k, "tensor")
                    def _(t, og=og, tc=tc):
                        for g in range(DG):
                            ins = t.matmul(po[:], opT[:, g, og * 128:(og + 1) * 128],
                                           oin[:, g, :],
                                           start=(g == 0), stop=(g == DG - 1))
                        k.inc(ins, "pe")

                    @S(k, "scalar")
                    def _(sc, og=og):
                        if bias_col is not None:
                            ins = sc.activation(out=ya[:], in_=po[:], func=AF.Identity,
                                                bias=bias_col[:, og:og + 1], scale=1.0)
                        else:
                            ins = sc.copy(out=ya[:], in_=po[:])
                        k.inc(ins, "act")

                    @S(k, "sync")
                    def _(sy, s=s, og=og, tc=tc):
                        i0 = (s * L + tc * 512) // 128
                        dst = out_dram[i0:i0 + 4, :, og * 128:(og + 1) * 128].rearrange("i p n -> p i n")
                        k.inc(sy.dma_start(out=dst, in_=ya[:].rearrange("p (i n) -> p i n", i=4)), "st")
        k.barrier()


def ph_l1_attn(k: K, xn_sb, a1):
    ph_flash_attn(k, xn_sb, a1, "l1a_", shared_kv=True, causal=False)


def ph_l3_attn(k: K, xn_sb, a3, bo_sb):
    ph_flash_attn(k, xn_sb, a3, "l3a_", shared_kv=False, causal=True, bias_col=bo_sb)



def ph_residual_sum(k: K, srcs, x_out, tag="sum"):
    nc = k.nc
    nsrc = len(srcs)
    with ExitStack() as ph:
        ld = [ph.enter_context(nc.sbuf_tensor(f"{tag}_ld{b}", [128, nsrc, DG * 128], F32))
              for b in range(2)]
        xw = [ph.enter_context(nc.sbuf_tensor(f"{tag}_x{b}", [128, DG * 128], F32))
              for b in range(2)]
        for i in range(NT128):
            b = i % 2

            @S(k, "sync")
            def _(s, i=i, b=b):
                for j in range(nsrc):
                    k.inc(s.dma_start(out=ld[b][:, j, :], in_=srcs[j](i)), "sdma")

            @S(k, "vector")
            def _(v, i=i, b=b):
                v.tensor_add(out=xw[b][:], in0=ld[b][:, 0, :], in1=ld[b][:, 1, :])
                for j in range(2, nsrc):
                    v.tensor_add(out=xw[b][:], in0=xw[b][:], in1=ld[b][:, j, :])
                k.inc(v.nop() if False else v.tensor_copy(out=xw[b][:], in_=xw[b][:]), "dve")

            @S(k, "sync")
            def _(s, i=i, b=b):
                k.inc(s.dma_start(out=x_out(i), in_=xw[b][:]), "st")
    k.barrier()



def build_B(stop_after=None, taps=()):
    k = K("B")
    x2 = k.din("x2", [NT128, 128, DG * 128], F32)
    counts_all = k.din("counts_all", [8, 8], F32)
    premask = k.din("premask", [8, 1], F32)
    xout = k.dout("xout", [NT128, 128, DG * 128], F32)
    for nm in ("qpT", "kpT", "vpT", "opT"):
        k.din("l3a_" + nm, [D, 1024], F32)
    k.din("l3a_bo", [128, DG], F32)
    k.din("l3f_wgT", [D, 8], F32)
    k.din("l3f_w1T", [8, D, 4096], F32)
    k.din("l3f_b1", [8, 128, 32], F32)
    k.din("l3f_woT", [2048, 1024], F32)
    k.din("l3f_bo", [128, DG], F32)
    decl_l2(k)

    k.block = k.es.enter_context(k.nc.Block())
    ph_preamble(k, CONSTVALS, ["ident", "tril01", "stril01", "striu01",
                               "diagbig", "ones", "iota8", "cmask"])
    xn_sb = k.sbuf("xn", [128, DG, T], BF16)
    bo_sb = k.sbuf("l3a_bo_sb", [128, DG], F32)
    xn_tok = k.dram("xn_tok", [T, D], BF16)

    @S(k, "sync")
    def _(s):
        k.inc(s.dma_start(out=bo_sb[:], in_=k.inputs["l3a_bo"]), "sdma")

    rtw_f = k.sbuf("rtw_f", [128, DG, 8], F32)
    wg_f = k.sbuf("wg_f", [128, DG, 8], F32)
    lg2_sb = k.sbuf("lg2", [128, NT128, 8], F32)
    lg3_sb = k.sbuf("lg3", [128, NT128, 8], F32)

    @S(k, "sync")
    def _(s):
        k.inc(s.dma_start(out=rtw_f[:], in_=k.inputs["rt_w"].rearrange("(c p) n -> p c n", p=128)), "sdma")
        k.inc(s.dma_start(out=wg_f[:], in_=k.inputs["l3f_wgT"].rearrange("(c p) n -> p c n", p=128)), "sdma")
    k.barrier()
    ph_residual_ln(k, [lambda i: x2[i]], None, xn_sb, xn_tok_hbm=xn_tok, tag="ln2",
                   router=(rtw_f, lg2_sb))
    if "xn2" in taps:
        ph_dump_sbuf(k, "xn2", xn_sb[:])
    if stop_after == "ln2":
        ph_finish(k)
        return k
    if stop_after in ("l3a2", "l3f2", "l2a2", "l2f2"):
        tmp = k.dout("bdump", [NT128, 128, DG * 128], F32)
        if stop_after == "l3a2":
            ph_l3_attn(k, xn_sb, tmp, bo_sb)
        elif stop_after == "l3f2":
            ph_l3_ffn(k, xn_sb, tmp, lg3_sb, test_logits=(wg_f,))
        elif stop_after == "l2a2":
            ph_l2_attn(k, xn_sb, tmp)
        elif stop_after == "l2f2":
            ph_l2_ffn(k, xn_sb, xn_tok, counts_all, premask, tmp, lg2_sb)
        ph_finish(k)
        return k

    a2 = k.dram("a2", [NT128, 128, DG * 128], F32)
    f2 = k.dram("f2", [NT128, 128, DG * 128], F32)
    ph_l2_ffn(k, xn_sb, xn_tok, counts_all, premask, f2, lg2_sb)
    ph_l2_attn(k, xn_sb, a2)

    x3 = k.dram("x3", [NT128, 128, DG * 128], F32)
    ph_residual_ln(k, [lambda i: x2[i], lambda i: a2[i], lambda i: f2[i]],
                   lambda i: x3[i], xn_sb, tag="ln3", router=(wg_f, lg3_sb))
    a3 = k.dram("a3", [NT128, 128, DG * 128], F32)
    f3 = k.dram("f3", [NT128, 128, DG * 128], F32)
    ph_l3_attn(k, xn_sb, a3, bo_sb)
    ph_l3_ffn(k, xn_sb, f3, lg3_sb)
    ph_residual_sum(k, [lambda i: x3[i], lambda i: a3[i], lambda i: f3[i]],
                    lambda i: xout[i], tag="fin")
    ph_finish(k)
    return k


def decl_l2(k: K):
    k.din("rt_w", [D, 8], F32)
    k.din("rt_b", [1, 8], F32)
    k.din("l2a_wipT", [D, 5312], F32)
    k.din("l2a_dtb", [1, 32], F32)
    k.din("l2a_alog", [1, 32], F32)
    k.din("l2a_dd", [1, 32], F32)
    k.din("l2a_bnw", [1, 64], F32)
    k.din("l2a_cnw", [1, 64], F32)
    k.din("l2a_bbias", [1, 2048], F32)
    k.din("l2a_cbias", [1, 2048], F32)
    k.din("l2a_normw", [128, 16], F32)
    k.din("l2a_wsmall", [D, 1216], F32)
    k.din("l2a_wopT", [2048, 1024], F32)
    k.din("l2f_w1", [8, D, 4096], F32)
    k.din("l2f_b1", [8, 128, 32], F32)
    k.din("l2f_w2", [8, 4096, 1024], F32)
    k.din("l2f_b2", [8, 1, 1024], F32)
    k.din("l2f_ow1", [D, 4096], F32)
    k.din("l2f_ob1", [128, 32], F32)
    k.din("l2f_ow2", [4096, 1024], F32)
    k.din("l2f_ob2", [1, 1024], F32)


def prep_weights_B(mi):
    w = {}
    w["rt_w"] = np.ascontiguousarray(mi["L2_ffn_router_w"], np.float32)
    w["rt_b"] = np.ascontiguousarray(mi["L2_ffn_router_b"].reshape(1, 8), np.float32)
    w["l3a_qpT"] = np.ascontiguousarray(mi["L3_attn_Wq"].T, np.float32)
    w["l3a_kpT"] = np.ascontiguousarray(mi["L3_attn_Wk"].T, np.float32)
    w["l3a_vpT"] = np.ascontiguousarray(mi["L3_attn_Wv"].T, np.float32)
    w["l3a_opT"] = np.ascontiguousarray(mi["L3_attn_Wo"].T, np.float32)
    w["l3a_bo"] = col128(mi["L3_attn_bo"])
    w["l3f_wgT"] = np.ascontiguousarray(mi["L3_ffn_Wg"].T, np.float32)
    w["l3f_w1T"] = np.ascontiguousarray(mi["L3_ffn_W1"].transpose(0, 2, 1), np.float32)
    w["l3f_b1"] = np.ascontiguousarray(
        np.stack([col128(mi["L3_ffn_b1"][e]) for e in range(8)]), np.float32)
    w["l3f_woT"] = np.ascontiguousarray(mi["L3_ffn_Wo"].T, np.float32)
    w["l3f_bo"] = col128(mi["L3_ffn_bo"])
    w["l2a_wipT"] = np.ascontiguousarray(mi["L2_attn_in_proj_w"].T, np.float32)
    w["l2a_dtb"] = mi["L2_attn_dt_bias"].reshape(1, 32).astype(np.float32)
    w["l2a_alog"] = mi["L2_attn_A_log"].reshape(1, 32).astype(np.float32)
    w["l2a_dd"] = mi["L2_attn_D"].reshape(1, 32).astype(np.float32)
    w["l2a_bnw"] = mi["L2_attn_B_norm_w"].reshape(1, 64).astype(np.float32)
    w["l2a_cnw"] = mi["L2_attn_C_norm_w"].reshape(1, 64).astype(np.float32)
    w["l2a_bbias"] = mi["L2_attn_B_bias"].reshape(1, 2048).astype(np.float32)
    w["l2a_cbias"] = mi["L2_attn_C_bias"].reshape(1, 2048).astype(np.float32)
    w["l2a_normw"] = col128(mi["L2_attn_norm_w"])
    wipT = np.asarray(mi["L2_attn_in_proj_w"].T, np.float32)
    w["l2a_wsmall"] = np.ascontiguousarray(np.concatenate(
        [wipT[:, 4224:4256], wipT[:, 5280:5312], wipT[:, 4096:4160],
         wipT[:, 4160:4224], wipT[:, 4256:5280]], axis=1))
    w["l2a_wopT"] = np.ascontiguousarray(mi["L2_attn_out_proj_w"].T, np.float32)
    w["l2f_w1"] = np.ascontiguousarray(mi["L2_ffn_w1"], np.float32)
    w["l2f_b1"] = np.ascontiguousarray(
        np.stack([col128(mi["L2_ffn_b1"][e]) for e in range(8)]), np.float32)
    w["l2f_w2"] = np.ascontiguousarray(mi["L2_ffn_w2"], np.float32)
    w["l2f_b2"] = mi["L2_ffn_b2"].reshape(8, 1, 1024).astype(np.float32)
    w["l2f_ow1"] = np.ascontiguousarray(mi["L2_ffn_ow1"], np.float32)
    w["l2f_ob1"] = col128(mi["L2_ffn_ob1"])
    w["l2f_ow2"] = np.ascontiguousarray(mi["L2_ffn_ow2"], np.float32)
    w["l2f_ob2"] = mi["L2_ffn_ob2"].reshape(1, 1024).astype(np.float32)
    return w



def ph_l3_ffn(k: K, xn_sb, f3, logits_sb, test_logits=None):
    nc = k.nc
    g_hbm = k.dram("l3f_glu", [2048, T], BF16)
    combf_hbm = k.dram("l3f_combf", [8, T], F32)
    with ExitStack() as ph:
        sb = lambda n, s, d=BF16: ph.enter_context(nc.sbuf_tensor("l3fs_" + n, s, d))
        ps = lambda n, s, d=F32: ph.enter_context(nc.psum_tensor("l3fs_" + n, s, d))
        wbig_t = sb("wbig", [128, 8 * 4096])
        wbig = wbig_t[:].rearrange("p (c n) -> p c n", c=8)
        b1c = sb("b1c", [128, 32], F32)
        boc = sb("boc", [128, DG], F32)
        iob = sb("iob", [128, 8], F32)
        gs = sb("gs", [128, 8], F32)
        cols = sb("cols", [128, 8], F32)
        comb = sb("comb", [128, 8], F32)
        crow = sb("crow", [1, 512], F32)
        hsb = sb("hsb", [128, 32, 512])
        glu = sb("glu", [128, 16, 512])
        tfp = sb("tfp", [128, 512], F32)
        tfp2 = sb("tfp2", [128, 2, 512], F32)
        eo = sb("eo", [128, 512], F32)
        eo2 = sb("eo2", [128, 2, 512], F32)
        zero = sb("zero", [128, DG * 128], F32)
        pg = ps("pg", [128, 8])
        ph_ = ps("ph", [128, 2, 512])
        pcb = ps("pcb", [128, 512])
        ptr = ps("ptr", [1, 128])

        @S(k, "sync")
        def _(s):
            k.inc(s.dma_start(out=boc[:], in_=k.inputs["l3f_bo"]), "sdma")
        k.barrier()

        @S(k, "tensor")
        def _(t):
            k.inc(t.matmul(pg[:], ones_row(k, 128), k.sb["cs_iota8"][0:1, :],
                           start=True, stop=True), "pe")

        @S(k, "vector")
        def _(v):
            k.inc(v.tensor_copy(out=iob[:], in_=pg[:]), "dve")
            k.inc(v.memset(zero[:], 0.0), "dve")
        for i in range(NT128):
            @S(k, "sync")
            def _(s, i=i):
                k.inc(s.dma_start(out=f3[i], in_=zero[:]), "st")
        k.barrier()

        tap_comb = k.dout("tap_comb", [NT128, 128, 8], F32) if DEBUG_TAPS.get("comb") else None
        if test_logits is not None:
            wgf = test_logits[0]
            xnb = sb("xnb32", [128, 128], F32)
            for i in range(NT128):
                for g in range(DG):
                    @S(k, "vector")
                    def _(v, i=i, g=g):
                        k.inc(v.tensor_copy(out=xnb[:], in_=xn_sb[:, g, i * 128:(i + 1) * 128]), "dve")

                    @S(k, "tensor")
                    def _(t, i=i, g=g):
                        k.inc(t.matmul(pg[:], xnb[:], wgf[:, g, :],
                                       start=(g == 0), stop=(g == DG - 1)), "pe")

                @S(k, "vector")
                def _(v, i=i):
                    k.inc(v.tensor_copy(out=logits_sb[:, i, :], in_=pg[:]), "dve")
            k.barrier()
        for i in range(NT128):
            @S(k, "vector")
            def _(v, i=i):
                k.inc(v.tensor_copy(out=gs[:], in_=logits_sb[:, i, :]), "dve")
                v.wait_ge(k.sems["dve"], k.cnt["dve"])
                k.inc(v.reduce_max(out=cols[:, 0:1], in_=gs[:], axis=AX.X), "dve")
                v.wait_ge(k.sems["dve"], k.cnt["dve"])
                k.inc(v.tensor_scalar(out=cols[:, 0:1], in0=cols[:, 0:1], scalar1=-1.0,
                                      scalar2=None, op0=ALU.mult), "dve")

            @S(k, "scalar")
            def _(sc, i=i):
                k.inc(sc.activation(out=gs[:], in_=gs[:], func=AF.Exp,
                                    bias=cols[:, 0:1], scale=1.0,
                                    accum_out=cols[:, 1:2]), "act")

            @S(k, "vector")
            def _(v, i=i):
                def step(ins):
                    c = k.inc(ins, "dve")
                    v.wait_ge(k.sems["dve"], c)
                step(v.reciprocal(out=cols[:, 1:2], in_=cols[:, 1:2]))
                step(v.tensor_scalar(out=gs[:], in0=gs[:], scalar1=cols[:, 1:2],
                                     scalar2=None, op0=ALU.mult))
                step(v.reduce_max(out=cols[:, 2:3], in_=gs[:], axis=AX.X))
                step(v.tensor_scalar(out=tfp[:, 0:8], in0=gs[:], scalar1=cols[:, 2:3],
                                     scalar2=None, op0=ALU.is_ge))
                step(v.scalar_tensor_tensor(out=tfp[:, 8:16], in0=tfp[:, 0:8],
                                            scalar=65536.0, in1=iob[:],
                                            op0=ALU.mult, op1=ALU.subtract))
                step(v.reduce_max(out=cols[:, 3:4], in_=tfp[:, 8:16], axis=AX.X))
                step(v.tensor_scalar(out=cols[:, 3:4], in0=cols[:, 3:4], scalar1=-1.0,
                                     scalar2=65536.0, op0=ALU.mult, op1=ALU.add))
                step(v.tensor_scalar(out=tfp[:, 0:8], in0=iob[:], scalar1=cols[:, 3:4],
                                     scalar2=None, op0=ALU.is_equal))
                step(v.scalar_tensor_tensor(out=tfp[:, 8:16], in0=tfp[:, 0:8],
                                            scalar=-1e9, in1=gs[:],
                                            op0=ALU.mult, op1=ALU.add))
                step(v.reduce_max(out=cols[:, 4:5], in_=tfp[:, 8:16], axis=AX.X))
                step(v.tensor_add(out=cols[:, 5:6], in0=cols[:, 2:3], in1=cols[:, 4:5]))
                step(v.tensor_scalar(out=cols[:, 5:6], in0=cols[:, 5:6], scalar1=1e-9,
                                     scalar2=None, op0=ALU.add))
                step(v.reciprocal(out=cols[:, 5:6], in_=cols[:, 5:6]))
                step(v.tensor_scalar(out=tfp[:, 0:8], in0=gs[:], scalar1=cols[:, 4:5],
                                     scalar2=None, op0=ALU.is_ge))
                step(v.tensor_mul(out=tfp[:, 8:16], in0=gs[:], in1=tfp[:, 0:8]))
                step(v.tensor_scalar(out=comb[:], in0=tfp[:, 8:16],
                                     scalar1=cols[:, 5:6], scalar2=None, op0=ALU.mult))

            if tap_comb is not None:
                @S(k, "sync")
                def _(s, i=i):
                    k.inc(s.dma_start(out=tap_comb[i], in_=comb[:]), "st")
            for e in range(8):
                @S(k, "tensor")
                def _(t, i=i, e=e):
                    k.inc(t.transpose(ptr[:], comb[:, e:e + 1], k.sb["cs_ident"][:]), "pe")

                @S(k, "vector")
                def _(v, i=i, e=e):
                    k.inc(v.tensor_copy(out=crow[0:1, 0:128], in_=ptr[0:1, :]), "dve")

                @S(k, "sync")
                def _(s, i=i, e=e):
                    k.inc(s.dma_start(out=combf_hbm[e:e + 1, i * 128:(i + 1) * 128],
                                      in_=crow[0:1, 0:128]), "st")
        k.barrier()

        for e in range(8):
            @S(k, "gpsimd")
            def _(g, e=e):
                k.inc(g.dma_start(out=wbig,
                                  in_=k.inputs["l3f_w1T"][e].rearrange("(c p) n -> p c n", p=128)), "gdma")

            @S(k, "sync")
            def _(s, e=e):
                k.inc(s.dma_start(out=b1c[:], in_=k.inputs["l3f_b1"][e]), "sdma")
            k.barrier()

            for tc in range(T // 512):
                ts_ = slice(tc * 512, (tc + 1) * 512)
                pe_jg = {}; act_jg = {}
                for jg in range(32):
                    pb = jg % 2
                    if jg == 0:
                        @S(k, "tensor")
                        def _(t, jg=jg, ts_=ts_, pb=pb):
                            for g in range(DG):
                                ins = t.matmul(ph_[:, pb, :], wbig[:, g, jg * 128:(jg + 1) * 128],
                                               xn_sb[:, g, ts_], start=(g == 0), stop=(g == DG - 1))
                            pe_jg[jg] = k.inc(ins, "pe")
                    else:
                        @k.block.tensor
                        def _(t, jg=jg, ts_=ts_, pb=pb):
                            if jg >= 2:
                                t.wait_ge(k.sems["act"], act_jg[jg - 2])
                            for g in range(DG):
                                ins = t.matmul(ph_[:, pb, :], wbig[:, g, jg * 128:(jg + 1) * 128],
                                               xn_sb[:, g, ts_], start=(g == 0), stop=(g == DG - 1))
                            pe_jg[jg] = k.inc(ins, "pe")

                    @k.block.scalar
                    def _(sc, jg=jg, pb=pb):
                        sc.wait_ge(k.sems["pe"], pe_jg[jg])
                        act_jg[jg] = k.inc(
                            sc.activation(out=hsb[:, jg, :], in_=ph_[:, pb, :], func=AF.Identity,
                                          bias=b1c[:, jg:jg + 1], scale=1.0), "act")

                act_jh = {}; dve_jh = {}
                for jh in range(16):
                    if jh == 0:
                        @S(k, "scalar")
                        def _(sc, jh=jh):
                            act_jh[jh] = k.inc(sc.activation(out=tfp[:, 0:512], in_=hsb[:, 16 + jh, :],
                                                             func=AF.Exp, scale=-1.0), "act")
                    else:
                        @k.block.scalar
                        def _(sc, jh=jh):
                            if jh >= 2:
                                sc.wait_ge(k.sems["dve"], dve_jh[jh - 2])
                            act_jh[jh] = k.inc(sc.activation(out=tfp[:, 512 * (jh % 2):512 * (jh % 2) + 512]
                                                             if False else tfp2[:, jh % 2, :],
                                                             in_=hsb[:, 16 + jh, :],
                                                             func=AF.Exp, scale=-1.0), "act")

                    @k.block.vector
                    def _(v, jh=jh):
                        v.wait_ge(k.sems["act"], act_jh[jh])
                        buf = tfp[:, 0:512] if jh == 0 else tfp2[:, jh % 2, :]
                        k.inc(v.tensor_scalar(out=buf, in0=buf, scalar1=1.0,
                                              scalar2=None, op0=ALU.add), "dve")
                        v.wait_ge(k.sems["dve"], k.cnt["dve"])
                        k.inc(v.reciprocal(out=buf, in_=buf), "dve")
                        v.wait_ge(k.sems["dve"], k.cnt["dve"])
                        dve_jh[jh] = k.inc(v.tensor_mul(out=glu[:, jh, :], in0=hsb[:, jh, :],
                                                        in1=buf), "dve")

                @S(k, "sync")
                def _(s, tc=tc, ts_=ts_):
                    k.inc(s.dma_start(out=g_hbm[:, ts_].rearrange("(c p) t -> p c t", p=128),
                                      in_=glu[:]), "st")

            @S(k, "gpsimd")
            def _(g, e=e):
                k.inc(g.dma_start(
                    out=wbig_t[:, 0:16384].rearrange("p (c n) -> p c n", c=16),
                    in_=k.inputs["l3f_woT"].rearrange("(c p) n -> p c n", p=128)), "gdma")
            k.barrier()
            wot = wbig_t[:, 0:16384].rearrange("p (c n) -> p c n", c=16)

            for tc in range(T // 512):
                ts_ = slice(tc * 512, (tc + 1) * 512)

                @S(k, "sync")
                def _(s, e=e, tc=tc, ts_=ts_):
                    k.inc(s.dma_start(out=glu[:],
                                      in_=g_hbm[:, ts_].rearrange("(c p) t -> p c t", p=128)), "sdma")
                    k.inc(s.dma_start(out=crow[0:1, 0:512],
                                      in_=combf_hbm[e:e + 1, ts_]), "sdma")

                @S(k, "tensor")
                def _(t, ts_=ts_):
                    k.inc(t.matmul(pcb[:], ones_row(k, 128), crow[0:1, 0:512],
                                   start=True, stop=True), "pe")

                pe_og = {}; act_og = {}; dve_og = {}; st_og = {}
                for og in range(DG):
                    pb = og % 2
                    if og == 0:
                        @S(k, "tensor")
                        def _(t, og=og, pb=pb):
                            for g2 in range(16):
                                ins = t.matmul(ph_[:, pb, :], wot[:, g2, og * 128:(og + 1) * 128],
                                               glu[:, g2, :], start=(g2 == 0), stop=(g2 == 15))
                            pe_og[og] = k.inc(ins, "pe")
                    else:
                        @k.block.tensor
                        def _(t, og=og, pb=pb):
                            if og >= 2:
                                t.wait_ge(k.sems["act"], act_og[og - 2])
                            for g2 in range(16):
                                ins = t.matmul(ph_[:, pb, :], wot[:, g2, og * 128:(og + 1) * 128],
                                               glu[:, g2, :], start=(g2 == 0), stop=(g2 == 15))
                            pe_og[og] = k.inc(ins, "pe")

                    @k.block.scalar
                    def _(sc, og=og, pb=pb):
                        sc.wait_ge(k.sems["pe"], pe_og[og])
                        if og >= 2:
                            sc.wait_ge(k.sems["dve"], dve_og[og - 2])
                        act_og[og] = k.inc(
                            sc.activation(out=eo2[:, pb, :], in_=ph_[:, pb, :], func=AF.Identity,
                                          bias=boc[:, og:og + 1], scale=1.0), "act")

                    @k.block.vector
                    def _(v, og=og, pb=pb):
                        v.wait_ge(k.sems["act"], act_og[og])
                        if og >= 2:
                            v.wait_ge(k.sems["st"], st_og[og - 2])
                        dve_og[og] = k.inc(v.tensor_mul(out=eo2[:, pb, :], in0=eo2[:, pb, :],
                                                        in1=pcb[:]), "dve")

                    @k.block.gpsimd
                    def _(s, tc=tc, og=og, pb=pb):
                        s.wait_ge(k.sems["dve"], dve_og[og])
                        i0 = tc * 4
                        dst = f3[i0:i0 + 4, :, og * 128:(og + 1) * 128].rearrange("i p n -> p i n")
                        st_og[og] = k.inc(s.dma_start(out=dst,
                                                      in_=eo2[:, pb, :].rearrange("p (i n) -> p i n", i=4),
                                                      accum_op=ALU.add), "st")
        k.barrier()




def ph_l2_ffn(k: K, xn_sb, xn_tok, counts_all_d, premask_d, f2, logits_sb):
    nc = k.nc
    C = 1280
    bufloc = k.dram("l2f_bufloc", [8 * C, D], BF16)
    yall = k.dram("l2f_yall", [8 * C, D], BF16)
    h2_hbm = k.dram("l2f_h2", [4096, C], BF16)
    g2_hbm = k.dram("l2f_g2", [4096, T], BF16)
    with ExitStack() as ph:
        sb = lambda n, s, d=BF16: ph.enter_context(nc.sbuf_tensor("l2fs_" + n, s, d))
        ps = lambda n, s, d=F32: ph.enter_context(nc.psum_tensor("l2fs_" + n, s, d))
        wbig_t = sb("wbig", [128, 8 * 4096])
        w1v = wbig_t[:].rearrange("p (c n) -> p c n", c=8)
        w2v = wbig_t[:].rearrange("p (c n) -> p c n", c=32)
        cnts = sb("cnts", [8, 8], F32)
        pmask = sb("pmask", [8, 1], F32)
        brow = sb("brow", [1, 8], F32)
        smalls = sb("smalls", [128, 40], F32)
        cols = sb("cols", [128, 8], F32)
        carry = sb("carry", [1, 8], F32)
        slots = sb("slots", [128, NT128], I32)
        valid = sb("valid", [128, NT128], F32)
        vcomp = sb("vcomp", [128, NT128], F32)
        xtk = sb("xtk", [128, 1024])
        bufeT = sb("bufeT", [128, DG, C])
        hrow = sb("hrow", [128, C])
        hTl = sb("hTl", [128, 32, 128])
        ysb = sb("ysb", [128, 1024], F32)
        b2bc = sb("b2bc", [128, 1024], F32)
        ygat = sb("ygat", [128, 1024])
        zero = sb("zero", [128, 1024])
        otf = sb("otf", [128, 1024], F32)
        b1c = sb("b1c", [128, 32], F32)
        b2row = sb("b2row", [1, 1024], F32)
        pall = ps("pall", [128, 4, 512])
        ptr = ps("ptr", [128, 128])
        ptrb = ps("ptrb", [128, 128], BF16)
        prow = ps("prow", [1, 16])

        @S(k, "sync")
        def _(s):
            k.inc(s.dma_start(out=cnts[:], in_=counts_all_d), "sdma")
            k.inc(s.dma_start(out=pmask[:], in_=premask_d), "sdma")
            k.inc(s.dma_start(out=brow[:], in_=k.inputs["rt_b"]), "sdma")

        bc_reg = [None]

        @S(k, "gpsimd")
        def _(g):
            bc_reg[0] = g.alloc_register("l2f_bc")
            g.reg_mov(bc_reg[0], 8 * C - 1)
        k.barrier()

        @S(k, "tensor")
        def _(t):
            k.inc(t.matmul(prow[0:1, 0:8], pmask[:], cnts[:], start=True, stop=True), "pe")

        @S(k, "vector")
        def _(v):
            k.inc(v.tensor_copy(out=smalls[0:1, 32:40], in_=prow[0:1, 0:8]), "dve")

        @S(k, "tensor")
        def _(t):
            t.matmul(pall[:, 0, 0:8], ones_row(k, 128), brow[:], start=True, stop=True)
            t.matmul(pall[:, 0, 8:16], ones_row(k, 128), k.sb["cs_iota8"][0:1, :],
                     start=True, stop=True)
            k.inc(t.matmul(pall[:, 0, 16:24], ones_row(k, 128), smalls[0:1, 32:40],
                           start=True, stop=True), "pe")

        @S(k, "vector")
        def _(v):
            k.inc(v.tensor_copy(out=smalls[:, 0:24], in_=pall[:, 0, 0:24]), "dve")
            k.inc(v.memset(zero[:], 0.0), "dve")
            k.inc(v.memset(carry[:], 0.0), "dve")

        for i in range(8 * C // 128):
            @S(k, "sync")
            def _(s, i=i):
                k.inc(s.dma_start(out=bufloc[i * 128:(i + 1) * 128, :], in_=zero[:]), "st")
        k.barrier()

        ge8 = smalls[:, 24:32]
        for i in range(NT128):
            @S(k, "vector")
            def _(v, i=i):
                def step(ins):
                    c = k.inc(ins, "dve")
                    v.wait_ge(k.sems["dve"], c)
                lg = cols[:, 0:8]
                step(v.tensor_add(out=lg, in0=logits_sb[:, i, :], in1=smalls[:, 0:8]))
                step(v.reduce_max(out=smalls[:, 36:37], in_=lg, axis=AX.X))
                step(v.tensor_scalar(out=ge8, in0=lg, scalar1=smalls[:, 36:37],
                                     scalar2=None, op0=ALU.is_ge))
                step(v.scalar_tensor_tensor(out=ge8, in0=ge8, scalar=65536.0,
                                            in1=smalls[:, 8:16], op0=ALU.mult,
                                            op1=ALU.subtract))
                step(v.reduce_max(out=smalls[:, 36:37], in_=ge8, axis=AX.X))
                step(v.tensor_scalar(out=smalls[:, 36:37], in0=smalls[:, 36:37],
                                     scalar1=-1.0, scalar2=65536.0,
                                     op0=ALU.mult, op1=ALU.add))
                step(v.tensor_scalar(out=ge8, in0=smalls[:, 8:16],
                                     scalar1=smalls[:, 36:37], scalar2=None,
                                     op0=ALU.is_equal))

            @S(k, "tensor")
            def _(t, i=i):
                t.matmul(ptr[:, 0:8], k.sb["cs_stril01"][:], ge8, start=True, stop=False)
                k.inc(t.matmul(ptr[:, 0:8], ones_row(k, 128), carry[:],
                               start=False, stop=True), "pe")

            @S(k, "vector")
            def _(v, i=i):
                def step(ins):
                    c = k.inc(ins, "dve")
                    v.wait_ge(k.sems["dve"], c)
                step(v.tensor_mul(out=cols[:, 0:8], in0=ge8, in1=ptr[:, 0:8]))
                step(v.reduce_sum(out=smalls[:, 37:38], in_=cols[:, 0:8], axis=AX.X))
                step(v.tensor_mul(out=cols[:, 0:8], in0=ge8, in1=smalls[:, 16:24]))
                step(v.reduce_sum(out=smalls[:, 38:39], in_=cols[:, 0:8], axis=AX.X))
                step(v.tensor_add(out=smalls[:, 39:40], in0=smalls[:, 37:38],
                                  in1=smalls[:, 38:39]))
                step(v.tensor_scalar(out=valid[:, i:i + 1], in0=smalls[:, 39:40],
                                     scalar1=1280.0, scalar2=None, op0=ALU.is_lt))
                step(v.tensor_scalar(out=vcomp[:, i:i + 1], in0=valid[:, i:i + 1],
                                     scalar1=-1.0, scalar2=1.0,
                                     op0=ALU.mult, op1=ALU.add))
                step(v.tensor_scalar(out=smalls[:, 36:37], in0=smalls[:, 36:37],
                                     scalar1=1280.0, scalar2=None, op0=ALU.mult))
                step(v.tensor_add(out=smalls[:, 36:37], in0=smalls[:, 36:37],
                                  in1=smalls[:, 37:38]))
                step(v.tensor_scalar(out=smalls[:, 38:39], in0=smalls[:, 37:38],
                                     scalar1=1280.0, scalar2=1e7, op0=ALU.is_ge,
                                     op1=ALU.mult))
                step(v.tensor_add(out=smalls[:, 36:37], in0=smalls[:, 36:37],
                                  in1=smalls[:, 38:39]))
                step(v.tensor_copy(out=slots[:, i:i + 1], in_=smalls[:, 36:37]))

            @S(k, "tensor")
            def _(t, i=i):
                k.inc(t.matmul(prow[0:1, 0:8], k.constcol(1.0), ge8,
                               start=(i == 0), stop=(i == NT128 - 1)), "pe")

            @S(k, "scalar")
            def _(sc, i=i):
                k.inc(sc.copy(out=carry[:], in_=prow[0:1, 0:8]), "act")

            @S(k, "sync")
            def _(s, i=i):
                k.inc(s.dma_start(out=xtk[:], in_=xn_tok[i * 128:(i + 1) * 128, :]), "sdma")

            @S(k, "gpsimd")
            def _(g, i=i):
                k.inc(g.indirect_dma_start(
                    out=bufloc, out_offset=IndirectOffsetOnAxis(slots[:, i:i + 1], 0),
                    in_=xtk[:], in_offset=None,
                    bounds_check=bc_reg[0], oob_is_err=False), "gdma")
        k.barrier()

        for e in range(8):
            for ct in range(C // 128):
                @S(k, "sync")
                def _(s, e=e, ct=ct):
                    k.inc(s.dma_start(out=xtk[:],
                                      in_=bufloc[e * C + ct * 128: e * C + (ct + 1) * 128, :]), "sdma")

                for g in range(DG):
                    @S(k, "tensor")
                    def _(t, ct=ct, g=g):
                        k.inc(t.transpose(ptrb[:], xtk[:, g * 128:(g + 1) * 128],
                                          k.sb["cs_ident_bf"][:]), "pe")

                    @S(k, "scalar")
                    def _(sc, ct=ct, g=g):
                        k.inc(sc.copy(out=bufeT[:, g, ct * 128:(ct + 1) * 128],
                                      in_=ptrb[:]), "act")

            @S(k, "gpsimd")
            def _(g, e=e):
                k.inc(g.dma_start(out=wbig_t[:].rearrange("p (c n) -> p c n", c=8),
                                  in_=k.inputs["l2f_w1"][e].rearrange("(c p) n -> p c n", p=128)), "gdma")

            @S(k, "sync")
            def _(s, e=e):
                k.inc(s.dma_start(out=b1c[:], in_=k.inputs["l2f_b1"][e]), "sdma")
            k.barrier()

            for jg in range(32):
                @S(k, "tensor")
                def _(t, jg=jg):
                    for cc in range(3):
                        w_ = 512 if cc < 2 else 256
                        for g in range(DG):
                            ins = t.matmul(pall[:, cc, 0:w_], w1v[:, g, jg * 128:(jg + 1) * 128],
                                           bufeT[:, g, cc * 512:cc * 512 + w_],
                                           start=(g == 0), stop=(g == DG - 1))
                    k.inc(ins, "pe")

                @S(k, "scalar")
                def _(sc, jg=jg):
                    sc.activation(out=hrow[:, 0:512], in_=pall[:, 0, :], func=AF.Relu,
                                  bias=b1c[:, jg:jg + 1], scale=1.0)
                    sc.activation(out=hrow[:, 512:1024], in_=pall[:, 1, :], func=AF.Relu,
                                  bias=b1c[:, jg:jg + 1], scale=1.0)
                    k.inc(sc.activation(out=hrow[:, 1024:1280], in_=pall[:, 2, 0:256],
                                        func=AF.Relu, bias=b1c[:, jg:jg + 1], scale=1.0), "act")

                @S(k, "sync")
                def _(s, jg=jg):
                    k.inc(s.dma_start(out=h2_hbm[jg * 128:(jg + 1) * 128, :], in_=hrow[:]), "st")

            @S(k, "gpsimd")
            def _(g, e=e):
                k.inc(g.dma_start(out=wbig_t[:].rearrange("p (c n) -> p c n", c=32),
                                  in_=k.inputs["l2f_w2"][e].rearrange("(c p) n -> p c n", p=128)), "gdma")

            @S(k, "sync")
            def _(s, e=e):
                k.inc(s.dma_start(out=b2row[:], in_=k.inputs["l2f_b2"][e]), "sdma")
            k.barrier()

            @S(k, "tensor")
            def _(t):
                t.matmul(pall[:, 2, :], ones_row(k, 128), b2row[0:1, 0:512],
                         start=True, stop=True)
                k.inc(t.matmul(pall[:, 3, :], ones_row(k, 128), b2row[0:1, 512:1024],
                               start=True, stop=True), "pe")

            @S(k, "vector")
            def _(v):
                v.tensor_copy(out=b2bc[:, 0:512], in_=pall[:, 2, :])
                k.inc(v.tensor_copy(out=b2bc[:, 512:1024], in_=pall[:, 3, :]), "dve")

            for ct in range(C // 128):
                @S(k, "sync")
                def _(s, ct=ct):
                    k.inc(s.dma_start(out=hTl[:],
                                      in_=h2_hbm[:, ct * 128:(ct + 1) * 128]
                                      .rearrange("(g p) c -> p g c", p=128)), "sdma")

                @S(k, "tensor")
                def _(t, ct=ct):
                    for jg in range(32):
                        ins = t.matmul(pall[:, 0, :], hTl[:, jg, :], w2v[:, jg, 0:512],
                                       start=(jg == 0), stop=(jg == 31))
                        ins = t.matmul(pall[:, 1, :], hTl[:, jg, :], w2v[:, jg, 512:1024],
                                       start=(jg == 0), stop=(jg == 31))
                    k.inc(ins, "pe")

                @S(k, "vector")
                def _(v, ct=ct):
                    v.tensor_add(out=ysb[:, 0:512], in0=pall[:, 0, :], in1=b2bc[:, 0:512])
                    k.inc(v.tensor_add(out=ysb[:, 512:1024], in0=pall[:, 1, :],
                                       in1=b2bc[:, 512:1024]), "dve")
                    v.wait_ge(k.sems["dve"], k.cnt["dve"])
                    k.inc(v.tensor_copy(out=ygat[:], in_=ysb[:]), "dve")

                @S(k, "sync")
                def _(s, e=e, ct=ct):
                    k.inc(s.dma_start(out=yall[e * C + ct * 128:e * C + (ct + 1) * 128, :],
                                      in_=ygat[:]), "st")
        k.barrier()

        @S(k, "gpsimd")
        def _(g):
            k.inc(g.dma_start(out=wbig_t[:].rearrange("p (c n) -> p c n", c=8),
                              in_=k.inputs["l2f_ow1"].rearrange("(c p) n -> p c n", p=128)), "gdma")

        @S(k, "sync")
        def _(s):
            k.inc(s.dma_start(out=b1c[:], in_=k.inputs["l2f_ob1"]), "sdma")
        k.barrier()

        for tc in range(T // 512):
            ts_ = slice(tc * 512, (tc + 1) * 512)
            for jg in range(32):
                @S(k, "tensor")
                def _(t, jg=jg, ts_=ts_):
                    for g in range(DG):
                        ins = t.matmul(pall[:, 0, :], w1v[:, g, jg * 128:(jg + 1) * 128],
                                       xn_sb[:, g, ts_], start=(g == 0), stop=(g == DG - 1))
                    k.inc(ins, "pe")

                @S(k, "scalar")
                def _(sc, jg=jg):
                    k.inc(sc.activation(out=hrow[:, 0:512], in_=pall[:, 0, :], func=AF.Relu,
                                        bias=b1c[:, jg:jg + 1], scale=1.0), "act")

                @S(k, "sync")
                def _(s, tc=tc, jg=jg):
                    k.inc(s.dma_start(out=g2_hbm[jg * 128:(jg + 1) * 128, tc * 512:(tc + 1) * 512],
                                      in_=hrow[:, 0:512]), "st")
        k.barrier()

        @S(k, "gpsimd")
        def _(g):
            k.inc(g.dma_start(out=wbig_t[:].rearrange("p (c n) -> p c n", c=32),
                              in_=k.inputs["l2f_ow2"].rearrange("(c p) n -> p c n", p=128)), "gdma")

        @S(k, "sync")
        def _(s):
            k.inc(s.dma_start(out=b2row[:], in_=k.inputs["l2f_ob2"]), "sdma")
        k.barrier()

        @S(k, "tensor")
        def _(t):
            t.matmul(pall[:, 2, :], ones_row(k, 128), b2row[0:1, 0:512], start=True, stop=True)
            k.inc(t.matmul(pall[:, 3, :], ones_row(k, 128), b2row[0:1, 512:1024],
                           start=True, stop=True), "pe")

        @S(k, "vector")
        def _(v):
            v.tensor_copy(out=b2bc[:, 0:512], in_=pall[:, 2, :])
            k.inc(v.tensor_copy(out=b2bc[:, 512:1024], in_=pall[:, 3, :]), "dve")

        for i in range(NT128):
            @S(k, "sync")
            def _(s, i=i):
                k.inc(s.dma_start(out=hTl[:],
                                  in_=g2_hbm[:, i * 128:(i + 1) * 128]
                                  .rearrange("(g p) t -> p g t", p=128)), "sdma")

            @S(k, "vector")
            def _(v, i=i):
                k.inc(v.memset(ygat[:], 0.0), "dve")

            @S(k, "gpsimd")
            def _(g, i=i):
                k.inc(g.indirect_dma_start(
                    out=ygat[:], out_offset=None,
                    in_=yall, in_offset=IndirectOffsetOnAxis(slots[:, i:i + 1], 0),
                    bounds_check=bc_reg[0], oob_is_err=False), "gdma")

            @S(k, "tensor")
            def _(t, i=i):
                for jg in range(32):
                    ins = t.matmul(pall[:, 0, :], hTl[:, jg, :], w2v[:, jg, 0:512],
                                   start=(jg == 0), stop=(jg == 31))
                    ins = t.matmul(pall[:, 1, :], hTl[:, jg, :], w2v[:, jg, 512:1024],
                                   start=(jg == 0), stop=(jg == 31))
                k.inc(ins, "pe")

            @S(k, "vector")
            def _(v, i=i):
                def step(ins):
                    c = k.inc(ins, "dve")
                    v.wait_ge(k.sems["dve"], c)
                step(v.tensor_add(out=ysb[:, 0:512], in0=pall[:, 0, :], in1=b2bc[:, 0:512]))
                step(v.tensor_add(out=ysb[:, 512:1024], in0=pall[:, 1, :], in1=b2bc[:, 512:1024]))
                step(v.tensor_scalar(out=ysb[:], in0=ysb[:], scalar1=vcomp[:, i:i + 1],
                                     scalar2=None, op0=ALU.mult))
                step(v.tensor_scalar(out=otf[:], in0=ygat[:], scalar1=valid[:, i:i + 1],
                                     scalar2=None, op0=ALU.mult))
                step(v.tensor_add(out=otf[:], in0=otf[:], in1=ysb[:]))

            for g in range(DG):
                @S(k, "tensor")
                def _(t, i=i, g=g):
                    k.inc(t.transpose(ptr[:], otf[:, g * 128:(g + 1) * 128],
                                      k.sb["cs_ident"][:]), "pe")

                @S(k, "scalar")
                def _(sc, i=i, g=g):
                    k.inc(sc.copy(out=ysb[:, 0:128], in_=ptr[:]), "act")

                @S(k, "sync")
                def _(s, i=i, g=g):
                    k.inc(s.dma_start(out=f2[i][:, g * 128:(g + 1) * 128],
                                      in_=ysb[:, 0:128]), "st")
        k.barrier()



def ph_l2_attn(k: K, xn_sb, a2):
    nc = k.nc
    H, P, N, DI = 32, 64, 64, 2048
    NCH = L // 128
    zf_hbm = k.dram("l2a_zf", [DI, T], BF16)
    xf_hbm = k.dram("l2a_xf", [DI, T], BF16)
    y_hbm = k.dram("l2a_y", [DI, T], BF16)
    bm_hbm = k.dram("l2a_bm", [L, DI], BF16)
    cm_hbm = k.dram("l2a_cm", [L, DI], BF16)
    q_hbm = k.dram("l2a_q", [L + 128, 32], F32)

    with ExitStack() as ph:
        sb = lambda n, s, d=BF16: ph.enter_context(nc.sbuf_tensor("l2a1_" + n, s, d))
        ps = lambda n, s, d=F32: ph.enter_context(nc.psum_tensor("l2a1_" + n, s, d))
        slab = sb("slab", [128, DG, 512])
        ytile = sb("ytile", [128, 512])
        pj = ps("pj", [128, 2, 512])
        for jslab in range(8):
            @S(k, "gpsimd")
            def _(g, jslab=jslab):
                k.inc(g.dma_start(out=slab[:],
                                  in_=k.inputs["l2a_wipT"][:, jslab * 512:(jslab + 1) * 512]
                                  .rearrange("(c p) n -> p c n", p=128)), "gdma")
            k.barrier()
            for jt in range(4):
                jglob = jslab * 512 + jt * 128
                dst_h = zf_hbm if jglob < DI else xf_hbm
                row0 = jglob if jglob < DI else jglob - DI
                for tc in range(T // 512):
                    @S(k, "tensor")
                    def _(t, jslab=jslab, jt=jt, tc=tc):
                        for g in range(DG):
                            ins = t.matmul(pj[:, tc % 2, :], slab[:, g, jt * 128:(jt + 1) * 128],
                                           xn_sb[:, g, tc * 512:(tc + 1) * 512],
                                           start=(g == 0), stop=(g == DG - 1))
                        k.inc(ins, "pe")

                    @S(k, "scalar")
                    def _(sc, tc=tc):
                        k.inc(sc.copy(out=ytile[:], in_=pj[:, tc % 2, :]), "act")

                    @S(k, "sync")
                    def _(s, dst_h=dst_h, row0=row0, tc=tc):
                        k.inc(s.dma_start(out=dst_h[row0:row0 + 128, tc * 512:(tc + 1) * 512],
                                          in_=ytile[:]), "st")
        k.barrier()

    with ExitStack() as ph:
        sb = lambda n, s, d=BF16: ph.enter_context(nc.sbuf_tensor("l2a2_" + n, s, d))
        ps = lambda n, s, d=F32: ph.enter_context(nc.psum_tensor("l2a2_" + n, s, d))
        wsm = sb("wsm", [128, DG, 1216])
        rows32 = sb("rows32", [1, 32], F32)
        dtb_bc = sb("dtb_bc", [128, 32], F32)
        a_bc = sb("a_bc", [128, 32], F32)
        bnw_bc = sb("bnw_bc", [128, 64], F32)
        cnw_bc = sb("cnw_bc", [128, 64], F32)
        bbias_bc = sb("bbias_bc", [128, DI], F32)
        cbias_bc = sb("cbias_bc", [128, DI], F32)
        drep = sb("drep", [128, 32], F32)
        dt_tok = sb("dt_tok", [128, NCH, 32], F32)
        la_tok = sb("la_tok", [128, NCH, 32], F32)
        g_tok = sb("g_tok", [128, NCH, 32], F32)
        q_tok = sb("q_tok", [128, NCH, 32], F32)
        lne_tok = sb("lne_tok", [128, NCH, 32], F32)
        gme_tok = sb("gme_tok", [128, NCH, 32], F32)
        s32a = sb("s32a", [128, 64], F32)
        s32b = sb("s32b", [128, 64], F32)
        rstdc = sb("rstdc", [128, 4], F32)
        bn_t = sb("bn_t", [128, 64], F32)
        cn_t = sb("cn_t", [128, 64], F32)
        thsb = sb("thsb", [128, 1024], F32)
        tcs = sb("tcs", [128, 1024], F32)
        tc2 = sb("tc2", [128, 1024], F32)
        kint = sb("kint", [128, 1024], I32)
        kf = sb("kf", [128, 1024], F32)
        sin_t = sb("sin_t", [128, 1024])
        cos_t = sb("cos_t", [128, 1024])
        vf = sb("vf", [128, DI], F32)
        bm_ch = sb("bm_ch", [128, DI])
        cm_ch = sb("cm_ch", [128, DI])
        carry_row = sb("carry_row", [1, 1024], F32)
        xt_ch = sb("xt_ch", [128, 16, 128])
        xfch = sb("xfch", [128, 16, 128])
        ych = sb("ych", [128, 16, 128])
        hcar = sb("hcar", [64, H, 64], F32)
        gm = sb("gm", [128, 128], F32)
        wg_t = sb("wg_t", [128, 128])
        wf_t = sb("wf_t", [128, 128])
        mt_t = sb("mt_t", [128, 128])
        bfh = sb("bfh", [64, 128])
        cfh = sb("cfh", [64, 128])
        crs = sb("crs", [64, 128])
        bw_t = sb("bw_t", [128, 64])
        er = sb("er", [1, 128], F32)
        wcf = sb("wcf", [128, 1], F32)
        hbf = sb("hbf", [64, 64])
        s64 = sb("s64", [64, 1], F32)
        lastriu = sb("lastriu", [128, 128])
        pS = ps("pS", [128, 3, 512])
        pG = ps("pG", [128, 128])
        pY = ps("pY", [64, 128])
        pH = ps("pH", [64, 64])
        pE = ps("pE", [64, 128])
        pGb = ps("pGb", [128, 128], BF16)
        ptb = pGb[0:64, :]

        @S(k, "sync")
        def _(s):
            k.inc(s.dma_start(out=rows32[:], in_=k.inputs["l2a_dtb"]), "sdma")

        @S(k, "gpsimd")
        def _(g):
            k.inc(g.dma_start(out=wsm[:],
                              in_=k.inputs["l2a_wsmall"].rearrange("(c p) n -> p c n", p=128)), "gdma")
        k.barrier()

        def bcast_row(srcrow, dst, width, extra=None):
            @S(k, "tensor")
            def _(t):
                k.inc(t.matmul(pS[:, 0, 0:width], ones_row(k, 128), srcrow,
                               start=True, stop=True), "pe")

            @S(k, "vector")
            def _(v):
                k.inc(v.tensor_copy(out=dst, in_=pS[:, 0, 0:width]), "dve")

        bcast_row(rows32[0:1, :], dtb_bc[:], 32)

        @S(k, "sync")
        def _(s):
            k.inc(s.dma_start(out=rows32[:], in_=k.inputs["l2a_alog"]), "sdma")

        @S(k, "scalar")
        def _(sc):
            k.inc(sc.activation(out=rows32[:], in_=rows32[:], func=AF.Exp), "act")

        @S(k, "vector")
        def _(v):
            k.inc(v.tensor_scalar(out=rows32[:], in0=rows32[:], scalar1=-1.0,
                                  scalar2=None, op0=ALU.mult), "dve")
        bcast_row(rows32[0:1, :], a_bc[:], 32)

        @S(k, "sync")
        def _(s):
            k.inc(s.dma_start(out=s32a[0:1, 0:64], in_=k.inputs["l2a_bnw"]), "sdma")
            k.inc(s.dma_start(out=s32b[0:1, 0:64], in_=k.inputs["l2a_cnw"]), "sdma")
        bcast_row(s32a[0:1, 0:64], bnw_bc[:], 64)
        bcast_row(s32b[0:1, 0:64], cnw_bc[:], 64)

        @S(k, "sync")
        def _(s):
            k.inc(s.dma_start(out=vf[0:1, 0:DI], in_=k.inputs["l2a_bbias"]), "sdma")
        bcast_row(vf[0:1, 0:512], bbias_bc[:, 0:512], 512)
        bcast_row(vf[0:1, 512:1024], bbias_bc[:, 512:1024], 512)
        bcast_row(vf[0:1, 1024:1536], bbias_bc[:, 1024:1536], 512)
        bcast_row(vf[0:1, 1536:2048], bbias_bc[:, 1536:2048], 512)

        @S(k, "sync")
        def _(s):
            k.inc(s.dma_start(out=vf[0:1, 0:DI], in_=k.inputs["l2a_cbias"]), "sdma")
        bcast_row(vf[0:1, 0:512], cbias_bc[:, 0:512], 512)
        bcast_row(vf[0:1, 512:1024], cbias_bc[:, 512:1024], 512)
        bcast_row(vf[0:1, 1024:1536], cbias_bc[:, 1024:1536], 512)
        bcast_row(vf[0:1, 1536:2048], cbias_bc[:, 1536:2048], 512)

        @S(k, "sync")
        def _(s):
            k.inc(s.dma_start(out=rows32[:], in_=k.inputs["l2a_dd"]), "sdma")

        @S(k, "tensor")
        def _(t):
            k.inc(t.matmul(pG[:, 0:32], ones_row(k, 128), rows32[:], start=True, stop=True), "pe")

        @S(k, "vector")
        def _(v):
            k.inc(v.tensor_copy(out=drep[:], in_=pG[:, 0:32]), "dve")
        k.barrier()

        TWO_PI = float(2.0 * np.pi)
        for s_ in range(B_PER_CORE):
            toff = s_ * L

            @S(k, "vector")
            def _(v):
                k.inc(v.memset(carry_row[:], 0.0), "dve")
            for ch in range(NCH):
                tg = slice(toff + ch * 128, toff + ch * 128 + 128)

                @S(k, "tensor")
                def _(t, ch=ch, tg=tg):
                    for cc in range(3):
                        w_ = 512 if cc < 2 else 192
                        for g in range(DG):
                            ins = t.matmul(pS[:, cc, 0:w_], xn_sb[:, g, tg],
                                           wsm[:, g, cc * 512:cc * 512 + w_],
                                           start=(g == 0), stop=(g == DG - 1))
                    k.inc(ins, "pe")

                @S(k, "vector")
                def _(v, ch=ch):
                    k.inc(v.tensor_add(out=s32a[:, 0:32], in0=pS[:, 0, 0:32], in1=dtb_bc[:, 0:32]), "dve")

                @S(k, "scalar")
                def _(sc, ch=ch):
                    sc.activation(out=s32b[:, 0:32], in_=s32a[:, 0:32], func=AF.Abs)
                    sc.activation(out=s32b[:, 0:32], in_=s32b[:, 0:32], func=AF.Exp, scale=-1.0)
                    k.inc(sc.activation(out=s32b[:, 0:32], in_=s32b[:, 0:32], func=AF.Ln, bias=1.0), "act")

                @S(k, "vector")
                def _(v, ch=ch):
                    def step(ins):
                        c = k.inc(ins, "dve")
                        v.wait_ge(k.sems["dve"], c)
                    step(v.tensor_scalar(out=s32a[:, 0:32], in0=s32a[:, 0:32], scalar1=0.0,
                                         scalar2=None, op0=ALU.max))
                    step(v.tensor_add(out=dt_tok[:, ch, :], in0=s32a[:, 0:32], in1=s32b[:, 0:32]))
                    step(v.tensor_mul(out=la_tok[:, ch, :], in0=dt_tok[:, ch, :], in1=a_bc[:, 0:32]))

                @S(k, "scalar")
                def _(sc, ch=ch):
                    k.inc(sc.activation(out=s32a[:, 0:32], in_=pS[:, 0, 32:64], func=AF.Exp,
                                        scale=-1.0), "act")

                @S(k, "vector")
                def _(v, ch=ch):
                    def step(ins):
                        c = k.inc(ins, "dve")
                        v.wait_ge(k.sems["dve"], c)
                    step(v.tensor_scalar(out=s32a[:, 0:32], in0=s32a[:, 0:32], scalar1=1.0,
                                         scalar2=None, op0=ALU.add))
                    step(v.reciprocal(out=s32a[:, 0:32], in_=s32a[:, 0:32]))
                    step(v.tensor_mul(out=g_tok[:, ch, :], in0=s32a[:, 0:32],
                                      in1=dt_tok[:, ch, :]))
                    step(v.tensor_sub(out=q_tok[:, ch, :], in0=dt_tok[:, ch, :],
                                      in1=g_tok[:, ch, :]))

                for which, nw_bc, out_t in (("b", bnw_bc, bn_t), ("c", cnw_bc, cn_t)):
                    off0 = 64 if which == "b" else 128

                    @S(k, "scalar")
                    def _(sc, off0=off0):
                        k.inc(sc.activation(out=s32a[:, 0:64], in_=pS[:, 0, off0:off0 + 64],
                                            func=AF.Square), "act")

                    @S(k, "vector")
                    def _(v, off0=off0):
                        k.inc(v.reduce_sum(out=rstdc[:, 0:1], in_=s32a[:, 0:64], axis=AX.X), "dve")

                    @S(k, "scalar")
                    def _(sc):
                        sc.activation(out=rstdc[:, 1:2], in_=rstdc[:, 0:1], func=AF.Ln,
                                      bias=1e-5, scale=1.0 / 64.0)
                        k.inc(sc.activation(out=rstdc[:, 1:2], in_=rstdc[:, 1:2],
                                            func=AF.Exp, scale=-0.5), "act")

                    @S(k, "vector")
                    def _(v, off0=off0, nw_bc=nw_bc, out_t=out_t):
                        def step(ins):
                            c = k.inc(ins, "dve")
                            v.wait_ge(k.sems["dve"], c)
                        step(v.tensor_scalar(out=out_t[:], in0=pS[:, 0, off0:off0 + 64],
                                             scalar1=rstdc[:, 1:2], scalar2=None, op0=ALU.mult))
                        step(v.tensor_mul(out=out_t[:], in0=out_t[:], in1=nw_bc[:]))

                @S(k, "vector")
                def _(v, ch=ch):
                    k.inc(v.tensor_copy(out=thsb[:], in_=pS[:].rearrange("p c f -> p (c f)")[:, 192:1216]), "dve")

                @S(k, "tensor")
                def _(t, ch=ch):
                    for half in range(2):
                        hs = slice(half * 512, (half + 1) * 512)
                        t.matmul(pS[:, half, :], k.sb["cs_tril01"][:], thsb[:, hs],
                                 start=True, stop=False)
                        ins = t.matmul(pS[:, half, :], ones_row(k, 128)[0:1, 0:128]
                                       if False else ones_row(k, 128),
                                       carry_row[0:1, hs], start=False, stop=True)
                    k.inc(ins, "pe")

                @S(k, "vector")
                def _(v, ch=ch):
                    k.inc(v.tensor_copy(out=tcs[:, 0:512], in_=pS[:, 0, :]), "dve")
                    k.inc(v.tensor_copy(out=tcs[:, 512:1024], in_=pS[:, 1, :]), "dve")

                for half in range(2):
                    hs = slice(half * 512, (half + 1) * 512)

                    @S(k, "tensor")
                    def _(t, hs=hs):
                        k.inc(t.matmul(pS[0:1, 2, 0:512], k.constcol(1.0), thsb[:, hs],
                                       start=True, stop=True), "pe")

                    @S(k, "vector")
                    def _(v, hs=hs):
                        k.inc(v.tensor_add(out=carry_row[0:1, hs], in0=carry_row[0:1, hs],
                                           in1=pS[0:1, 2, 0:512]), "dve")

                @S(k, "vector")
                def _(v, ch=ch):
                    def step(ins):
                        c = k.inc(ins, "dve")
                        v.wait_ge(k.sems["dve"], c)
                    step(v.tensor_scalar(out=kf[:], in0=tcs[:], scalar1=1.0 / TWO_PI,
                                         scalar2=None, op0=ALU.mult))
                    step(v.tensor_copy(out=kint[:], in_=kf[:]))
                    step(v.tensor_copy(out=kf[:], in_=kint[:]))
                    step(v.scalar_tensor_tensor(out=tc2[:], in0=kf[:], scalar=-TWO_PI,
                                                in1=tcs[:], op0=ALU.mult, op1=ALU.add))
                    step(v.tensor_scalar(out=tcs[:], in0=tcs[:], scalar1=float(np.pi / 2),
                                         scalar2=1.0 / TWO_PI, op0=ALU.add, op1=ALU.mult))
                    step(v.tensor_copy(out=kint[:], in_=tcs[:]))
                    step(v.tensor_copy(out=kf[:], in_=kint[:]))

                @S(k, "scalar")
                def _(sc, ch=ch):
                    k.inc(sc.activation(out=sin_t[:], in_=tc2[:], func=AF.Sin), "act")

                @S(k, "vector")
                def _(v, ch=ch):
                    k.inc(v.tensor_sub(out=tc2[:], in0=tcs[:], in1=kf[:]), "dve")
                    v.wait_ge(k.sems["dve"], k.cnt["dve"])
                    k.inc(v.tensor_scalar(out=tc2[:], in0=tc2[:], scalar1=TWO_PI,
                                          scalar2=None, op0=ALU.mult), "dve")

                @S(k, "scalar")
                def _(sc, ch=ch):
                    k.inc(sc.activation(out=cos_t[:], in_=tc2[:], func=AF.Sin), "act")

                for which, bias_bc, nsb, dsth in (("b", bbias_bc, bn_t, bm_hbm),
                                                  ("c", cbias_bc, cn_t, cm_hbm)):
                    osb_t = bm_ch if which == "b" else cm_ch

                    @S(k, "vector")
                    def _(v, bias_bc=bias_bc, nsb=nsb, osb_t=osb_t):
                        ins = None
                        for h in range(H):
                            ins = v.tensor_add(out=vf[:, h * 64:(h + 1) * 64],
                                               in0=bias_bc[:, h * 64:(h + 1) * 64], in1=nsb[:])
                        k.inc(ins, "dve")
                        v.wait_ge(k.sems["dve"], k.cnt["dve"])
                        v1 = vf[:].rearrange("p (h n) -> p h n", h=H)[:, :, 0:32]
                        v2 = vf[:].rearrange("p (h n) -> p h n", h=H)[:, :, 32:64]
                        o1 = osb_t[:].rearrange("p (h n) -> p h n", h=H)[:, :, 0:32]
                        o2 = osb_t[:].rearrange("p (h n) -> p h n", h=H)[:, :, 32:64]
                        cw = cos_t[:].rearrange("p (h n) -> p h n", h=H, n=32)
                        sw = sin_t[:].rearrange("p (h n) -> p h n", h=H, n=32)
                        scr = tcs[:].rearrange("p (h n) -> p h n", h=H, n=32)
                        def step(ins):
                            c = k.inc(ins, "dve")
                            v.wait_ge(k.sems["dve"], c)
                        step(v.tensor_mul(out=o1, in0=v1, in1=cw))
                        step(v.tensor_mul(out=o2, in0=v2, in1=sw))
                        step(v.tensor_sub(out=o1, in0=o1, in1=o2))
                        step(v.tensor_mul(out=o2, in0=v2, in1=cw))
                        step(v.tensor_mul(out=scr, in0=v1, in1=sw))
                        step(v.tensor_add(out=o2, in0=o2, in1=scr))

                    @S(k, "sync")
                    def _(s, ch=ch, dsth=dsth, osb_t=osb_t):
                        k.inc(s.dma_start(out=dsth[ch * 128:(ch + 1) * 128, :],
                                          in_=osb_t[:]), "st")
            @S(k, "vector")
            def _(v):
                k.inc(v.memset(s32a[:], 0.0), "dve")

            @S(k, "sync")
            def _(s):
                k.inc(s.dma_start(out=q_hbm[0:L, :].rearrange("(c p) n -> p c n", p=128),
                                  in_=q_tok[:]), "st")
                k.inc(s.dma_start(out=q_hbm[L:L + 128, :], in_=s32a[:, 0:32]), "st")
                s.wait_ge(k.sems["st"], k.cnt["st"])
                k.inc(s.dma_start(out=q_tok[:],
                                  in_=q_hbm[1:L + 1, :].rearrange("(c p) n -> p c n", p=128)), "sdma")

            @S(k, "vector")
            def _(v):
                k.inc(v.tensor_add(out=gme_tok[:], in0=g_tok[:], in1=q_tok[:]), "dve")

            @S(k, "scalar")
            def _(sc):
                k.inc(sc.activation(out=lne_tok[:], in_=gme_tok[:], func=AF.Ln), "act")

            @S(k, "vector")
            def _(v):
                k.inc(v.tensor_sub(out=gme_tok[:], in0=g_tok[:], in1=gme_tok[:]), "dve")

            @S(k, "vector")
            def _(v):
                k.inc(v.memset(hcar[:], 0.0), "dve")

            for ch in range(NCH):
                tg = slice(toff + ch * 128, toff + ch * 128 + 128)

                @S(k, "sync")
                def _(s, s_=s_, ch=ch, tg=tg):
                    k.inc(s.dma_start(out=bm_ch[:], in_=bm_hbm[ch * 128:(ch + 1) * 128, :]), "sdma")
                    k.inc(s.dma_start(out=cm_ch[:], in_=cm_hbm[ch * 128:(ch + 1) * 128, :]), "sdma")
                    k.inc(s.dma_start(out=xfch[:],
                                      in_=xf_hbm[:, tg].rearrange("(g p) t -> p g t", p=128)), "sdma")
                for g in range(16):
                    @S(k, "tensor")
                    def _(t, g=g):
                        k.inc(t.transpose(pGb[:], xfch[:, g, :], k.sb["cs_ident_bf"][:]), "pe")

                    @S(k, "scalar")
                    def _(sc, g=g):
                        k.inc(sc.copy(out=xt_ch[:, g, :], in_=pGb[:]), "act")

                for h in range(H):
                    la_col = la_tok[:, ch, h:h + 1]
                    lne_col = lne_tok[:, ch, h:h + 1]
                    gme_col = gme_tok[:, ch, h:h + 1]
                    xsl = xt_ch[:, h // 2, (h % 2) * 64:(h % 2) * 64 + 64]
                    bsl = bm_ch[:, h * 64:(h + 1) * 64]
                    csl = cm_ch[:, h * 64:(h + 1) * 64]

                    @S(k, "vector")
                    def _(v, la_col=la_col):
                        k.inc(v.tensor_scalar(out=lastriu[:], in0=k.sb["cs_striu01_bf"][:],
                                              scalar1=la_col, scalar2=None, op0=ALU.mult), "dve")

                    @S(k, "tensor")
                    def _(t, h=h):
                        t.matmul(pG[:], lastriu[:], k.sb["cs_tril01_bf"][:], start=True, stop=False)
                        t.matmul(pG[:], k.sb["cs_diagbig_bf"][:], k.sb["cs_striu01_bf"][:],
                                 start=False, stop=True)
                        k.inc(t.matmul(pE[0:1, :], la_col, k.sb["cs_tril01"][:],
                                       start=True, stop=True), "pe")
                        k.inc(t.transpose(ptb, bsl, k.sb["cs_ident_bf"][:]), "pe")

                    @S(k, "scalar")
                    def _(sc, h=h):
                        k.inc(sc.copy(out=bfh[:], in_=ptb), "act")
                        k.inc(sc.activation(out=er[:], in_=pE[0:1, :], func=AF.Exp), "act")

                    @S(k, "tensor")
                    def _(t, h=h):
                        k.inc(t.transpose(ptb, csl, k.sb["cs_ident_bf"][:]), "pe")

                    @S(k, "scalar")
                    def _(sc, h=h):
                        k.inc(sc.copy(out=cfh[:], in_=ptb), "act")

                    @S(k, "vector")
                    def _(v, lne_col=lne_col):
                        k.inc(v.tensor_scalar(out=gm[:], in0=pG[:], scalar1=lne_col,
                                              scalar2=None, op0=ALU.add), "dve")

                    @S(k, "scalar")
                    def _(sc):
                        k.inc(sc.activation(out=wg_t[:], in_=gm[:], func=AF.Exp), "act")

                    @S(k, "vector")
                    def _(v, gme_col=gme_col):
                        k.inc(v.scalar_tensor_tensor(out=wf_t[:], in0=k.sb["cs_ident_bf"][:],
                                                     scalar=gme_col, in1=wg_t[:],
                                                     op0=ALU.mult, op1=ALU.add), "dve")

                    @S(k, "tensor")
                    def _(t):
                        k.inc(t.matmul(pG[:], bfh[:], cfh[:], start=True, stop=True), "pe")

                    @S(k, "vector")
                    def _(v, bsl=bsl):
                        k.inc(v.tensor_mul(out=mt_t[:], in0=wf_t[:], in1=pG[:]), "dve")
                        k.inc(v.tensor_copy(out=wcf[:], in_=wg_t[:, 127:128]), "dve")
                        v.wait_ge(k.sems["dve"], k.cnt["dve"])
                        k.inc(v.tensor_scalar(out=bw_t[:], in0=bsl,
                                              scalar1=wcf[:, 0:1], scalar2=None,
                                              op0=ALU.mult), "dve")

                    @S(k, "tensor")
                    def _(t):
                        k.inc(t.matmul(pE[:, 0:128] if False else pE[:],
                                       ones_row(k, 64), er[:], start=True, stop=True), "pe")

                    @S(k, "vector")
                    def _(v, h=h):
                        k.inc(v.tensor_mul(out=crs[:], in0=cfh[:], in1=pE[:]), "dve")
                        k.inc(v.tensor_copy(out=hbf[:], in_=hcar[:, h, :]), "dve")

                    @S(k, "tensor")
                    def _(t, h=h):
                        t.matmul(pY[:], xsl, mt_t[:], start=True, stop=False)
                        k.inc(t.matmul(pY[:], hbf[:], crs[:], start=False, stop=True), "pe")
                        k.inc(t.matmul(pH[:], bw_t[:], xsl, start=True, stop=True), "pe")

                    @S(k, "scalar")
                    def _(sc):
                        k.inc(sc.copy(out=s64[:], in_=pE[:, 127:128]), "act")

                    @S(k, "vector")
                    def _(v, h=h):
                        def step(ins):
                            c = k.inc(ins, "dve")
                            v.wait_ge(k.sems["dve"], c)
                        step(v.scalar_tensor_tensor(out=hcar[:, h, :], in0=hcar[:, h, :],
                                                    scalar=s64[:, 0:1], in1=pH[:],
                                                    op0=ALU.mult, op1=ALU.add))
                        xslf = xfch[64 * (h % 2):64 * (h % 2) + 64, h // 2, :]
                        step(v.scalar_tensor_tensor(out=ych[64 * (h % 2):64 * (h % 2) + 64,
                                                             h // 2, :],
                                                    in0=xslf,
                                                    scalar=drep[64 * (h % 2):64 * (h % 2) + 64,
                                                                h:h + 1],
                                                    in1=pY[:], op0=ALU.mult, op1=ALU.add))

                @S(k, "sync")
                def _(s, s_=s_, ch=ch, tg=tg):
                    k.inc(s.dma_start(out=y_hbm[:, tg].rearrange("(g p) t -> p g t", p=128),
                                      in_=ych[:]), "st")
        k.barrier()

    with ExitStack() as ph:
        sb = lambda n, s, d=BF16: ph.enter_context(nc.sbuf_tensor("l2a3_" + n, s, d))
        ps = lambda n, s, d=F32: ph.enter_context(nc.psum_tensor("l2a3_" + n, s, d))
        wop = sb("wop", [128, 16, 1024])
        nrmc = sb("nrmc", [128, 16], F32)
        y2 = sb("y2", [128, 16, 512])
        zt = sb("zt", [128, 16, 512])
        sq = sb("sq", [128, 512], F32)
        r_row = sb("r_row", [1, 512], F32)
        ytile = sb("ytile", [128, 512], F32)
        pj = ps("pj", [128, 2, 512])
        pst = ps("pst", [1, 512])
        pbc = ps("pbc", [128, 512])

        @S(k, "gpsimd")
        def _(g):
            k.inc(g.dma_start(out=wop[:],
                              in_=k.inputs["l2a_wopT"].rearrange("(c p) n -> p c n", p=128)), "gdma")

        @S(k, "sync")
        def _(s):
            k.inc(s.dma_start(out=nrmc[:], in_=k.inputs["l2a_normw"]), "sdma")
        k.barrier()

        for tc in range(T // 512):
            ts_ = slice(tc * 512, (tc + 1) * 512)

            @S(k, "sync")
            def _(s, ts_=ts_):
                k.inc(s.dma_start(out=y2[:], in_=y_hbm[:, ts_]
                                  .rearrange("(g p) t -> p g t", p=128)), "sdma")
                k.inc(s.dma_start(out=zt[:], in_=zf_hbm[:, ts_]
                                  .rearrange("(g p) t -> p g t", p=128)), "sdma")

            for g in range(16):
                @S(k, "scalar")
                def _(sc, g=g):
                    k.inc(sc.activation(out=sq[:], in_=zt[:, g, :], func=AF.Silu), "act")

                @S(k, "vector")
                def _(v, g=g):
                    k.inc(v.tensor_mul(out=y2[:, g, :], in0=y2[:, g, :], in1=sq[:]), "dve")

            for g in range(16):
                @S(k, "scalar")
                def _(sc, g=g):
                    k.inc(sc.activation(out=sq[:], in_=y2[:, g, :], func=AF.Square), "act")

                @S(k, "tensor")
                def _(t, g=g):
                    k.inc(t.matmul(pst[:], k.constcol(1.0), sq[:],
                                   start=(g == 0), stop=(g == 15)), "pe")

            @S(k, "scalar")
            def _(sc):
                sc.activation(out=r_row[:], in_=pst[0:1, :], func=AF.Ln,
                              bias=1e-5, scale=1.0 / DI)
                k.inc(sc.activation(out=r_row[:], in_=r_row[:], func=AF.Exp, scale=-0.5), "act")

            @S(k, "tensor")
            def _(t):
                k.inc(t.matmul(pbc[:], ones_row(k, 128), r_row[:], start=True, stop=True), "pe")

            for g in range(16):
                @S(k, "vector")
                def _(v, g=g):
                    k.inc(v.tensor_mul(out=y2[:, g, :], in0=y2[:, g, :], in1=pbc[:]), "dve")
                    v.wait_ge(k.sems["dve"], k.cnt["dve"])
                    k.inc(v.tensor_scalar(out=y2[:, g, :], in0=y2[:, g, :],
                                          scalar1=nrmc[:, g:g + 1], scalar2=None,
                                          op0=ALU.mult), "dve")

            for og in range(DG):
                @S(k, "tensor")
                def _(t, og=og):
                    for g in range(16):
                        ins = t.matmul(pj[:, og % 2, :], wop[:, g, og * 128:(og + 1) * 128],
                                       y2[:, g, :], start=(g == 0), stop=(g == 15))
                    k.inc(ins, "pe")

                @S(k, "scalar")
                def _(sc, og=og):
                    k.inc(sc.copy(out=ytile[:], in_=pj[:, og % 2, :]), "act")

                @S(k, "sync")
                def _(s, tc=tc, og=og):
                    i0 = tc * 4
                    dst = a2[i0:i0 + 4, :, og * 128:(og + 1) * 128].rearrange("i p n -> p i n")
                    k.inc(s.dma_start(out=dst, in_=ytile[:].rearrange("p (i n) -> p i n", i=4)), "st")
        k.barrier()



def _swz(xc):
    return np.ascontiguousarray(
        xc.reshape(NT128, 128, DG, 128).transpose(0, 3, 2, 1).reshape(NT128, 128, DG * 128))


def _unswz(a):
    return np.asarray(a, np.float32).reshape(NT128, 128, DG, 128) \
        .transpose(0, 3, 2, 1).reshape(T, D)


_CACHE = {}


def kernel(**inputs):
    x = np.asarray(inputs["L0_attn_x"], np.float32)
    consts = make_consts()
    wA = prep_weights_A(inputs)
    wB = prep_weights_B(inputs)

    if "A" not in _CACHE:
        _CACHE["A"] = build_A()
        _CACHE["B"] = build_B()
    kA, kB = _CACHE["A"], _CACHE["B"]

    in_mapsA = []
    for c in range(NCORES):
        m = {"xin": _swz(x[2 * c:2 * c + 2].reshape(T, D))}
        for name in kA.inputs:
            if name.startswith("c_"):
                m[name] = consts[name[2:]]
            elif name in wA:
                m[name] = wA[name]
        assert len(m) == len(kA.inputs), (sorted(kA.inputs), sorted(m))
        in_mapsA.append(m)
    resA = run_bass_kernel_spmd(kA.nc, in_mapsA, core_ids=list(range(NCORES)))

    counts_all = np.ascontiguousarray(
        np.stack([resA.results[c]["counts"][:, 0] for c in range(NCORES)])).astype(np.float32)

    in_mapsB = []
    for c in range(NCORES):
        m = {"x2": resA.results[c]["x2"],
             "counts_all": counts_all,
             "premask": (np.arange(8) < c).astype(np.float32).reshape(8, 1)}
        for name in kB.inputs:
            if name.startswith("c_"):
                m[name] = consts[name[2:]]
            elif name in wB:
                m[name] = wB[name]
        assert len(m) == len(kB.inputs), (sorted(kB.inputs), sorted(m))
        in_mapsB.append(m)
    resB = run_bass_kernel_spmd(kB.nc, in_mapsB, core_ids=list(range(NCORES)))

    out = np.stack([_unswz(resB.results[c]["xout"]) for c in range(NCORES)])
    return out.reshape(16, 2048, D).astype(np.float32)


# version 04 (final)
# speedup vs baseline: 1.2805x; 1.0422x over previous
import math
import sys

sys.path.insert(0, "/opt/trn_rl_repo")

import numpy as np
import concourse.bass as bass
import concourse.mybir as mybir
from concourse.bass import IndirectOffsetOnAxis
from concourse.bass_utils import run_bass_kernel_spmd
from contextlib import ExitStack

F32 = mybir.dt.float32
BF16 = mybir.dt.bfloat16
I32 = mybir.dt.int32
AF = mybir.ActivationFunctionType
ALU = mybir.AluOpType
AX = mybir.AxisListType

NCORES = 8
B_PER_CORE = 2
L = 2048
D = 1024
T = B_PER_CORE * L
NT128 = T // 128
DG = D // 128

DEBUG_TAPS = {}



def _tri(n=128):
    r = np.arange(n)
    return r[:, None], r[None, :]


def make_consts():
    c = {}
    i, j = _tri()
    c["ident"] = np.eye(128, dtype=np.float32)
    c["tril01"] = (i <= j).astype(np.float32)
    c["stril01"] = (i < j).astype(np.float32)
    c["striu01"] = (i > j).astype(np.float32)
    c["diagbig"] = (-1e30 * np.eye(128)).astype(np.float32)
    c["ones"] = np.ones((128, 512), dtype=np.float32)
    c["iota8"] = np.arange(8, dtype=np.float32).reshape(1, 8)
    kk = np.arange(128)[:, None]
    qq = np.arange(512)[None, :]
    cm = np.zeros((128, 4, 512), np.float32)
    for j in range(4):
        cm[:, j, :] = np.where(kk <= qq - j * 128, 0.0, -1e30)
    c["cmask"] = cm.reshape(128, 4 * 512)
    seln = np.zeros((32, 32, 128), np.float32)
    for n in range(32):
        seln[n, n, :] = 1.0
    c["seln"] = seln.reshape(32, 32 * 128)
    d = np.arange(D)
    ang = 2.0 * np.pi * (d[:, None] * d[None, :] % D) / D
    c["fr"] = np.cos(ang).astype(np.float32)
    c["fs"] = np.sin(ang).astype(np.float32)
    c["fsneg"] = (-np.sin(ang)).astype(np.float32)
    c["gr"] = (np.cos(ang) / D).astype(np.float32)
    c["gsneg"] = (-np.sin(ang) / D).astype(np.float32)
    return c



class K:

    SEMS = ["sdma", "gdma", "pe", "act", "dve", "pool", "st"]
    DMA_SEMS = {"sdma", "gdma", "st"}

    def __init__(self, name):
        self.nc = bass.Bass("TRN2", target_bir_lowering=False)
        self.es = ExitStack()
        self.sems = {}
        self.cnt = {}
        for s in self.SEMS:
            self.sems[s] = self.es.enter_context(self.nc.semaphore("sem_" + s))
            self.cnt[s] = 0
        self._seen = {}
        self.inputs = {}
        self.outputs = {}
        self.sb = {}
        self._constcols = {}
        self.block = None
        self.taps = {}

    def din(self, name, shape, dtype=F32):
        ap = self.nc.dram_tensor(name, list(shape), dtype, kind="ExternalInput").ap()
        self.inputs[name] = ap
        return ap

    def dout(self, name, shape, dtype=F32):
        ap = self.nc.dram_tensor(name, list(shape), dtype, kind="ExternalOutput").ap()
        self.outputs[name] = ap
        return ap

    def dram(self, name, shape, dtype=F32):
        return self.nc.dram_tensor(name, list(shape), dtype, kind="Internal").ap()

    def sbuf(self, name, shape, dtype=F32):
        t = self.es.enter_context(self.nc.sbuf_tensor(name, list(shape), dtype))
        self.sb[name] = t
        return t

    def psum(self, name, shape, dtype=F32):
        return self.es.enter_context(self.nc.psum_tensor(name, list(shape), dtype))

    def inc(self, instr, sem):
        n = 16 if sem in self.DMA_SEMS else 1
        instr.then_inc(self.sems[sem], n)
        self.cnt[sem] += n
        return self.cnt[sem]

    def wait(self, eng, sem, val=None):
        eng.wait_ge(self.sems[sem], self.cnt[sem] if val is None else val)

    def snap(self):
        return dict(self.cnt)

    def wait_snap(self, eng, snap, sems=None, engname=None):
        seen = self._seen.setdefault(engname, {}) if engname else None
        for s in (sems or self.SEMS):
            v = snap.get(s, 0)
            if v > 0 and (seen is None or seen.get(s, 0) < v):
                eng.wait_ge(self.sems[s], v)
                if seen is not None:
                    seen[s] = v

    def barrier(self):
        snap = self.snap()
        for name in ("sync", "gpsimd", "scalar", "vector", "tensor"):
            mk = getattr(self.block, name)

            def _f(eng, snap=snap, name=name):
                self.wait_snap(eng, snap, engname=name)
            mk(_f)

    def constcol(self, value):
        key = float(value)
        if key not in self._constcols:
            raise KeyError(f"constcol {value} not pre-registered")
        return self._constcols[key]

    def build_preamble(self, const_values):
        pool = self.sbuf("constpool", [128, max(1, len(const_values))], F32)
        @self.block.vector
        def _(v):
            for i, val in enumerate(const_values):
                ins = v.memset(pool[:, i:i + 1], float(val))
                self._constcols[float(val)] = pool[:, i:i + 1]
            self.inc(ins, "dve")

    def tap_dram(self, name, shape, dtype=F32):
        if DEBUG_TAPS.get(name):
            ap = self.dout("tap_" + name, shape, dtype)
            self.taps[name] = ap
            return ap
        return None



CONST_TABLES = {
    "ident":   ((128, 128), (F32, BF16)),
    "tril01":  ((128, 128), (F32, BF16)),
    "stril01": ((128, 128), (F32,)),
    "striu01": ((128, 128), (F32, BF16)),
    "diagbig": ((128, 128), (BF16,)),
    "ones":    ((128, 512), (F32, BF16)),
    "iota8":   ((1, 8), (F32,)),
    "seln":    ((32, 32 * 128), (F32,)),
    "cmask":   ((128, 4 * 512), (BF16,)),
}


def ph_preamble(k: K, const_values, tables):
    pool = k.sbuf("constpool", [128, max(1, len(const_values))], F32)
    for name in tables:
        shape, dts = CONST_TABLES[name]
        ap = k.din("c_" + name, shape, F32)
        for dt in dts:
            sfx = "" if dt is F32 else "_bf"
            k.sbuf("cs_" + name + sfx, list(shape), dt)

    @k.block.vector
    def _(v):
        ins = None
        for i, val in enumerate(const_values):
            ins = v.memset(pool[:, i:i + 1], float(val))
            k._constcols[float(val)] = pool[:, i:i + 1]
            k.nc.const_aps.register(F32, float(val), pool[:, i:i + 1])
        if ins is not None:
            k.inc(ins, "dve")

    @k.block.sync
    def _(s):
        for name in tables:
            shape, dts = CONST_TABLES[name]
            if F32 in dts:
                ins = s.dma_start(out=k.sb["cs_" + name][:shape[0], :],
                                  in_=k.inputs["c_" + name])
                k.inc(ins, "sdma")

    @k.block.gpsimd
    def _(g):
        for name in tables:
            shape, dts = CONST_TABLES[name]
            if BF16 in dts:
                ins = g.dma_start(out=k.sb["cs_" + name + "_bf"][:shape[0], :],
                                  in_=k.inputs["c_" + name])
                k.inc(ins, "gdma")

    k.barrier()


def ones_col(k):
    return k.constcol(1.0)


def ones_row(k, n):
    return k.sb["cs_ones"][0:1, 0:n]



def ph_residual_ln(k: K, srcs, x_out, xn_sb, xn_tok_hbm=None, tag="ln", router=None):
    nc = k.nc
    nsrc = len(srcs)
    with ExitStack() as ph:
        sb = lambda n, s, d=F32: ph.enter_context(nc.sbuf_tensor(f"{tag}_{n}", s, d))
        ps = lambda n, s, d=F32: ph.enter_context(nc.psum_tensor(f"{tag}_{n}", s, d))
        ld = [sb(f"ld{b}", [128, nsrc, DG * 128]) for b in range(2)]
        xw = [sb(f"x{b}", [128, DG * 128]) for b in range(2)]
        xsq = sb("sq", [128, DG * 128])
        rows = sb("rows", [1, 5 * 128])
        tmp = sb("tmp", [128, 128])
        xnf = sb("xnf", [128, 128])
        prt = ps("prt", [128, 16]) if router is not None else None
        ps_stat = ps("pst", [1, 256])
        ps_bc = ps("pbc", [128, 256])
        if xn_tok_hbm is not None:
            xnt_ps = ps("tps", [128, DG * 128], BF16)
            xnt_sb = sb("tsb", [128, DG * 128], BF16)

        ld_done = [0] * NT128; sum_done = [0] * NT128; sq_done = [0] * NT128
        stat_done = [0] * NT128; row_done = [0] * NT128; bc_done = [0] * NT128
        xn_done = [0] * NT128; xst_done = [0] * NT128; tr_done = [0] * NT128
        tcp_done = [0] * NT128; tst_done = [0] * NT128

        for i in range(NT128):
            b = i % 2

            @k.block.sync
            def _(s, i=i, b=b):
                if i >= 2:
                    s.wait_ge(k.sems["dve"], xn_done[i - 2])
                    s.wait_ge(k.sems["act"], sq_done[i - 2])
                    s.wait_ge(k.sems["pe"], stat_done[i - 2])
                    s.wait_ge(k.sems["st"], xst_done[i - 2])
                for j in range(nsrc):
                    ld_done[i] = k.inc(s.dma_start(out=ld[b][:, j, :], in_=srcs[j](i)), "sdma")

            @k.block.vector
            def _(v, i=i, b=b):
                v.wait_ge(k.sems["sdma"], ld_done[i])
                if nsrc == 1:
                    ins = v.tensor_copy(out=xw[b][:], in_=ld[b][:, 0, :])
                else:
                    ins = v.tensor_add(out=xw[b][:], in0=ld[b][:, 0, :], in1=ld[b][:, 1, :])
                    for j in range(2, nsrc):
                        ins = v.tensor_add(out=xw[b][:], in0=xw[b][:], in1=ld[b][:, j, :])
                sum_done[i] = k.inc(ins, "dve")

            if x_out is not None:
                @k.block.sync
                def _(s, i=i, b=b):
                    s.wait_ge(k.sems["dve"], sum_done[i])
                    ins = s.dma_start(out=x_out(i), in_=xw[b][:])
                    xst_done[i] = k.inc(ins, "st")
            else:
                xst_done[i] = k.cnt["st"]

            @k.block.scalar
            def _(sc, i=i, b=b):
                sc.wait_ge(k.sems["dve"], sum_done[i])
                if i >= 1:
                    sc.wait_ge(k.sems["pe"], stat_done[i - 1])
                ins = sc.activation(out=xsq[:], in_=xw[b][:], func=AF.Square)
                sq_done[i] = k.inc(ins, "act")

            @k.block.tensor
            def _(t, i=i, b=b):
                t.wait_ge(k.sems["act"], sq_done[i])
                if i >= 1:
                    t.wait_ge(k.sems["dve"], row_done[i - 1])
                oc = ones_col(k)
                for g in range(DG):
                    t.matmul(ps_stat[0:1, 0:128], oc, xw[b][:, g * 128:(g + 1) * 128],
                             start=(g == 0), stop=(g == DG - 1))
                for g in range(DG):
                    ins = t.matmul(ps_stat[0:1, 128:256], oc, xsq[:, g * 128:(g + 1) * 128],
                                   start=(g == 0), stop=(g == DG - 1))
                stat_done[i] = k.inc(ins, "pe")

            @k.block.vector
            def _(v, i=i, b=b):
                v.wait_ge(k.sems["pe"], stat_done[i])
                if i >= 1:
                    v.wait_ge(k.sems["pe"], bc_done[i - 1])
                m = rows[0:1, 0:128]; m2 = rows[0:1, 128:256]
                var = rows[0:1, 256:384]; mrstd = rows[0:1, 512:640]
                v.tensor_scalar(out=m, in0=ps_stat[0:1, 0:128], scalar1=1.0 / D,
                                scalar2=None, op0=ALU.mult)
                v.tensor_mul(out=m2, in0=m, in1=m)
                v.tensor_scalar(out=var, in0=ps_stat[0:1, 128:256], scalar1=1.0 / D,
                                scalar2=None, op0=ALU.mult)
                ins = v.tensor_sub(out=var, in0=var, in1=m2)
                row_done[i] = k.inc(ins, "dve")

            @k.block.scalar
            def _(sc, i=i):
                sc.wait_ge(k.sems["dve"], row_done[i])
                var = rows[0:1, 256:384]; rstd = rows[0:1, 384:512]
                sc.activation(out=rstd, in_=var, func=AF.Ln, bias=1e-5)
                ins = sc.activation(out=rstd, in_=rstd, func=AF.Exp, scale=-0.5)
                sq_done[i] = k.inc(ins, "act")

            @k.block.vector
            def _(v, i=i):
                v.wait_ge(k.sems["act"], sq_done[i])
                rstd = rows[0:1, 384:512]; m = rows[0:1, 0:128]
                mrstd = rows[0:1, 512:640]
                ins = v.tensor_mul(out=mrstd, in0=m, in1=rstd)
                row_done[i] = k.inc(ins, "dve")

            @k.block.tensor
            def _(t, i=i):
                t.wait_ge(k.sems["dve"], row_done[i])
                if i >= 1:
                    t.wait_ge(k.sems["dve"], xn_done[i - 1])
                orow = ones_row(k, 128)
                t.matmul(ps_bc[:, 0:128], orow, rows[0:1, 384:512], start=True, stop=True)
                ins = t.matmul(ps_bc[:, 128:256], orow, rows[0:1, 512:640], start=True, stop=True)
                bc_done[i] = k.inc(ins, "pe")

            if router is None:
                @k.block.vector
                def _(v, i=i, b=b):
                    v.wait_ge(k.sems["pe"], bc_done[i])
                    ins = None
                    for g in range(DG):
                        v.tensor_mul(out=tmp[:], in0=xw[b][:, g * 128:(g + 1) * 128],
                                     in1=ps_bc[:, 0:128])
                        ins = v.tensor_sub(out=xn_sb[:, g, i * 128:(i + 1) * 128],
                                           in0=tmp[:], in1=ps_bc[:, 128:256])
                    xn_done[i] = k.inc(ins, "dve")
            else:
                rw_sb, logits_sb = router
                E = rw_sb.shape[2]
                for g in range(DG):
                    @k.block.vector
                    def _(v, i=i, b=b, g=g):
                        if g == 0:
                            v.wait_ge(k.sems["pe"], bc_done[i])
                        else:
                            v.wait_ge(k.sems["pe"], k.cnt["pe"])
                        v.tensor_mul(out=tmp[:], in0=xw[b][:, g * 128:(g + 1) * 128],
                                     in1=ps_bc[:, 0:128])
                        v.tensor_sub(out=xnf[:], in0=tmp[:], in1=ps_bc[:, 128:256])
                        ins = v.tensor_copy(out=xn_sb[:, g, i * 128:(i + 1) * 128],
                                            in_=xnf[:])
                        xn_done[i] = k.inc(ins, "dve")

                    @k.block.tensor
                    def _(t, i=i, g=g, E=E):
                        t.wait_ge(k.sems["dve"], k.cnt["dve"])
                        ins = t.matmul(prt[:, 0:E], xnf[:], rw_sb[:, g, :],
                                       start=(g == 0), stop=(g == DG - 1))
                        k.inc(ins, "pe")

                @k.block.vector
                def _(v, i=i, E=E):
                    v.wait_ge(k.sems["pe"], k.cnt["pe"])
                    ins = v.tensor_copy(out=logits_sb[:, i, :], in_=prt[:, 0:E])
                    xn_done[i] = k.inc(ins, "dve")

            if xn_tok_hbm is not None:
                @k.block.tensor
                def _(t, i=i):
                    t.wait_ge(k.sems["dve"], xn_done[i])
                    if i >= 1:
                        t.wait_ge(k.sems["act"], tcp_done[i - 1])
                    idb = k.sb["cs_ident_bf"]
                    for g in range(DG):
                        ins = t.transpose(xnt_ps[:, g * 128:(g + 1) * 128],
                                          xn_sb[:, g, i * 128:(i + 1) * 128], idb[:])
                    tr_done[i] = k.inc(ins, "pe")

                @k.block.scalar
                def _(sc, i=i):
                    sc.wait_ge(k.sems["pe"], tr_done[i])
                    if i >= 1:
                        sc.wait_ge(k.sems["st"], tst_done[i - 1])
                    ins = sc.copy(out=xnt_sb[:], in_=xnt_ps[:])
                    tcp_done[i] = k.inc(ins, "act")

                @k.block.sync
                def _(s, i=i):
                    s.wait_ge(k.sems["act"], tcp_done[i])
                    ins = s.dma_start(out=xn_tok_hbm[i * 128:(i + 1) * 128, :],
                                      in_=xnt_sb[:])
                    tst_done[i] = k.inc(ins, "st")

        k.barrier()



def ph_dump_sbuf(k: K, name, sb_ap):
    shape = list(sb_ap.shape)
    out = k.dout("tap_" + name, shape, sb_ap.dtype)

    @k.block.sync
    def _(s):
        ins = s.dma_start(out=out, in_=sb_ap)
        k.inc(ins, "st")
    k.barrier()


def ph_finish(k: K):
    k.barrier()



def swz(dram_ap, i):
    return dram_ap[i]


def build_A(stop_after=None, taps=()):
    k = K("A")
    xin = k.din("xin", [NT128, 128, DG * 128], F32)
    k.din("l0f_wr", [D, D], F32)
    k.din("l0f_wi", [D, D], F32)
    k.din("l0f_bias", [128, DG], F32)
    for nm in ("fr", "fs", "fsneg", "gr", "gsneg"):
        k.din("c_" + nm, [D, D], F32)
    k.din("l1f_wqT", [D, D], F32)
    k.din("l1f_routT", [D, 255], F32)
    k.din("l1f_Da", [128, 8 * 256], F32)
    k.din("l1f_Db", [128, 8 * 256], F32)
    k.din("l1f_lv", [256, D], F32)
    k.din("l1f_woutT", [D, D], F32)
    k.din("l1f_wob", [128, DG], F32)
    k.din("l1f_c", [128, 16], F32)
    k.din("rt_w", [D, 8], F32)
    k.din("rt_b", [1, 8], F32)
    k.din("l0a_wbc", [D, 32], F32)
    k.din("l0a_wdt", [D, 64], F32)
    k.din("l0a_wdp", [64, D], F32)
    k.din("l0a_bdp", [128, DG], F32)
    k.din("l0a_alog", [D, 16], F32)
    k.din("l0a_d", [128, DG], F32)
    k.din("l1a_qpT", [D, 1024], F32)
    k.din("l1a_kpT", [D, 64], F32)
    k.din("l1a_vpT", [D, 64], F32)
    k.din("l1a_opT", [D, 1024], F32)
    counts_out = k.dout("counts", [8, 1], F32)
    x2_out = k.dout("x2", [NT128, 128, DG * 128], F32)

    k.block = k.es.enter_context(k.nc.Block())
    ph_preamble(k, CONSTVALS, ["ident", "tril01", "stril01", "striu01",
                               "ones", "iota8", "seln", "cmask"])

    xn_sb = k.sbuf("xn", [128, DG, T], BF16)

    ph_residual_ln(k, [lambda i: xin[i]], None, xn_sb, tag="ln0")
    if "xn0" in taps:
        ph_dump_sbuf(k, "xn0", xn_sb[:])
    if stop_after == "ln0":
        ph_finish(k)
        return k
    if stop_after in ("l1f0", "l1a0", "rt0", "l2a0", "l2f0", "l3a0", "l3f0"):
        tmp = k.dout("bdump", [NT128, 128, DG * 128], F32)
        if stop_after == "l1f0":
            ph_l1_ffn(k, xn_sb, tmp)
        elif stop_after == "l1a0":
            ph_l1_attn(k, xn_sb, tmp)
        elif stop_after == "rt0":
            pass
        ph_finish(k)
        return k
    mk0a = k.dout if stop_after == "l0a" else k.dram
    mk0f = k.dout if stop_after == "l0f" else k.dram
    a0 = mk0a("a0", [NT128, 128, DG * 128], F32)
    f0 = mk0f("f0", [NT128, 128, DG * 128], F32)
    if stop_after != "l0f":
        ph_l0_attn(k, xn_sb, a0)
    if stop_after == "l0a":
        ph_finish(k)
        return k
    ph_l0_ffn(k, xn_sb, f0)
    if stop_after == "l0f":
        ph_finish(k)
        return k

    x1 = k.dram("x1", [NT128, 128, DG * 128], F32)
    ph_residual_ln(k, [lambda i: xin[i], lambda i: a0[i], lambda i: f0[i]],
                   lambda i: x1[i], xn_sb, tag="ln1")
    if "xn1" in taps:
        ph_dump_sbuf(k, "xn1", xn_sb[:])
    if stop_after == "ln1":
        ph_finish(k)
        return k
    mk1a = k.dout if stop_after == "l1a" else k.dram
    mk1f = k.dout if stop_after == "l1f" else k.dram
    a1 = mk1a("a1", [NT128, 128, DG * 128], F32)
    f1 = mk1f("f1", [NT128, 128, DG * 128], F32)
    if stop_after != "l1f":
        ph_l1_attn(k, xn_sb, a1)
    if stop_after == "l1a":
        ph_finish(k)
        return k
    ph_l1_ffn(k, xn_sb, f1)
    if stop_after == "l1f":
        ph_finish(k)
        return k

    rtw_f = k.sbuf("rtw_f", [128, DG, 8], F32)
    lg2_sb = k.sbuf("lg2", [128, NT128, 8], F32)

    @S(k, "sync")
    def _(s):
        k.inc(s.dma_start(out=rtw_f[:], in_=k.inputs["rt_w"].rearrange("(c p) n -> p c n", p=128)), "sdma")
    k.barrier()
    ph_residual_ln(k, [lambda i: x1[i], lambda i: a1[i], lambda i: f1[i]],
                   lambda i: x2_out[i], xn_sb, tag="ln2", router=(rtw_f, lg2_sb))
    if "xn2" in taps:
        ph_dump_sbuf(k, "xn2", xn_sb[:])
    ph_router_counts(k, lg2_sb, counts_out, taps)
    ph_finish(k)
    return k


CONSTVALS = [1.0, 0.0, -1.0, 0.5, -0.5, 1e-5, float(np.pi / 2), float(-np.pi), 1e-9, 2.0]



def ph_l0_ffn(k: K, xn_sb, f0):
    nc = k.nc
    wr_d = k.inputs["l0f_wr"]; wi_d = k.inputs["l0f_wi"]
    fr_d = k.inputs["c_fr"]; fs_d = k.inputs["c_fs"]; fsneg_d = k.inputs["c_fsneg"]
    gr_d = k.inputs["c_gr"]; gsneg_d = k.inputs["c_gsneg"]
    bias_d = k.inputs["l0f_bias"]

    with ExitStack() as ph:
        sb = lambda n, s, d=BF16: ph.enter_context(nc.sbuf_tensor("l0fs_" + n, s, d))
        ps = lambda n, s, d=F32: ph.enter_context(nc.psum_tensor("l0fs_" + n, s, d))
        wr = sb("wr", [128, DG, 1024]); wi = sb("wi", [128, DG, 1024])
        slab = sb("slab", [128, 3, DG, 512])
        t1 = sb("t1", [128, DG, 1024]); t2 = sb("t2", [128, DG, 1024])
        msb = sb("m", [128, DG, 1024])
        bias = sb("bias", [128, DG], F32)
        pst = ps("pst", [128, 2, 1024])
        ld_done = {}

        @k.block.gpsimd
        def _(g):
            k.inc(g.dma_start(out=wr[:], in_=wr_d.rearrange("(c p) n -> p c n", p=128)), "gdma")
            ins = g.dma_start(out=wi[:], in_=wi_d.rearrange("(c p) n -> p c n", p=128))
            ld_done["w"] = k.inc(ins, "gdma")

        @k.block.sync
        def _(s):
            ins = s.dma_start(out=bias[:], in_=bias_d)
            ld_done["b"] = k.inc(ins, "sdma")

        for dc in range(2):
            @k.block.gpsimd
            def _(g, dc=dc):
                if dc >= 1:
                    g.wait_ge(k.sems["pe"], k.cnt["pe"])
                for j, src in enumerate((fr_d, fs_d, fsneg_d)):
                    k.inc(g.dma_start(
                        out=slab[:, j, :, :],
                        in_=src[:, dc * 512:(dc + 1) * 512].rearrange("(c p) n -> p c n", p=128)), "gdma")

            for kt in range(DG):
                pb = kt % 2

                @k.block.tensor
                def _(t, dc=dc, kt=kt, pb=pb):
                    if kt == 0:
                        t.wait_ge(k.sems["gdma"], k.cnt["gdma"])
                    if kt >= 2 or dc >= 1:
                        t.wait_ge(k.sems["act"], k.cnt["act"])
                    for m in range(DG):
                        t.matmul(pst[:, pb, 0:512], wr[:, m, kt * 128:(kt + 1) * 128],
                                 slab[:, 0, m, :], start=(m == 0), stop=False)
                    for m in range(DG):
                        t.matmul(pst[:, pb, 0:512], wi[:, m, kt * 128:(kt + 1) * 128],
                                 slab[:, 1, m, :], start=False, stop=(m == DG - 1))
                    for m in range(DG):
                        t.matmul(pst[:, pb, 512:1024], wi[:, m, kt * 128:(kt + 1) * 128],
                                 slab[:, 0, m, :], start=(m == 0), stop=False)
                    for m in range(DG):
                        ins = t.matmul(pst[:, pb, 512:1024], wr[:, m, kt * 128:(kt + 1) * 128],
                                       slab[:, 2, m, :], start=False, stop=(m == DG - 1))
                    k.inc(ins, "pe")

                @k.block.scalar
                def _(sc, dc=dc, kt=kt, pb=pb):
                    sc.wait_ge(k.sems["pe"], k.cnt["pe"])
                    sc.copy(out=t1[:, kt, dc * 512:(dc + 1) * 512], in_=pst[:, pb, 0:512])
                    ins = sc.copy(out=t2[:, kt, dc * 512:(dc + 1) * 512], in_=pst[:, pb, 512:1024])
                    k.inc(ins, "act")
        k.barrier()

        for oc in range(2):
            @k.block.gpsimd
            def _(g, oc=oc):
                if oc >= 1:
                    g.wait_ge(k.sems["pe"], k.cnt["pe"])
                for j, src in enumerate((gr_d, gsneg_d)):
                    k.inc(g.dma_start(
                        out=slab[:, j, :, :],
                        in_=src[:, oc * 512:(oc + 1) * 512].rearrange("(c p) n -> p c n", p=128)), "gdma")

            for dt_ in range(DG):
                pb = dt_ % 2

                @k.block.tensor
                def _(t, oc=oc, dt_=dt_, pb=pb):
                    if dt_ == 0:
                        t.wait_ge(k.sems["gdma"], k.cnt["gdma"])
                    if dt_ >= 2 or oc >= 1:
                        t.wait_ge(k.sems["act"], k.cnt["act"])
                    for m in range(DG):
                        t.matmul(pst[:, pb, 0:512], t1[:, m, dt_ * 128:(dt_ + 1) * 128],
                                 slab[:, 0, m, :], start=(m == 0), stop=False)
                    for m in range(DG):
                        ins = t.matmul(pst[:, pb, 0:512], t2[:, m, dt_ * 128:(dt_ + 1) * 128],
                                       slab[:, 1, m, :], start=False, stop=(m == DG - 1))
                    k.inc(ins, "pe")

                @k.block.scalar
                def _(sc, oc=oc, dt_=dt_, pb=pb):
                    sc.wait_ge(k.sems["pe"], k.cnt["pe"])
                    ins = sc.copy(out=msb[:, dt_, oc * 512:(oc + 1) * 512], in_=pst[:, pb, 0:512])
                    k.inc(ins, "act")
        k.barrier()

        ya = sb("ya", [128, 2, 512], F32)
        st_done = {}
        for tc in range(T // 512):
            for og in range(DG):
                pb = (tc * DG + og) % 2

                @k.block.tensor
                def _(t, tc=tc, og=og, pb=pb):
                    t.wait_ge(k.sems["sdma"], ld_done["b"])
                    if (tc, og) not in ((0, 0), (0, 1)):
                        t.wait_ge(k.sems["act"], k.cnt["act"])
                    for g in range(DG):
                        ins = t.matmul(pst[:, pb, 0:512], msb[:, g, og * 128:(og + 1) * 128],
                                       xn_sb[:, g, tc * 512:(tc + 1) * 512],
                                       start=(g == 0), stop=(g == DG - 1))
                    k.inc(ins, "pe")

                @k.block.scalar
                def _(sc, tc=tc, og=og, pb=pb):
                    sc.wait_ge(k.sems["pe"], k.cnt["pe"])
                    if (tc, og) not in ((0, 0), (0, 1)):
                        sc.wait_ge(k.sems["st"], st_done[pb])
                    ins = sc.activation(out=ya[:, pb, :], in_=pst[:, pb, 0:512],
                                        func=AF.Identity, bias=bias[:, og:og + 1], scale=1.0)
                    k.inc(ins, "act")

                @k.block.sync
                def _(s, tc=tc, og=og, pb=pb):
                    s.wait_ge(k.sems["act"], k.cnt["act"])
                    i0 = tc * 4
                    dst = f0[i0:i0 + 4, :, og * 128:(og + 1) * 128].rearrange("i p n -> p i n")
                    ins = s.dma_start(out=dst,
                                      in_=ya[:, pb, :].rearrange("p (i n) -> p i n", i=4))
                    st_done[pb] = k.inc(ins, "st")
        k.barrier()



def col128(v):
    v = np.asarray(v, np.float32).reshape(-1)
    return np.ascontiguousarray(v.reshape(-1, 128).T)


def prep_weights_A(mi):
    w = {}
    w["l0f_wr"] = np.ascontiguousarray(mi["L0_ffn_w_freq_real"], np.float32)
    w["l0f_wi"] = np.ascontiguousarray(mi["L0_ffn_w_freq_imag"], np.float32)
    w["l0f_bias"] = col128(mi["L0_ffn_bias"])
    w["l1f_wqT"] = np.ascontiguousarray(mi["L1_ffn_Wq"].T, np.float32)
    w["l1f_routT"] = np.ascontiguousarray(mi["L1_ffn_routers"].T, np.float32)
    w["l1f_lv"] = np.ascontiguousarray(mi["L1_ffn_leaf_values"], np.float32)
    w["l1f_woutT"] = np.ascontiguousarray(mi["L1_ffn_Wout_w"].T, np.float32)
    w["l1f_wob"] = col128(mi["L1_ffn_Wout_b"])
    da = np.zeros((128, 8, 256), np.float32)
    db = np.zeros((128, 8, 256), np.float32)
    cc = np.zeros((128, 16), np.float32)
    for lev in range(8):
        n = 2 ** lev
        half = 256 // (2 * n)
        for m in range(256):
            j = m // (2 * half)
            side = (m // half) % 2
            jj = n - 1 + j
            val = 1.0 if side == 1 else -1.0
            if jj < 128:
                da[jj, lev, m] = val
            else:
                db[jj - 128, lev, m] = val
            cval = 0.0 if side == 1 else 1.0
            hf, p = divmod(m, 128)
            cc[p, hf * 8 + lev] = cval
    w["l1f_Da"] = da.reshape(128, 8 * 256)
    w["l1f_Db"] = db.reshape(128, 8 * 256)
    w["l1f_c"] = cc
    w["l0a_wbc"] = np.ascontiguousarray(mi["L0_attn_W_bc"], np.float32)
    w["l0a_wdt"] = np.ascontiguousarray(mi["L0_attn_W_dt"], np.float32)
    w["l0a_wdp"] = np.ascontiguousarray(mi["L0_attn_W_dtproj"], np.float32)
    w["l0a_bdp"] = col128(mi["L0_attn_b_dtproj"])
    w["l0a_alog"] = np.ascontiguousarray(mi["L0_attn_A_log"], np.float32)
    w["l0a_d"] = col128(mi["L0_attn_D_param"])
    w["l1a_qpT"] = np.ascontiguousarray(
        mi["L1_attn_query_proj"].transpose(2, 0, 1).reshape(D, 1024), np.float32)
    w["l1a_kpT"] = np.ascontiguousarray(mi["L1_attn_key_proj"], np.float32)
    w["l1a_vpT"] = np.ascontiguousarray(mi["L1_attn_value_proj"], np.float32)
    w["l1a_opT"] = np.ascontiguousarray(
        mi["L1_attn_output_proj"].transpose(1, 2, 0).reshape(1024, D), np.float32)
    w["rt_w"] = np.ascontiguousarray(mi["L2_ffn_router_w"], np.float32)
    w["rt_b"] = np.ascontiguousarray(mi["L2_ffn_router_b"].reshape(1, 8), np.float32)
    return w



def S(k: K, engname, sems=None):
    snap = k.snap()
    mk = getattr(k.block, engname)

    def deco(fn):
        def body(eng):
            k.wait_snap(eng, snap, sems, engname=engname)
            fn(eng)
        mk(body)
    return deco



def ph_l1_ffn(k: K, xn_sb, f1):
    nc = k.nc
    with ExitStack() as ph:
        sb = lambda n, s, d=BF16: ph.enter_context(nc.sbuf_tensor("l1fs_" + n, s, d))
        ps = lambda n, s, d=F32: ph.enter_context(nc.psum_tensor("l1fs_" + n, s, d))
        wq = sb("wq", [128, DG, 1024]); rt = sb("rt", [128, DG, 255])
        da = sb("da", [128, 8, 256]); db = sb("db", [128, 8, 256])
        lva = sb("lva", [128, 1024]); lvb = sb("lvb", [128, 1024])
        wo = sb("wo", [128, DG, 1024]); wob = sb("wob", [128, DG], F32)
        ccol = sb("ccol", [128, 16], F32)
        q = sb("q", [128, DG, 512]); gate = sb("gate", [128, 2, 512], F32)
        gbf = sb("gbf", [128, 2, 512]); lp = sb("lp", [128, 2, 512], F32)
        fsb = sb("fsb", [128, 512], F32); lpb = sb("lpb", [128, 2, 512])
        ret = sb("ret", [128, DG, 512]); ya = sb("ya", [128, 512], F32)
        pq = ps("pq", [128, 512]); pf = ps("pf", [128, 512])
        po = ps("po", [128, 512])

        @S(k, "gpsimd")
        def _(g):
            k.inc(g.dma_start(out=wq[:], in_=k.inputs["l1f_wqT"].rearrange("(c p) n -> p c n", p=128)), "gdma")
            k.inc(g.dma_start(out=rt[:], in_=k.inputs["l1f_routT"].rearrange("(c p) n -> p c n", p=128)), "gdma")
            k.inc(g.dma_start(out=da[:], in_=k.inputs["l1f_Da"].rearrange("p (l m) -> p l m", l=8)), "gdma")
            k.inc(g.dma_start(out=db[:], in_=k.inputs["l1f_Db"].rearrange("p (l m) -> p l m", l=8)), "gdma")
            k.inc(g.dma_start(out=lva[:], in_=k.inputs["l1f_lv"][0:128, :]), "gdma")
            k.inc(g.dma_start(out=lvb[:], in_=k.inputs["l1f_lv"][128:256, :]), "gdma")
            k.inc(g.dma_start(out=wo[:], in_=k.inputs["l1f_woutT"].rearrange("(c p) n -> p c n", p=128)), "gdma")

        @S(k, "sync")
        def _(s):
            k.inc(s.dma_start(out=wob[:], in_=k.inputs["l1f_wob"]), "sdma")
            k.inc(s.dma_start(out=ccol[:], in_=k.inputs["l1f_c"]), "sdma")
        k.barrier()

        for tc in range(T // 512):
            ts_ = slice(tc * 512, (tc + 1) * 512)

            for og in range(DG):
                @S(k, "tensor")
                def _(t, og=og, ts_=ts_):
                    for g in range(DG):
                        ins = t.matmul(pq[:], wq[:, g, og * 128:(og + 1) * 128],
                                       xn_sb[:, g, ts_], start=(g == 0), stop=(g == DG - 1))
                    k.inc(ins, "pe")

                @S(k, "scalar")
                def _(sc, og=og):
                    k.inc(sc.copy(out=q[:, og, :], in_=pq[:]), "act")

            for jt in range(2):
                nrow = 128 if jt == 0 else 127

                @S(k, "tensor")
                def _(t, jt=jt, nrow=nrow):
                    for g in range(DG):
                        ins = t.matmul(pq[0:nrow, :], rt[:, g, jt * 128:jt * 128 + nrow],
                                       q[:, g, :], start=(g == 0), stop=(g == DG - 1))
                    k.inc(ins, "pe")

                @S(k, "scalar")
                def _(sc, jt=jt, nrow=nrow):
                    k.inc(sc.activation(out=gate[0:nrow, jt, :], in_=pq[0:nrow, :],
                                        func=AF.Exp, scale=-1.0), "act")

            @S(k, "vector")
            def _(v):
                v.tensor_scalar(out=gate[:, :, :], in0=gate[:, :, :], scalar1=1.0,
                                scalar2=None, op0=ALU.add)
                v.reciprocal(out=gate[:, :, :], in_=gate[:, :, :])
                k.inc(v.tensor_copy(out=gbf[:], in_=gate[:]), "dve")

            for lev in range(8):
                for half in range(2):
                    @S(k, "tensor")
                    def _(t, lev=lev, half=half):
                        t.matmul(pf[:], da[:, lev, half * 128:(half + 1) * 128],
                                 gbf[:, 0, :], start=True, stop=False)
                        ins = t.matmul(pf[:], db[0:127, lev, half * 128:(half + 1) * 128],
                                       gbf[0:127, 1, :], start=False, stop=True)
                        k.inc(ins, "pe")

                    @S(k, "vector")
                    def _(v, lev=lev, half=half):
                        ins = v.tensor_scalar(out=fsb[:], in0=pf[:],
                                              scalar1=ccol[:, half * 8 + lev:half * 8 + lev + 1],
                                              scalar2=None, op0=ALU.add)
                        if lev == 0:
                            ins = v.tensor_copy(out=lp[:, half, :], in_=fsb[:])
                        else:
                            ins = v.tensor_mul(out=lp[:, half, :], in0=lp[:, half, :], in1=fsb[:])
                        k.inc(ins, "dve")

            @S(k, "vector")
            def _(v):
                k.inc(v.tensor_copy(out=lpb[:], in_=lp[:]), "dve")

            for og in range(DG):
                @S(k, "tensor")
                def _(t, og=og):
                    t.matmul(pq[:], lva[:, og * 128:(og + 1) * 128], lpb[:, 0, :],
                             start=True, stop=False)
                    ins = t.matmul(pq[:], lvb[:, og * 128:(og + 1) * 128], lpb[:, 1, :],
                                   start=False, stop=True)
                    k.inc(ins, "pe")

                @S(k, "scalar")
                def _(sc, og=og):
                    k.inc(sc.copy(out=ret[:, og, :], in_=pq[:]), "act")

            for og in range(DG):
                @S(k, "tensor")
                def _(t, og=og):
                    for g in range(DG):
                        ins = t.matmul(po[:], wo[:, g, og * 128:(og + 1) * 128],
                                       ret[:, g, :], start=(g == 0), stop=(g == DG - 1))
                    k.inc(ins, "pe")

                @S(k, "scalar")
                def _(sc, og=og):
                    k.inc(sc.activation(out=ya[:], in_=po[:], func=AF.Identity,
                                        bias=wob[:, og:og + 1], scale=1.0), "act")

                @S(k, "sync")
                def _(s, tc=tc, og=og):
                    i0 = tc * 4
                    dst = f1[i0:i0 + 4, :, og * 128:(og + 1) * 128].rearrange("i p n -> p i n")
                    k.inc(s.dma_start(out=dst, in_=ya[:].rearrange("p (i n) -> p i n", i=4)), "st")
        k.barrier()



def ph_router_counts(k: K, logits_sb, counts_out, taps=()):
    nc = k.nc
    with ExitStack() as ph:
        sb = lambda n, s, d=F32: ph.enter_context(nc.sbuf_tensor("rt_" + n, s, d))
        ps = lambda n, s, d=F32: ph.enter_context(nc.psum_tensor("rt_" + n, s, d))
        brow = sb("brow", [1, 8], F32)
        bbc = sb("bbc", [128, 8], F32)
        iob = sb("iob", [128, 8], F32)
        iobig = sb("iobig", [128, 8], F32)
        lg = sb("lg", [128, 8], F32)
        mx = sb("mx", [128, 1], F32)
        cand = sb("cand", [128, 8], F32)
        eid = sb("eid", [128, 1], F32)
        oh = sb("oh", [128, 8], F32)
        cnt_sb = sb("cnt", [8, 1], F32)
        plg = ps("plg", [128, 8])
        pbc = ps("pbc", [128, 16])
        pcnt = ps("pcnt", [8, 1])
        tap_eid = None
        if "eid" in taps:
            tap_eid = k.dout("tap_eid", [NT128, 128, 1], F32)

        @S(k, "sync")
        def _(s):
            k.inc(s.dma_start(out=brow[:], in_=k.inputs["rt_b"]), "sdma")
        k.barrier()

        @S(k, "tensor")
        def _(t):
            t.matmul(pbc[:, 0:8], ones_row(k, 128), brow[:], start=True, stop=True)
            k.inc(t.matmul(pbc[:, 8:16], ones_row(k, 128), k.sb["cs_iota8"][0:1, :],
                           start=True, stop=True), "pe")

        @S(k, "vector")
        def _(v):
            v.tensor_copy(out=bbc[:], in_=pbc[:, 0:8])
            v.tensor_copy(out=iob[:], in_=pbc[:, 8:16])
            k.inc(v.tensor_scalar(out=iobig[:], in0=pbc[:, 8:16], scalar1=1e9,
                                  scalar2=None, op0=ALU.add), "dve")

        for i in range(NT128):
            @S(k, "vector")
            def _(v, i=i):
                k.inc(v.tensor_add(out=lg[:], in0=logits_sb[:, i, :], in1=bbc[:]), "dve")
                v.wait_ge(k.sems["dve"], k.cnt["dve"])
                k.inc(v.reduce_max(out=mx[:], in_=lg[:], axis=AX.X), "dve")
                v.wait_ge(k.sems["dve"], k.cnt["dve"])
                k.inc(v.tensor_scalar(out=oh[:], in0=lg[:], scalar1=mx[:, 0:1], scalar2=None,
                                      op0=ALU.is_ge), "dve")
                v.wait_ge(k.sems["dve"], k.cnt["dve"])
                k.inc(v.scalar_tensor_tensor(out=cand[:], in0=oh[:], scalar=65536.0,
                                             in1=iob[:], op0=ALU.mult, op1=ALU.subtract), "dve")
                v.wait_ge(k.sems["dve"], k.cnt["dve"])
                k.inc(v.reduce_max(out=eid[:], in_=cand[:], axis=AX.X), "dve")
                v.wait_ge(k.sems["dve"], k.cnt["dve"])
                k.inc(v.tensor_scalar(out=eid[:], in0=eid[:], scalar1=-1.0, scalar2=65536.0,
                                      op0=ALU.mult, op1=ALU.add), "dve")
                v.wait_ge(k.sems["dve"], k.cnt["dve"])
                k.inc(v.tensor_scalar(out=oh[:], in0=iob[:], scalar1=eid[:, 0:1],
                                      scalar2=None, op0=ALU.is_equal), "dve")

            @S(k, "tensor")
            def _(t, i=i):
                k.inc(t.matmul(pcnt[:], oh[:], k.constcol(1.0), start=(i == 0),
                               stop=(i == NT128 - 1)), "pe")

            if tap_eid is not None:
                @S(k, "sync")
                def _(s, i=i):
                    k.inc(s.dma_start(out=tap_eid[i], in_=eid[:]), "st")

        @S(k, "scalar")
        def _(sc):
            k.inc(sc.copy(out=cnt_sb[:], in_=pcnt[:]), "act")

        @S(k, "sync")
        def _(s):
            k.inc(s.dma_start(out=counts_out, in_=cnt_sb[:]), "st")
        k.barrier()



def ph_l0_attn(k: K, xn_sb, a0):
    nc = k.nc
    NB, NC512 = B_PER_CORE, L // 512
    xbc_hbm = k.dram("l0a_xbc", [32, T], F32)
    with ExitStack() as ph:
        sb = lambda n, s, d=F32: ph.enter_context(nc.sbuf_tensor("l0as_" + n, s, d))
        ps = lambda n, s, d=F32: ph.enter_context(nc.psum_tensor("l0as_" + n, s, d))
        wbc = sb("wbc", [128, DG, 32], BF16)
        wdt = sb("wdt", [128, DG, 64], BF16)
        wdp = sb("wdp", [64, DG, 128], BF16)
        bdp = sb("bdp", [128, DG])
        acoef = sb("acoef", [128, DG, 16])
        dcol = sb("dcol", [128, DG])
        dtl = sb("dtl", [64, T], BF16)
        xbcrow = sb("xbcrow", [32, 512])
        delta = sb("delta", [128, T])
        dx = sb("dx", [128, T])
        t1 = sb("t1", [128, 512]); t2 = sb("t2", [128, 512])
        abar = sb("abar", [128, 512]); bx = sb("bx", [128, 512])
        hn = sb("hn", [128, 512]); yt = sb("yt", [128, 512])
        yacc = sb("yacc", [128, 512]); hcar = sb("hcar", [128, 16])
        pj = ps("pj", [128, 512])
        pbc = ps("pbc", [128, 2, 512])

        @S(k, "gpsimd")
        def _(g):
            k.inc(g.dma_start(out=wbc[:], in_=k.inputs["l0a_wbc"].rearrange("(c p) n -> p c n", p=128)), "gdma")
            k.inc(g.dma_start(out=wdt[:], in_=k.inputs["l0a_wdt"].rearrange("(c p) n -> p c n", p=128)), "gdma")
            k.inc(g.dma_start(out=wdp[:], in_=k.inputs["l0a_wdp"].rearrange("r (c n) -> r c n", c=DG)), "gdma")

        @S(k, "sync")
        def _(s):
            k.inc(s.dma_start(out=bdp[:], in_=k.inputs["l0a_bdp"]), "sdma")
            k.inc(s.dma_start(out=acoef[:], in_=k.inputs["l0a_alog"].rearrange("(c p) n -> p c n", p=128)), "sdma")
            k.inc(s.dma_start(out=dcol[:], in_=k.inputs["l0a_d"]), "sdma")
        k.barrier()

        @S(k, "scalar")
        def _(sc):
            k.inc(sc.activation(out=acoef[:], in_=acoef[:], func=AF.Exp), "act")

        @S(k, "vector")
        def _(v):
            k.inc(v.tensor_scalar(out=acoef[:], in0=acoef[:], scalar1=-1.0,
                                  scalar2=None, op0=ALU.mult), "dve")

        for tc in range(T // 512):
            ts_ = slice(tc * 512, (tc + 1) * 512)

            @S(k, "tensor")
            def _(t, ts_=ts_):
                for g in range(DG):
                    t.matmul(pj[0:32, :], wbc[:, g, :], xn_sb[:, g, ts_],
                             start=(g == 0), stop=(g == DG - 1))
                for g in range(DG):
                    ins = t.matmul(pj[64:128, :], wdt[:, g, :], xn_sb[:, g, ts_],
                                   start=(g == 0), stop=(g == DG - 1))
                k.inc(ins, "pe")

            @S(k, "scalar")
            def _(sc, ts_=ts_):
                k.inc(sc.copy(out=xbcrow[:], in_=pj[0:32, :]), "act")
                k.inc(sc.copy(out=dtl[:, ts_], in_=pj[64:128, :]), "act")

            @S(k, "sync")
            def _(s, ts_=ts_):
                k.inc(s.dma_start(out=xbc_hbm[:, ts_], in_=xbcrow[:]), "st")

        k.barrier()

        for dg in range(DG):
            for tc in range(T // 512):
                ts_ = slice(tc * 512, (tc + 1) * 512)

                @S(k, "tensor")
                def _(t, dg=dg, ts_=ts_):
                    k.inc(t.matmul(pj[:], wdp[:, dg, :], dtl[:, ts_],
                                   start=True, stop=True), "pe")

                @S(k, "vector")
                def _(v, dg=dg, ts_=ts_):
                    k.inc(v.tensor_scalar(out=t1[:], in0=pj[:], scalar1=bdp[:, dg:dg + 1],
                                          scalar2=None, op0=ALU.add), "dve")

                @S(k, "scalar")
                def _(sc, ts_=ts_):
                    sc.activation(out=t2[:], in_=t1[:], func=AF.Abs)
                    sc.activation(out=t2[:], in_=t2[:], func=AF.Exp, scale=-1.0)
                    k.inc(sc.activation(out=t2[:], in_=t2[:], func=AF.Ln, bias=1.0), "act")

                @S(k, "vector")
                def _(v, dg=dg, ts_=ts_):
                    k.inc(v.tensor_scalar(out=t1[:], in0=t1[:], scalar1=0.0,
                                          scalar2=None, op0=ALU.max), "dve")
                    v.wait_ge(k.sems["dve"], k.cnt["dve"])
                    k.inc(v.tensor_add(out=delta[:, ts_], in0=t1[:], in1=t2[:]), "dve")
                    v.wait_ge(k.sems["dve"], k.cnt["dve"])
                    k.inc(v.tensor_mul(out=dx[:, ts_], in0=delta[:, ts_],
                                       in1=xn_sb[:, dg, ts_]), "dve")

            for b in range(NB):
                @S(k, "vector")
                def _(v):
                    k.inc(v.memset(hcar[:], 0.0), "dve")
                for c5 in range(NC512):
                    tglob = slice(b * L + c5 * 512, b * L + (c5 + 1) * 512)

                    @S(k, "sync")
                    def _(s, b=b, c5=c5):
                        k.inc(s.dma_start(out=xbcrow[:],
                                          in_=xbc_hbm[:, b * L + c5 * 512: b * L + (c5 + 1) * 512]), "sdma")

                    for n in range(16):
                        @S(k, "tensor")
                        def _(t, n=n):
                            sel = k.sb["cs_seln"]
                            t.matmul(pbc[:, 0, :], sel[:, n * 128:(n + 1) * 128],
                                     xbcrow[:], start=True, stop=True)
                            k.inc(t.matmul(pbc[:, 1, :], sel[:, (16 + n) * 128:(17 + n) * 128],
                                           xbcrow[:], start=True, stop=True), "pe")

                        @S(k, "scalar")
                        def _(sc, dg=dg, n=n, tglob=tglob):
                            k.inc(sc.activation(out=abar[:], in_=delta[:, tglob], func=AF.Exp,
                                                scale=acoef[:, dg, n:n + 1]), "act")

                        @S(k, "vector")
                        def _(v, dg=dg, n=n, tglob=tglob, first=(n == 0)):
                            k.inc(v.tensor_mul(out=bx[:], in0=dx[:, tglob], in1=pbc[:, 0, :]), "dve")
                            v.wait_ge(k.sems["dve"], k.cnt["dve"])
                            k.inc(v.tensor_tensor_scan(out=hn[:], data0=abar[:], data1=bx[:],
                                                       initial=hcar[:, n:n + 1],
                                                       op0=ALU.mult, op1=ALU.add), "dve")
                            v.wait_ge(k.sems["dve"], k.cnt["dve"])
                            k.inc(v.tensor_copy(out=hcar[:, n:n + 1], in_=hn[:, 511:512]), "dve")
                            k.inc(v.tensor_mul(out=yt[:], in0=hn[:], in1=pbc[:, 1, :]), "dve")
                            v.wait_ge(k.sems["dve"], k.cnt["dve"])
                            if first:
                                k.inc(v.tensor_copy(out=yacc[:], in_=yt[:]), "dve")
                            else:
                                k.inc(v.tensor_add(out=yacc[:], in0=yacc[:], in1=yt[:]), "dve")
                            v.wait_ge(k.sems["dve"], k.cnt["dve"])

                    @S(k, "vector")
                    def _(v, dg=dg, tglob=tglob):
                        k.inc(v.scalar_tensor_tensor(out=yt[:], in0=xn_sb[:, dg, tglob],
                                                     scalar=dcol[:, dg:dg + 1], in1=yacc[:],
                                                     op0=ALU.mult, op1=ALU.add), "dve")

                    @S(k, "sync")
                    def _(s, dg=dg, b=b, c5=c5):
                        i0 = (b * L + c5 * 512) // 128
                        dst = a0[i0:i0 + 4, :, dg * 128:(dg + 1) * 128].rearrange("i p n -> p i n")
                        k.inc(s.dma_start(out=dst, in_=yt[:].rearrange("p (i n) -> p i n", i=4)), "st")
        k.barrier()



FLASH_LIMIT = None
FLASH_STAGE = "all"


def ph_flash_attn(k: K, xn_sb, out_dram, pfx, shared_kv, causal, bias_col=None):
    nc = k.nc
    osb_hbm = k.dram(pfx + "osb", [D, L], BF16)
    H, HD = 16, 64
    NSEQ = B_PER_CORE
    if FLASH_LIMIT is not None:
        NSEQ, H = FLASH_LIMIT
    HK = HD if shared_kv else H * HD
    NQC = L // 512
    NKT = L // 128
    with ExitStack() as ph:
        sb = lambda n, s, d=BF16: ph.enter_context(nc.sbuf_tensor(pfx + "s" + n, s, d))
        ps = lambda n, s, d=F32: ph.enter_context(nc.psum_tensor(pfx + "s" + n, s, d))
        if shared_kv:
            qpT = sb("qpT", [128, DG, 1024])[:, :, :]
            kpT = sb("kpT", [128, DG, HK])[:, :, :]
            vpT = sb("vpT", [128, DG, HK])[:, :, :]
            opT = sb("opT", [128, DG, 1024])[:, :, :]
        else:
            wkv = sb("wkv", [128, DG, 2048])
            qpT = wkv[:, :, 0:1024]
            kpT = wkv[:, :, 1024:2048]
            vpT = wkv[:, :, 0:1024]
            opT = wkv[:, :, 0:1024]
        kfeat = sb("kfeat", [128, 1, L]) if shared_kv else None
        vtok = sb("vtok", [128, NKT, HK])
        kaug = sb("kaug", [66, L])
        qaug = sb("qaug", [66, L])
        sqbuf = sb("sqbuf", [64, 512], F32)
        norms = sb("norms", [1, L + 520], F32)
        ebf = sb("ebf", [128, 2, 512])
        otile = sb("otile", [64, 512], F32)
        obf = sb("obf", [64, 512])
        oin = sb("oin", [128, DG, 512])
        ya = sb("ya", [128, 512], F32)
        mtile = ya
        pk = ps("pk", [128, 512])
        pe_ = ps("pe", [128, 2, 512])
        pden = ps("pden", [1, 512])
        pqn = ps("pqn", [1, 512])
        pov = ps("pov", [64, 512])
        pbc = ps("pbc", [64, 512])
        po = ps("po", [128, 512])

        if shared_kv:
            @S(k, "gpsimd")
            def _(g):
                k.inc(g.dma_start(out=qpT, in_=k.inputs[pfx + "qpT"].rearrange("(c p) n -> p c n", p=128)), "gdma")
                k.inc(g.dma_start(out=kpT, in_=k.inputs[pfx + "kpT"].rearrange("(c p) n -> p c n", p=128)), "gdma")
                k.inc(g.dma_start(out=vpT, in_=k.inputs[pfx + "vpT"].rearrange("(c p) n -> p c n", p=128)), "gdma")
                k.inc(g.dma_start(out=opT, in_=k.inputs[pfx + "opT"].rearrange("(c p) n -> p c n", p=128)), "gdma")
            k.barrier()

        for s in range(NSEQ):
            toff = s * L
            if not shared_kv:
                @S(k, "gpsimd")
                def _(g):
                    k.inc(g.dma_start(out=vpT, in_=k.inputs[pfx + "vpT"].rearrange("(c p) n -> p c n", p=128)), "gdma")
                k.barrier()
            if shared_kv:
                for tc in range(L // 512):
                    @S(k, "tensor")
                    def _(t, tc=tc):
                        for g in range(DG):
                            ins = t.matmul(pk[0:64, :], kpT[:, g, 0:64],
                                           xn_sb[:, g, toff + tc * 512: toff + (tc + 1) * 512],
                                           start=(g == 0), stop=(g == DG - 1))
                        k.inc(ins, "pe")

                    @S(k, "scalar")
                    def _(sc, tc=tc):
                        k.inc(sc.copy(out=kfeat[0:64, 0, tc * 512:(tc + 1) * 512],
                                      in_=pk[0:64, :]), "act")
            for it in range(NKT):
                for hc in range(max(1, HK // 512)):
                    w_ = min(512, HK)

                    @S(k, "tensor")
                    def _(t, it=it, hc=hc, w_=w_):
                        for g in range(DG):
                            ins = t.matmul(pk[:, 0:w_],
                                           xn_sb[:, g, toff + it * 128: toff + (it + 1) * 128],
                                           vpT[:, g, hc * 512:hc * 512 + w_],
                                           start=(g == 0), stop=(g == DG - 1))
                        k.inc(ins, "pe")

                    @S(k, "scalar")
                    def _(sc, it=it, hc=hc, w_=w_):
                        k.inc(sc.copy(out=vtok[:, it, hc * 512:hc * 512 + w_],
                                      in_=pk[:, 0:w_]), "act")

            if FLASH_STAGE == "kv":
                continue
            if not shared_kv:
                @S(k, "gpsimd")
                def _(g):
                    k.inc(g.dma_start(out=qpT, in_=k.inputs[pfx + "qpT"].rearrange("(c p) n -> p c n", p=128)), "gdma")
                    k.inc(g.dma_start(out=kpT, in_=k.inputs[pfx + "kpT"].rearrange("(c p) n -> p c n", p=128)), "gdma")
                k.barrier()

            for h in range(H):
                def kslc(sl):
                    return kaug[0:64, sl]
                @S(k, "vector")
                def _(v):
                    k.inc(v.memset(kaug[64:65, :], 1.0), "dve")

                if shared_kv:
                    @S(k, "scalar")
                    def _(sc):
                        k.inc(sc.copy(out=kaug[0:64, :], in_=kfeat[0:64, 0, :]), "act")
                else:
                    for tc in range(L // 512):
                        @S(k, "tensor")
                        def _(t, h=h, tc=tc):
                            for g in range(DG):
                                ins = t.matmul(pk[0:64, :], kpT[:, g, h * 64:(h + 1) * 64],
                                               xn_sb[:, g, toff + tc * 512: toff + (tc + 1) * 512],
                                               start=(g == 0), stop=(g == DG - 1))
                            k.inc(ins, "pe")

                        @S(k, "scalar")
                        def _(sc, tc=tc):
                            k.inc(sc.copy(out=kaug[0:64, tc * 512:(tc + 1) * 512],
                                          in_=pk[0:64, :]), "act")

                if FLASH_STAGE == "kaug1":
                    continue
                for tc in range(L // 512):
                    @S(k, "scalar")
                    def _(sc, tc=tc):
                        k.inc(sc.activation(out=sqbuf[:], in_=kslc(slice(tc * 512, (tc + 1) * 512)),
                                            func=AF.Square), "act")

                    if FLASH_STAGE == "kaug2a":
                        continue

                    @S(k, "tensor")
                    def _(t, tc=tc):
                        k.inc(t.matmul(pden[0:1, :], k.constcol(1.0)[0:64, :], sqbuf[:],
                                       start=True, stop=True), "pe")

                    if FLASH_STAGE == "kaug2b":
                        continue

                    @S(k, "vector")
                    def _(v, tc=tc):
                        k.inc(v.reduce_max(out=norms[0:1, L + 513:L + 514],
                                           in_=pden[0:1, :], axis=AX.X), "dve")
                        v.wait_ge(k.sems["dve"], k.cnt["dve"])
                        if tc == 0:
                            k.inc(v.tensor_copy(out=norms[0:1, L + 512:L + 513],
                                                in_=norms[0:1, L + 513:L + 514]), "dve")
                        else:
                            k.inc(v.tensor_max(out=norms[0:1, L + 512:L + 513],
                                               in0=norms[0:1, L + 512:L + 513],
                                               in1=norms[0:1, L + 513:L + 514]), "dve")

                if FLASH_STAGE == "kaug2":
                    continue

                @S(k, "scalar")
                def _(sc):
                    sc.activation(out=norms[0:1, L + 514:L + 515],
                                  in_=norms[0:1, L + 512:L + 513], func=AF.Ln)
                    k.inc(sc.activation(out=norms[0:1, L + 514:L + 515],
                                        in_=norms[0:1, L + 514:L + 515],
                                        func=AF.Exp, scale=0.5), "act")

                if FLASH_STAGE == "kaug":
                    continue
                for tc in range(L // 512):
                    @S(k, "tensor")
                    def _(t, h=h, tc=tc):
                        for g in range(DG):
                            ins = t.matmul(pk[0:64, :], qpT[:, g, h * 64:(h + 1) * 64],
                                           xn_sb[:, g, toff + tc * 512: toff + (tc + 1) * 512],
                                           start=(g == 0), stop=(g == DG - 1))
                        k.inc(ins, "pe")

                    @S(k, "scalar")
                    def _(sc, tc=tc):
                        k.inc(sc.activation(out=qaug[0:64, tc * 512:(tc + 1) * 512],
                                            in_=pk[0:64, :], func=AF.Identity, scale=0.125), "act")
                        k.inc(sc.activation(out=sqbuf[:], in_=pk[0:64, :], func=AF.Square), "act")

                    @S(k, "tensor")
                    def _(t, tc=tc):
                        k.inc(t.matmul(pqn[0:1, :], k.constcol(1.0)[0:64, :], sqbuf[:],
                                       start=True, stop=True), "pe")

                    @S(k, "vector")
                    def _(v, tc=tc):
                        k.inc(v.tensor_copy(out=norms[0:1, tc * 512:(tc + 1) * 512],
                                            in_=pqn[0:1, :]), "dve")

                @S(k, "scalar")
                def _(sc):
                    sc.activation(out=norms[0:1, 0:L], in_=norms[0:1, 0:L], func=AF.Ln)
                    k.inc(sc.activation(out=norms[0:1, 0:L], in_=norms[0:1, 0:L],
                                        func=AF.Exp, scale=0.5), "act")

                @S(k, "vector")
                def _(v):
                    k.inc(v.tensor_scalar(out=norms[0:1, 0:L], in0=norms[0:1, 0:L],
                                          scalar1=norms[0:1, L + 514:L + 515],
                                          scalar2=-0.125, op0=ALU.mult, op1=ALU.mult), "dve")
                    v.wait_ge(k.sems["dve"], k.cnt["dve"])
                    k.inc(v.tensor_copy(out=qaug[64:65, :], in_=norms[0:1, 0:L]), "dve")

                if DEBUG_TAPS.get("flashdbg") and s == 0 and h == 0:
                    ph_dump_sbuf(k, "kaug", kaug[0:66, :])
                    ph_dump_sbuf(k, "qaug", qaug[0:66, :])
                    ph_dump_sbuf(k, "norms", norms[0:1, :])
                if FLASH_STAGE == "cq":
                    continue
                for qc in range(NQC):
                    ktmax = NKT if not causal else min(NKT, (qc + 1) * 4)
                    pe_ct = {}; act_ct = {}; mm2_ct = {}

                    def emit_stile(kt, qc=qc, h=h):
                        pb = kt % 2

                        def body(t, kt=kt, pb=pb):
                            if kt >= 2:
                                t.wait_ge(k.sems["act"], act_ct[kt - 2])
                            pe_ct[kt] = k.inc(
                                t.matmul(pe_[:, pb, :], kaug[0:65, kt * 128:(kt + 1) * 128],
                                         qaug[0:65, qc * 512:(qc + 1) * 512],
                                         start=True, stop=True), "pe")
                        return body

                    @S(k, "tensor")
                    def _(t, emit=emit_stile(0)):
                        emit(t)

                    for kt in range(ktmax):
                        pb = kt % 2
                        off = kt * 128 - qc * 512
                        if causal and 0 <= off <= 384:
                            @k.block.vector
                            def _(v, kt=kt, pb=pb, off=off):
                                v.wait_ge(k.sems["pe"], pe_ct[kt])
                                v.wait_ge(k.sems["act"], k.cnt["act"])
                                if kt >= 2:
                                    v.wait_ge(k.sems["pe"], mm2_ct[kt - 2])
                                j = off // 128
                                k.inc(v.tensor_add(out=mtile[:], in0=pe_[:, pb, :],
                                                   in1=k.sb["cs_cmask_bf"][:, j * 512:(j + 1) * 512]), "dve")

                            @k.block.scalar
                            def _(sc, kt=kt, pb=pb):
                                sc.wait_ge(k.sems["dve"], k.cnt["dve"])
                                act_ct[kt] = k.inc(sc.activation(out=ebf[:, pb, :], in_=mtile[:],
                                                                 func=AF.Exp), "act")
                        else:
                            @k.block.scalar
                            def _(sc, kt=kt, pb=pb):
                                sc.wait_ge(k.sems["pe"], pe_ct[kt])
                                if kt >= 2:
                                    sc.wait_ge(k.sems["pe"], mm2_ct[kt - 2])
                                act_ct[kt] = k.inc(sc.activation(out=ebf[:, pb, :],
                                                                 in_=pe_[:, pb, :],
                                                                 func=AF.Exp), "act")

                        @k.block.tensor
                        def _(t, kt=kt, pb=pb, first=(kt == 0), last=(kt == ktmax - 1)):
                            if kt + 1 < ktmax:
                                emit_stile(kt + 1)(t)
                            t.wait_ge(k.sems["act"], act_ct[kt])
                            t.matmul(pden[0:1, :], k.sb["cs_ones_bf"][:, 0:1], ebf[:, pb, :],
                                     start=first, stop=last)
                            mm2_ct[kt] = k.inc(
                                t.matmul(pov[:], vtok[:, kt, (0 if shared_kv else h) * 64:
                                                     (1 if shared_kv else h + 1) * 64],
                                         ebf[:, pb, :], start=first, stop=last), "pe")

                    @S(k, "vector")
                    def _(v):
                        k.inc(v.reciprocal(out=norms[0:1, L:L + 512], in_=pden[0:1, :]), "dve")

                    @S(k, "tensor")
                    def _(t):
                        k.inc(t.matmul(pbc[:], ones_row(k, 64), norms[0:1, L:L + 512],
                                       start=True, stop=True), "pe")

                    @S(k, "scalar")
                    def _(sc):
                        k.inc(sc.copy(out=otile[:], in_=pov[:]), "act")

                    @S(k, "vector")
                    def _(v, h=h, qc=qc):
                        k.inc(v.tensor_mul(out=obf[:], in0=otile[:], in1=pbc[:]), "dve")

                    @S(k, "sync")
                    def _(s, h=h, qc=qc):
                        k.inc(s.dma_start(out=osb_hbm[h * 64:(h + 1) * 64,
                                                      qc * 512:(qc + 1) * 512],
                                          in_=obf[:]), "st")

            if not shared_kv:
                @S(k, "gpsimd")
                def _(g):
                    k.inc(g.dma_start(out=opT, in_=k.inputs[pfx + "opT"].rearrange("(c p) n -> p c n", p=128)), "gdma")
                k.barrier()
            for tc in range(L // 512):
                @S(k, "sync")
                def _(s, tc=tc):
                    k.inc(s.dma_start(out=oin[:],
                                      in_=osb_hbm[:, tc * 512:(tc + 1) * 512]
                                      .rearrange("(g p) t -> p g t", p=128)), "sdma")

                for og in range(DG):
                    @S(k, "tensor")
                    def _(t, og=og, tc=tc):
                        for g in range(DG):
                            ins = t.matmul(po[:], opT[:, g, og * 128:(og + 1) * 128],
                                           oin[:, g, :],
                                           start=(g == 0), stop=(g == DG - 1))
                        k.inc(ins, "pe")

                    @S(k, "scalar")
                    def _(sc, og=og):
                        if bias_col is not None:
                            ins = sc.activation(out=ya[:], in_=po[:], func=AF.Identity,
                                                bias=bias_col[:, og:og + 1], scale=1.0)
                        else:
                            ins = sc.copy(out=ya[:], in_=po[:])
                        k.inc(ins, "act")

                    @S(k, "sync")
                    def _(sy, s=s, og=og, tc=tc):
                        i0 = (s * L + tc * 512) // 128
                        dst = out_dram[i0:i0 + 4, :, og * 128:(og + 1) * 128].rearrange("i p n -> p i n")
                        k.inc(sy.dma_start(out=dst, in_=ya[:].rearrange("p (i n) -> p i n", i=4)), "st")
        k.barrier()


def ph_l1_attn(k: K, xn_sb, a1):
    ph_flash_attn(k, xn_sb, a1, "l1a_", shared_kv=True, causal=False)


def ph_l3_attn(k: K, xn_sb, a3, bo_sb):
    ph_flash_attn(k, xn_sb, a3, "l3a_", shared_kv=False, causal=True, bias_col=bo_sb)



def ph_residual_sum(k: K, srcs, x_out, tag="sum"):
    nc = k.nc
    nsrc = len(srcs)
    with ExitStack() as ph:
        ld = [ph.enter_context(nc.sbuf_tensor(f"{tag}_ld{b}", [128, nsrc, DG * 128], F32))
              for b in range(2)]
        xw = [ph.enter_context(nc.sbuf_tensor(f"{tag}_x{b}", [128, DG * 128], F32))
              for b in range(2)]
        for i in range(NT128):
            b = i % 2

            @S(k, "sync")
            def _(s, i=i, b=b):
                for j in range(nsrc):
                    k.inc(s.dma_start(out=ld[b][:, j, :], in_=srcs[j](i)), "sdma")

            @S(k, "vector")
            def _(v, i=i, b=b):
                v.tensor_add(out=xw[b][:], in0=ld[b][:, 0, :], in1=ld[b][:, 1, :])
                for j in range(2, nsrc):
                    v.tensor_add(out=xw[b][:], in0=xw[b][:], in1=ld[b][:, j, :])
                k.inc(v.nop() if False else v.tensor_copy(out=xw[b][:], in_=xw[b][:]), "dve")

            @S(k, "sync")
            def _(s, i=i, b=b):
                k.inc(s.dma_start(out=x_out(i), in_=xw[b][:]), "st")
    k.barrier()



def build_B(stop_after=None, taps=()):
    k = K("B")
    x2 = k.din("x2", [NT128, 128, DG * 128], F32)
    counts_all = k.din("counts_all", [8, 8], F32)
    premask = k.din("premask", [8, 1], F32)
    xout = k.dout("xout", [NT128, 128, DG * 128], F32)
    for nm in ("qpT", "kpT", "vpT", "opT"):
        k.din("l3a_" + nm, [D, 1024], F32)
    k.din("l3a_bo", [128, DG], F32)
    k.din("l3f_wgT", [D, 8], F32)
    k.din("l3f_w1T", [8, D, 4096], F32)
    k.din("l3f_b1", [8, 128, 32], F32)
    k.din("l3f_woT", [2048, 1024], F32)
    k.din("l3f_bo", [128, DG], F32)
    decl_l2(k)

    k.block = k.es.enter_context(k.nc.Block())
    ph_preamble(k, CONSTVALS, ["ident", "tril01", "stril01", "striu01",
                               "diagbig", "ones", "iota8", "cmask"])
    xn_sb = k.sbuf("xn", [128, DG, T], BF16)
    bo_sb = k.sbuf("l3a_bo_sb", [128, DG], F32)
    xn_tok = k.dram("xn_tok", [T, D], BF16)

    @S(k, "sync")
    def _(s):
        k.inc(s.dma_start(out=bo_sb[:], in_=k.inputs["l3a_bo"]), "sdma")

    rtw_f = k.sbuf("rtw_f", [128, DG, 8], F32)
    wg_f = k.sbuf("wg_f", [128, DG, 8], F32)
    lg2_sb = k.sbuf("lg2", [128, NT128, 8], F32)
    lg3_sb = k.sbuf("lg3", [128, NT128, 8], F32)

    @S(k, "sync")
    def _(s):
        k.inc(s.dma_start(out=rtw_f[:], in_=k.inputs["rt_w"].rearrange("(c p) n -> p c n", p=128)), "sdma")
        k.inc(s.dma_start(out=wg_f[:], in_=k.inputs["l3f_wgT"].rearrange("(c p) n -> p c n", p=128)), "sdma")
    k.barrier()
    ph_residual_ln(k, [lambda i: x2[i]], None, xn_sb, xn_tok_hbm=xn_tok, tag="ln2",
                   router=(rtw_f, lg2_sb))
    if "xn2" in taps:
        ph_dump_sbuf(k, "xn2", xn_sb[:])
    if stop_after == "ln2":
        ph_finish(k)
        return k
    if stop_after in ("l3a2", "l3f2", "l2a2", "l2f2"):
        tmp = k.dout("bdump", [NT128, 128, DG * 128], F32)
        if stop_after == "l3a2":
            ph_l3_attn(k, xn_sb, tmp, bo_sb)
        elif stop_after == "l3f2":
            ph_l3_ffn(k, xn_sb, tmp, lg3_sb, test_logits=(wg_f,))
        elif stop_after == "l2a2":
            ph_l2_attn(k, xn_sb, tmp)
        elif stop_after == "l2f2":
            ph_l2_ffn(k, xn_sb, xn_tok, counts_all, premask, tmp, lg2_sb)
        ph_finish(k)
        return k

    a2 = k.dram("a2", [NT128, 128, DG * 128], F32)
    f2 = k.dram("f2", [NT128, 128, DG * 128], F32)
    ph_l2_ffn(k, xn_sb, xn_tok, counts_all, premask, f2, lg2_sb)
    ph_l2_attn(k, xn_sb, a2)

    x3 = k.dram("x3", [NT128, 128, DG * 128], F32)
    ph_residual_ln(k, [lambda i: x2[i], lambda i: a2[i], lambda i: f2[i]],
                   lambda i: x3[i], xn_sb, tag="ln3", router=(wg_f, lg3_sb))
    a3 = k.dram("a3", [NT128, 128, DG * 128], F32)
    f3 = k.dram("f3", [NT128, 128, DG * 128], F32)
    ph_l3_attn(k, xn_sb, a3, bo_sb)
    ph_l3_ffn(k, xn_sb, f3, lg3_sb)
    ph_residual_sum(k, [lambda i: x3[i], lambda i: a3[i], lambda i: f3[i]],
                    lambda i: xout[i], tag="fin")
    ph_finish(k)
    return k


def decl_l2(k: K):
    k.din("rt_w", [D, 8], F32)
    k.din("rt_b", [1, 8], F32)
    k.din("l2a_wipT", [D, 5312], F32)
    k.din("l2a_dtb", [1, 32], F32)
    k.din("l2a_alog", [1, 32], F32)
    k.din("l2a_dd", [1, 32], F32)
    k.din("l2a_bnw", [1, 64], F32)
    k.din("l2a_cnw", [1, 64], F32)
    k.din("l2a_bbias", [1, 2048], F32)
    k.din("l2a_cbias", [1, 2048], F32)
    k.din("l2a_normw", [128, 16], F32)
    k.din("l2a_wsmall", [D, 1216], F32)
    k.din("l2a_wopT", [2048, 1024], F32)
    k.din("l2f_w1", [8, D, 4096], F32)
    k.din("l2f_b1", [8, 128, 32], F32)
    k.din("l2f_w2", [8, 4096, 1024], F32)
    k.din("l2f_b2", [8, 1, 1024], F32)
    k.din("l2f_ow1", [D, 4096], F32)
    k.din("l2f_ob1", [128, 32], F32)
    k.din("l2f_ow2", [4096, 1024], F32)
    k.din("l2f_ob2", [1, 1024], F32)


def prep_weights_B(mi):
    w = {}
    w["rt_w"] = np.ascontiguousarray(mi["L2_ffn_router_w"], np.float32)
    w["rt_b"] = np.ascontiguousarray(mi["L2_ffn_router_b"].reshape(1, 8), np.float32)
    w["l3a_qpT"] = np.ascontiguousarray(mi["L3_attn_Wq"].T, np.float32)
    w["l3a_kpT"] = np.ascontiguousarray(mi["L3_attn_Wk"].T, np.float32)
    w["l3a_vpT"] = np.ascontiguousarray(mi["L3_attn_Wv"].T, np.float32)
    w["l3a_opT"] = np.ascontiguousarray(mi["L3_attn_Wo"].T, np.float32)
    w["l3a_bo"] = col128(mi["L3_attn_bo"])
    w["l3f_wgT"] = np.ascontiguousarray(mi["L3_ffn_Wg"].T, np.float32)
    w["l3f_w1T"] = np.ascontiguousarray(mi["L3_ffn_W1"].transpose(0, 2, 1), np.float32)
    w["l3f_b1"] = np.ascontiguousarray(
        np.stack([col128(mi["L3_ffn_b1"][e]) for e in range(8)]), np.float32)
    w["l3f_woT"] = np.ascontiguousarray(mi["L3_ffn_Wo"].T, np.float32)
    w["l3f_bo"] = col128(mi["L3_ffn_bo"])
    w["l2a_wipT"] = np.ascontiguousarray(mi["L2_attn_in_proj_w"].T, np.float32)
    w["l2a_dtb"] = mi["L2_attn_dt_bias"].reshape(1, 32).astype(np.float32)
    w["l2a_alog"] = mi["L2_attn_A_log"].reshape(1, 32).astype(np.float32)
    w["l2a_dd"] = mi["L2_attn_D"].reshape(1, 32).astype(np.float32)
    w["l2a_bnw"] = mi["L2_attn_B_norm_w"].reshape(1, 64).astype(np.float32)
    w["l2a_cnw"] = mi["L2_attn_C_norm_w"].reshape(1, 64).astype(np.float32)
    w["l2a_bbias"] = mi["L2_attn_B_bias"].reshape(1, 2048).astype(np.float32)
    w["l2a_cbias"] = mi["L2_attn_C_bias"].reshape(1, 2048).astype(np.float32)
    w["l2a_normw"] = col128(mi["L2_attn_norm_w"])
    wipT = np.asarray(mi["L2_attn_in_proj_w"].T, np.float32)
    w["l2a_wsmall"] = np.ascontiguousarray(np.concatenate(
        [wipT[:, 4224:4256], wipT[:, 5280:5312], wipT[:, 4096:4160],
         wipT[:, 4160:4224], wipT[:, 4256:5280]], axis=1))
    w["l2a_wopT"] = np.ascontiguousarray(mi["L2_attn_out_proj_w"].T, np.float32)
    w["l2f_w1"] = np.ascontiguousarray(mi["L2_ffn_w1"], np.float32)
    w["l2f_b1"] = np.ascontiguousarray(
        np.stack([col128(mi["L2_ffn_b1"][e]) for e in range(8)]), np.float32)
    w["l2f_w2"] = np.ascontiguousarray(mi["L2_ffn_w2"], np.float32)
    w["l2f_b2"] = mi["L2_ffn_b2"].reshape(8, 1, 1024).astype(np.float32)
    w["l2f_ow1"] = np.ascontiguousarray(mi["L2_ffn_ow1"], np.float32)
    w["l2f_ob1"] = col128(mi["L2_ffn_ob1"])
    w["l2f_ow2"] = np.ascontiguousarray(mi["L2_ffn_ow2"], np.float32)
    w["l2f_ob2"] = mi["L2_ffn_ob2"].reshape(1, 1024).astype(np.float32)
    return w



def ph_l3_ffn(k: K, xn_sb, f3, logits_sb, test_logits=None):
    nc = k.nc
    g_hbm = k.dram("l3f_glu", [2048, T], BF16)
    combf_hbm = k.dram("l3f_combf", [8, T], F32)
    with ExitStack() as ph:
        sb = lambda n, s, d=BF16: ph.enter_context(nc.sbuf_tensor("l3fs_" + n, s, d))
        ps = lambda n, s, d=F32: ph.enter_context(nc.psum_tensor("l3fs_" + n, s, d))
        wbig_t = sb("wbig", [128, 8 * 4096])
        wbig = wbig_t[:].rearrange("p (c n) -> p c n", c=8)
        b1c = sb("b1c", [128, 32], F32)
        boc = sb("boc", [128, DG], F32)
        iob = sb("iob", [128, 8], F32)
        gs = sb("gs", [128, 8], F32)
        cols = sb("cols", [128, 8], F32)
        comb = sb("comb", [128, 8], F32)
        crow = sb("crow", [1, 512], F32)
        hsb = sb("hsb", [128, 32, 512])
        glu = sb("glu", [128, 16, 512])
        tfp = sb("tfp", [128, 512], F32)
        tfp2 = sb("tfp2", [128, 2, 512], F32)
        eo = sb("eo", [128, 512], F32)
        eo2 = sb("eo2", [128, 2, 512], F32)
        zero = sb("zero", [128, DG * 128], F32)
        pg = ps("pg", [128, 8])
        ph_ = ps("ph", [128, 2, 512])
        pcb = ps("pcb", [128, 512])
        ptr = ps("ptr", [1, 128])

        @S(k, "sync")
        def _(s):
            k.inc(s.dma_start(out=boc[:], in_=k.inputs["l3f_bo"]), "sdma")
        k.barrier()

        @S(k, "tensor")
        def _(t):
            k.inc(t.matmul(pg[:], ones_row(k, 128), k.sb["cs_iota8"][0:1, :],
                           start=True, stop=True), "pe")

        @S(k, "vector")
        def _(v):
            k.inc(v.tensor_copy(out=iob[:], in_=pg[:]), "dve")
            k.inc(v.memset(zero[:], 0.0), "dve")
        for i in range(NT128):
            @S(k, "sync")
            def _(s, i=i):
                k.inc(s.dma_start(out=f3[i], in_=zero[:]), "st")
        k.barrier()

        tap_comb = k.dout("tap_comb", [NT128, 128, 8], F32) if DEBUG_TAPS.get("comb") else None
        if test_logits is not None:
            wgf = test_logits[0]
            xnb = sb("xnb32", [128, 128], F32)
            for i in range(NT128):
                for g in range(DG):
                    @S(k, "vector")
                    def _(v, i=i, g=g):
                        k.inc(v.tensor_copy(out=xnb[:], in_=xn_sb[:, g, i * 128:(i + 1) * 128]), "dve")

                    @S(k, "tensor")
                    def _(t, i=i, g=g):
                        k.inc(t.matmul(pg[:], xnb[:], wgf[:, g, :],
                                       start=(g == 0), stop=(g == DG - 1)), "pe")

                @S(k, "vector")
                def _(v, i=i):
                    k.inc(v.tensor_copy(out=logits_sb[:, i, :], in_=pg[:]), "dve")
            k.barrier()
        for i in range(NT128):
            @S(k, "vector")
            def _(v, i=i):
                k.inc(v.tensor_copy(out=gs[:], in_=logits_sb[:, i, :]), "dve")
                v.wait_ge(k.sems["dve"], k.cnt["dve"])
                k.inc(v.reduce_max(out=cols[:, 0:1], in_=gs[:], axis=AX.X), "dve")
                v.wait_ge(k.sems["dve"], k.cnt["dve"])
                k.inc(v.tensor_scalar(out=cols[:, 0:1], in0=cols[:, 0:1], scalar1=-1.0,
                                      scalar2=None, op0=ALU.mult), "dve")

            @S(k, "scalar")
            def _(sc, i=i):
                k.inc(sc.activation(out=gs[:], in_=gs[:], func=AF.Exp,
                                    bias=cols[:, 0:1], scale=1.0,
                                    accum_out=cols[:, 1:2]), "act")

            @S(k, "vector")
            def _(v, i=i):
                def step(ins):
                    c = k.inc(ins, "dve")
                    v.wait_ge(k.sems["dve"], c)
                step(v.reciprocal(out=cols[:, 1:2], in_=cols[:, 1:2]))
                step(v.tensor_scalar(out=gs[:], in0=gs[:], scalar1=cols[:, 1:2],
                                     scalar2=None, op0=ALU.mult))
                step(v.reduce_max(out=cols[:, 2:3], in_=gs[:], axis=AX.X))
                step(v.tensor_scalar(out=tfp[:, 0:8], in0=gs[:], scalar1=cols[:, 2:3],
                                     scalar2=None, op0=ALU.is_ge))
                step(v.scalar_tensor_tensor(out=tfp[:, 8:16], in0=tfp[:, 0:8],
                                            scalar=65536.0, in1=iob[:],
                                            op0=ALU.mult, op1=ALU.subtract))
                step(v.reduce_max(out=cols[:, 3:4], in_=tfp[:, 8:16], axis=AX.X))
                step(v.tensor_scalar(out=cols[:, 3:4], in0=cols[:, 3:4], scalar1=-1.0,
                                     scalar2=65536.0, op0=ALU.mult, op1=ALU.add))
                step(v.tensor_scalar(out=tfp[:, 0:8], in0=iob[:], scalar1=cols[:, 3:4],
                                     scalar2=None, op0=ALU.is_equal))
                step(v.scalar_tensor_tensor(out=tfp[:, 8:16], in0=tfp[:, 0:8],
                                            scalar=-1e9, in1=gs[:],
                                            op0=ALU.mult, op1=ALU.add))
                step(v.reduce_max(out=cols[:, 4:5], in_=tfp[:, 8:16], axis=AX.X))
                step(v.tensor_add(out=cols[:, 5:6], in0=cols[:, 2:3], in1=cols[:, 4:5]))
                step(v.tensor_scalar(out=cols[:, 5:6], in0=cols[:, 5:6], scalar1=1e-9,
                                     scalar2=None, op0=ALU.add))
                step(v.reciprocal(out=cols[:, 5:6], in_=cols[:, 5:6]))
                step(v.tensor_scalar(out=tfp[:, 0:8], in0=gs[:], scalar1=cols[:, 4:5],
                                     scalar2=None, op0=ALU.is_ge))
                step(v.tensor_mul(out=tfp[:, 8:16], in0=gs[:], in1=tfp[:, 0:8]))
                step(v.tensor_scalar(out=comb[:], in0=tfp[:, 8:16],
                                     scalar1=cols[:, 5:6], scalar2=None, op0=ALU.mult))

            if tap_comb is not None:
                @S(k, "sync")
                def _(s, i=i):
                    k.inc(s.dma_start(out=tap_comb[i], in_=comb[:]), "st")
            for e in range(8):
                @S(k, "tensor")
                def _(t, i=i, e=e):
                    k.inc(t.transpose(ptr[:], comb[:, e:e + 1], k.sb["cs_ident"][:]), "pe")

                @S(k, "vector")
                def _(v, i=i, e=e):
                    k.inc(v.tensor_copy(out=crow[0:1, 0:128], in_=ptr[0:1, :]), "dve")

                @S(k, "sync")
                def _(s, i=i, e=e):
                    k.inc(s.dma_start(out=combf_hbm[e:e + 1, i * 128:(i + 1) * 128],
                                      in_=crow[0:1, 0:128]), "st")
        k.barrier()

        for e in range(8):
            @S(k, "gpsimd")
            def _(g, e=e):
                k.inc(g.dma_start(out=wbig,
                                  in_=k.inputs["l3f_w1T"][e].rearrange("(c p) n -> p c n", p=128)), "gdma")

            @S(k, "sync")
            def _(s, e=e):
                k.inc(s.dma_start(out=b1c[:], in_=k.inputs["l3f_b1"][e]), "sdma")
            k.barrier()

            for tc in range(T // 512):
                ts_ = slice(tc * 512, (tc + 1) * 512)
                pe_jg = {}; act_jg = {}
                for jg in range(32):
                    pb = jg % 2
                    if jg == 0:
                        @S(k, "tensor")
                        def _(t, jg=jg, ts_=ts_, pb=pb):
                            for g in range(DG):
                                ins = t.matmul(ph_[:, pb, :], wbig[:, g, jg * 128:(jg + 1) * 128],
                                               xn_sb[:, g, ts_], start=(g == 0), stop=(g == DG - 1))
                            pe_jg[jg] = k.inc(ins, "pe")
                    else:
                        @k.block.tensor
                        def _(t, jg=jg, ts_=ts_, pb=pb):
                            if jg >= 2:
                                t.wait_ge(k.sems["act"], act_jg[jg - 2])
                            for g in range(DG):
                                ins = t.matmul(ph_[:, pb, :], wbig[:, g, jg * 128:(jg + 1) * 128],
                                               xn_sb[:, g, ts_], start=(g == 0), stop=(g == DG - 1))
                            pe_jg[jg] = k.inc(ins, "pe")

                    @k.block.scalar
                    def _(sc, jg=jg, pb=pb):
                        sc.wait_ge(k.sems["pe"], pe_jg[jg])
                        act_jg[jg] = k.inc(
                            sc.activation(out=hsb[:, jg, :], in_=ph_[:, pb, :], func=AF.Identity,
                                          bias=b1c[:, jg:jg + 1], scale=1.0), "act")

                act_jh = {}; dve_jh = {}
                for jh in range(16):
                    if jh == 0:
                        @S(k, "scalar")
                        def _(sc, jh=jh):
                            act_jh[jh] = k.inc(sc.activation(out=tfp[:, 0:512], in_=hsb[:, 16 + jh, :],
                                                             func=AF.Exp, scale=-1.0), "act")
                    else:
                        @k.block.scalar
                        def _(sc, jh=jh):
                            if jh >= 2:
                                sc.wait_ge(k.sems["dve"], dve_jh[jh - 2])
                            act_jh[jh] = k.inc(sc.activation(out=tfp[:, 512 * (jh % 2):512 * (jh % 2) + 512]
                                                             if False else tfp2[:, jh % 2, :],
                                                             in_=hsb[:, 16 + jh, :],
                                                             func=AF.Exp, scale=-1.0), "act")

                    @k.block.vector
                    def _(v, jh=jh):
                        v.wait_ge(k.sems["act"], act_jh[jh])
                        buf = tfp[:, 0:512] if jh == 0 else tfp2[:, jh % 2, :]
                        k.inc(v.tensor_scalar(out=buf, in0=buf, scalar1=1.0,
                                              scalar2=None, op0=ALU.add), "dve")
                        v.wait_ge(k.sems["dve"], k.cnt["dve"])
                        k.inc(v.reciprocal(out=buf, in_=buf), "dve")
                        v.wait_ge(k.sems["dve"], k.cnt["dve"])
                        dve_jh[jh] = k.inc(v.tensor_mul(out=glu[:, jh, :], in0=hsb[:, jh, :],
                                                        in1=buf), "dve")

                @S(k, "sync")
                def _(s, tc=tc, ts_=ts_):
                    k.inc(s.dma_start(out=g_hbm[:, ts_].rearrange("(c p) t -> p c t", p=128),
                                      in_=glu[:]), "st")

            @S(k, "gpsimd")
            def _(g, e=e):
                k.inc(g.dma_start(
                    out=wbig_t[:, 0:16384].rearrange("p (c n) -> p c n", c=16),
                    in_=k.inputs["l3f_woT"].rearrange("(c p) n -> p c n", p=128)), "gdma")
            k.barrier()
            wot = wbig_t[:, 0:16384].rearrange("p (c n) -> p c n", c=16)

            for tc in range(T // 512):
                ts_ = slice(tc * 512, (tc + 1) * 512)

                @S(k, "sync")
                def _(s, e=e, tc=tc, ts_=ts_):
                    k.inc(s.dma_start(out=glu[:],
                                      in_=g_hbm[:, ts_].rearrange("(c p) t -> p c t", p=128)), "sdma")
                    k.inc(s.dma_start(out=crow[0:1, 0:512],
                                      in_=combf_hbm[e:e + 1, ts_]), "sdma")

                @S(k, "tensor")
                def _(t, ts_=ts_):
                    k.inc(t.matmul(pcb[:], ones_row(k, 128), crow[0:1, 0:512],
                                   start=True, stop=True), "pe")

                pe_og = {}; act_og = {}; dve_og = {}; st_og = {}
                for og in range(DG):
                    pb = og % 2
                    if og == 0:
                        @S(k, "tensor")
                        def _(t, og=og, pb=pb):
                            for g2 in range(16):
                                ins = t.matmul(ph_[:, pb, :], wot[:, g2, og * 128:(og + 1) * 128],
                                               glu[:, g2, :], start=(g2 == 0), stop=(g2 == 15))
                            pe_og[og] = k.inc(ins, "pe")
                    else:
                        @k.block.tensor
                        def _(t, og=og, pb=pb):
                            if og >= 2:
                                t.wait_ge(k.sems["act"], act_og[og - 2])
                            for g2 in range(16):
                                ins = t.matmul(ph_[:, pb, :], wot[:, g2, og * 128:(og + 1) * 128],
                                               glu[:, g2, :], start=(g2 == 0), stop=(g2 == 15))
                            pe_og[og] = k.inc(ins, "pe")

                    @k.block.scalar
                    def _(sc, og=og, pb=pb):
                        sc.wait_ge(k.sems["pe"], pe_og[og])
                        if og >= 2:
                            sc.wait_ge(k.sems["dve"], dve_og[og - 2])
                        act_og[og] = k.inc(
                            sc.activation(out=eo2[:, pb, :], in_=ph_[:, pb, :], func=AF.Identity,
                                          bias=boc[:, og:og + 1], scale=1.0), "act")

                    @k.block.vector
                    def _(v, og=og, pb=pb):
                        v.wait_ge(k.sems["act"], act_og[og])
                        if og >= 2:
                            v.wait_ge(k.sems["st"], st_og[og - 2])
                        dve_og[og] = k.inc(v.tensor_mul(out=eo2[:, pb, :], in0=eo2[:, pb, :],
                                                        in1=pcb[:]), "dve")

                    @k.block.gpsimd
                    def _(s, tc=tc, og=og, pb=pb):
                        s.wait_ge(k.sems["dve"], dve_og[og])
                        i0 = tc * 4
                        dst = f3[i0:i0 + 4, :, og * 128:(og + 1) * 128].rearrange("i p n -> p i n")
                        st_og[og] = k.inc(s.dma_start(out=dst,
                                                      in_=eo2[:, pb, :].rearrange("p (i n) -> p i n", i=4),
                                                      accum_op=ALU.add), "st")
        k.barrier()




def ph_l2_ffn(k: K, xn_sb, xn_tok, counts_all_d, premask_d, f2, logits_sb):
    nc = k.nc
    C = 1280
    bufloc = k.dram("l2f_bufloc", [8 * C, D], BF16)
    yall = k.dram("l2f_yall", [8 * C, D], BF16)
    h2_hbm = k.dram("l2f_h2", [4096, C], BF16)
    g2_hbm = k.dram("l2f_g2", [4096, T], BF16)
    with ExitStack() as ph:
        sb = lambda n, s, d=BF16: ph.enter_context(nc.sbuf_tensor("l2fs_" + n, s, d))
        ps = lambda n, s, d=F32: ph.enter_context(nc.psum_tensor("l2fs_" + n, s, d))
        wbig_t = sb("wbig", [128, 8 * 4096])
        w1v = wbig_t[:].rearrange("p (c n) -> p c n", c=8)
        w2v = wbig_t[:].rearrange("p (c n) -> p c n", c=32)
        cnts = sb("cnts", [8, 8], F32)
        pmask = sb("pmask", [8, 1], F32)
        brow = sb("brow", [1, 8], F32)
        smalls = sb("smalls", [128, 40], F32)
        cols = sb("cols", [128, 8], F32)
        carry = sb("carry", [1, 8], F32)
        slots = sb("slots", [128, NT128], I32)
        valid = sb("valid", [128, NT128], F32)
        vcomp = sb("vcomp", [128, NT128], F32)
        xtk = sb("xtk", [128, 1024])
        bufeT = sb("bufeT", [128, DG, C])
        hrow = sb("hrow", [128, C])
        hrow2 = sb("hrow2", [128, 2, C])
        hTl = sb("hTl", [128, 32, 128])
        ysb = sb("ysb", [128, 1024], F32)
        b2bc = sb("b2bc", [128, 1024], F32)
        ygat = sb("ygat", [128, 1024])
        zero = sb("zero", [128, 1024])
        otf = sb("otf", [128, 1024], F32)
        b1c = sb("b1c", [128, 32], F32)
        b2row = sb("b2row", [1, 1024], F32)
        pall3 = ps("pall3", [128, 2, 3, 512])
        pall = pall3[:, :, 0, :]
        ptr = ps("ptr", [128, 128])
        ptrb = ps("ptrb", [128, 128], BF16)
        prow = pall3[0:1, 1, 2, 0:16]

        @S(k, "sync")
        def _(s):
            k.inc(s.dma_start(out=cnts[:], in_=counts_all_d), "sdma")
            k.inc(s.dma_start(out=pmask[:], in_=premask_d), "sdma")
            k.inc(s.dma_start(out=brow[:], in_=k.inputs["rt_b"]), "sdma")

        bc_reg = [None]

        @S(k, "gpsimd")
        def _(g):
            bc_reg[0] = g.alloc_register("l2f_bc")
            g.reg_mov(bc_reg[0], 8 * C - 1)
        k.barrier()

        @S(k, "tensor")
        def _(t):
            k.inc(t.matmul(prow[0:1, 0:8], pmask[:], cnts[:], start=True, stop=True), "pe")

        @S(k, "vector")
        def _(v):
            k.inc(v.tensor_copy(out=smalls[0:1, 32:40], in_=prow[0:1, 0:8]), "dve")

        @S(k, "tensor")
        def _(t):
            t.matmul(pall[:, 0, 0:8], ones_row(k, 128), brow[:], start=True, stop=True)
            t.matmul(pall[:, 0, 8:16], ones_row(k, 128), k.sb["cs_iota8"][0:1, :],
                     start=True, stop=True)
            k.inc(t.matmul(pall[:, 0, 16:24], ones_row(k, 128), smalls[0:1, 32:40],
                           start=True, stop=True), "pe")

        @S(k, "vector")
        def _(v):
            k.inc(v.tensor_copy(out=smalls[:, 0:24], in_=pall[:, 0, 0:24]), "dve")
            k.inc(v.memset(zero[:], 0.0), "dve")
            k.inc(v.memset(carry[:], 0.0), "dve")

        for i in range(8 * C // 128):
            @S(k, "sync")
            def _(s, i=i):
                k.inc(s.dma_start(out=bufloc[i * 128:(i + 1) * 128, :], in_=zero[:]), "st")
        k.barrier()

        ge8 = smalls[:, 24:32]
        for i in range(NT128):
            @S(k, "vector")
            def _(v, i=i):
                def step(ins):
                    c = k.inc(ins, "dve")
                    v.wait_ge(k.sems["dve"], c)
                lg = cols[:, 0:8]
                step(v.tensor_add(out=lg, in0=logits_sb[:, i, :], in1=smalls[:, 0:8]))
                step(v.reduce_max(out=smalls[:, 36:37], in_=lg, axis=AX.X))
                step(v.tensor_scalar(out=ge8, in0=lg, scalar1=smalls[:, 36:37],
                                     scalar2=None, op0=ALU.is_ge))
                step(v.scalar_tensor_tensor(out=ge8, in0=ge8, scalar=65536.0,
                                            in1=smalls[:, 8:16], op0=ALU.mult,
                                            op1=ALU.subtract))
                step(v.reduce_max(out=smalls[:, 36:37], in_=ge8, axis=AX.X))
                step(v.tensor_scalar(out=smalls[:, 36:37], in0=smalls[:, 36:37],
                                     scalar1=-1.0, scalar2=65536.0,
                                     op0=ALU.mult, op1=ALU.add))
                step(v.tensor_scalar(out=ge8, in0=smalls[:, 8:16],
                                     scalar1=smalls[:, 36:37], scalar2=None,
                                     op0=ALU.is_equal))

            @S(k, "tensor")
            def _(t, i=i):
                t.matmul(ptr[:, 0:8], k.sb["cs_stril01"][:], ge8, start=True, stop=False)
                k.inc(t.matmul(ptr[:, 0:8], ones_row(k, 128), carry[:],
                               start=False, stop=True), "pe")

            @S(k, "vector")
            def _(v, i=i):
                def step(ins):
                    c = k.inc(ins, "dve")
                    v.wait_ge(k.sems["dve"], c)
                step(v.tensor_mul(out=cols[:, 0:8], in0=ge8, in1=ptr[:, 0:8]))
                step(v.reduce_sum(out=smalls[:, 37:38], in_=cols[:, 0:8], axis=AX.X))
                step(v.tensor_mul(out=cols[:, 0:8], in0=ge8, in1=smalls[:, 16:24]))
                step(v.reduce_sum(out=smalls[:, 38:39], in_=cols[:, 0:8], axis=AX.X))
                step(v.tensor_add(out=smalls[:, 39:40], in0=smalls[:, 37:38],
                                  in1=smalls[:, 38:39]))
                step(v.tensor_scalar(out=valid[:, i:i + 1], in0=smalls[:, 39:40],
                                     scalar1=1280.0, scalar2=None, op0=ALU.is_lt))
                step(v.tensor_scalar(out=vcomp[:, i:i + 1], in0=valid[:, i:i + 1],
                                     scalar1=-1.0, scalar2=1.0,
                                     op0=ALU.mult, op1=ALU.add))
                step(v.tensor_scalar(out=smalls[:, 36:37], in0=smalls[:, 36:37],
                                     scalar1=1280.0, scalar2=None, op0=ALU.mult))
                step(v.tensor_add(out=smalls[:, 36:37], in0=smalls[:, 36:37],
                                  in1=smalls[:, 37:38]))
                step(v.tensor_scalar(out=smalls[:, 38:39], in0=smalls[:, 37:38],
                                     scalar1=1280.0, scalar2=1e7, op0=ALU.is_ge,
                                     op1=ALU.mult))
                step(v.tensor_add(out=smalls[:, 36:37], in0=smalls[:, 36:37],
                                  in1=smalls[:, 38:39]))
                step(v.tensor_copy(out=slots[:, i:i + 1], in_=smalls[:, 36:37]))

            @S(k, "tensor")
            def _(t, i=i):
                k.inc(t.matmul(prow[0:1, 0:8], k.constcol(1.0), ge8,
                               start=(i == 0), stop=(i == NT128 - 1)), "pe")

            @S(k, "scalar")
            def _(sc, i=i):
                k.inc(sc.copy(out=carry[:], in_=prow[0:1, 0:8]), "act")

            @S(k, "sync")
            def _(s, i=i):
                k.inc(s.dma_start(out=xtk[:], in_=xn_tok[i * 128:(i + 1) * 128, :]), "sdma")

            @S(k, "gpsimd")
            def _(g, i=i):
                k.inc(g.indirect_dma_start(
                    out=bufloc, out_offset=IndirectOffsetOnAxis(slots[:, i:i + 1], 0),
                    in_=xtk[:], in_offset=None,
                    bounds_check=bc_reg[0], oob_is_err=False), "gdma")
        k.barrier()

        for e in range(8):
            for ct in range(C // 128):
                @S(k, "sync")
                def _(s, e=e, ct=ct):
                    k.inc(s.dma_start(out=xtk[:],
                                      in_=bufloc[e * C + ct * 128: e * C + (ct + 1) * 128, :]), "sdma")

                for g in range(DG):
                    @S(k, "tensor")
                    def _(t, ct=ct, g=g):
                        k.inc(t.transpose(ptrb[:], xtk[:, g * 128:(g + 1) * 128],
                                          k.sb["cs_ident_bf"][:]), "pe")

                    @S(k, "scalar")
                    def _(sc, ct=ct, g=g):
                        k.inc(sc.copy(out=bufeT[:, g, ct * 128:(ct + 1) * 128],
                                      in_=ptrb[:]), "act")

            @S(k, "gpsimd")
            def _(g, e=e):
                k.inc(g.dma_start(out=wbig_t[:].rearrange("p (c n) -> p c n", c=8),
                                  in_=k.inputs["l2f_w1"][e].rearrange("(c p) n -> p c n", p=128)), "gdma")

            @S(k, "sync")
            def _(s, e=e):
                k.inc(s.dma_start(out=b1c[:], in_=k.inputs["l2f_b1"][e]), "sdma")
            k.barrier()

            pe_jg = {}; act_jg = {}; st_jg = {}
            for jg in range(32):
                pb = jg % 2
                if jg == 0:
                    @S(k, "tensor")
                    def _(t, jg=jg, pb=pb):
                        for cc in range(3):
                            w_ = 512 if cc < 2 else 256
                            for g in range(DG):
                                ins = t.matmul(pall[:, (pb * 3 + cc) % 4 if False else cc, 0:w_]
                                               if False else pall3[:, pb, cc, 0:w_],
                                               w1v[:, g, jg * 128:(jg + 1) * 128],
                                               bufeT[:, g, cc * 512:cc * 512 + w_],
                                               start=(g == 0), stop=(g == DG - 1))
                        pe_jg[jg] = k.inc(ins, "pe")
                else:
                    @k.block.tensor
                    def _(t, jg=jg, pb=pb):
                        if jg >= 2:
                            t.wait_ge(k.sems["act"], act_jg[jg - 2])
                        for cc in range(3):
                            w_ = 512 if cc < 2 else 256
                            for g in range(DG):
                                ins = t.matmul(pall3[:, pb, cc, 0:w_],
                                               w1v[:, g, jg * 128:(jg + 1) * 128],
                                               bufeT[:, g, cc * 512:cc * 512 + w_],
                                               start=(g == 0), stop=(g == DG - 1))
                        pe_jg[jg] = k.inc(ins, "pe")

                @k.block.scalar
                def _(sc, jg=jg, pb=pb):
                    sc.wait_ge(k.sems["pe"], pe_jg[jg])
                    if jg >= 2:
                        sc.wait_ge(k.sems["st"], st_jg[jg - 2])
                    sc.activation(out=hrow2[:, pb, 0:512], in_=pall3[:, pb, 0, :], func=AF.Relu,
                                  bias=b1c[:, jg:jg + 1], scale=1.0)
                    sc.activation(out=hrow2[:, pb, 512:1024], in_=pall3[:, pb, 1, :], func=AF.Relu,
                                  bias=b1c[:, jg:jg + 1], scale=1.0)
                    act_jg[jg] = k.inc(sc.activation(out=hrow2[:, pb, 1024:1280],
                                                     in_=pall3[:, pb, 2, 0:256],
                                                     func=AF.Relu, bias=b1c[:, jg:jg + 1],
                                                     scale=1.0), "act")

                @k.block.sync
                def _(s, jg=jg, pb=pb):
                    s.wait_ge(k.sems["act"], act_jg[jg])
                    st_jg[jg] = k.inc(s.dma_start(out=h2_hbm[jg * 128:(jg + 1) * 128, :],
                                                  in_=hrow2[:, pb, :]), "st")

            @S(k, "gpsimd")
            def _(g, e=e):
                k.inc(g.dma_start(out=wbig_t[:].rearrange("p (c n) -> p c n", c=32),
                                  in_=k.inputs["l2f_w2"][e].rearrange("(c p) n -> p c n", p=128)), "gdma")

            @S(k, "sync")
            def _(s, e=e):
                k.inc(s.dma_start(out=b2row[:], in_=k.inputs["l2f_b2"][e]), "sdma")
            k.barrier()

            @S(k, "tensor")
            def _(t):
                t.matmul(pall3[:, 0, 0, :], ones_row(k, 128), b2row[0:1, 0:512],
                         start=True, stop=True)
                k.inc(t.matmul(pall3[:, 0, 1, :], ones_row(k, 128), b2row[0:1, 512:1024],
                               start=True, stop=True), "pe")

            @S(k, "vector")
            def _(v):
                v.tensor_copy(out=b2bc[:, 0:512], in_=pall3[:, 0, 0, :])
                k.inc(v.tensor_copy(out=b2bc[:, 512:1024], in_=pall3[:, 0, 1, :]), "dve")

            for ct in range(C // 128):
                @S(k, "sync")
                def _(s, ct=ct):
                    k.inc(s.dma_start(out=hTl[:],
                                      in_=h2_hbm[:, ct * 128:(ct + 1) * 128]
                                      .rearrange("(g p) c -> p g c", p=128)), "sdma")

                @S(k, "tensor")
                def _(t, ct=ct):
                    for jg in range(32):
                        ins = t.matmul(pall[:, 0, :], hTl[:, jg, :], w2v[:, jg, 0:512],
                                       start=(jg == 0), stop=(jg == 31))
                        ins = t.matmul(pall[:, 1, :], hTl[:, jg, :], w2v[:, jg, 512:1024],
                                       start=(jg == 0), stop=(jg == 31))
                    k.inc(ins, "pe")

                @S(k, "vector")
                def _(v, ct=ct):
                    v.tensor_add(out=ysb[:, 0:512], in0=pall[:, 0, :], in1=b2bc[:, 0:512])
                    k.inc(v.tensor_add(out=ysb[:, 512:1024], in0=pall[:, 1, :],
                                       in1=b2bc[:, 512:1024]), "dve")
                    v.wait_ge(k.sems["dve"], k.cnt["dve"])
                    k.inc(v.tensor_copy(out=ygat[:], in_=ysb[:]), "dve")

                @S(k, "sync")
                def _(s, e=e, ct=ct):
                    k.inc(s.dma_start(out=yall[e * C + ct * 128:e * C + (ct + 1) * 128, :],
                                      in_=ygat[:]), "st")
        k.barrier()

        @S(k, "gpsimd")
        def _(g):
            k.inc(g.dma_start(out=wbig_t[:].rearrange("p (c n) -> p c n", c=8),
                              in_=k.inputs["l2f_ow1"].rearrange("(c p) n -> p c n", p=128)), "gdma")

        @S(k, "sync")
        def _(s):
            k.inc(s.dma_start(out=b1c[:], in_=k.inputs["l2f_ob1"]), "sdma")
        k.barrier()

        for tc in range(T // 512):
            ts_ = slice(tc * 512, (tc + 1) * 512)
            pe_jg = {}; act_jg = {}; st_jg = {}
            for jg in range(32):
                pb = jg % 2
                if jg == 0:
                    @S(k, "tensor")
                    def _(t, jg=jg, ts_=ts_, pb=pb):
                        for g in range(DG):
                            ins = t.matmul(pall[:, pb, :], w1v[:, g, jg * 128:(jg + 1) * 128],
                                           xn_sb[:, g, ts_], start=(g == 0), stop=(g == DG - 1))
                        pe_jg[jg] = k.inc(ins, "pe")
                else:
                    @k.block.tensor
                    def _(t, jg=jg, ts_=ts_, pb=pb):
                        if jg >= 2:
                            t.wait_ge(k.sems["act"], act_jg[jg - 2])
                        for g in range(DG):
                            ins = t.matmul(pall[:, pb, :], w1v[:, g, jg * 128:(jg + 1) * 128],
                                           xn_sb[:, g, ts_], start=(g == 0), stop=(g == DG - 1))
                        pe_jg[jg] = k.inc(ins, "pe")

                @k.block.scalar
                def _(sc, jg=jg, pb=pb):
                    sc.wait_ge(k.sems["pe"], pe_jg[jg])
                    if jg >= 2:
                        sc.wait_ge(k.sems["st"], st_jg[jg - 2])
                    act_jg[jg] = k.inc(sc.activation(out=hrow2[:, pb, 0:512], in_=pall[:, pb, :],
                                                     func=AF.Relu, bias=b1c[:, jg:jg + 1],
                                                     scale=1.0), "act")

                @k.block.sync
                def _(s, tc=tc, jg=jg, pb=pb):
                    s.wait_ge(k.sems["act"], act_jg[jg])
                    st_jg[jg] = k.inc(s.dma_start(out=g2_hbm[jg * 128:(jg + 1) * 128,
                                                             tc * 512:(tc + 1) * 512],
                                                  in_=hrow2[:, pb, 0:512]), "st")
        k.barrier()

        @S(k, "gpsimd")
        def _(g):
            k.inc(g.dma_start(out=wbig_t[:].rearrange("p (c n) -> p c n", c=32),
                              in_=k.inputs["l2f_ow2"].rearrange("(c p) n -> p c n", p=128)), "gdma")

        @S(k, "sync")
        def _(s):
            k.inc(s.dma_start(out=b2row[:], in_=k.inputs["l2f_ob2"]), "sdma")
        k.barrier()

        @S(k, "tensor")
        def _(t):
            t.matmul(pall3[:, 0, 0, :], ones_row(k, 128), b2row[0:1, 0:512], start=True, stop=True)
            k.inc(t.matmul(pall3[:, 0, 1, :], ones_row(k, 128), b2row[0:1, 512:1024],
                           start=True, stop=True), "pe")

        @S(k, "vector")
        def _(v):
            v.tensor_copy(out=b2bc[:, 0:512], in_=pall3[:, 0, 0, :])
            k.inc(v.tensor_copy(out=b2bc[:, 512:1024], in_=pall3[:, 0, 1, :]), "dve")

        for i in range(NT128):
            @S(k, "sync")
            def _(s, i=i):
                k.inc(s.dma_start(out=hTl[:],
                                  in_=g2_hbm[:, i * 128:(i + 1) * 128]
                                  .rearrange("(g p) t -> p g t", p=128)), "sdma")

            @S(k, "vector")
            def _(v, i=i):
                k.inc(v.memset(ygat[:], 0.0), "dve")

            @S(k, "gpsimd")
            def _(g, i=i):
                k.inc(g.indirect_dma_start(
                    out=ygat[:], out_offset=None,
                    in_=yall, in_offset=IndirectOffsetOnAxis(slots[:, i:i + 1], 0),
                    bounds_check=bc_reg[0], oob_is_err=False), "gdma")

            @S(k, "tensor")
            def _(t, i=i):
                for jg in range(32):
                    ins = t.matmul(pall[:, 0, :], hTl[:, jg, :], w2v[:, jg, 0:512],
                                   start=(jg == 0), stop=(jg == 31))
                    ins = t.matmul(pall[:, 1, :], hTl[:, jg, :], w2v[:, jg, 512:1024],
                                   start=(jg == 0), stop=(jg == 31))
                k.inc(ins, "pe")

            @S(k, "vector")
            def _(v, i=i):
                def step(ins):
                    c = k.inc(ins, "dve")
                    v.wait_ge(k.sems["dve"], c)
                step(v.tensor_add(out=ysb[:, 0:512], in0=pall[:, 0, :], in1=b2bc[:, 0:512]))
                step(v.tensor_add(out=ysb[:, 512:1024], in0=pall[:, 1, :], in1=b2bc[:, 512:1024]))
                step(v.tensor_scalar(out=ysb[:], in0=ysb[:], scalar1=vcomp[:, i:i + 1],
                                     scalar2=None, op0=ALU.mult))
                step(v.tensor_scalar(out=otf[:], in0=ygat[:], scalar1=valid[:, i:i + 1],
                                     scalar2=None, op0=ALU.mult))
                step(v.tensor_add(out=otf[:], in0=otf[:], in1=ysb[:]))

            for g in range(DG):
                @S(k, "tensor")
                def _(t, i=i, g=g):
                    k.inc(t.transpose(ptr[:], otf[:, g * 128:(g + 1) * 128],
                                      k.sb["cs_ident"][:]), "pe")

                @S(k, "scalar")
                def _(sc, i=i, g=g):
                    k.inc(sc.copy(out=ysb[:, 0:128], in_=ptr[:]), "act")

                @S(k, "sync")
                def _(s, i=i, g=g):
                    k.inc(s.dma_start(out=f2[i][:, g * 128:(g + 1) * 128],
                                      in_=ysb[:, 0:128]), "st")
        k.barrier()



def ph_l2_attn(k: K, xn_sb, a2):
    nc = k.nc
    H, P, N, DI = 32, 64, 64, 2048
    NCH = L // 128
    zf_hbm = k.dram("l2a_zf", [DI, T], BF16)
    xf_hbm = k.dram("l2a_xf", [DI, T], BF16)
    y_hbm = k.dram("l2a_y", [DI, T], BF16)
    bm_hbm = k.dram("l2a_bm", [L, DI], BF16)
    cm_hbm = k.dram("l2a_cm", [L, DI], BF16)
    q_hbm = k.dram("l2a_q", [L + 128, 32], F32)

    with ExitStack() as ph:
        sb = lambda n, s, d=BF16: ph.enter_context(nc.sbuf_tensor("l2a1_" + n, s, d))
        ps = lambda n, s, d=F32: ph.enter_context(nc.psum_tensor("l2a1_" + n, s, d))
        slab = sb("slab", [128, DG, 512])
        ytile = sb("ytile", [128, 512])
        pj = ps("pj", [128, 2, 512])
        for jslab in range(8):
            @S(k, "gpsimd")
            def _(g, jslab=jslab):
                k.inc(g.dma_start(out=slab[:],
                                  in_=k.inputs["l2a_wipT"][:, jslab * 512:(jslab + 1) * 512]
                                  .rearrange("(c p) n -> p c n", p=128)), "gdma")
            k.barrier()
            for jt in range(4):
                jglob = jslab * 512 + jt * 128
                dst_h = zf_hbm if jglob < DI else xf_hbm
                row0 = jglob if jglob < DI else jglob - DI
                for tc in range(T // 512):
                    @S(k, "tensor")
                    def _(t, jslab=jslab, jt=jt, tc=tc):
                        for g in range(DG):
                            ins = t.matmul(pj[:, tc % 2, :], slab[:, g, jt * 128:(jt + 1) * 128],
                                           xn_sb[:, g, tc * 512:(tc + 1) * 512],
                                           start=(g == 0), stop=(g == DG - 1))
                        k.inc(ins, "pe")

                    @S(k, "scalar")
                    def _(sc, tc=tc):
                        k.inc(sc.copy(out=ytile[:], in_=pj[:, tc % 2, :]), "act")

                    @S(k, "sync")
                    def _(s, dst_h=dst_h, row0=row0, tc=tc):
                        k.inc(s.dma_start(out=dst_h[row0:row0 + 128, tc * 512:(tc + 1) * 512],
                                          in_=ytile[:]), "st")
        k.barrier()

    with ExitStack() as ph:
        sb = lambda n, s, d=BF16: ph.enter_context(nc.sbuf_tensor("l2a2_" + n, s, d))
        ps = lambda n, s, d=F32: ph.enter_context(nc.psum_tensor("l2a2_" + n, s, d))
        wsm = sb("wsm", [128, DG, 1216])
        rows32 = sb("rows32", [1, 32], F32)
        dtb_bc = sb("dtb_bc", [128, 32], F32)
        a_bc = sb("a_bc", [128, 32], F32)
        bnw_bc = sb("bnw_bc", [128, 64], F32)
        cnw_bc = sb("cnw_bc", [128, 64], F32)
        bbias_bc = sb("bbias_bc", [128, DI], F32)
        cbias_bc = sb("cbias_bc", [128, DI], F32)
        drep = sb("drep", [128, 32], F32)
        dt_tok = sb("dt_tok", [128, NCH, 32], F32)
        la_tok = sb("la_tok", [128, NCH, 32], F32)
        g_tok = sb("g_tok", [128, NCH, 32], F32)
        q_tok = sb("q_tok", [128, NCH, 32], F32)
        lne_tok = sb("lne_tok", [128, NCH, 32], F32)
        gme_tok = sb("gme_tok", [128, NCH, 32], F32)
        s32a = sb("s32a", [128, 64], F32)
        s32b = sb("s32b", [128, 64], F32)
        rstdc = sb("rstdc", [128, 4], F32)
        bn_t = sb("bn_t", [128, 64], F32)
        cn_t = sb("cn_t", [128, 64], F32)
        thsb = sb("thsb", [128, 1024], F32)
        tcs = sb("tcs", [128, 1024], F32)
        tc2 = sb("tc2", [128, 1024], F32)
        kint = sb("kint", [128, 1024], I32)
        kf = sb("kf", [128, 1024], F32)
        sin_t = sb("sin_t", [128, 1024])
        cos_t = sb("cos_t", [128, 1024])
        vf = sb("vf", [128, DI], F32)
        bm_ch = sb("bm_ch", [128, DI])
        cm_ch = sb("cm_ch", [128, DI])
        carry_row = sb("carry_row", [1, 1024], F32)
        xt_ch = sb("xt_ch", [128, 16, 128])
        xfch = sb("xfch", [128, 16, 128])
        ych = sb("ych", [128, 16, 128])
        hcar = sb("hcar", [64, H, 64], F32)
        gm = sb("gm", [128, 128], F32)
        wg_t = sb("wg_t", [128, 128])
        wf_t = sb("wf_t", [128, 128])
        mt_t = sb("mt_t", [128, 128])
        bfh = sb("bfh", [64, 128])
        cfh = sb("cfh", [64, 128])
        crs = sb("crs", [64, 128])
        bw_t = sb("bw_t", [128, 64])
        er = sb("er", [1, 128], F32)
        wcf = sb("wcf", [128, 1], F32)
        hbf = sb("hbf", [64, 64])
        s64 = sb("s64", [64, 1], F32)
        lastriu = sb("lastriu", [128, 128])
        pS = ps("pS", [128, 3, 512])
        pG = ps("pG", [128, 128])
        pY = ps("pY", [64, 128])
        pH = ps("pH", [64, 64])
        pE = ps("pE", [64, 128])
        pGb = ps("pGb", [128, 128], BF16)
        ptb = pGb[0:64, :]

        @S(k, "sync")
        def _(s):
            k.inc(s.dma_start(out=rows32[:], in_=k.inputs["l2a_dtb"]), "sdma")

        @S(k, "gpsimd")
        def _(g):
            k.inc(g.dma_start(out=wsm[:],
                              in_=k.inputs["l2a_wsmall"].rearrange("(c p) n -> p c n", p=128)), "gdma")
        k.barrier()

        def bcast_row(srcrow, dst, width, extra=None):
            @S(k, "tensor")
            def _(t):
                k.inc(t.matmul(pS[:, 0, 0:width], ones_row(k, 128), srcrow,
                               start=True, stop=True), "pe")

            @S(k, "vector")
            def _(v):
                k.inc(v.tensor_copy(out=dst, in_=pS[:, 0, 0:width]), "dve")

        bcast_row(rows32[0:1, :], dtb_bc[:], 32)

        @S(k, "sync")
        def _(s):
            k.inc(s.dma_start(out=rows32[:], in_=k.inputs["l2a_alog"]), "sdma")

        @S(k, "scalar")
        def _(sc):
            k.inc(sc.activation(out=rows32[:], in_=rows32[:], func=AF.Exp), "act")

        @S(k, "vector")
        def _(v):
            k.inc(v.tensor_scalar(out=rows32[:], in0=rows32[:], scalar1=-1.0,
                                  scalar2=None, op0=ALU.mult), "dve")
        bcast_row(rows32[0:1, :], a_bc[:], 32)

        @S(k, "sync")
        def _(s):
            k.inc(s.dma_start(out=s32a[0:1, 0:64], in_=k.inputs["l2a_bnw"]), "sdma")
            k.inc(s.dma_start(out=s32b[0:1, 0:64], in_=k.inputs["l2a_cnw"]), "sdma")
        bcast_row(s32a[0:1, 0:64], bnw_bc[:], 64)
        bcast_row(s32b[0:1, 0:64], cnw_bc[:], 64)

        @S(k, "sync")
        def _(s):
            k.inc(s.dma_start(out=vf[0:1, 0:DI], in_=k.inputs["l2a_bbias"]), "sdma")
        bcast_row(vf[0:1, 0:512], bbias_bc[:, 0:512], 512)
        bcast_row(vf[0:1, 512:1024], bbias_bc[:, 512:1024], 512)
        bcast_row(vf[0:1, 1024:1536], bbias_bc[:, 1024:1536], 512)
        bcast_row(vf[0:1, 1536:2048], bbias_bc[:, 1536:2048], 512)

        @S(k, "sync")
        def _(s):
            k.inc(s.dma_start(out=vf[0:1, 0:DI], in_=k.inputs["l2a_cbias"]), "sdma")
        bcast_row(vf[0:1, 0:512], cbias_bc[:, 0:512], 512)
        bcast_row(vf[0:1, 512:1024], cbias_bc[:, 512:1024], 512)
        bcast_row(vf[0:1, 1024:1536], cbias_bc[:, 1024:1536], 512)
        bcast_row(vf[0:1, 1536:2048], cbias_bc[:, 1536:2048], 512)

        @S(k, "sync")
        def _(s):
            k.inc(s.dma_start(out=rows32[:], in_=k.inputs["l2a_dd"]), "sdma")

        @S(k, "tensor")
        def _(t):
            k.inc(t.matmul(pG[:, 0:32], ones_row(k, 128), rows32[:], start=True, stop=True), "pe")

        @S(k, "vector")
        def _(v):
            k.inc(v.tensor_copy(out=drep[:], in_=pG[:, 0:32]), "dve")
        k.barrier()

        TWO_PI = float(2.0 * np.pi)
        for s_ in range(B_PER_CORE):
            toff = s_ * L

            @S(k, "vector")
            def _(v):
                k.inc(v.memset(carry_row[:], 0.0), "dve")
            for ch in range(NCH):
                tg = slice(toff + ch * 128, toff + ch * 128 + 128)

                @S(k, "tensor")
                def _(t, ch=ch, tg=tg):
                    for cc in range(3):
                        w_ = 512 if cc < 2 else 192
                        for g in range(DG):
                            ins = t.matmul(pS[:, cc, 0:w_], xn_sb[:, g, tg],
                                           wsm[:, g, cc * 512:cc * 512 + w_],
                                           start=(g == 0), stop=(g == DG - 1))
                    k.inc(ins, "pe")

                @S(k, "vector")
                def _(v, ch=ch):
                    k.inc(v.tensor_add(out=s32a[:, 0:32], in0=pS[:, 0, 0:32], in1=dtb_bc[:, 0:32]), "dve")

                @S(k, "scalar")
                def _(sc, ch=ch):
                    sc.activation(out=s32b[:, 0:32], in_=s32a[:, 0:32], func=AF.Abs)
                    sc.activation(out=s32b[:, 0:32], in_=s32b[:, 0:32], func=AF.Exp, scale=-1.0)
                    k.inc(sc.activation(out=s32b[:, 0:32], in_=s32b[:, 0:32], func=AF.Ln, bias=1.0), "act")

                @S(k, "vector")
                def _(v, ch=ch):
                    def step(ins):
                        c = k.inc(ins, "dve")
                        v.wait_ge(k.sems["dve"], c)
                    step(v.tensor_scalar(out=s32a[:, 0:32], in0=s32a[:, 0:32], scalar1=0.0,
                                         scalar2=None, op0=ALU.max))
                    step(v.tensor_add(out=dt_tok[:, ch, :], in0=s32a[:, 0:32], in1=s32b[:, 0:32]))
                    step(v.tensor_mul(out=la_tok[:, ch, :], in0=dt_tok[:, ch, :], in1=a_bc[:, 0:32]))

                @S(k, "scalar")
                def _(sc, ch=ch):
                    k.inc(sc.activation(out=s32a[:, 0:32], in_=pS[:, 0, 32:64], func=AF.Exp,
                                        scale=-1.0), "act")

                @S(k, "vector")
                def _(v, ch=ch):
                    def step(ins):
                        c = k.inc(ins, "dve")
                        v.wait_ge(k.sems["dve"], c)
                    step(v.tensor_scalar(out=s32a[:, 0:32], in0=s32a[:, 0:32], scalar1=1.0,
                                         scalar2=None, op0=ALU.add))
                    step(v.reciprocal(out=s32a[:, 0:32], in_=s32a[:, 0:32]))
                    step(v.tensor_mul(out=g_tok[:, ch, :], in0=s32a[:, 0:32],
                                      in1=dt_tok[:, ch, :]))
                    step(v.tensor_sub(out=q_tok[:, ch, :], in0=dt_tok[:, ch, :],
                                      in1=g_tok[:, ch, :]))

                for which, nw_bc, out_t in (("b", bnw_bc, bn_t), ("c", cnw_bc, cn_t)):
                    off0 = 64 if which == "b" else 128

                    @S(k, "scalar")
                    def _(sc, off0=off0):
                        k.inc(sc.activation(out=s32a[:, 0:64], in_=pS[:, 0, off0:off0 + 64],
                                            func=AF.Square), "act")

                    @S(k, "vector")
                    def _(v, off0=off0):
                        k.inc(v.reduce_sum(out=rstdc[:, 0:1], in_=s32a[:, 0:64], axis=AX.X), "dve")

                    @S(k, "scalar")
                    def _(sc):
                        sc.activation(out=rstdc[:, 1:2], in_=rstdc[:, 0:1], func=AF.Ln,
                                      bias=1e-5, scale=1.0 / 64.0)
                        k.inc(sc.activation(out=rstdc[:, 1:2], in_=rstdc[:, 1:2],
                                            func=AF.Exp, scale=-0.5), "act")

                    @S(k, "vector")
                    def _(v, off0=off0, nw_bc=nw_bc, out_t=out_t):
                        def step(ins):
                            c = k.inc(ins, "dve")
                            v.wait_ge(k.sems["dve"], c)
                        step(v.tensor_scalar(out=out_t[:], in0=pS[:, 0, off0:off0 + 64],
                                             scalar1=rstdc[:, 1:2], scalar2=None, op0=ALU.mult))
                        step(v.tensor_mul(out=out_t[:], in0=out_t[:], in1=nw_bc[:]))

                @S(k, "vector")
                def _(v, ch=ch):
                    k.inc(v.tensor_copy(out=thsb[:], in_=pS[:].rearrange("p c f -> p (c f)")[:, 192:1216]), "dve")

                @S(k, "tensor")
                def _(t, ch=ch):
                    for half in range(2):
                        hs = slice(half * 512, (half + 1) * 512)
                        t.matmul(pS[:, half, :], k.sb["cs_tril01"][:], thsb[:, hs],
                                 start=True, stop=False)
                        ins = t.matmul(pS[:, half, :], ones_row(k, 128)[0:1, 0:128]
                                       if False else ones_row(k, 128),
                                       carry_row[0:1, hs], start=False, stop=True)
                    k.inc(ins, "pe")

                @S(k, "vector")
                def _(v, ch=ch):
                    k.inc(v.tensor_copy(out=tcs[:, 0:512], in_=pS[:, 0, :]), "dve")
                    k.inc(v.tensor_copy(out=tcs[:, 512:1024], in_=pS[:, 1, :]), "dve")

                for half in range(2):
                    hs = slice(half * 512, (half + 1) * 512)

                    @S(k, "tensor")
                    def _(t, hs=hs):
                        k.inc(t.matmul(pS[0:1, 2, 0:512], k.constcol(1.0), thsb[:, hs],
                                       start=True, stop=True), "pe")

                    @S(k, "vector")
                    def _(v, hs=hs):
                        k.inc(v.tensor_add(out=carry_row[0:1, hs], in0=carry_row[0:1, hs],
                                           in1=pS[0:1, 2, 0:512]), "dve")

                @S(k, "vector")
                def _(v, ch=ch):
                    def step(ins):
                        c = k.inc(ins, "dve")
                        v.wait_ge(k.sems["dve"], c)
                    step(v.tensor_scalar(out=kf[:], in0=tcs[:], scalar1=1.0 / TWO_PI,
                                         scalar2=None, op0=ALU.mult))
                    step(v.tensor_copy(out=kint[:], in_=kf[:]))
                    step(v.tensor_copy(out=kf[:], in_=kint[:]))
                    step(v.scalar_tensor_tensor(out=tc2[:], in0=kf[:], scalar=-TWO_PI,
                                                in1=tcs[:], op0=ALU.mult, op1=ALU.add))
                    step(v.tensor_scalar(out=tcs[:], in0=tcs[:], scalar1=float(np.pi / 2),
                                         scalar2=1.0 / TWO_PI, op0=ALU.add, op1=ALU.mult))
                    step(v.tensor_copy(out=kint[:], in_=tcs[:]))
                    step(v.tensor_copy(out=kf[:], in_=kint[:]))

                @S(k, "scalar")
                def _(sc, ch=ch):
                    k.inc(sc.activation(out=sin_t[:], in_=tc2[:], func=AF.Sin), "act")

                @S(k, "vector")
                def _(v, ch=ch):
                    k.inc(v.tensor_sub(out=tc2[:], in0=tcs[:], in1=kf[:]), "dve")
                    v.wait_ge(k.sems["dve"], k.cnt["dve"])
                    k.inc(v.tensor_scalar(out=tc2[:], in0=tc2[:], scalar1=TWO_PI,
                                          scalar2=None, op0=ALU.mult), "dve")

                @S(k, "scalar")
                def _(sc, ch=ch):
                    k.inc(sc.activation(out=cos_t[:], in_=tc2[:], func=AF.Sin), "act")

                for which, bias_bc, nsb, dsth in (("b", bbias_bc, bn_t, bm_hbm),
                                                  ("c", cbias_bc, cn_t, cm_hbm)):
                    osb_t = bm_ch if which == "b" else cm_ch

                    @S(k, "vector")
                    def _(v, bias_bc=bias_bc, nsb=nsb, osb_t=osb_t):
                        ins = None
                        for h in range(H):
                            ins = v.tensor_add(out=vf[:, h * 64:(h + 1) * 64],
                                               in0=bias_bc[:, h * 64:(h + 1) * 64], in1=nsb[:])
                        k.inc(ins, "dve")
                        v.wait_ge(k.sems["dve"], k.cnt["dve"])
                        v1 = vf[:].rearrange("p (h n) -> p h n", h=H)[:, :, 0:32]
                        v2 = vf[:].rearrange("p (h n) -> p h n", h=H)[:, :, 32:64]
                        o1 = osb_t[:].rearrange("p (h n) -> p h n", h=H)[:, :, 0:32]
                        o2 = osb_t[:].rearrange("p (h n) -> p h n", h=H)[:, :, 32:64]
                        cw = cos_t[:].rearrange("p (h n) -> p h n", h=H, n=32)
                        sw = sin_t[:].rearrange("p (h n) -> p h n", h=H, n=32)
                        scr = tcs[:].rearrange("p (h n) -> p h n", h=H, n=32)
                        def step(ins):
                            c = k.inc(ins, "dve")
                            v.wait_ge(k.sems["dve"], c)
                        step(v.tensor_mul(out=o1, in0=v1, in1=cw))
                        step(v.tensor_mul(out=o2, in0=v2, in1=sw))
                        step(v.tensor_sub(out=o1, in0=o1, in1=o2))
                        step(v.tensor_mul(out=o2, in0=v2, in1=cw))
                        step(v.tensor_mul(out=scr, in0=v1, in1=sw))
                        step(v.tensor_add(out=o2, in0=o2, in1=scr))

                    @S(k, "sync")
                    def _(s, ch=ch, dsth=dsth, osb_t=osb_t):
                        k.inc(s.dma_start(out=dsth[ch * 128:(ch + 1) * 128, :],
                                          in_=osb_t[:]), "st")
            @S(k, "vector")
            def _(v):
                k.inc(v.memset(s32a[:], 0.0), "dve")

            @S(k, "sync")
            def _(s):
                k.inc(s.dma_start(out=q_hbm[0:L, :].rearrange("(c p) n -> p c n", p=128),
                                  in_=q_tok[:]), "st")
                k.inc(s.dma_start(out=q_hbm[L:L + 128, :], in_=s32a[:, 0:32]), "st")
                s.wait_ge(k.sems["st"], k.cnt["st"])
                k.inc(s.dma_start(out=q_tok[:],
                                  in_=q_hbm[1:L + 1, :].rearrange("(c p) n -> p c n", p=128)), "sdma")

            @S(k, "vector")
            def _(v):
                k.inc(v.tensor_add(out=gme_tok[:], in0=g_tok[:], in1=q_tok[:]), "dve")

            @S(k, "scalar")
            def _(sc):
                k.inc(sc.activation(out=lne_tok[:], in_=gme_tok[:], func=AF.Ln), "act")

            @S(k, "vector")
            def _(v):
                k.inc(v.tensor_sub(out=gme_tok[:], in0=g_tok[:], in1=gme_tok[:]), "dve")

            @S(k, "vector")
            def _(v):
                k.inc(v.memset(hcar[:], 0.0), "dve")

            for ch in range(NCH):
                tg = slice(toff + ch * 128, toff + ch * 128 + 128)

                @S(k, "sync")
                def _(s, s_=s_, ch=ch, tg=tg):
                    k.inc(s.dma_start(out=bm_ch[:], in_=bm_hbm[ch * 128:(ch + 1) * 128, :]), "sdma")
                    k.inc(s.dma_start(out=cm_ch[:], in_=cm_hbm[ch * 128:(ch + 1) * 128, :]), "sdma")
                    k.inc(s.dma_start(out=xfch[:],
                                      in_=xf_hbm[:, tg].rearrange("(g p) t -> p g t", p=128)), "sdma")
                for g in range(16):
                    @S(k, "tensor")
                    def _(t, g=g):
                        k.inc(t.transpose(pGb[:], xfch[:, g, :], k.sb["cs_ident_bf"][:]), "pe")

                    @S(k, "scalar")
                    def _(sc, g=g):
                        k.inc(sc.copy(out=xt_ch[:, g, :], in_=pGb[:]), "act")

                for h in range(H):
                    la_col = la_tok[:, ch, h:h + 1]
                    lne_col = lne_tok[:, ch, h:h + 1]
                    gme_col = gme_tok[:, ch, h:h + 1]
                    xsl = xt_ch[:, h // 2, (h % 2) * 64:(h % 2) * 64 + 64]
                    bsl = bm_ch[:, h * 64:(h + 1) * 64]
                    csl = cm_ch[:, h * 64:(h + 1) * 64]

                    @S(k, "vector")
                    def _(v, la_col=la_col):
                        k.inc(v.tensor_scalar(out=lastriu[:], in0=k.sb["cs_striu01_bf"][:],
                                              scalar1=la_col, scalar2=None, op0=ALU.mult), "dve")

                    @S(k, "tensor")
                    def _(t, h=h):
                        t.matmul(pG[:], lastriu[:], k.sb["cs_tril01_bf"][:], start=True, stop=False)
                        t.matmul(pG[:], k.sb["cs_diagbig_bf"][:], k.sb["cs_striu01_bf"][:],
                                 start=False, stop=True)
                        k.inc(t.matmul(pE[0:1, :], la_col, k.sb["cs_tril01"][:],
                                       start=True, stop=True), "pe")
                        k.inc(t.transpose(ptb, bsl, k.sb["cs_ident_bf"][:]), "pe")

                    @S(k, "scalar")
                    def _(sc, h=h):
                        k.inc(sc.copy(out=bfh[:], in_=ptb), "act")
                        k.inc(sc.activation(out=er[:], in_=pE[0:1, :], func=AF.Exp), "act")

                    @S(k, "tensor")
                    def _(t, h=h):
                        k.inc(t.transpose(ptb, csl, k.sb["cs_ident_bf"][:]), "pe")

                    @S(k, "scalar")
                    def _(sc, h=h):
                        k.inc(sc.copy(out=cfh[:], in_=ptb), "act")

                    @S(k, "vector")
                    def _(v, lne_col=lne_col):
                        k.inc(v.tensor_scalar(out=gm[:], in0=pG[:], scalar1=lne_col,
                                              scalar2=None, op0=ALU.add), "dve")

                    @S(k, "scalar")
                    def _(sc):
                        k.inc(sc.activation(out=wg_t[:], in_=gm[:], func=AF.Exp), "act")

                    @S(k, "vector")
                    def _(v, gme_col=gme_col):
                        k.inc(v.scalar_tensor_tensor(out=wf_t[:], in0=k.sb["cs_ident_bf"][:],
                                                     scalar=gme_col, in1=wg_t[:],
                                                     op0=ALU.mult, op1=ALU.add), "dve")

                    @S(k, "tensor")
                    def _(t):
                        k.inc(t.matmul(pG[:], bfh[:], cfh[:], start=True, stop=True), "pe")

                    @S(k, "vector")
                    def _(v, bsl=bsl):
                        k.inc(v.tensor_mul(out=mt_t[:], in0=wf_t[:], in1=pG[:]), "dve")
                        k.inc(v.tensor_copy(out=wcf[:], in_=wg_t[:, 127:128]), "dve")
                        v.wait_ge(k.sems["dve"], k.cnt["dve"])
                        k.inc(v.tensor_scalar(out=bw_t[:], in0=bsl,
                                              scalar1=wcf[:, 0:1], scalar2=None,
                                              op0=ALU.mult), "dve")

                    @S(k, "tensor")
                    def _(t):
                        k.inc(t.matmul(pE[:, 0:128] if False else pE[:],
                                       ones_row(k, 64), er[:], start=True, stop=True), "pe")

                    @S(k, "vector")
                    def _(v, h=h):
                        k.inc(v.tensor_mul(out=crs[:], in0=cfh[:], in1=pE[:]), "dve")
                        k.inc(v.tensor_copy(out=hbf[:], in_=hcar[:, h, :]), "dve")

                    @S(k, "tensor")
                    def _(t, h=h):
                        t.matmul(pY[:], xsl, mt_t[:], start=True, stop=False)
                        k.inc(t.matmul(pY[:], hbf[:], crs[:], start=False, stop=True), "pe")
                        k.inc(t.matmul(pH[:], bw_t[:], xsl, start=True, stop=True), "pe")

                    @S(k, "scalar")
                    def _(sc):
                        k.inc(sc.copy(out=s64[:], in_=pE[:, 127:128]), "act")

                    @S(k, "vector")
                    def _(v, h=h):
                        def step(ins):
                            c = k.inc(ins, "dve")
                            v.wait_ge(k.sems["dve"], c)
                        step(v.scalar_tensor_tensor(out=hcar[:, h, :], in0=hcar[:, h, :],
                                                    scalar=s64[:, 0:1], in1=pH[:],
                                                    op0=ALU.mult, op1=ALU.add))
                        xslf = xfch[64 * (h % 2):64 * (h % 2) + 64, h // 2, :]
                        step(v.scalar_tensor_tensor(out=ych[64 * (h % 2):64 * (h % 2) + 64,
                                                             h // 2, :],
                                                    in0=xslf,
                                                    scalar=drep[64 * (h % 2):64 * (h % 2) + 64,
                                                                h:h + 1],
                                                    in1=pY[:], op0=ALU.mult, op1=ALU.add))

                @S(k, "sync")
                def _(s, s_=s_, ch=ch, tg=tg):
                    k.inc(s.dma_start(out=y_hbm[:, tg].rearrange("(g p) t -> p g t", p=128),
                                      in_=ych[:]), "st")
        k.barrier()

    with ExitStack() as ph:
        sb = lambda n, s, d=BF16: ph.enter_context(nc.sbuf_tensor("l2a3_" + n, s, d))
        ps = lambda n, s, d=F32: ph.enter_context(nc.psum_tensor("l2a3_" + n, s, d))
        wop = sb("wop", [128, 16, 1024])
        nrmc = sb("nrmc", [128, 16], F32)
        y2 = sb("y2", [128, 16, 512])
        zt = sb("zt", [128, 16, 512])
        sq = sb("sq", [128, 512], F32)
        r_row = sb("r_row", [1, 512], F32)
        ytile = sb("ytile", [128, 512], F32)
        pj = ps("pj", [128, 2, 512])
        pst = ps("pst", [1, 512])
        pbc = ps("pbc", [128, 512])

        @S(k, "gpsimd")
        def _(g):
            k.inc(g.dma_start(out=wop[:],
                              in_=k.inputs["l2a_wopT"].rearrange("(c p) n -> p c n", p=128)), "gdma")

        @S(k, "sync")
        def _(s):
            k.inc(s.dma_start(out=nrmc[:], in_=k.inputs["l2a_normw"]), "sdma")
        k.barrier()

        for tc in range(T // 512):
            ts_ = slice(tc * 512, (tc + 1) * 512)

            @S(k, "sync")
            def _(s, ts_=ts_):
                k.inc(s.dma_start(out=y2[:], in_=y_hbm[:, ts_]
                                  .rearrange("(g p) t -> p g t", p=128)), "sdma")
                k.inc(s.dma_start(out=zt[:], in_=zf_hbm[:, ts_]
                                  .rearrange("(g p) t -> p g t", p=128)), "sdma")

            for g in range(16):
                @S(k, "scalar")
                def _(sc, g=g):
                    k.inc(sc.activation(out=sq[:], in_=zt[:, g, :], func=AF.Silu), "act")

                @S(k, "vector")
                def _(v, g=g):
                    k.inc(v.tensor_mul(out=y2[:, g, :], in0=y2[:, g, :], in1=sq[:]), "dve")

            for g in range(16):
                @S(k, "scalar")
                def _(sc, g=g):
                    k.inc(sc.activation(out=sq[:], in_=y2[:, g, :], func=AF.Square), "act")

                @S(k, "tensor")
                def _(t, g=g):
                    k.inc(t.matmul(pst[:], k.constcol(1.0), sq[:],
                                   start=(g == 0), stop=(g == 15)), "pe")

            @S(k, "scalar")
            def _(sc):
                sc.activation(out=r_row[:], in_=pst[0:1, :], func=AF.Ln,
                              bias=1e-5, scale=1.0 / DI)
                k.inc(sc.activation(out=r_row[:], in_=r_row[:], func=AF.Exp, scale=-0.5), "act")

            @S(k, "tensor")
            def _(t):
                k.inc(t.matmul(pbc[:], ones_row(k, 128), r_row[:], start=True, stop=True), "pe")

            for g in range(16):
                @S(k, "vector")
                def _(v, g=g):
                    k.inc(v.tensor_mul(out=y2[:, g, :], in0=y2[:, g, :], in1=pbc[:]), "dve")
                    v.wait_ge(k.sems["dve"], k.cnt["dve"])
                    k.inc(v.tensor_scalar(out=y2[:, g, :], in0=y2[:, g, :],
                                          scalar1=nrmc[:, g:g + 1], scalar2=None,
                                          op0=ALU.mult), "dve")

            for og in range(DG):
                @S(k, "tensor")
                def _(t, og=og):
                    for g in range(16):
                        ins = t.matmul(pj[:, og % 2, :], wop[:, g, og * 128:(og + 1) * 128],
                                       y2[:, g, :], start=(g == 0), stop=(g == 15))
                    k.inc(ins, "pe")

                @S(k, "scalar")
                def _(sc, og=og):
                    k.inc(sc.copy(out=ytile[:], in_=pj[:, og % 2, :]), "act")

                @S(k, "sync")
                def _(s, tc=tc, og=og):
                    i0 = tc * 4
                    dst = a2[i0:i0 + 4, :, og * 128:(og + 1) * 128].rearrange("i p n -> p i n")
                    k.inc(s.dma_start(out=dst, in_=ytile[:].rearrange("p (i n) -> p i n", i=4)), "st")
        k.barrier()



def _swz(xc):
    return np.ascontiguousarray(
        xc.reshape(NT128, 128, DG, 128).transpose(0, 3, 2, 1).reshape(NT128, 128, DG * 128))


def _unswz(a):
    return np.asarray(a, np.float32).reshape(NT128, 128, DG, 128) \
        .transpose(0, 3, 2, 1).reshape(T, D)


_CACHE = {}


def kernel(**inputs):
    x = np.asarray(inputs["L0_attn_x"], np.float32)
    consts = make_consts()
    wA = prep_weights_A(inputs)
    wB = prep_weights_B(inputs)

    if "A" not in _CACHE:
        _CACHE["A"] = build_A()
        _CACHE["B"] = build_B()
    kA, kB = _CACHE["A"], _CACHE["B"]

    in_mapsA = []
    for c in range(NCORES):
        m = {"xin": _swz(x[2 * c:2 * c + 2].reshape(T, D))}
        for name in kA.inputs:
            if name.startswith("c_"):
                m[name] = consts[name[2:]]
            elif name in wA:
                m[name] = wA[name]
        assert len(m) == len(kA.inputs), (sorted(kA.inputs), sorted(m))
        in_mapsA.append(m)
    resA = run_bass_kernel_spmd(kA.nc, in_mapsA, core_ids=list(range(NCORES)))

    counts_all = np.ascontiguousarray(
        np.stack([resA.results[c]["counts"][:, 0] for c in range(NCORES)])).astype(np.float32)

    in_mapsB = []
    for c in range(NCORES):
        m = {"x2": resA.results[c]["x2"],
             "counts_all": counts_all,
             "premask": (np.arange(8) < c).astype(np.float32).reshape(8, 1)}
        for name in kB.inputs:
            if name.startswith("c_"):
                m[name] = consts[name[2:]]
            elif name in wB:
                m[name] = wB[name]
        assert len(m) == len(kB.inputs), (sorted(kB.inputs), sorted(m))
        in_mapsB.append(m)
    resB = run_bass_kernel_spmd(kB.nc, in_mapsB, core_ids=list(range(NCORES)))

    out = np.stack([_unswz(resB.results[c]["xout"]) for c in range(NCORES)])
    return out.reshape(16, 2048, D).astype(np.float32)
